# version 04 (final)
# baseline (speedup 1.0000x reference)
import numpy as np
from contextlib import ExitStack

import concourse.bass as bass
import concourse.mybir as mybir
from concourse.bass_utils import run_bass_kernel_spmd
from concourse.alu_op_type import AluOpType as OP

F32 = mybir.dt.float32
BF16 = mybir.dt.bfloat16
U32 = mybir.dt.uint32
U8 = mybir.dt.uint8
AF = mybir.ActivationFunctionType
AX = mybir.AxisListType
IOA = bass.IndirectOffsetOnAxis

NCORES = 8
T = 2048
D = 1024
NFC = D // 128



def merge_deps(deps):
    best = {}

    def add(d):
        if d is None:
            return
        if isinstance(d, list):
            for x in d:
                add(x)
        else:
            j, v = d
            if v > best.get(j, 0):
                best[j] = v
    add(deps)
    return [(j, v) for j, v in best.items() if v > 0]


class Sig:
    def __init__(self, p, i):
        self.p, self.i, self.n = p, i, 0

    def __call__(self, instr, dma=False):
        n = 16 if dma else 1
        instr.then_inc(self.p.sems[self.i], n)
        self.p.val[self.i] += n
        self.n += n
        return instr


class P:
    def __init__(self, nc, nsem=96, ndma=16, nsw=16):
        self.nc = nc
        self.ctx = ExitStack()
        self.sems = [self.ctx.enter_context(nc.semaphore(f"sm{i}")) for i in range(nsem)]
        self.val = [0] * nsem
        self.block = self.ctx.enter_context(nc.Block(no_gpsimd_drain=True))
        self.ncomp = nsem - ndma - nsw
        self.ndma = ndma
        self.nsw = nsw
        self.nsem = nsem
        self.rr = 0
        self.rrd = 0
        self.rrs = 0
        self.mem_tok = None
        self._names = {}

    def _uname(self, name):
        k = self._names.get(name, 0)
        self._names[name] = k + 1
        return name if k == 0 else f"{name}_{k}"

    def sbuf(self, name, shape, dt=F32, ctx=None):
        return (ctx or self.ctx).enter_context(
            self.nc.sbuf_tensor(self._uname(name), shape, dt))

    def psum(self, name, shape, dt=F32, ctx=None):
        return (ctx or self.ctx).enter_context(
            self.nc.psum_tensor(self._uname(name), shape, dt))

    def scope(self):
        return ExitStack()

    def close_scope(self, sc, tok):
        sc.close()
        self.mem_tok = merge_deps([self.mem_tok, tok])
        if len(self.mem_tok) > 2:
            def body(eng, sig):
                sig(eng.nop())

            self.mem_tok = [self.sec("sync", [], body)]

    def sec(self, eng_name, deps, body, dma=False):
        if dma == "sw":
            i = self.ncomp + self.ndma + self.rrs
            self.rrs = (self.rrs + 1) % self.nsw
        elif dma:
            i = self.ncomp + self.rrd
            self.rrd = (self.rrd + 1) % self.ndma
        else:
            i = self.rr
            self.rr = (self.rr + 1) % self.ncomp
        deps = merge_deps([deps, self.mem_tok])
        sig = Sig(self, i)
        pp = self

        def fn(eng):
            for (j, v) in deps:
                eng.wait_ge(pp.sems[j], v)
            body(eng, sig)

        getattr(self.block, eng_name)(fn)
        assert sig.n > 0, f"section on {eng_name} signed nothing"
        return (i, self.val[i])

    def barrier_close(self):
        pp = self

        def fn(eng):
            for j, v in enumerate(pp.val):
                if v:
                    eng.wait_ge(pp.sems[j], v)

        self.block.sync(fn)
        self.ctx.close()





def fence(p, deps, eng="sync"):
    def body(eng_, sig):
        sig(eng_.nop())
    return p.sec(eng, [deps], body)

def ind_dma(eng, fn):
    h = eng.alloc_register()
    nm, engt = h.name, h.engine
    eng.free_register(h)
    ins = fn()
    try:
        eng.free_register(bass.RegisterHandle(nm, engt))
    except Exception:
        pass
    return ins

def emit_transpose_grid(p, deps, pairs, ident_ap, dt, evac_eng="vector", tag="tp"):
    group = 8 if dt == BF16 else 4
    sc = p.scope()
    ps = [p.psum(tag, [128, group * 128], dt, ctx=sc) for _ in range(2)]
    evac_tok = [None, None]
    last = None
    for g0 in range(0, len(pairs), group):
        chunk = pairs[g0:g0 + group]
        b = (g0 // group) % 2

        def tbody(eng, sig, chunk=chunk, b=b):
            for k, pair in enumerate(chunk):
                src, dst = pair[0], pair[1]
                ident = pair[2] if len(pair) > 2 else ident_ap[0:src.shape[0], 0:src.shape[0]]
                pa, fb = src.shape[0], src.shape[-1]
                ins = eng.transpose(ps[b][0:fb, k * 128:k * 128 + pa], src, ident)
            sig(ins)

        t_tok = p.sec("tensor", [deps, evac_tok[b]], tbody)

        def ebody(eng, sig, chunk=chunk, b=b):
            for k, pair in enumerate(chunk):
                src, dst = pair[0], pair[1]
                pa, fb = src.shape[0], src.shape[-1]
                ins = eng.tensor_copy(dst, ps[b][0:fb, k * 128:k * 128 + pa])
            sig(ins)

        last = p.sec(evac_eng, [t_tok], ebody)
        evac_tok[b] = last
    p.close_scope(sc, [evac_tok[0], evac_tok[1]])
    return last


def load_weight_panel(p, w_dram, K, m0, m1, wbuf, deps):
    kb = (K + 127) // 128

    def body(eng, sig):
        src = w_dram[0:K, m0:m1].rearrange("(kb p) m -> p kb m", p=128)
        sig(eng.dma_start(out=wbuf[:, 0:kb, 0:m1 - m0], in_=src), dma=True)

    return p.sec("gpsimd", deps, body, dma="sw")


def emit_gemm_fm(p, name, deps, w_src, K, Mtot, rhs_tiles, out_fn, *,
                 bias_sb=None, act=None, act_scale=None, Tlen=T, nch=512, panel=512,
                 resid_fm=None, resid_dep=None, w_tiles=None, resid_off=0, w_bf16=False):
    kb_n = (K + 127) // 128
    sc = p.scope()
    stream = w_src is not None
    if stream:
        wbufs = [p.sbuf(f"{name}_w", [128, kb_n, panel], BF16, ctx=sc) for _ in range(2)]
    pss = [p.psum(f"{name}_ps", [128, nch], F32, ctx=sc) for _ in range(2)]
    npanels = (Mtot + panel - 1) // panel if stream else 1
    load_toks = [None] * max(npanels, 1)
    comp_done = [[] for _ in range(max(npanels, 1))]
    if stream:
        load_toks[0] = load_weight_panel(p, w_src, K, 0, min(Mtot, panel), wbufs[0], [deps])
    ps_free = [None, None]
    psi = 0
    toks = []
    mb_list = [(mb * 128, min(128, Mtot - mb * 128)) for mb in range((Mtot + 127) // 128)]
    last_pa = -1
    for mb, (mof, mw) in enumerate(mb_list):
        pa = mof // panel if stream else 0
        if stream and pa != last_pa:
            last_pa = pa
            if pa + 1 < npanels:
                m0, m1 = (pa + 1) * panel, min(Mtot, (pa + 2) * panel)
                dep = [deps] if pa == 0 else [deps, comp_done[pa - 1]]
                load_toks[pa + 1] = load_weight_panel(p, w_src, K, m0, m1, wbufs[(pa + 1) % 2], dep)
        for c0 in range(0, Tlen, nch):
            w = min(nch, Tlen - c0)
            b = psi % 2
            psi += 1
            ps = pss[b]

            def tbody(eng, sig, mb=mb, mof=mof, mw=mw, c0=c0, w=w, ps=ps, pa=pa):
                for k in range(kb_n):
                    kwd = min(128, K - k * 128)
                    if stream:
                        lhsT = wbufs[pa % 2][0:kwd, k, mof - pa * panel:mof - pa * panel + mw]
                    else:
                        lhsT = w_tiles(k, mb)
                    ins = eng.matmul(ps[0:mw, 0:w], lhsT,
                                     rhs_tiles(k)[0:kwd, c0:c0 + w],
                                     start=(k == 0), stop=(k == kb_n - 1))
                sig(ins)

            t_tok = p.sec("tensor", [load_toks[pa] if stream else deps, ps_free[b]], tbody)
            if stream:
                comp_done[pa].append(t_tok)

            def ebody(eng, sig, mb=mb, mw=mw, c0=c0, w=w, ps=ps):
                if resid_fm is not None:
                    dst = resid_fm[:, mb, resid_off + c0:resid_off + c0 + w]
                    if bias_sb is not None:
                        ins = eng.scalar_tensor_tensor(dst, ps[0:mw, 0:w], bias_sb[:, mb:mb + 1],
                                                       dst, OP.add, OP.add)
                    else:
                        ins = eng.tensor_tensor(dst, ps[0:mw, 0:w], dst, OP.add)
                elif act is not None:
                    kw = {}
                    if bias_sb is not None:
                        kw["bias"] = bias_sb[:, mb:mb + 1]
                    if act_scale is not None:
                        kw["scale"] = act_scale
                    ins = eng.activation(out=out_fn(mb, c0, w), in_=ps[0:mw, 0:w], func=act, **kw)
                else:
                    if bias_sb is not None:
                        ins = eng.tensor_scalar(out_fn(mb, c0, w), ps[0:mw, 0:w],
                                                bias_sb[:, mb:mb + 1], None, OP.add)
                    else:
                        ins = eng.tensor_copy(out_fn(mb, c0, w), ps[0:mw, 0:w])
                sig(ins)

            eng_name = "scalar" if (act is not None and resid_fm is None) else "vector"
            e_tok = p.sec(eng_name, [t_tok, resid_dep], ebody)
            ps_free[b] = e_tok
            toks.append(e_tok)
    p.close_scope(sc, toks)
    return toks



def emit_outer_ln(p, deps, x_fm, xn_fm, cst, mean_sb=None, rstd_sb=None,
                  gain_col=None, bias_col=None):
    sc = p.scope()
    ps_sx = p.psum("ln_sx", [1, 512], F32, ctx=sc)
    ps_sq = p.psum("ln_sq", [1, 512], F32, ctx=sc)
    ps_mb = p.psum("ln_mb", [128, 512], F32, ctx=sc)
    ps_rb = p.psum("ln_rb", [128, 512], F32, ctx=sc)
    xb = p.sbuf("ln_xb", [128, 2, 512], BF16, ctx=sc)
    xq = p.sbuf("ln_xq", [128, 2, 512], BF16, ctx=sc)
    stats = p.sbuf("ln_st", [1, 4, T], F32, ctx=sc)
    own_bc = mean_sb is None
    if own_bc:
        mean_sb = p.sbuf("ln_meanb", [128, T], F32, ctx=sc)
        rstd_sb = p.sbuf("ln_rstdb", [128, T], F32, ctx=sc)
    ones1x128_f = cst["ones1x128_f"]
    ones128x1_b = cst["ones128x1_b"]
    mean = stats[:, 0, :]
    ex2 = stats[:, 1, :]
    msq = stats[:, 2, :]
    rstd = stats[:, 3, :]
    st_toks = []
    for ci, c0 in enumerate(range(0, T, 512)):
        mm_tok = None
        cast_toks = []
        for fc in range(NFC):
            pb = fc % 2

            def cbody(eng, sig, fc=fc, c0=c0, pb=pb):
                eng.activation(out=xb[:, pb, :], in_=x_fm[:, fc, c0:c0 + 512], func=AF.Identity)
                sig(eng.activation(out=xq[:, pb, :], in_=x_fm[:, fc, c0:c0 + 512], func=AF.Square))

            prev_mm = mm_tok if fc >= 2 else None
            c_tok = p.sec("scalar", [deps, cast_toks[fc - 2] if fc >= 2 else None, prev_mm,
                                     st_toks[-1] if st_toks else None], cbody)
            cast_toks.append(c_tok)

            def mbody(eng, sig, fc=fc, pb=pb):
                eng.matmul(ps_sx[:, :], ones128x1_b[:, 0:1], xb[:, pb, :],
                           start=(fc == 0), stop=(fc == NFC - 1))
                sig(eng.matmul(ps_sq[:, :], ones128x1_b[:, 0:1], xq[:, pb, :],
                               start=(fc == 0), stop=(fc == NFC - 1)))

            mm_tok = p.sec("tensor", [c_tok], mbody)

        def sbody(eng, sig, c0=c0):
            eng.tensor_scalar(mean[:, c0:c0 + 512], ps_sx[:, :], 1.0 / D, None, OP.mult)
            eng.tensor_scalar(ex2[:, c0:c0 + 512], ps_sq[:, :], 1.0 / D, None, OP.mult)
            eng.drain()
            eng.tensor_tensor(msq[:, c0:c0 + 512], mean[:, c0:c0 + 512],
                              mean[:, c0:c0 + 512], OP.mult)
            eng.drain()
            sig(eng.scalar_tensor_tensor(ex2[:, c0:c0 + 512], ex2[:, c0:c0 + 512], 1e-5,
                                         msq[:, c0:c0 + 512], OP.add, OP.subtract))

        st_toks.append(p.sec("vector", [mm_tok], sbody))

    def abody(eng, sig):
        eng.activation(out=msq, in_=ex2, func=AF.Ln)
        eng.drain()
        sig(eng.activation(out=rstd, in_=msq, func=AF.Exp, scale=-0.5))

    a_tok = p.sec("scalar", [st_toks], abody)

    bc_toks = []
    ev_toks = []
    for ci, c0 in enumerate(range(0, T, 512)):
        def bcast(eng, sig, c0=c0):
            eng.matmul(ps_mb[:, :], ones1x128_f[0:1, :], mean[0:1, c0:c0 + 512],
                       start=True, stop=True)
            sig(eng.matmul(ps_rb[:, :], ones1x128_f[0:1, :], rstd[0:1, c0:c0 + 512],
                           start=True, stop=True))

        b_tok = p.sec("tensor", [a_tok, ev_toks[-1] if ev_toks else None], bcast)
        bc_toks.append(b_tok)

        def ebody(eng, sig, c0=c0):
            eng.tensor_copy(mean_sb[:, c0:c0 + 512], ps_mb[:, :])
            sig(eng.tensor_copy(rstd_sb[:, c0:c0 + 512], ps_rb[:, :]))

        ev_toks.append(p.sec("vector", [b_tok], ebody))

    n_toks = []
    for fc in range(NFC):
        def nbody(eng, sig, fc=fc):
            for c0 in range(0, T, 512):
                pb = (c0 // 512) % 2
                tmp = xb[:, pb, :]
                eng.tensor_tensor(tmp, x_fm[:, fc, c0:c0 + 512], mean_sb[:, c0:c0 + 512],
                                  OP.subtract)
                eng.drain()
                ins = eng.tensor_tensor(xn_fm[:, fc, c0:c0 + 512], tmp,
                                        rstd_sb[:, c0:c0 + 512], OP.mult)
                if gain_col is not None:
                    eng.drain()
                    ins = eng.tensor_scalar(xn_fm[:, fc, c0:c0 + 512],
                                            xn_fm[:, fc, c0:c0 + 512],
                                            gain_col[:, fc:fc + 1], bias_col[:, fc:fc + 1],
                                            OP.mult, OP.add)
            sig(ins)

        n_toks.append(p.sec("vector", [ev_toks], nbody))
    last = merge_deps(n_toks)
    p.close_scope(sc, last)
    return last



def _col128(v):
    v = np.asarray(v, np.float32)
    n = v.shape[0]
    assert n % 128 == 0
    return np.ascontiguousarray(v.reshape(n // 128, 128).T)


def make_consts():
    c = {}
    c["ident"] = np.eye(128, dtype=np.float32)
    import ml_dtypes
    c["identb"] = np.eye(128, dtype=np.float32).astype(ml_dtypes.bfloat16)
    c["ones1x128_b"] = np.ones((1, 128), np.float32).astype(ml_dtypes.bfloat16)
    c["ones128x1_b"] = np.ones((128, 1), np.float32).astype(ml_dtypes.bfloat16)
    c["ones1x128_f"] = np.ones((1, 128), np.float32)
    hi = np.zeros((128, 64), np.float32)
    hi[64:128, :] = np.eye(64)
    c["identb_hi"] = hi.astype(ml_dtypes.bfloat16)
    return c


def declare_consts(nc, c):
    aps = {}
    for k, v in c.items():
        dt = {np.dtype("float32"): F32}.get(v.dtype, None)
        if dt is None:
            import ml_dtypes
            dt = BF16 if v.dtype == ml_dtypes.bfloat16 else U32
        aps[k] = nc.dram_tensor(k, list(v.shape), dt, kind="ExternalInput").ap()
    return aps


def load_const_tiles(p, aps, specs):
    def body(eng, sig):
        for name, tile in specs:
            sig(eng.dma_start(out=tile, in_=aps[name]), dma=True)
    return p.sec("sync", [], body, dma=True)



def emit_load_x(p, x_dram, x_tm):
    def body(eng, sig):
        src = x_dram.rearrange("(tc p) d -> p tc d", p=128)
        sig(eng.dma_start(out=x_tm[:, :, :], in_=src), dma=True)
    return p.sec("sync", [], body, dma=True)


def emit_tm_to_fm(p, deps, tm, fm, ident, dt, nI=16, nJ=NFC, tag="t2f"):
    pairs = []
    for j in range(nJ):
        for i in range(nI):
            pairs.append((tm[:, i, j * 128:(j + 1) * 128], fm[:, j, i * 128:(i + 1) * 128]))
    return emit_transpose_grid(p, deps, pairs, ident, dt, tag=tag)


def emit_fm_to_tm(p, deps, fm, tm, ident, dt, nJ=NFC, nI=16, tag="f2t"):
    pairs = []
    for j in range(nJ):
        for i in range(nI):
            pairs.append((fm[:, j, i * 128:(i + 1) * 128], tm[:, i, j * 128:(j + 1) * 128]))
    return emit_transpose_grid(p, deps, pairs, ident, dt, tag=tag)


def emit_store_x(p, deps, x_fm, y_dram, ident, scratch_tm):
    t_tok = emit_fm_to_tm(p, deps, x_fm, scratch_tm, ident, F32, tag="sto")

    def body(eng, sig):
        dst = y_dram.rearrange("(tc p) d -> p tc d", p=128)
        sig(eng.dma_start(out=dst, in_=scratch_tm[:, :, :]), dma=True)
    return p.sec("sync", [t_tok], body)



def l0_qkv_perm():
    perm = []
    for g in range(4):
        for pair in range(2):
            for part in (0, 64, 128):
                for jj in (0, 1):
                    h = g * 4 + pair * 2 + jj
                    base = h * 192 + part
                    perm.extend(range(base, base + 64))
    return np.array(perm)


def emit_l0_attn(p, deps, xn_fm, x_fm, io, cst, x_dep):
    sc_all = p.scope()
    qkvg = p.sbuf("l0_qkvg", [128, 6, T], BF16, ctx=sc_all)
    v_tm = p.sbuf("l0_vtm", [128, 16, 256], BF16, ctx=sc_all)
    values = p.sbuf("l0_val", [128, 2, T], BF16, ctx=sc_all)
    S_sb = p.sbuf("l0_S", [128, T], F32, ctx=sc_all)
    P_sb = p.sbuf("l0_P", [128, T], BF16, ctx=sc_all)
    PT_sb = p.sbuf("l0_PT", [128, 16, 128], BF16, ctx=sc_all)
    mcol = p.sbuf("l0_m", [128, 4], F32, ctx=sc_all)
    identb = cst["identb"]
    x_tok = x_dep
    for g in range(4):
        def qout(mb, c0, w, qkvg=qkvg):
            return qkvg[:, mb, c0:c0 + w]
        gdeps = [deps] if g == 0 else [deps, attn_done]
        qkv_toks = emit_gemm_fm(
            p, "l0qkv", gdeps, io["Wqkv"][:, g * 768:(g + 1) * 768], D, 768,
            lambda kb: xn_fm[:, kb, :], qout,
            bias_sb=io["bqkvT_sb"][:, g * 6:(g + 1) * 6])
        pairs = []
        for j in range(4):
            tv, ov = 3 * (j // 2) + 2, 64 * (j % 2)
            for tc in range(16):
                idn = cst["identb"][0:64, 0:64] if ov == 0 else cst["identb_hi"][64:128, 0:64]
                pairs.append((qkvg[:, tv, tc * 128:(tc + 1) * 128][ov:ov + 64, :],
                              v_tm[:, tc, j * 64:(j + 1) * 64], idn))
        vt_tok = emit_transpose_grid(p, qkv_toks, pairs, identb, BF16, tag="l0vt")

        sc = p.scope()
        ps_s = [p.psum("l0_sps", [128, 512], F32, ctx=sc) for _ in range(4)]
        ps_o = p.psum("l0_ops", [128, 128], F32, ctx=sc)
        prev_cycle = None if g == 0 else attn_done
        prev_pv = None if g == 0 else attn_done
        attn_toks = []
        for j in range(4):
            tq, oq = 3 * (j // 2), 64 * (j % 2)
            tk, ok = 3 * (j // 2) + 1, 64 * (j % 2)
            for qb in range(16):
                kvext = (qb + 1) * 128
                nkc = (kvext + 511) // 512

                def sbody(eng, sig, tq=tq, oq=oq, tk=tk, ok=ok, qb=qb, kvext=kvext, nkc=nkc):
                    for c in range(nkc):
                        w = min(512, kvext - c * 512)
                        ins = eng.matmul(ps_s[c][:, 0:w],
                                         qkvg[:, tq, qb * 128:(qb + 1) * 128][oq:oq + 64, :],
                                         qkvg[:, tk, c * 512:c * 512 + w][ok:ok + 64, :],
                                         start=True, stop=True)
                    sig(ins)

                s_tok = p.sec("tensor", [vt_tok, prev_cycle], sbody)

                def evbody(eng, sig, kvext=kvext, nkc=nkc):
                    for c in range(nkc):
                        w = min(512, kvext - c * 512)
                        ins = eng.copy(out=S_sb[:, c * 512:c * 512 + w], in_=ps_s[c][:, 0:w])
                    sig(ins)

                ev_tok = p.sec("scalar", [s_tok], evbody)

                def mkbody(eng, sig, qb=qb):
                    sig(eng.affine_select(S_sb[:, qb * 128:(qb + 1) * 128],
                                          S_sb[:, qb * 128:(qb + 1) * 128],
                                          pattern=[[-1, 128]], compare_op=OP.is_ge,
                                          fill=cst["regs"]["neg1e9"], base=0,
                                          channel_multiplier=1))

                mk_tok = p.sec("gpsimd", [ev_tok], mkbody)

                def mxbody(eng, sig, kvext=kvext):
                    eng.tensor_reduce(mcol[:, 0:1], S_sb[:, 0:kvext], AX.X, OP.max)
                    eng.drain()
                    sig(eng.tensor_scalar(mcol[:, 1:2], mcol[:, 0:1], -0.125, None, OP.mult))

                mx_tok = p.sec("vector", [mk_tok], mxbody)

                def expbody(eng, sig, kvext=kvext):
                    sig(eng.activation(out=P_sb[:, 0:kvext], in_=S_sb[:, 0:kvext], func=AF.Exp,
                                       scale=0.125, bias=mcol[:, 1:2], accum_out=mcol[:, 2:3]))

                ex_tok = p.sec("scalar", [mx_tok], expbody)

                def nrmbody(eng, sig, kvext=kvext):
                    eng.reciprocal(mcol[:, 3:4], mcol[:, 2:3])
                    eng.drain()
                    sig(eng.tensor_scalar(P_sb[:, 0:kvext], P_sb[:, 0:kvext],
                                          mcol[:, 3:4], None, OP.mult))

                nr_tok = p.sec("vector", [ex_tok], nrmbody)

                tp_pairs = [(P_sb[:, c * 128:(c + 1) * 128], PT_sb[:, c, :])
                            for c in range(qb + 1)]
                pt_tok = emit_transpose_grid(p, [nr_tok], tp_pairs, identb, BF16, tag="l0pt")
                prev_cycle = pt_tok

                def pvbody(eng, sig, j=j, qb=qb):
                    ob = (j % 2) * 64
                    for c in range(qb + 1):
                        ins = eng.matmul(ps_o[ob:ob + 64, :],
                                         v_tm[:, c, j * 64:(j + 1) * 64],
                                         PT_sb[:, c, :],
                                         start=(c == 0), stop=(c == qb))
                    sig(ins)

                pv_tok = p.sec("tensor", [pt_tok, prev_pv], pvbody)

                def vebody(eng, sig, j=j, qb=qb):
                    ob = (j % 2) * 64
                    sig(eng.tensor_copy(values[:, j // 2, qb * 128:(qb + 1) * 128][ob:ob + 64, :],
                                        ps_o[ob:ob + 64, :]))

                ve_tok = p.sec("vector", [pv_tok], vebody)
                prev_pv = ve_tok
                attn_toks.append(ve_tok)
        p.close_scope(sc, merge_deps(attn_toks + [prev_cycle]))

        op_toks = emit_gemm_fm(
            p, "l0op", attn_toks, io["Wout"][g * 256:(g + 1) * 256, :], 256, D,
            lambda kb: values[:, kb, :], None,
            bias_sb=io["boutT_sb"] if g == 0 else None,
            resid_fm=x_fm, resid_dep=x_tok)
        x_tok = op_toks
        attn_done = merge_deps(op_toks)
    p.close_scope(sc_all, attn_done)
    return merge_deps(x_tok)



def declare_io_l0(nc, stage, pref="", xap=None, yap=None):
    io = {}
    io["x"] = xap if xap is not None else \
        nc.dram_tensor("x", [T, D], F32, kind="ExternalInput").ap()
    io["y"] = yap if yap is not None else \
        nc.dram_tensor("y", [T, D], F32, kind="ExternalOutput").ap()
    names = []
    if stage in ("attn", "full"):
        names += [("Wqkv", [D, 3072]), ("bqkvT", [128, 24]), ("Wout", [D, D]),
                  ("boutT", [128, 8])]
    if stage in ("ffn", "full"):
        names += [("wg", [2816, D]), ("wu", [2816, D]), ("wd", [D, 2816])]
    for nm, shp in names:
        io[nm] = nc.dram_tensor(pref + nm, shp, F32, kind="ExternalInput").ap()
    return io


def emit_common_prelude(p, caps, io, xctx=None, cctx=None):
    cst = {}
    cst["ident"] = p.sbuf("c_ident", [128, 128], F32, ctx=cctx)
    cst["identb"] = p.sbuf("c_identb", [128, 128], BF16, ctx=cctx)
    cst["ones1x128_f"] = p.sbuf("c_ones1f", [1, 128], F32, ctx=cctx)
    cst["ones128x1_b"] = p.sbuf("c_onescb", [128, 1], BF16, ctx=cctx)
    cst["identb_hi"] = p.sbuf("c_identbh", [128, 64], BF16, ctx=cctx)
    c_tok = load_const_tiles(p, caps, [("ident", cst["ident"][:, :]),
                                       ("identb", cst["identb"][:, :]),
                                       ("ones1x128_f", cst["ones1x128_f"][:, :]),
                                       ("ones128x1_b", cst["ones128x1_b"][:, :]),
                                       ("identb_hi", cst["identb_hi"][:, :])])
    regs = {}
    scratch1 = p.sbuf("c_scr1", [1, 4], F32, ctx=cctx)

    def rbody(eng, sig):
        regs["neg1e9"] = eng.to_reg(-1e9)
        sig(eng.memset(scratch1[0:1, 0:4], 0.0))

    p.sec("gpsimd", [], rbody)
    cst["regs"] = regs
    x_fm = p.sbuf("x_fm", [128, NFC, T], F32, ctx=xctx)
    sc0 = p.scope()
    x_tm = p.sbuf("x_tm", [128, 16, D], F32, ctx=sc0)
    l_tok = emit_load_x(p, io["x"], x_tm)
    t_tok = emit_tm_to_fm(p, [c_tok, l_tok], x_tm, x_fm, cst["ident"][:, :], F32)
    p.close_scope(sc0, t_tok)
    return cst, x_fm, t_tok


def emit_store_y(p, deps, x_fm, y_dram, cst):
    sc = p.scope()
    s_tm = p.sbuf("sto_tm", [128, 16, D], F32, ctx=sc)
    t_tok = emit_fm_to_tm(p, deps, x_fm, s_tm, cst["ident"][:, :], F32, tag="sto")

    def body(eng, sig):
        dst = y_dram.rearrange("(tc p) d -> p tc d", p=128)
        sig(eng.dma_start(out=dst, in_=s_tm[:, :, :]), dma=True)

    tok = p.sec("sync", [t_tok], body, dma=True)
    p.close_scope(sc, tok)
    return tok


def build_l0(stage="attn"):
    nc = bass.Bass("TRN2", target_bir_lowering=False)
    consts = make_consts()
    caps = declare_consts(nc, consts)
    io = declare_io_l0(nc, stage)
    p = P(nc)
    cst, x_fm, t_tok = emit_common_prelude(p, caps, io)
    do_ffn = stage in ("ffn", "full")
    do_attn = stage in ("attn", "full")
    xq_fm = p.sbuf("xq_fm", [128, NFC, T], BF16) if do_ffn else None
    if do_attn:
        io2 = {"Wqkv": io["Wqkv"], "Wout": io["Wout"]}
        io2["bqkvT_sb"] = p.sbuf("bqkvT_sb", [128, 24], F32)
        io2["boutT_sb"] = p.sbuf("boutT_sb", [128, 8], F32)
        b_tok = load_const_tiles(p, {"bqkvT": io["bqkvT"], "boutT": io["boutT"]},
                                 [("bqkvT", io2["bqkvT_sb"][:, :]),
                                  ("boutT", io2["boutT_sb"][:, :])])
    scA = p.scope()
    xn_fm = p.sbuf("xn_fm", [128, NFC, T], BF16, ctx=scA)
    scB = p.scope()
    mean_sb = p.sbuf("ln_mean", [128, T], F32, ctx=scB)
    rstd_sb = p.sbuf("ln_rstd", [128, T], F32, ctx=scB)
    n_tok = emit_outer_ln(p, [t_tok], x_fm, xn_fm, cst, mean_sb, rstd_sb)
    x_tok = n_tok
    if do_ffn:
        q_tok = emit_quant_x_fm(p, [n_tok], x_fm, mean_sb, rstd_sb, xq_fm)
        p.close_scope(scB, q_tok)
    else:
        p.close_scope(scB, n_tok)
    if do_attn:
        x_tok = emit_l0_attn(p, [n_tok, b_tok], xn_fm, x_fm, io2, cst, x_tok)
    p.close_scope(scA, merge_deps([x_tok, q_tok if do_ffn else None]))
    if do_ffn:
        io["wgq"] = nc.dram_tensor("wgq", [D, 2816], BF16).ap()
        io["wuq"] = nc.dram_tensor("wuq", [D, 2816], BF16).ap()
        io["wdq"] = nc.dram_tensor("wdq", [2816, D], BF16).ap()
        x_tok = emit_l0_ffn(p, [q_tok], xq_fm, x_fm, io, cst, x_tok)
    emit_store_y(p, [x_tok], x_fm, io["y"], cst)
    p.barrier_close()
    return nc, consts



def emit_fp4_quant(p, deps, src, dst, scr, W):
    G = W // 16

    def abody(eng, sig):
        sig(eng.activation(out=scr["a"][:, 0:W], in_=src, func=AF.Abs))

    a_tok = p.sec("scalar", [deps, scr.get("last")], abody)

    def vbody(eng, sig):
        a = scr["a"][:, 0:W]
        a3 = scr["a"][:, 0:W].rearrange("p (g k) -> p g k", k=16)
        q = scr["q"][:, 0:W]
        r = scr["r"][:, 0:W]
        m = scr["m"][:, 0:G]
        s = scr["s"][:, 0:G]
        eng.tensor_reduce(m, a3, AX.X, OP.max)
        eng.drain()
        eng.tensor_scalar(s, m, 1.0 / 6.0, 1e-12, OP.mult, OP.max)
        eng.drain()
        eng.reciprocal(m, s)
        eng.drain()
        eng.tensor_tensor(a3, a3, scr["m"][:, 0:G, None].to_broadcast([128, G, 16]), OP.mult)
        eng.drain()
        eng.tensor_scalar(q, a, 0.25, None, OP.is_gt)
        eng.drain()
        for th in (0.75, 1.25, 1.75):
            eng.scalar_tensor_tensor(q, a, th, q, OP.is_gt, OP.add)
            eng.drain()
        eng.tensor_scalar(r, a, 2.5, None, OP.is_gt)
        eng.drain()
        eng.scalar_tensor_tensor(r, a, 3.5, r, OP.is_gt, OP.add)
        eng.drain()
        eng.scalar_tensor_tensor(q, q, 0.5, r, OP.mult, OP.add)
        eng.drain()
        sig(eng.tensor_scalar(r, a, 5.0, 2.0, OP.is_gt, OP.mult))

    v_tok = p.sec("vector", [a_tok], vbody)

    def sbody(eng, sig):
        sig(eng.activation(out=scr["a"][:, 0:W], in_=src, func=AF.Sign))

    s_tok = p.sec("scalar", [v_tok], sbody)

    def v2body(eng, sig):
        a = scr["a"][:, 0:W]
        q = scr["q"][:, 0:W]
        r = scr["r"][:, 0:W]
        eng.tensor_tensor(q, q, r, OP.add)
        eng.drain()
        eng.tensor_tensor(q, q, a, OP.mult)
        eng.drain()
        q3 = scr["q"][:, 0:W].rearrange("p (g k) -> p g k", k=16)
        d3 = dst.rearrange("p (g k) -> p g k", k=16)
        sig(eng.tensor_tensor(d3, q3, scr["s"][:, 0:G, None].to_broadcast([128, G, 16]), OP.mult))

    tok = p.sec("vector", [s_tok], v2body)
    scr["last"] = tok
    return tok


def emit_quant_fm_via_vt(p, deps, src_fm_slice, dst_fm_slice, scr, vt_scr):
    def t1(eng, sig):
        sig(eng.transpose(vt_scr[:, 0:T], src_fm_slice))

    tok1 = p.sec("vector", deps, t1)
    tok2 = emit_fp4_quant(p, [tok1], vt_scr[:, 0:T], vt_scr[:, T:2 * T], scr, T)

    def t2(eng, sig):
        sig(eng.transpose(dst_fm_slice, vt_scr[:, T:2 * T]))

    return p.sec("vector", [tok2], t2)



def mk_quant_scr(p, sc, maxW=2816):
    return {
        "m": p.sbuf("q_m", [128, maxW // 16], F32, ctx=sc),
        "s": p.sbuf("q_s", [128, maxW // 16], F32, ctx=sc),
        "a": p.sbuf("q_a", [128, maxW], F32, ctx=sc),
        "q": p.sbuf("q_q", [128, maxW], BF16, ctx=sc),
        "r": p.sbuf("q_r", [128, maxW], BF16, ctx=sc),
    }


def emit_quant_x_fm(p, deps, x_fm, mean_sb, rstd_sb, xq_fm):
    sc = p.scope()
    scr = mk_quant_scr(p, sc, maxW=2048)
    xnf = p.sbuf("qx_f", [128, T], F32, ctx=sc)
    vt = p.sbuf("qx_vt", [128, T], F32, ctx=sc)
    vtq = p.sbuf("qx_vq", [128, T], BF16, ctx=sc)
    tok = merge_deps([deps])
    for fc in range(NFC):
        def fbody(eng, sig, fc=fc):
            eng.tensor_tensor(xnf[:, :], x_fm[:, fc, :], mean_sb[:, :], OP.subtract)
            eng.drain()
            eng.tensor_tensor(xnf[:, :], xnf[:, :], rstd_sb[:, :], OP.mult)
            eng.drain()
            sig(eng.transpose(vt[:, :], xnf[:, :]))

        f_tok = p.sec("vector", [tok], fbody)
        q_tok = emit_fp4_quant(p, [f_tok], vt[:, :], vtq[:, :], scr, T)

        def tbody(eng, sig, fc=fc):
            sig(eng.transpose(xq_fm[:, fc, :], vtq[:, :]))

        tok = p.sec("vector", [q_tok], tbody)
    p.close_scope(sc, tok)
    return tok


def emit_quant_w_to_dram(p, deps, w_dram, Mrows, K, wq_dram, cst, tag):
    sc = p.scope()
    scr = mk_quant_scr(p, sc, maxW=K)
    wrow = [p.sbuf(f"{tag}_wr", [128, K], F32, ctx=sc) for _ in range(2)]
    wq = [p.sbuf(f"{tag}_wq", [128, K], BF16, ctx=sc) for _ in range(2)]
    wT = [p.sbuf(f"{tag}_wt", [128, K // 128, 128], BF16, ctx=sc) for _ in range(2)]
    identb = cst["identb"]
    prev = [None, None]
    toks = []
    for mb in range(Mrows // 128):
        bi = mb % 2

        def lbody(eng, sig, mb=mb, bi=bi):
            sig(eng.dma_start(out=wrow[bi][:, :], in_=w_dram[mb * 128:(mb + 1) * 128, 0:K]),
                dma=True)

        l_tok = p.sec("sync", [deps, prev[bi]], lbody, dma=True)
        q_tok = emit_fp4_quant(p, [l_tok], wrow[bi][:, :], wq[bi][:, :], scr, K)
        pairs = [(wq[bi][:, kb * 128:(kb + 1) * 128], wT[bi][:, kb, :])
                 for kb in range(K // 128)]
        t_tok = emit_transpose_grid(p, [q_tok], pairs, identb, BF16, tag=f"{tag}_t")

        def obody(eng, sig, mb=mb, bi=bi):
            dst = wq_dram[:, mb * 128:(mb + 1) * 128].rearrange("(kb p) m -> p kb m", p=128)
            sig(eng.dma_start(out=dst, in_=wT[bi][:, :, :]), dma=True)

        o_tok = p.sec("sync", [t_tok], obody, dma=True)
        prev[bi] = o_tok
        toks.append(o_tok)
    last = merge_deps(toks)
    p.close_scope(sc, last)
    return last


def emit_l0_ffn(p, deps, xq_fm, x_fm, io, cst, x_dep):
    IFF = 2816
    IB = IFF // 128
    wq_tok = emit_quant_w_to_dram(p, deps, io["wg"], IFF, D, io["wgq"], cst, "l0g")
    wq_tok = merge_deps([wq_tok, emit_quant_w_to_dram(p, [wq_tok], io["wu"], IFF, D,
                                                      io["wuq"], cst, "l0u")])
    wq_tok = merge_deps([wq_tok, emit_quant_w_to_dram(p, [wq_tok], io["wd"], D, IFF,
                                                      io["wdq"], cst, "l0d")])

    sc_all = p.scope()
    hq = p.sbuf("l0f_hq", [128, IB, 1024], BF16, ctx=sc_all)
    x_tok = x_dep
    for tc in range(2):
        c_base = tc * 1024
        sc = p.scope()
        scr = mk_quant_scr(p, sc, maxW=1024)
        gpan = [p.sbuf("l0f_gp", [128, 8, 512], BF16, ctx=sc) for _ in range(2)]
        upan = [p.sbuf("l0f_up", [128, 8, 512], BF16, ctx=sc) for _ in range(2)]
        gu0 = p.sbuf("l0f_g0", [128, 1024], F32, ctx=sc)
        gu1 = p.sbuf("l0f_g1", [128, 1024], F32, ctx=sc)
        vt = p.sbuf("l0f_vt", [128, 1024], F32, ctx=sc)
        vtq = p.sbuf("l0f_vq", [128, 1024], BF16, ctx=sc)
        ps = [p.psum("l0f_ps", [128, 512], F32, ctx=sc) for _ in range(2)]
        hq_toks = []
        pan_tok = [None, None]
        pan_guard = [None, None]
        for pa in range(6):
            m0 = pa * 512
            m1 = min(IFF, m0 + 512)
            bi = pa % 2

            def lpan(eng, sig, m0=m0, m1=m1, bi=bi):
                for wsrc, buf in ((io["wgq"], gpan), (io["wuq"], upan)):
                    src = wsrc[:, m0:m1].rearrange("(kb p) m -> p kb m", p=128)
                    sig(eng.dma_start(out=buf[bi][:, :, 0:m1 - m0], in_=src), dma=True)

            lp_tok = p.sec("gpsimd", [wq_tok, pan_guard[bi]], lpan, dma="sw")
            pan_tok[bi] = lp_tok
            for mi in range((m1 - m0) // 128):
                mb = pa * 4 + mi
                mm_toks = []
                for half, (pan_, dst_, actf) in enumerate(
                        ((gpan, gu0, AF.Silu), (upan, gu1, None))):
                    for nc2 in range(2):
                        def gmm(eng, sig, pan_=pan_, mi=mi, nc2=nc2, bi=bi, c_base=c_base):
                            for kb in range(8):
                                ins = eng.matmul(
                                    ps[nc2][:, :], pan_[bi][:, kb, mi * 128:(mi + 1) * 128],
                                    xq_fm[:, kb, c_base + nc2 * 512:c_base + (nc2 + 1) * 512],
                                    start=(kb == 0), stop=(kb == 7))
                            sig(ins)

                        mmt = p.sec("tensor", [lp_tok, hq_toks[-1] if hq_toks else None,
                                               mm_toks[-1] if mm_toks else None], gmm)

                        if actf is not None:
                            def gev(eng, sig, dst_=dst_, nc2=nc2):
                                sig(eng.activation(out=dst_[:, nc2 * 512:(nc2 + 1) * 512],
                                                   in_=ps[nc2][:, :], func=actf))
                            ev = p.sec("scalar", [mmt], gev)
                        else:
                            def uev(eng, sig, dst_=dst_, nc2=nc2):
                                sig(eng.tensor_copy(dst_[:, nc2 * 512:(nc2 + 1) * 512],
                                                    ps[nc2][:, :]))
                            ev = p.sec("vector", [mmt], uev)
                        mm_toks.append(ev)

                def hbody(eng, sig):
                    eng.tensor_tensor(gu0[:, :], gu0[:, :], gu1[:, :], OP.mult)
                    eng.drain()
                    sig(eng.transpose(vt[:, :], gu0[:, :]))

                h_tok = p.sec("vector", [mm_toks], hbody)
                q_tok = emit_fp4_quant(p, [h_tok], vt[:, :], vtq[:, :], scr, 1024)

                def hv2(eng, sig, mb=mb):
                    sig(eng.transpose(hq[:, mb, :], vtq[:, :]))

                hq_tok = p.sec("vector", [q_tok], hv2)
                hq_toks.append(hq_tok)
            pan_guard[bi] = merge_deps(hq_toks)
        dn_toks = emit_gemm_fm(p, "l0dn", hq_toks, io["wdq"], IFF, D,
                               lambda kb: hq[:, kb, :], None, Tlen=1024,
                               resid_fm=x_fm, resid_dep=x_tok, resid_off=c_base)
        x_tok = merge_deps([dn_toks])
        p.close_scope(sc, merge_deps([x_tok, pan_tok[0], pan_tok[1]]))
    p.close_scope(sc_all, x_tok)
    return x_tok



def build_l0(stage="attn"):
    nc = bass.Bass("TRN2", target_bir_lowering=False)
    consts = make_consts()
    caps = declare_consts(nc, consts)
    io = declare_io_l0(nc, stage)
    p = P(nc)
    cst, x_fm, t_tok = emit_common_prelude(p, caps, io)
    do_ffn = stage in ("ffn", "full")
    do_attn = stage in ("attn", "full")
    xq_fm = p.sbuf("xq_fm", [128, NFC, T], BF16) if do_ffn else None
    if do_attn:
        io2 = {"Wqkv": io["Wqkv"], "Wout": io["Wout"]}
        io2["bqkvT_sb"] = p.sbuf("bqkvT_sb", [128, 24], F32)
        io2["boutT_sb"] = p.sbuf("boutT_sb", [128, 8], F32)
        b_tok = load_const_tiles(p, {"bqkvT": io["bqkvT"], "boutT": io["boutT"]},
                                 [("bqkvT", io2["bqkvT_sb"][:, :]),
                                  ("boutT", io2["boutT_sb"][:, :])])
    scA = p.scope()
    xn_fm = p.sbuf("xn_fm", [128, NFC, T], BF16, ctx=scA)
    scB = p.scope()
    mean_sb = p.sbuf("ln_mean", [128, T], F32, ctx=scB)
    rstd_sb = p.sbuf("ln_rstd", [128, T], F32, ctx=scB)
    n_tok = emit_outer_ln(p, [t_tok], x_fm, xn_fm, cst, mean_sb, rstd_sb)
    x_tok = n_tok
    if do_ffn:
        q_tok = emit_quant_x_fm(p, [n_tok], x_fm, mean_sb, rstd_sb, xq_fm)
        p.close_scope(scB, q_tok)
    else:
        p.close_scope(scB, n_tok)
    if do_attn:
        x_tok = emit_l0_attn(p, [n_tok, b_tok], xn_fm, x_fm, io2, cst, x_tok)
    p.close_scope(scA, merge_deps([x_tok, q_tok if do_ffn else None]))
    if do_ffn:
        io["wgq"] = nc.dram_tensor("wgq", [D, 2816], BF16).ap()
        io["wuq"] = nc.dram_tensor("wuq", [D, 2816], BF16).ap()
        io["wdq"] = nc.dram_tensor("wdq", [2816, D], BF16).ap()
        x_tok = emit_l0_ffn(p, [q_tok], xq_fm, x_fm, io, cst, x_tok)
    emit_store_y(p, [x_tok], x_fm, io["y"], cst)
    p.barrier_close()
    return nc, consts



def emit_fp4_quant(p, deps, src, dst, scr, W):
    G = W // 16

    def abody(eng, sig):
        sig(eng.activation(out=scr["a"][:, 0:W], in_=src, func=AF.Abs))

    a_tok = p.sec("scalar", [deps, scr.get("last")], abody)

    def vbody(eng, sig):
        a = scr["a"][:, 0:W]
        a3 = scr["a"][:, 0:W].rearrange("p (g k) -> p g k", k=16)
        q = scr["q"][:, 0:W]
        r = scr["r"][:, 0:W]
        m = scr["m"][:, 0:G]
        s = scr["s"][:, 0:G]
        eng.tensor_reduce(m, a3, AX.X, OP.max)
        eng.drain()
        eng.tensor_scalar(s, m, 1.0 / 6.0, 1e-12, OP.mult, OP.max)
        eng.drain()
        eng.reciprocal(m, s)
        eng.drain()
        eng.tensor_tensor(a3, a3, scr["m"][:, 0:G, None].to_broadcast([128, G, 16]), OP.mult)
        eng.drain()
        eng.tensor_scalar(q, a, 0.25, None, OP.is_gt)
        eng.drain()
        for th in (0.75, 1.25, 1.75):
            eng.scalar_tensor_tensor(q, a, th, q, OP.is_gt, OP.add)
            eng.drain()
        eng.tensor_scalar(r, a, 2.5, None, OP.is_gt)
        eng.drain()
        eng.scalar_tensor_tensor(r, a, 3.5, r, OP.is_gt, OP.add)
        eng.drain()
        eng.scalar_tensor_tensor(q, q, 0.5, r, OP.mult, OP.add)
        eng.drain()
        sig(eng.tensor_scalar(r, a, 5.0, 2.0, OP.is_gt, OP.mult))

    v_tok = p.sec("vector", [a_tok], vbody)

    def sbody(eng, sig):
        sig(eng.activation(out=scr["a"][:, 0:W], in_=src, func=AF.Sign))

    s_tok = p.sec("scalar", [v_tok], sbody)

    def v2body(eng, sig):
        a = scr["a"][:, 0:W]
        q = scr["q"][:, 0:W]
        r = scr["r"][:, 0:W]
        eng.tensor_tensor(q, q, r, OP.add)
        eng.drain()
        eng.tensor_tensor(q, q, a, OP.mult)
        eng.drain()
        q3 = scr["q"][:, 0:W].rearrange("p (g k) -> p g k", k=16)
        d3 = dst.rearrange("p (g k) -> p g k", k=16)
        sig(eng.tensor_tensor(d3, q3, scr["s"][:, 0:G, None].to_broadcast([128, G, 16]), OP.mult))

    tok = p.sec("vector", [s_tok], v2body)
    scr["last"] = tok
    return tok


def emit_quant_fm_via_vt(p, deps, src_fm_slice, dst_fm_slice, scr, vt_scr):
    def t1(eng, sig):
        sig(eng.transpose(vt_scr[:, 0:T], src_fm_slice))

    tok1 = p.sec("vector", deps, t1)
    tok2 = emit_fp4_quant(p, [tok1], vt_scr[:, 0:T], vt_scr[:, T:2 * T], scr, T)

    def t2(eng, sig):
        sig(eng.transpose(dst_fm_slice, vt_scr[:, T:2 * T]))

    return p.sec("vector", [tok2], t2)



def mk_quant_scr(p, sc, maxW=2816):
    return {
        "m": p.sbuf("q_m", [128, maxW // 16], F32, ctx=sc),
        "s": p.sbuf("q_s", [128, maxW // 16], F32, ctx=sc),
        "a": p.sbuf("q_a", [128, maxW], F32, ctx=sc),
        "q": p.sbuf("q_q", [128, maxW], BF16, ctx=sc),
        "r": p.sbuf("q_r", [128, maxW], BF16, ctx=sc),
    }


def emit_quant_x_fm(p, deps, x_fm, mean_sb, rstd_sb, xq_fm):
    sc = p.scope()
    scr = mk_quant_scr(p, sc, maxW=2048)
    xnf = p.sbuf("qx_f", [128, T], F32, ctx=sc)
    vt = p.sbuf("qx_vt", [128, T], F32, ctx=sc)
    vtq = p.sbuf("qx_vq", [128, T], BF16, ctx=sc)
    tok = merge_deps([deps])
    for fc in range(NFC):
        def fbody(eng, sig, fc=fc):
            eng.tensor_tensor(xnf[:, :], x_fm[:, fc, :], mean_sb[:, :], OP.subtract)
            eng.drain()
            eng.tensor_tensor(xnf[:, :], xnf[:, :], rstd_sb[:, :], OP.mult)
            eng.drain()
            sig(eng.transpose(vt[:, :], xnf[:, :]))

        f_tok = p.sec("vector", [tok], fbody)
        q_tok = emit_fp4_quant(p, [f_tok], vt[:, :], vtq[:, :], scr, T)

        def tbody(eng, sig, fc=fc):
            sig(eng.transpose(xq_fm[:, fc, :], vtq[:, :]))

        tok = p.sec("vector", [q_tok], tbody)
    p.close_scope(sc, tok)
    return tok


def emit_l0_ffn(p, deps, xq_fm, x_fm, io, cst, x_dep):
    sc_all = p.scope()
    scr = mk_quant_scr(p, sc_all, maxW=2816)
    vt_scr = p.sbuf("l0f_vt", [128, 2 * T], BF16, ctx=sc_all)
    xq_tok = merge_deps([deps])

    IFF = 2816
    IB = IFF // 128
    hq = p.sbuf("l0f_hq", [128, IB, 1024], BF16, ctx=sc_all)
    wrow = [p.sbuf("l0f_wr", [128, 2816], BF16, ctx=sc_all) for _ in range(2)]
    wq = [p.sbuf("l0f_wq", [128, 2816], BF16, ctx=sc_all) for _ in range(2)]
    wT = [p.sbuf("l0f_wT", [128, 22, 128], BF16, ctx=sc_all) for _ in range(2)]
    gu = [vt_scr[:, 1024:2048], vt_scr[:, T + 1024:2 * T]]
    ps = [p.psum("l0f_ps", [128, 512], F32, ctx=sc_all) for _ in range(2)]
    identb = cst["identb"]
    x_tok = x_dep

    def stream_wq(tag, w_dram, mb, K, buf_i, deps2):
        def lbody(eng, sig):
            src = w_dram[mb * 128:(mb + 1) * 128, 0:K].rearrange("p m -> p m")
            sig(eng.dma_start(out=wrow[buf_i][:, 0:K], in_=src), dma=True)

        l_tok = p.sec("gpsimd", deps2, lbody, dma="sw")
        q_tok = emit_fp4_quant(p, [l_tok], wrow[buf_i][:, 0:K], wq[buf_i][:, 0:K], scr, K)
        pairs = [(wq[buf_i][:, kb * 128:(kb + 1) * 128], wT[buf_i][:, kb, :])
                 for kb in range(K // 128)]
        return emit_transpose_grid(p, [q_tok], pairs, identb, BF16, tag=f"l0f_{tag}")

    carry = [None, None]
    for tc in range(2):
        c_base = tc * 1024
        hq_toks = []
        prev = list(carry)
        for mb in range(IB):
            bi = mb % 2
            wg_tok = stream_wq("g", io["wg"], mb, D, bi, [xq_tok, prev[bi]])
            g_toks = []
            for nc_ in range(2):
                def gmm(eng, sig, bi=bi, nc_=nc_, c_base=c_base):
                    for kb in range(8):
                        ins = eng.matmul(ps[nc_][:, :], wT[bi][:, kb, :],
                                         xq_fm[:, kb, c_base + nc_ * 512:c_base + (nc_ + 1) * 512],
                                         start=(kb == 0), stop=(kb == 7))
                    sig(ins)

                mm_tok = p.sec("tensor", [wg_tok, prev[bi]], gmm)

                def gev(eng, sig, nc_=nc_):
                    sig(eng.activation(out=gu[0][:, nc_ * 512:(nc_ + 1) * 512],
                                       in_=ps[nc_][:, :], func=AF.Silu))

                g_toks.append(p.sec("scalar", [mm_tok], gev))
            wu_tok = stream_wq("u", io["wu"], mb, D, bi, [g_toks])
            u_toks = []
            for nc_ in range(2):
                def umm(eng, sig, bi=bi, nc_=nc_, c_base=c_base):
                    for kb in range(8):
                        ins = eng.matmul(ps[nc_][:, :], wT[bi][:, kb, :],
                                         xq_fm[:, kb, c_base + nc_ * 512:c_base + (nc_ + 1) * 512],
                                         start=(kb == 0), stop=(kb == 7))
                    sig(ins)

                mm_tok = p.sec("tensor", [wu_tok, g_toks], umm)

                def uev(eng, sig, nc_=nc_):
                    sig(eng.tensor_copy(gu[1][:, nc_ * 512:(nc_ + 1) * 512], ps[nc_][:, :]))

                u_toks.append(p.sec("vector", [mm_tok], uev))

            def hbody(eng, sig):
                sig(eng.tensor_tensor(gu[0][:, :], gu[0][:, :], gu[1][:, :], OP.mult))

            h_tok = p.sec("vector", [u_toks], hbody)

            def hv1(eng, sig):
                sig(eng.transpose(vt_scr[:, 0:1024], gu[0][:, :]))

            hv1_tok = p.sec("vector", [h_tok], hv1)
            hq_q = emit_fp4_quant(p, [hv1_tok], vt_scr[:, 0:1024], vt_scr[:, T:T + 1024],
                                  scr, 1024)

            def hv2(eng, sig, mb=mb):
                sig(eng.transpose(hq[:, mb, :], vt_scr[:, T:T + 1024]))

            hq_tok = p.sec("vector", [hq_q], hv2)
            hq_toks.append(hq_tok)
            prev[bi] = hq_tok

        prev_d = list(prev)
        for ob in range(NFC):
            bi = ob % 2
            wd_tok = stream_wq("d", io["wd"], ob, IFF, bi, [hq_toks, prev_d[bi]])
            for nc_ in range(2):
                def dmm(eng, sig, bi=bi, nc_=nc_, c_base=c_base):
                    for kb in range(22):
                        ins = eng.matmul(ps[nc_][:, :], wT[bi][:, kb, :],
                                         hq[:, kb, nc_ * 512:(nc_ + 1) * 512],
                                         start=(kb == 0), stop=(kb == 21))
                    sig(ins)

                mm_tok = p.sec("tensor", [wd_tok, prev_d[bi]], dmm)

                def dev(eng, sig, ob=ob, nc_=nc_, c_base=c_base):
                    dst = x_fm[:, ob, c_base + nc_ * 512:c_base + (nc_ + 1) * 512]
                    sig(eng.tensor_tensor(dst, ps[nc_][:, :], dst, OP.add))

                e_tok = p.sec("vector", [mm_tok, x_tok], dev)
                prev_d[bi] = e_tok
        x_tok = merge_deps([x_tok, prev_d[0], prev_d[1]])
        carry = list(prev_d)
    p.close_scope(sc_all, x_tok)
    return x_tok



def l1_q_slotmap():
    slots = [None] * 16
    even = [s for s in range(16) if s % 2 == 0]
    odd = [s for s in range(16) if s % 2 == 1]
    for h in range(16):
        if (h // 4) % 2 == 0:
            slots[h] = even.pop(0)
        else:
            slots[h] = odd.pop(0)
    return slots


def make_l1_consts(c):
    import ml_dtypes
    HD = 64
    nb = HD // 4
    inv = 1.0 / (10000.0 ** (np.arange(nb) / nb))
    ang = np.arange(T)[:, None] * inv[None, :]
    sin_h = np.repeat(np.sin(ang), 2, axis=-1)
    cos_h = np.repeat(np.cos(ang), 2, axis=-1)
    sin = np.concatenate([sin_h, sin_h], -1)
    cos = np.concatenate([cos_h, cos_h], -1)
    c["l1_cos"] = np.ascontiguousarray(np.tile(cos.T, (2, 1))).astype(ml_dtypes.bfloat16)
    c["l1_sin"] = np.ascontiguousarray(np.tile(sin.T, (2, 1))).astype(ml_dtypes.bfloat16)
    R = np.zeros((64, 64), np.float32)
    for i in range(32):
        R[i + 32, i] = -1.0
    for i in range(32, 64):
        R[i - 32, i] = 1.0
    RB = np.zeros((128, 128), np.float32)
    RB[0:64, 0:64] = R
    RB[64:128, 64:128] = R
    c["l1_rot"] = RB.astype(ml_dtypes.bfloat16)
    i_ = np.arange(128)[:, None]
    j_ = np.arange(128)[None, :]
    NEG = -1e9
    Z = np.zeros((128, 128), np.float32)
    D_ = np.where(np.abs(i_ - j_) <= 12, 0.0, NEG) + Z
    L_ = np.where(np.abs(i_ - j_ + 128) <= 12, 0.0, NEG) + Z
    U_ = np.where(np.abs(i_ - j_ - 128) <= 12, 0.0, NEG) + Z
    G_ = np.where(j_ < 8, 0.0, NEG) + Z
    R0 = np.where(i_ < 8, 0.0, NEG) + Z
    combos = [np.maximum(np.maximum(R0, D_), G_),
              np.maximum(R0, U_),
              R0,
              np.maximum(L_, G_),
              G_,
              L_,
              D_,
              U_]
    c["l1_masks"] = np.stack(combos, 1).astype(np.float32)
    return c


def emit_l1_attn(p, deps, xn_fm, x_fm, io, cst, x_dep):
    slots = l1_q_slotmap()
    head_of_slot = [None] * 16
    for h in range(16):
        head_of_slot[slots[h]] = h
    sc_kv = p.scope()
    k_fm = p.sbuf("l1_k", [128, 2, T], BF16, ctx=sc_kv)
    v_fm = p.sbuf("l1_v", [128, 2, T], BF16, ctx=sc_kv)
    k_toks = emit_gemm_fm(p, "l1k", deps, io["Wk"], D, 256,
                          lambda kb: xn_fm[:, kb, :],
                          lambda mb, c0, w: k_fm[:, mb, c0:c0 + w],
                          bias_sb=io["bkT_sb"])
    v_toks = emit_gemm_fm(p, "l1v", deps, io["Wv"], D, 256,
                          lambda kb: xn_fm[:, kb, :],
                          lambda mb, c0, w: v_fm[:, mb, c0:c0 + w],
                          bias_sb=io["bvT_sb"])
    v_tm = p.sbuf("l1_vtm", [128, 16, 256], BF16, ctx=sc_kv)
    pairs = []
    for kvh in range(4):
        tv, ov = kvh // 2, (kvh % 2) * 64
        for tcch in range(16):
            idn = cst["identb"][0:64, 0:64] if ov == 0 else cst["identb_hi"][64:128, 0:64]
            pairs.append((v_fm[:, tv, tcch * 128:(tcch + 1) * 128][ov:ov + 64, :],
                          v_tm[:, tcch, kvh * 64:(kvh + 1) * 64], idn))
    vt_tok = emit_transpose_grid(p, merge_deps(v_toks), pairs, cst["identb"], BF16,
                                 tag="l1vt")

    def emit_rope(tiles, ti, is_q, base_tok, prev_ev, ps_rot, rope_scr, hw_col):
        toks = []
        for c0 in range(0, T, 512):
            def rbody(eng, sig, ti=ti, c0=c0):
                sig(eng.matmul(ps_rot[:, :], cst["l1_rot_sb"][:, :], tiles[:, ti, c0:c0 + 512],
                               start=True, stop=True))

            r_tok = p.sec("tensor", [base_tok, prev_ev], rbody)

            def vbody(eng, sig, ti=ti, c0=c0):
                eng.tensor_tensor(rope_scr[:, :], ps_rot[:, :],
                                  cst["l1_sin"][:, c0:c0 + 512], OP.mult)
                eng.drain()
                eng.tensor_tensor(tiles[:, ti, c0:c0 + 512], tiles[:, ti, c0:c0 + 512],
                                  cst["l1_cos"][:, c0:c0 + 512], OP.mult)
                eng.drain()
                if is_q:
                    eng.tensor_tensor(tiles[:, ti, c0:c0 + 512],
                                      tiles[:, ti, c0:c0 + 512], rope_scr[:, :], OP.add)
                    eng.drain()
                    ins = eng.tensor_scalar(tiles[:, ti, c0:c0 + 512],
                                            tiles[:, ti, c0:c0 + 512],
                                            hw_col, None, OP.mult)
                else:
                    ins = eng.tensor_tensor(tiles[:, ti, c0:c0 + 512],
                                            tiles[:, ti, c0:c0 + 512], rope_scr[:, :],
                                            OP.add)
                sig(ins)

            prev_ev = p.sec("vector", [r_tok], vbody)
            toks.append(prev_ev)
        return merge_deps(toks), prev_ev

    sc_r = p.scope()
    ps_rot = p.psum("l1_rot_ps", [128, 512], F32, ctx=sc_r)
    rope_scr = p.sbuf("l1_ropes", [128, 512], BF16, ctx=sc_r)
    prev_ev = None
    k_rope_toks = []
    for ti in range(2):
        tk_, prev_ev = emit_rope(k_fm, ti, False, merge_deps(k_toks), prev_ev,
                                 ps_rot, rope_scr, None)
        k_rope_toks.append(tk_)
    k_rope = merge_deps(k_rope_toks)
    p.close_scope(sc_r, k_rope)

    masks = cst["l1_masks_sb"]
    x_tok = x_dep
    gdone = None
    for g in range(2):
        sc_g = p.scope()
        q_fm = p.sbuf("l1_q", [128, 4, T], BF16, ctx=sc_g)
        gslots = list(range(g * 8, (g + 1) * 8))
        q_toks = emit_gemm_fm(p, "l1q", [deps, gdone], io["Wq"][:, g * 512:(g + 1) * 512],
                              D, 512, lambda kb: xn_fm[:, kb, :],
                              lambda mb, c0, w: q_fm[:, mb, c0:c0 + w],
                              bias_sb=io["bqT_sb"][:, g * 4:(g + 1) * 4])
        sc_r2 = p.scope()
        ps_rot2 = p.psum("l1_rot_ps2", [128, 512], F32, ctx=sc_r2)
        rope_scr2 = p.sbuf("l1_ropes2", [128, 512], BF16, ctx=sc_r2)
        prev_ev2 = None
        q_rope_toks = []
        for ti in range(4):
            tq_, prev_ev2 = emit_rope(q_fm, ti, True, merge_deps(q_toks), prev_ev2,
                                      ps_rot2, rope_scr2,
                                      io["hwT_sb"][:, g * 4 + ti:g * 4 + ti + 1])
            q_rope_toks.append(tq_)
        q_rope = merge_deps(q_rope_toks)
        p.close_scope(sc_r2, q_rope)

        sc_a = p.scope()
        S_sb = p.sbuf("l1_S", [128, T], F32, ctx=sc_a)
        P_sb = p.sbuf("l1_P", [128, T], BF16, ctx=sc_a)
        PT_sb = p.sbuf("l1_PT", [128, 16, 128], BF16, ctx=sc_a)
        mcol = p.sbuf("l1_m", [128, 4], F32, ctx=sc_a)
        values = p.sbuf("l1_val", [128, 4, T], BF16, ctx=sc_a)
        ps_s = [p.psum("l1_sps", [128, 512], F32, ctx=sc_a) for _ in range(4)]
        ps_o = p.psum("l1_ops", [128, 128], F32, ctx=sc_a)
        prev_cycle = None
        prev_pv = None
        attn_toks = []
        for si in range(8):
            s_ = gslots[si]
            h = head_of_slot[s_]
            kvh = h // 4
            tq, oq = si // 2, (si % 2) * 64
            tk, ok = kvh // 2, (kvh % 2) * 64
            assert oq == ok, (s_, h, kvh)
            for rb in range(16):
                if rb == 0:
                    cbs = list(range(16))
                else:
                    cbs = sorted(set(c for c in (0, rb - 1, rb, rb + 1) if 0 <= c <= 15))

                def mclass(rb, cb):
                    if rb == 0:
                        return 0 if cb == 0 else (1 if cb == 1 else 2)
                    if cb == 0:
                        return 3 if rb == 1 else 4
                    return {rb - 1: 5, rb: 6, rb + 1: 7}[cb]

                def sbody(eng, sig, tq=tq, oq=oq, tk=tk, ok=ok, rb=rb, cbs=cbs):
                    for ci, cb in enumerate(cbs):
                        ins = eng.matmul(
                            ps_s[ci // 4][:, (ci % 4) * 128:(ci % 4 + 1) * 128],
                            q_fm[:, tq, rb * 128:(rb + 1) * 128][oq:oq + 64, :],
                            k_fm[:, tk, cb * 128:(cb + 1) * 128][ok:ok + 64, :],
                            start=True, stop=True)
                    sig(ins)

                s_tok = p.sec("tensor", [q_rope, k_rope, vt_tok, prev_cycle], sbody)

                def evbody(eng, sig, rb=rb, cbs=cbs):
                    for ci, cb in enumerate(cbs):
                        ins = eng.tensor_tensor(
                            S_sb[:, ci * 128:(ci + 1) * 128],
                            ps_s[ci // 4][:, (ci % 4) * 128:(ci % 4 + 1) * 128],
                            masks[:, mclass(rb, cb), :], OP.add)
                    sig(ins)

                ev_tok = p.sec("vector", [s_tok], evbody)
                kvext = len(cbs) * 128

                def mxbody(eng, sig, kvext=kvext):
                    eng.tensor_reduce(mcol[:, 0:1], S_sb[:, 0:kvext], AX.X, OP.max)
                    eng.drain()
                    sig(eng.tensor_scalar(mcol[:, 1:2], mcol[:, 0:1], -0.125, None, OP.mult))

                mx_tok = p.sec("vector", [ev_tok], mxbody)

                def expbody(eng, sig, kvext=kvext):
                    sig(eng.activation(out=P_sb[:, 0:kvext], in_=S_sb[:, 0:kvext],
                                       func=AF.Exp, scale=0.125, bias=mcol[:, 1:2],
                                       accum_out=mcol[:, 2:3]))

                ex_tok = p.sec("scalar", [mx_tok], expbody)

                def nrmbody(eng, sig, kvext=kvext):
                    eng.reciprocal(mcol[:, 3:4], mcol[:, 2:3])
                    eng.drain()
                    sig(eng.tensor_scalar(P_sb[:, 0:kvext], P_sb[:, 0:kvext],
                                          mcol[:, 3:4], None, OP.mult))

                nr_tok = p.sec("vector", [ex_tok], nrmbody)
                tp_pairs = [(P_sb[:, ci * 128:(ci + 1) * 128], PT_sb[:, ci, :])
                            for ci in range(len(cbs))]
                pt_tok = emit_transpose_grid(p, [nr_tok], tp_pairs, cst["identb"], BF16,
                                             tag="l1pt")
                prev_cycle = pt_tok

                def pvbody(eng, sig, si=si, kvh=kvh, cbs=cbs):
                    ob = (si % 2) * 64
                    for ci, cb in enumerate(cbs):
                        ins = eng.matmul(ps_o[ob:ob + 64, :],
                                         v_tm[:, cb, kvh * 64:(kvh + 1) * 64],
                                         PT_sb[:, ci, :],
                                         start=(ci == 0), stop=(ci == len(cbs) - 1))
                    sig(ins)

                pv_tok = p.sec("tensor", [pt_tok, prev_pv], pvbody)

                def vebody(eng, sig, si=si, rb=rb):
                    ob = (si % 2) * 64
                    sig(eng.tensor_copy(
                        values[:, si // 2, rb * 128:(rb + 1) * 128][ob:ob + 64, :],
                        ps_o[ob:ob + 64, :]))

                ve_tok = p.sec("vector", [pv_tok], vebody)
                prev_pv = ve_tok
                attn_toks.append(ve_tok)

        op_toks = emit_gemm_fm(p, "l1op", attn_toks, io["Wo"][g * 512:(g + 1) * 512, :],
                               512, D, lambda kb: values[:, kb, :], None,
                               bias_sb=io["boT_sb"] if g == 0 else None,
                               resid_fm=x_fm, resid_dep=x_tok)
        x_tok = merge_deps(op_toks)
        gdone = x_tok
        p.close_scope(sc_a, x_tok)
        p.close_scope(sc_g, x_tok)
    p.close_scope(sc_kv, x_tok)
    return x_tok



def make_l1f_consts(c):
    import ml_dtypes
    DIM_HEAD, WINDOW = 64, 512
    half_idx = np.arange(0, DIM_HEAD, 2, dtype=np.float64)
    inv_freq = 1.0 / 10000.0 ** (half_idx / DIM_HEAD)
    t = np.arange(T, dtype=np.float64)
    freqs = t[:, None] * inv_freq[None, :]
    freqs = np.concatenate([freqs, freqs], -1)
    base = (half_idx + 0.4 * DIM_HEAD) / (1.4 * DIM_HEAD)
    power = (t - T // 2) / float(WINDOW // 2)
    scale = base[None, :] ** power[:, None]
    scale = np.concatenate([scale, scale], -1)
    cos, sin = np.cos(freqs), np.sin(freqs)
    for nm, arr in (("l1f_cs", cos * scale), ("l1f_ss", sin * scale),
                    ("l1f_ci", cos / scale), ("l1f_si", sin / scale)):
        c[nm] = np.ascontiguousarray(np.tile(arr.T, (2, 1))).astype(ml_dtypes.bfloat16)
    NEG = np.float32(-1e30)
    m = np.zeros((128, 4, 1024), np.float32)
    for s in range(4):
        for r in range(128):
            i_rel = 128 * s + r
            for cchunk in range(8):
                pass
    r_ = np.arange(128)[:, None]
    cloc = np.arange(1024)[None, :]
    for s in range(4):
        i_rel = 128 * s + r_
        j_rel = cloc - 512
        ok = (i_rel - j_rel >= 0) & (i_rel - j_rel <= 512)
        m[:, s, :] = np.where(ok, 0.0, NEG)
    c["l1f_mask"] = m
    return c


def emit_inner_ln(p, deps, src_fm, dst_fm, cst, g_col, b_col, src_dt=BF16):
    sc = p.scope()
    mean_sb = p.sbuf("iln_mean", [128, T], F32, ctx=sc)
    rstd_sb = p.sbuf("iln_rstd", [128, T], F32, ctx=sc)
    tok = emit_outer_ln(p, deps, src_fm, dst_fm, cst, mean_sb, rstd_sb,
                        gain_col=g_col, bias_col=b_col)
    p.close_scope(sc, tok)
    return tok


def emit_y0_to_dram(p, deps, x_fm, mean_sb, rstd_sb, y0_dram):
    sc = p.scope()
    scr = p.sbuf("y0_scr", [128, T], F32, ctx=sc)
    toks = []
    prev = None
    for fc in range(NFC):
        def yb(eng, sig, fc=fc):
            eng.tensor_tensor(scr[:, :], x_fm[:, fc, :], mean_sb[:, :], OP.subtract)
            eng.drain()
            sig(eng.tensor_tensor(scr[:, :], scr[:, :], rstd_sb[:, :], OP.mult))

        v_tok = p.sec("vector", [deps, prev], yb)

        def db(eng, sig, fc=fc):
            sig(eng.dma_start(out=y0_dram[fc, :, :], in_=scr[:, :]), dma=True)

        prev = p.sec("sync", [v_tok], db, dma=True)
        toks.append(prev)
    last = merge_deps(toks)
    p.close_scope(sc, last)
    return last


def emit_l1_ffn(p, deps, y_out_dram, y0_dram, io, cst, cst_store):
    sc_y = p.scope()
    y_fm = p.sbuf("l1f_y", [128, NFC, T], F32, ctx=sc_y)

    def ldy(eng, sig):
        sig(eng.dma_start(out=y_fm[:, :, :],
                          in_=y0_dram.rearrange("fc p t -> p fc t")), dma=True)

    y_tok = p.sec("sync", [deps], ldy, dma=True)

    for il in range(2):
        y_tok = emit_l1f_layer(p, y_tok, y_fm, io, cst, il)

    sc_s = p.scope()
    s_tm = p.sbuf("l1f_stm", [128, 16, D], F32, ctx=sc_s)
    t_tok = emit_fm_to_tm(p, [y_tok], y_fm, s_tm, cst_store["ident"][:, :], F32, tag="l1fst")

    def acc(eng, sig):
        dst = y_out_dram.rearrange("(tc p) d -> p tc d", p=128)
        sig(eng.dma_start(out=dst, in_=s_tm[:, :, :], accum_op=OP.add), dma=True)

    out = p.sec("gpsimd", [t_tok], acc, dma="sw")
    p.close_scope(sc_s, out)
    p.close_scope(sc_y, out)
    return out


def emit_shift_cast(p, deps, y_fm, h_bf, tag):
    toks = []
    for fc in range(NFC):
        if fc < 4:
            def sb(eng, sig, fc=fc):
                sig(eng.activation(out=h_bf[:, fc, :], in_=y_fm[:, fc, :], func=AF.Identity))

            toks.append(p.sec("scalar", [deps], sb))
        else:
            def vb(eng, sig, fc=fc):
                eng.memset(h_bf[:, fc, 0:1], 0.0)
                sig(eng.tensor_copy(h_bf[:, fc, 1:T], y_fm[:, fc, 0:T - 1]))

            toks.append(p.sec("vector", [deps], vb))
    return merge_deps(toks)


def emit_l1f_layer(p, y_tok, y_fm, io, cst, il):
    INNER = 512
    sc1 = p.scope()
    h_bf = p.sbuf("l1f_h1", [128, NFC, T], BF16, ctx=sc1)
    sh_tok = emit_shift_cast(p, [y_tok], y_fm, h_bf, f"l1fs{il}")

    qkv = p.sbuf("l1f_qkv", [128, 12, T], BF16, ctx=sc1)
    qkv_toks = emit_gemm_fm(p, f"l1fqkv{il}", sh_tok, io["wqkv"][il], D, 3 * INNER,
                            lambda kb: h_bf[:, kb, :],
                            lambda mb, c0, w: qkv[:, mb, c0:c0 + w], panel=256)
    qkv_tok = merge_deps(qkv_toks)

    sc2 = p.scope()
    for nm in ("l1f_cs", "l1f_ss", "l1f_ci", "l1f_si"):
        cst[nm] = p.sbuf(nm + "S", [128, T], BF16, ctx=sc2)
    tab_tok = load_const_tiles(p, io["caps"], [(nm, cst[nm][:, :]) for nm in
                                               ("l1f_cs", "l1f_ss", "l1f_ci", "l1f_si")])
    ps_ss = p.psum("l1f_ss_ps", [2, 512], F32, ctx=sc2)
    ps_bc = p.psum("l1f_bc_ps", [128, 512], F32, ctx=sc2)
    ps_rot = p.psum("l1f_rot_ps", [128, 512], F32, ctx=sc2)
    sq_sb = p.sbuf("l1f_sq", [128, T], BF16, ctx=sc2)
    ss_sb = p.sbuf("l1f_ssn", [2, 2, T], F32, ctx=sc2)
    rope_scr = p.sbuf("l1f_rsc", [128, 512], BF16, ctx=sc2)
    rope_toks = []
    prev = None
    for which, toff, sccol, tabs in (("q", 0, "qsT_sb", ("l1f_cs", "l1f_ss")),
                                     ("k", 4, "ksT_sb", ("l1f_ci", "l1f_si"))):
        for ti in range(4):
            tile = qkv[:, toff + ti, :]

            def sqb(eng, sig, tile=tile):
                sig(eng.activation(out=sq_sb[:, :], in_=tile, func=AF.Square))

            sq_tok = p.sec("scalar", [qkv_tok, prev, tab_tok], sqb)

            iv_tok = None
            for cs in range(0, T, 512):
                def smb(eng, sig, cs=cs):
                    sig(eng.matmul(ps_ss[:, :], cst["blk64T_b"][:, 0:2],
                                   sq_sb[:, cs:cs + 512], start=True, stop=True))

                sm_tok = p.sec("tensor", [sq_tok, iv_tok], smb)

                def invb(eng, sig, cs=cs):
                    sig(eng.tensor_scalar(ss_sb[:, 0, cs:cs + 512], ps_ss[:, :],
                                          1e-24, None, OP.max))

                iv_tok = p.sec("vector", [sm_tok], invb)

            def lnb(eng, sig):
                eng.activation(out=ss_sb[:, 1, :], in_=ss_sb[:, 0, :], func=AF.Ln)
                eng.drain()
                sig(eng.activation(out=ss_sb[:, 0, :], in_=ss_sb[:, 1, :], func=AF.Exp,
                                   scale=-0.5))

            ln_tok = p.sec("scalar", [iv_tok], lnb)
            for c0 in range(0, T, 512):
                def bcb(eng, sig, c0=c0):
                    sig(eng.matmul(ps_bc[:, :], cst["blk64_f"][0:2, :], ss_sb[:, 0, c0:c0 + 512],
                                   start=True, stop=True))

                bc_tok = p.sec("tensor", [ln_tok, prev], bcb)

                def nb(eng, sig, tile=tile, c0=c0, sccol=sccol):
                    sig(eng.scalar_tensor_tensor(tile[:, c0:c0 + 512], tile[:, c0:c0 + 512],
                                                 io[sccol][il][:, 0:1], ps_bc[:, :],
                                                 OP.mult, OP.mult))

                n_tok = p.sec("vector", [bc_tok], nb)

                def rotb(eng, sig, tile=tile, c0=c0):
                    sig(eng.matmul(ps_rot[:, :], cst["l1_rot_sb"][:, :], tile[:, c0:c0 + 512],
                                   start=True, stop=True))

                r_tok = p.sec("tensor", [n_tok], rotb)

                def rapp(eng, sig, tile=tile, c0=c0, tabs=tabs):
                    eng.tensor_tensor(rope_scr[:, :], ps_rot[:, :],
                                      cst[tabs[1]][:, c0:c0 + 512], OP.mult)
                    eng.drain()
                    eng.tensor_tensor(tile[:, c0:c0 + 512], tile[:, c0:c0 + 512],
                                      cst[tabs[0]][:, c0:c0 + 512], OP.mult)
                    eng.drain()
                    sig(eng.tensor_tensor(tile[:, c0:c0 + 512], tile[:, c0:c0 + 512],
                                          rope_scr[:, :], OP.add))

                prev = p.sec("vector", [r_tok], rapp)
                rope_toks.append(prev)
    rope_tok = merge_deps(rope_toks)
    p.close_scope(sc2, rope_tok)

    sc3 = p.scope()
    v_tm = p.sbuf("l1f_vtm", [128, 16, 512], BF16, ctx=sc3)
    pairs = []
    for h in range(8):
        tv, ov = 8 + h // 2, (h % 2) * 64
        for tcch in range(16):
            idn = cst["identb"][0:64, 0:64] if ov == 0 else cst["identb_hi"][64:128, 0:64]
            pairs.append((qkv[:, tv, tcch * 128:(tcch + 1) * 128][ov:ov + 64, :],
                          v_tm[:, tcch, h * 64:(h + 1) * 64], idn))
    vt_tok = emit_transpose_grid(p, [qkv_tok], pairs, cst["identb"], BF16, tag="l1fvt")

    S_sb = p.sbuf("l1f_S", [128, 1024], F32, ctx=sc3)
    P_sb = p.sbuf("l1f_P", [128, 1024], BF16, ctx=sc3)
    PT_sb = p.sbuf("l1f_PT", [128, 8, 128], BF16, ctx=sc3)
    mcol = p.sbuf("l1f_m", [128, 4], F32, ctx=sc3)
    values = p.sbuf("l1f_val", [128, 4, T], BF16, ctx=sc3)
    ps_s = [p.psum("l1f_sps", [128, 512], F32, ctx=sc3) for _ in range(2)]
    ps_o = p.psum("l1f_ops", [128, 128], F32, ctx=sc3)
    prev_cycle = None
    prev_pv = None
    attn_toks = []
    for h in range(8):
        tq, oq = h // 2, (h % 2) * 64
        tk, ok = 4 + h // 2, (h % 2) * 64
        for w_ in range(4):
            for s_ in range(4):
                qb = w_ * 4 + s_
                kv0 = (w_ - 1) * 512

                def sbody(eng, sig, tq=tq, oq=oq, tk=tk, ok=ok, qb=qb, kv0=kv0):
                    for half in range(2):
                        a0 = kv0 + half * 512
                        if a0 < 0:
                            continue
                        ins = eng.matmul(ps_s[half][:, :],
                                         qkv[:, tq, qb * 128:(qb + 1) * 128][oq:oq + 64, :],
                                         qkv[:, tk, a0:a0 + 512][ok:ok + 64, :],
                                         start=True, stop=True)
                    sig(ins)

                s_tok = p.sec("tensor", [rope_tok, vt_tok, prev_cycle], sbody)
                mask = cst["l1f_mask_sb"]

                def evbody(eng, sig, s_=s_, kv0=kv0, mask=mask):
                    for half in range(2):
                        a0 = kv0 + half * 512
                        if a0 < 0:
                            ins = eng.memset(S_sb[:, half * 512:(half + 1) * 512], -1e30)
                        else:
                            ins = eng.tensor_tensor(S_sb[:, half * 512:(half + 1) * 512],
                                                    ps_s[half][:, :],
                                                    mask[:, s_, half * 512:(half + 1) * 512],
                                                    OP.add)
                    sig(ins)

                ev_tok = p.sec("vector", [s_tok], evbody)

                def mxbody(eng, sig):
                    eng.tensor_reduce(mcol[:, 0:1], S_sb[:, :], AX.X, OP.max)
                    eng.drain()
                    sig(eng.tensor_scalar(mcol[:, 1:2], mcol[:, 0:1], -8.0, None, OP.mult))

                mx_tok = p.sec("vector", [ev_tok], mxbody)

                def expbody(eng, sig):
                    sig(eng.activation(out=P_sb[:, :], in_=S_sb[:, :], func=AF.Exp,
                                       scale=8.0, bias=mcol[:, 1:2], accum_out=mcol[:, 2:3]))

                ex_tok = p.sec("scalar", [mx_tok], expbody)

                def nrmbody(eng, sig):
                    eng.reciprocal(mcol[:, 3:4], mcol[:, 2:3])
                    eng.drain()
                    sig(eng.tensor_scalar(P_sb[:, :], P_sb[:, :], mcol[:, 3:4], None, OP.mult))

                nr_tok = p.sec("vector", [ex_tok], nrmbody)
                tp_pairs = [(P_sb[:, ci * 128:(ci + 1) * 128], PT_sb[:, ci, :])
                            for ci in range(8)]
                pt_tok = emit_transpose_grid(p, [nr_tok], tp_pairs, cst["identb"], BF16,
                                             tag="l1fpt")
                prev_cycle = pt_tok

                def pvbody(eng, sig, h=h, qb=qb, kv0=kv0):
                    ob = (h % 2) * 64
                    first = True
                    for ci in range(8):
                        a0 = kv0 + ci * 128
                        if a0 < 0:
                            continue
                        tc_i = a0 // 128
                        ins = eng.matmul(ps_o[ob:ob + 64, :],
                                         v_tm[:, tc_i, h * 64:(h + 1) * 64],
                                         PT_sb[:, ci, :],
                                         start=first, stop=(ci == 7))
                        first = False
                    sig(ins)

                pv_tok = p.sec("tensor", [pt_tok, prev_pv], pvbody)

                def vebody(eng, sig, h=h, qb=qb):
                    ob = (h % 2) * 64
                    sig(eng.tensor_copy(
                        values[:, h // 2, qb * 128:(qb + 1) * 128][ob:ob + 64, :],
                        ps_o[ob:ob + 64, :]))

                ve_tok = p.sec("vector", [pv_tok], vebody)
                prev_pv = ve_tok
                attn_toks.append(ve_tok)

    op_toks = emit_gemm_fm(p, f"l1fo{il}", attn_toks, io["wo"][il], INNER, D,
                           lambda kb: values[:, kb, :], None,
                           resid_fm=y_fm, resid_dep=y_tok)
    y_tok2 = merge_deps(op_toks)
    p.close_scope(sc3, y_tok2)
    p.close_scope(sc1, y_tok2)

    HIDP = 2816
    sc4 = p.scope()
    h2ln = p.sbuf("l1f_h2ln", [128, NFC, T], BF16, ctx=sc4)
    sc5 = p.scope()
    h2 = p.sbuf("l1f_h2", [128, NFC, T], BF16, ctx=sc5)
    sh2_tok = emit_shift_cast(p, [y_tok2], y_fm, h2, f"l1fs2{il}")
    ln_tok = emit_inner_ln(p, [sh2_tok], h2, h2ln, cst,
                           io["lngT_sb"][il], io["lnbT_sb"][il])
    p.close_scope(sc5, ln_tok)

    for tc in range(2):
        c_base = tc * 1024
        sc6 = p.scope()
        gg = p.sbuf("l1f_gg", [128, 22, 1024], BF16, ctx=sc6)
        hh = gg
        g_toks = emit_gemm_fm(p, f"l1fg{il}", ln_tok, io["w1p"][il], D, HIDP,
                              lambda kb: h2ln[:, kb, c_base:c_base + 1024],
                              lambda mb, c0, w: gg[:, mb, c0:c0 + w],
                              bias_sb=io["b1gT_sb"][il], act=AF.Gelu, Tlen=1024)
        a_toks = emit_gemm_fm_amul(p, f"l1fa{il}", [g_toks, ln_tok], io["w1p"][il], D, HIDP,
                                   lambda kb: h2ln[:, kb, c_base:c_base + 1024],
                                   gg, hh, io["b1aT_sb"][il], col_off=HIDP, Tlen=1024)
        dn_toks = emit_gemm_fm(p, f"l1fw2{il}", a_toks, io["w2p"][il], HIDP, D,
                               lambda kb: hh[:, kb, :], None, Tlen=1024,
                               bias_sb=io["b2T_sb"][il],
                               resid_fm=y_fm, resid_dep=y_tok2, resid_off=c_base)
        y_tok2 = merge_deps([y_tok2, dn_toks])
        p.close_scope(sc6, y_tok2)
    p.close_scope(sc4, y_tok2)
    return y_tok2


def emit_gemm_fm_amul(p, name, deps, w_src, K, Mtot, rhs_tiles, gg, hh, bias_sb,
                      col_off, Tlen):
    kb_n = (K + 127) // 128
    sc = p.scope()
    wbufs = [p.sbuf(f"{name}_w", [128, kb_n, 512], BF16, ctx=sc) for _ in range(2)]
    pss = [p.psum(f"{name}_ps", [128, 512], F32, ctx=sc) for _ in range(2)]
    npanels = (Mtot + 511) // 512
    load_toks = [None] * npanels
    comp_done = [[] for _ in range(npanels)]
    load_toks[0] = load_weight_panel(p, w_src[:, col_off:], K, 0, 512, wbufs[0], [deps])
    ps_free = [None, None]
    psi = 0
    toks = []
    last_pa = -1
    for mb in range(Mtot // 128):
        pa = (mb * 128) // 512
        if pa != last_pa:
            last_pa = pa
            if pa + 1 < npanels:
                m0, m1 = (pa + 1) * 512, min(Mtot, (pa + 2) * 512)
                dep = [deps] if pa == 0 else [deps, comp_done[pa - 1]]
                load_toks[pa + 1] = load_weight_panel(p, w_src[:, col_off:], K, m0, m1,
                                                      wbufs[(pa + 1) % 2], dep)
        for c0 in range(0, Tlen, 512):
            w = min(512, Tlen - c0)
            b = psi % 2
            psi += 1
            ps = pss[b]

            def tbody(eng, sig, mb=mb, c0=c0, w=w, ps=ps, pa=pa):
                for k in range(kb_n):
                    ins = eng.matmul(ps[:, 0:w],
                                     wbufs[pa % 2][:, k, mb * 128 - pa * 512:mb * 128 - pa * 512 + 128],
                                     rhs_tiles(k)[:, c0:c0 + w],
                                     start=(k == 0), stop=(k == kb_n - 1))
                sig(ins)

            t_tok = p.sec("tensor", [load_toks[pa], ps_free[b]], tbody)
            comp_done[pa].append(t_tok)

            def ebody(eng, sig, mb=mb, c0=c0, w=w, ps=ps):
                sig(eng.scalar_tensor_tensor(hh[:, mb, c0:c0 + w], ps[:, 0:w],
                                             bias_sb[:, mb:mb + 1], gg[:, mb, c0:c0 + w],
                                             OP.add, OP.mult))

            e_tok = p.sec("vector", [t_tok], ebody)
            ps_free[b] = e_tok
            toks.append(e_tok)
    p.close_scope(sc, toks)
    return toks


def declare_io_l1(nc, stage, pref="", xap=None, yap=None):
    io = {"x": xap if xap is not None else
          nc.dram_tensor("x", [T, D], F32, kind="ExternalInput").ap(),
          "y": yap if yap is not None else
          nc.dram_tensor("y", [T, D], F32, kind="ExternalOutput").ap()}
    names = []
    if stage in ("attn", "full"):
        names += [("Wq", [D, D]), ("bqT", [128, 8]), ("Wk", [D, 256]), ("bkT", [128, 2]),
                  ("Wv", [D, 256]), ("bvT", [128, 2]), ("Wo", [D, D]), ("boT", [128, 8]),
                  ("hwT", [128, 8])]
    if stage in ("ffn", "full"):
        names += [("wqkv0", [D, 1536]), ("wqkv1", [D, 1536]),
                  ("wo0", [512, D]), ("wo1", [512, D]),
                  ("w1p0", [D, 5632]), ("w1p1", [D, 5632]),
                  ("w2p0", [2816, D]), ("w2p1", [2816, D]),
                  ("qsT", [128, 2]), ("ksT", [128, 2]),
                  ("lngT", [128, 16]), ("lnbT", [128, 16]),
                  ("b1gT", [128, 44]), ("b1aT", [128, 44]), ("b2T", [128, 16])]
    for nm, shp in names:
        io[nm] = nc.dram_tensor(pref + nm, shp, F32, kind="ExternalInput").ap()
    return io


def host_prep_l1(inp):
    d = {}
    slots = l1_q_slotmap()
    Wq = np.asarray(inp["L1_attn_Wq"])
    bq = np.asarray(inp["L1_attn_bq"])
    Wo = np.asarray(inp["L1_attn_Wo"])
    hw = np.asarray(inp["L1_attn_head_weights"])
    qperm = np.zeros(D, np.int64)
    for h in range(16):
        s_ = slots[h]
        qperm[s_ * 64:(s_ + 1) * 64] = np.arange(h * 64, (h + 1) * 64)
    d["Wq"] = np.ascontiguousarray(Wq[:, qperm])
    d["bqT"] = _col128(bq[qperm])
    d["Wk"] = np.ascontiguousarray(inp["L1_attn_Wk"])
    d["bkT"] = np.ascontiguousarray(np.asarray(inp["L1_attn_bk"]).reshape(2, 128).T)
    d["Wv"] = np.ascontiguousarray(inp["L1_attn_Wv"])
    d["bvT"] = np.ascontiguousarray(np.asarray(inp["L1_attn_bv"]).reshape(2, 128).T)
    d["Wo"] = np.ascontiguousarray(Wo[qperm, :])
    d["boT"] = _col128(np.asarray(inp["L1_attn_bo"]))
    hwT = np.zeros((128, 8), np.float32)
    for h in range(16):
        s_ = slots[h]
        hwT[(s_ % 2) * 64:(s_ % 2) * 64 + 64, s_ // 2] = hw[h]
    d["hwT"] = hwT
    wqkv = np.asarray(inp["L1_ffn_wqkv"])
    wo = np.asarray(inp["L1_ffn_wo"])
    w1 = np.asarray(inp["L1_ffn_w1"])
    b1 = np.asarray(inp["L1_ffn_b1"])
    w2 = np.asarray(inp["L1_ffn_w2"])
    b2 = np.asarray(inp["L1_ffn_b2"])
    HID = 2730
    HIDP = 2816
    for il in range(2):
        d[f"wqkv{il}"] = np.ascontiguousarray(wqkv[il])
        d[f"wo{il}"] = np.ascontiguousarray(wo[il])
        w1a, w1g = w1[il][:, :HID], w1[il][:, HID:]
        w1p = np.zeros((D, 2 * HIDP), np.float32)
        w1p[:, 0:HID] = w1g
        w1p[:, HIDP:HIDP + HID] = w1a
        d[f"w1p{il}"] = w1p
        w2p = np.zeros((HIDP, D), np.float32)
        w2p[0:HID, :] = w2[il]
        d[f"w2p{il}"] = w2p
    d["qsT"] = np.stack([np.tile(np.asarray(inp["L1_ffn_q_scale"][il]), 2) for il in range(2)], 1)
    d["ksT"] = np.stack([np.tile(np.asarray(inp["L1_ffn_k_scale"][il]), 2) for il in range(2)], 1)
    d["lngT"] = np.concatenate([_col128(np.asarray(inp["L1_ffn_ln_g"][il])) for il in range(2)], 1)
    d["lnbT"] = np.concatenate([_col128(np.asarray(inp["L1_ffn_ln_b"][il])) for il in range(2)], 1)
    b1gT = []
    b1aT = []
    b2T = []
    for il in range(2):
        b1a, b1g = b1[il][:HID], b1[il][HID:]
        pg = np.zeros(HIDP, np.float32)
        pg[:HID] = b1g
        pa = np.zeros(HIDP, np.float32)
        pa[:HID] = b1a
        b1gT.append(_col128(pg))
        b1aT.append(_col128(pa))
        b2T.append(_col128(b2[il]))
    d["b1gT"] = np.concatenate(b1gT, 1)
    d["b1aT"] = np.concatenate(b1aT, 1)
    d["b2T"] = np.concatenate(b2T, 1)
    return d


def build_l1(stage="full"):
    import ml_dtypes
    nc = bass.Bass("TRN2", target_bir_lowering=False)
    consts = make_consts()
    make_l1_consts(consts)
    make_l1f_consts(consts)
    consts["l1_masks"] = consts["l1_masks"].astype(ml_dtypes.bfloat16)
    consts["l1f_mask"] = consts["l1f_mask"].astype(ml_dtypes.bfloat16)
    blk = np.zeros((128, 2), np.float32)
    blk[0:64, 0] = 1.0
    blk[64:128, 1] = 1.0
    consts["blk64T_b"] = blk.astype(ml_dtypes.bfloat16)
    consts["blk64_f"] = np.ascontiguousarray(blk.T)
    caps = declare_consts(nc, consts)
    io = declare_io_l1(nc, stage)
    do_attn = stage in ("attn", "full")
    do_ffn = stage in ("ffn", "full")
    y0_dram = nc.dram_tensor("y0scr", [NFC, 128, T], F32).ap()
    p = P(nc)
    small = []
    if do_attn:
        for nm in ("bqT", "bkT", "bvT", "boT", "hwT"):
            io[nm + "_sb"] = p.sbuf(nm + "S", list(io[nm].shape), F32)
            small.append((nm, io[nm + "_sb"][:, :]))
    if do_ffn:
        for nm in ("qsT", "ksT", "lngT", "lnbT", "b1gT", "b1aT", "b2T"):
            t_ = p.sbuf(nm + "S", list(io[nm].shape), F32)
            cols = io[nm].shape[1] // 2
            io[nm + "_sb"] = [t_[:, il * cols:(il + 1) * cols] for il in range(2)]
            small.append((nm, t_[:, :]))
        io["wqkv"] = [io["wqkv0"], io["wqkv1"]]
        io["wo"] = [io["wo0"], io["wo1"]]
        io["w1p"] = [io["w1p0"], io["w1p1"]]
        io["w2p"] = [io["w2p0"], io["w2p1"]]
        io["caps"] = caps
    cst_pre = {}
    cst_pre["blk64T_b"] = p.sbuf("blk64T", [128, 2], BF16)
    cst_pre["blk64_f"] = p.sbuf("blk64f", [2, 128], F32)
    cst_pre["l1_rot_sb"] = p.sbuf("l1rot", [128, 128], BF16)
    small += [("blk64T_b", cst_pre["blk64T_b"][:, :]), ("blk64_f", cst_pre["blk64_f"][:, :]),
              ("l1_rot", cst_pre["l1_rot_sb"][:, :])]

    scSX = p.scope()
    cst, x_fm, t_tok = emit_common_prelude(p, caps, io, xctx=scSX)
    cst.update(cst_pre)
    sm_tok = load_const_tiles(p, dict(list(caps.items()) + list(io.items())), small)
    scX = p.scope()
    xn_fm = p.sbuf("xn_fm", [128, NFC, T], BF16, ctx=scX)
    scMR = p.scope()
    mean_sb = p.sbuf("ln_mean", [128, T], F32, ctx=scMR)
    rstd_sb = p.sbuf("ln_rstd", [128, T], F32, ctx=scMR)
    n_tok = emit_outer_ln(p, [t_tok, sm_tok], x_fm, xn_fm, cst, mean_sb, rstd_sb)
    y0_tok = emit_y0_to_dram(p, [n_tok], x_fm, mean_sb, rstd_sb, y0_dram)
    p.close_scope(scMR, y0_tok)
    x_tok = n_tok
    if do_attn:
        scATT = p.scope()
        cst["l1_cos"] = p.sbuf("l1cos", [128, T], BF16, ctx=scATT)
        cst["l1_sin"] = p.sbuf("l1sin", [128, T], BF16, ctx=scATT)
        cst["l1_masks_sb"] = p.sbuf("l1mask", [128, 8, 128], BF16, ctx=scATT)
        at_tok = load_const_tiles(p, caps, [("l1_cos", cst["l1_cos"][:, :]),
                                            ("l1_sin", cst["l1_sin"][:, :]),
                                            ("l1_masks", cst["l1_masks_sb"][:, :, :])])
        x_tok = emit_l1_attn(p, [n_tok, sm_tok, at_tok], xn_fm, x_fm, io, cst, x_tok)
        p.close_scope(scATT, x_tok)
    p.close_scope(scX, merge_deps([x_tok, y0_tok]))
    st_tok = emit_store_y(p, [x_tok], x_fm, io["y"], cst)
    p.close_scope(scSX, st_tok)
    if do_ffn:
        scF = p.scope()
        cst2 = dict(cst_pre)
        for nm in ("identb", "identb_hi", "ones1x128_f", "ones128x1_b", "ident", "regs"):
            cst2[nm] = cst[nm]
        cst2["l1f_mask_sb"] = p.sbuf("l1fmask", [128, 4, 1024], BF16, ctx=scF)
        f_tok = load_const_tiles(p, caps, [("l1f_mask", cst2["l1f_mask_sb"][:, :, :])])
        emit_l1_ffn(p, [y0_tok, sm_tok, f_tok, st_tok], io["y"], y0_dram, io, cst2, cst)
        p.close_scope(scF, None)
    p.barrier_close()
    return nc, consts



def make_l2_consts(c):
    import ml_dtypes
    oct_ = np.zeros((128, 16, 128), np.float32)
    for g in range(16):
        for pp in range(128):
            oct_[8 * g + pp // 16, g, pp] = 1.0
    c["l2_octmap"] = oct_.astype(ml_dtypes.bfloat16)
    ym = np.zeros((128, 16, 128), np.float32)
    for g in range(16):
        for pp in range(128):
            ym[pp, g, 8 * g + pp // 16] = 1.0
    c["l2_ymap"] = ym.astype(ml_dtypes.bfloat16)
    nm = np.zeros((128, 128), np.float32)
    for pp in range(128):
        nm[pp % 16, pp] = 1.0
        nm[32 + pp % 16, pp] = 1.0
    c["l2_nmap"] = nm.astype(ml_dtypes.bfloat16)
    return c


def emit_l2_attn(p, deps, xn_fm, io, cst, y_out_dram):
    DI = 2048
    DIB = 16
    caps = io["caps"]
    res_dram = io["l2_res_dram"]
    yd_dram = io["l2_y_dram"]
    sc1 = p.scope()
    res_toks = emit_gemm_to_dram(p, "l2res", deps, io["W_in"][:, DI:], D, DI,
                                 lambda kb: xn_fm[:, kb, :], res_dram,
                                 act=AF.Silu)
    xc = p.sbuf("l2_xc", [128, DIB, T], BF16, ctx=sc1)
    scx = p.scope()
    x_in = p.sbuf("l2_xin", [128, DIB, T], BF16, ctx=scx)
    xin_toks = emit_gemm_fm(p, "l2xin", deps, io["W_in"][:, 0:DI], D, DI,
                            lambda kb: xn_fm[:, kb, :],
                            lambda mb, c0, w: x_in[:, mb, c0:c0 + w])
    acc = p.sbuf("l2_cacc", [128, T], F32, ctx=scx)
    conv_toks = []
    prev_c = None
    for ch in range(DIB):
        def cbody(eng, sig, ch=ch):
            eng.tensor_scalar(acc[:, :], x_in[:, ch, :], io["cwT_sb"][:, 3 * 16 + ch:3 * 16 + ch + 1],
                              io["cbT_sb"][:, ch:ch + 1], OP.mult, OP.add)
            for k in range(3):
                sh = 3 - k
                eng.drain()
                ins = eng.scalar_tensor_tensor(acc[:, sh:T], x_in[:, ch, 0:T - sh],
                                               io["cwT_sb"][:, k * 16 + ch:k * 16 + ch + 1],
                                               acc[:, sh:T], OP.mult, OP.add)
            sig(ins)

        c_tok = p.sec("vector", [xin_toks, prev_c], cbody)

        def sbody(eng, sig, ch=ch):
            sig(eng.activation(out=xc[:, ch, :], in_=acc[:, :], func=AF.Silu))

        prev_c = p.sec("scalar", [c_tok], sbody)
        conv_toks.append(prev_c)
    conv_tok = merge_deps(conv_toks)
    p.close_scope(scx, conv_tok)

    sc2 = p.scope()
    dbc = p.sbuf("l2_dbc", [128, T], F32, ctx=sc2)
    ps_dbc = [p.psum("l2_dbc_ps", [128, 512], F32, ctx=sc2) for _ in range(2)]
    wx_sb = p.sbuf("l2_wx", [128, DIB, 128], BF16, ctx=sc2)

    def wxload(eng, sig):
        src = io["Wx_pad"].rearrange("(kb p) m -> p kb m", p=128)
        sig(eng.dma_start(out=wx_sb[:, :, :], in_=src), dma=True)

    wx_tok = p.sec("gpsimd", [deps], wxload, dma="sw")
    dbc_toks = []
    ev_prev = None
    for cs in range(4):
        def dmm(eng, sig, cs=cs):
            for kb in range(DIB):
                ins = eng.matmul(ps_dbc[cs % 2][:, :], wx_sb[:, kb, :],
                                 xc[:, kb, cs * 512:(cs + 1) * 512],
                                 start=(kb == 0), stop=(kb == DIB - 1))
            sig(ins)

        m_tok = p.sec("tensor", [conv_tok, wx_tok, ev_prev], dmm)

        def dev(eng, sig, cs=cs):
            sig(eng.tensor_copy(dbc[:, cs * 512:(cs + 1) * 512], ps_dbc[cs % 2][:, :]))

        ev_prev = p.sec("vector", [m_tok], dev)
        dbc_toks.append(ev_prev)
    dbc_tok = merge_deps(dbc_toks)
    dbc_b = p.sbuf("l2_dbcb", [128, T], BF16, ctx=sc2)

    def dbccast(eng, sig):
        sig(eng.activation(out=dbc_b[:, :], in_=dbc[:, :], func=AF.Identity))

    dbc_tok = p.sec("scalar", [dbc_tok], dbccast)

    A_sb = p.sbuf("l2_A", [128, 256], F32, ctx=sc2)
    Bb = p.sbuf("l2_Bb", [128, T], BF16, ctx=sc2)
    Cb = p.sbuf("l2_Cb", [128, T], BF16, ctx=sc2)

    def abody(eng, sig):
        sig(eng.activation(out=A_sb[:, :], in_=io["AlogT_sb"][:, :], func=AF.Exp, scale=1.0))

    a1 = p.sec("scalar", [deps], abody)

    def a2body(eng, sig):
        sig(eng.tensor_scalar(A_sb[:, :], A_sb[:, :], -1.0, None, OP.mult))

    a_tok = p.sec("vector", [a1], a2body)

    bc_toks = []
    ev2 = None
    for which, dst, base in (("B", Bb, 0), ("C", Cb, 32)):
        for cs in range(4):
            def bmm(eng, sig, base=base, cs=cs):
                sig(eng.matmul(ps_dbc[cs % 2][:, :], cst["l2_nmap_sb"][base:base + 16, :],
                               dbc_b[base:base + 16, cs * 512:(cs + 1) * 512],
                               start=True, stop=True))

            m_tok = p.sec("tensor", [dbc_tok, ev2], bmm)

            def bev(eng, sig, dst=dst, cs=cs):
                sig(eng.tensor_copy(dst[:, cs * 512:(cs + 1) * 512], ps_dbc[cs % 2][:, :]))

            ev2 = p.sec("vector", [m_tok], bev)
            bc_toks.append(ev2)
    bc_tok = merge_deps(bc_toks)

    delta = p.sbuf("l2_delta", [128, T], F32, ctx=sc2)
    delta_b = p.sbuf("l2_deltab", [128, T], BF16, ctx=sc2)
    du = p.sbuf("l2_du", [128, T], BF16, ctx=sc2)
    sp1 = p.sbuf("l2_sp1", [128, T], F32, ctx=sc2)
    dA = p.sbuf("l2_dA", [128, T], BF16, ctx=sc2)
    dBu = p.sbuf("l2_dBu", [128, T], BF16, ctx=sc2)
    h_sb = p.sbuf("l2_h", [128, T], BF16, ctx=sc2)
    ysb = p.sbuf("l2_ysb", [128, T], BF16, ctx=sc2)
    ps_g = [p.psum("l2_gps", [128, 512], F32, ctx=sc2) for _ in range(2)]
    ps_y = [p.psum("l2_yps", [128, 512], F32, ctx=sc2) for _ in range(4)]
    octmap = cst["l2_octmap_sb"]
    ymap = cst["l2_ymap_sb"]
    prev_round = None
    scan_toks = []
    for dc in range(DIB):
        dl_toks = []
        ev3 = None
        for cs in range(4):
            def dtmm(eng, sig, dc=dc, cs=cs):
                sig(eng.matmul(ps_dbc[cs % 2][:, :],
                               io["wdt_sb"][64:128, dc * 128:(dc + 1) * 128],
                               dbc_b[64:128, cs * 512:(cs + 1) * 512], start=True, stop=True))

            m_tok = p.sec("tensor", [dbc_tok, bc_tok, a_tok, io["wdt_tok"], ev3, prev_round], dtmm)

            def spl(eng, sig, dc=dc, cs=cs):
                sl = slice(cs * 512, (cs + 1) * 512)
                eng.activation(out=sp1[:, sl], in_=ps_dbc[cs % 2][:, :], func=AF.Abs,
                               bias=io["bdtT_sb"][:, dc:dc + 1])
                eng.drain()
                eng.activation(out=delta[:, sl], in_=ps_dbc[cs % 2][:, :], func=AF.Relu,
                               bias=io["bdtT_sb"][:, dc:dc + 1])
                sig(eng.activation(out=sp1[:, sl], in_=sp1[:, sl], func=AF.Exp, scale=-1.0))

            ev3 = p.sec("scalar", [m_tok], spl)
            dl_toks.append(ev3)

        def spl2(eng, sig, dc=dc):
            sig(eng.activation(out=sp1[:, :], in_=sp1[:, :], func=AF.Ln, bias=1.0))

        l_tok = p.sec("scalar", [dl_toks], spl2)

        def dsum(eng, sig, dc=dc):
            eng.tensor_tensor(delta[:, :], delta[:, :], sp1[:, :], OP.add)
            eng.drain()
            eng.tensor_copy(delta_b[:, :], delta[:, :])
            eng.drain()
            sig(eng.tensor_tensor(du[:, :], delta[:, :], xc[:, dc, :], OP.mult))

        d_tok = p.sec("vector", [l_tok], dsum)

        y_done = []
        for g in range(16):
            acol = A_sb[:, dc * 16 + g:dc * 16 + g + 1]
            da_toks = []
            for cs in range(4):
                def gmm(eng, sig, g=g, cs=cs):
                    sig(eng.matmul(ps_g[0][:, :], octmap[:, g, :],
                                   delta_b[:, cs * 512:(cs + 1) * 512], start=True, stop=True))

                m1 = p.sec("tensor", [d_tok, da_toks[-1] if da_toks else None,
                                      scan_toks[-1] if scan_toks else None], gmm)

                def dexp(eng, sig, cs=cs, acol=acol):
                    sig(eng.activation(out=dA[:, cs * 512:(cs + 1) * 512], in_=ps_g[0][:, :],
                                       func=AF.Exp, scale=acol))

                e1 = p.sec("scalar", [m1], dexp)

                def gmm2(eng, sig, g=g, cs=cs):
                    sig(eng.matmul(ps_g[1][:, :], octmap[:, g, :],
                                   du[:, cs * 512:(cs + 1) * 512], start=True, stop=True))

                m2 = p.sec("tensor", [e1], gmm2)

                def dbu(eng, sig, cs=cs):
                    sig(eng.tensor_tensor(dBu[:, cs * 512:(cs + 1) * 512], ps_g[1][:, :],
                                          Bb[:, cs * 512:(cs + 1) * 512], OP.mult))

                da_toks.append(p.sec("vector", [m2, bc_tok, a_tok], dbu))

            def scanb(eng, sig):
                eng.tensor_tensor_scan(h_sb[:, :], dA[:, :], dBu[:, :], 0.0, OP.mult, OP.add)
                eng.drain()
                sig(eng.tensor_tensor(h_sb[:, :], h_sb[:, :], Cb[:, :], OP.mult))

            s_tok = p.sec("vector", [da_toks], scanb)
            scan_toks.append(s_tok)

            def ymm(eng, sig, g=g):
                for cs in range(4):
                    ins = eng.matmul(ps_y[cs][:, :], ymap[:, g, :],
                                     h_sb[:, cs * 512:(cs + 1) * 512],
                                     start=(g == 0), stop=(g == 15))
                sig(ins)

            y_done.append(p.sec("tensor", [s_tok, prev_round if g == 0 else None], ymm))

        def yev(eng, sig, dc=dc):
            for cs in range(4):
                sl = slice(cs * 512, (cs + 1) * 512)
                ins = eng.scalar_tensor_tensor(ysb[:, sl], xc[:, dc, sl],
                                               io["DT_sb"][:, dc:dc + 1], ps_y[cs][:, :],
                                               OP.mult, OP.add)
            sig(ins)

        ye_tok = p.sec("vector", [y_done], yev)

        def ydma(eng, sig, dc=dc):
            sig(eng.dma_start(out=yd_dram[dc, :, :], in_=ysb[:, :]), dma=True)

        prev_round = p.sec("sync", [ye_tok], ydma, dma=True)
        scan_toks.append(prev_round)
    scan_done = merge_deps([prev_round])
    p.close_scope(sc2, scan_done)
    p.close_scope(sc1, merge_deps([scan_done, res_toks]))

    sc3 = p.scope()
    yr = p.sbuf("l2_yr", [128, DIB, T], BF16, ctx=sc3)
    ld = p.sbuf("l2_ld", [128, 2, T], BF16, ctx=sc3)
    yr_toks = []
    prev_l = None
    for kb in range(DIB):
        def lbody(eng, sig, kb=kb):
            sig(eng.dma_start(out=ld[:, 0, :], in_=yd_dram[kb, :, :]), dma=True)
            sig(eng.dma_start(out=ld[:, 1, :], in_=res_dram[kb, :, :]), dma=True)

        l_tok = p.sec("sync", [scan_done, res_toks, prev_l], lbody, dma=True)

        def mbody(eng, sig, kb=kb):
            sig(eng.tensor_tensor(yr[:, kb, :], ld[:, 0, :], ld[:, 1, :], OP.mult))

        prev_l = p.sec("vector", [l_tok], mbody)
        yr_toks.append(prev_l)

    out_toks = emit_gemm_to_dram(p, "l2out", yr_toks, io["W_out"], DI, D,
                                 lambda kb: yr[:, kb, :], io["l2_attnout_dram"])
    p.close_scope(sc3, out_toks)
    return out_toks


def emit_gemm_to_dram(p, name, deps, w_src, K, Mtot, rhs_tiles, out_dram, *, act=None,
                      bias_sb=None, nch=512, panel=512):
    sc = p.scope()
    stage = p.sbuf(f"{name}_st", [128, 2, nch], BF16, ctx=sc)
    dma_tok = [None, None]

    def out_fn(mb, c0, w, b=[0]):
        return None

    toks = []
    kb_n = (K + 127) // 128
    wbufs = [p.sbuf(f"{name}_w", [128, kb_n, panel], BF16, ctx=sc) for _ in range(2)]
    pss = [p.psum(f"{name}_ps", [128, nch], F32, ctx=sc) for _ in range(2)]
    npanels = (Mtot + panel - 1) // panel
    load_toks = [None] * npanels
    comp_done = [[] for _ in range(npanels)]
    load_toks[0] = load_weight_panel(p, w_src, K, 0, min(Mtot, panel), wbufs[0], [deps])
    ps_free = [None, None]
    psi = 0
    last_pa = -1
    for mb in range((Mtot + 127) // 128):
        pa = (mb * 128) // panel
        if pa != last_pa:
            last_pa = pa
            if pa + 1 < npanels:
                m0, m1 = (pa + 1) * panel, min(Mtot, (pa + 2) * panel)
                dep = [deps] if pa == 0 else [deps, comp_done[pa - 1]]
                load_toks[pa + 1] = load_weight_panel(p, w_src, K, m0, m1,
                                                      wbufs[(pa + 1) % 2], dep)
        for c0 in range(0, T, nch):
            w = min(nch, T - c0)
            b = psi % 2
            sb = psi % 2
            psi += 1
            ps = pss[b]

            def tbody(eng, sig, mb=mb, c0=c0, w=w, ps=ps, pa=pa):
                for k in range(kb_n):
                    ins = eng.matmul(ps[:, 0:w],
                                     wbufs[pa % 2][:, k, mb * 128 - pa * panel:mb * 128 - pa * panel + 128],
                                     rhs_tiles(k)[:, c0:c0 + w],
                                     start=(k == 0), stop=(k == kb_n - 1))
                sig(ins)

            t_tok = p.sec("tensor", [load_toks[pa], ps_free[b]], tbody)
            comp_done[pa].append(t_tok)

            def ebody(eng, sig, mb=mb, c0=c0, w=w, ps=ps, sb=sb):
                if act is not None:
                    kw = {}
                    if bias_sb is not None:
                        kw["bias"] = bias_sb[:, mb:mb + 1]
                    ins = eng.activation(out=stage[:, sb, 0:w], in_=ps[:, 0:w], func=act, **kw)
                elif bias_sb is not None:
                    ins = eng.tensor_scalar(stage[:, sb, 0:w], ps[:, 0:w],
                                            bias_sb[:, mb:mb + 1], None, OP.add)
                else:
                    ins = eng.tensor_copy(stage[:, sb, 0:w], ps[:, 0:w])
                sig(ins)

            e_tok = p.sec("scalar" if act is not None else "vector",
                          [t_tok, dma_tok[sb]], ebody)
            ps_free[b] = e_tok

            def dbody(eng, sig, mb=mb, c0=c0, w=w, sb=sb):
                sig(eng.dma_start(out=out_dram[mb, :, c0:c0 + w], in_=stage[:, sb, 0:w]),
                    dma=True)

            dma_tok[sb] = p.sec("sync", [e_tok], dbody, dma=True)
            toks.append(dma_tok[sb])
    last = merge_deps(toks)
    p.close_scope(sc, last)
    return last



def emit_gemm_fm_rowmajor(p, name, deps, w_dram, Mrows, K, rhs_tiles, out_fn, cst,
                          act=None, Tlen=T, nch=512, bias_sb=None):
    sc = p.scope()
    kb_n = K // 128
    wrow = [p.sbuf(f"{name}_wr", [128, K], BF16, ctx=sc) for _ in range(2)]
    wT = [p.sbuf(f"{name}_wt", [128, kb_n, 128], BF16, ctx=sc) for _ in range(2)]
    pss = [p.psum(f"{name}_ps", [128, nch], F32, ctx=sc) for _ in range(2)]
    prev = [None, None]
    toks = []
    psi = 0
    ps_free = [None, None]
    for mb in range(Mrows // 128):
        bi = mb % 2

        def lbody(eng, sig, mb=mb, bi=bi):
            sig(eng.dma_start(out=wrow[bi][:, :], in_=w_dram[mb * 128:(mb + 1) * 128, 0:K]),
                dma=True)

        l_tok = p.sec("gpsimd", [deps, prev[bi]], lbody, dma="sw")
        pairs = [(wrow[bi][:, kb * 128:(kb + 1) * 128], wT[bi][:, kb, :])
                 for kb in range(kb_n)]
        t_tok = emit_transpose_grid(p, [l_tok], pairs, cst["identb"], BF16, tag=f"{name}_t")
        mm_toks = []
        for c0 in range(0, Tlen, nch):
            w = min(nch, Tlen - c0)
            b = psi % 2
            psi += 1
            ps = pss[b]

            def tbody(eng, sig, bi=bi, c0=c0, w=w, ps=ps):
                for k in range(kb_n):
                    ins = eng.matmul(ps[:, 0:w], wT[bi][:, k, :],
                                     rhs_tiles(k)[:, c0:c0 + w],
                                     start=(k == 0), stop=(k == kb_n - 1))
                sig(ins)

            m_tok = p.sec("tensor", [t_tok, ps_free[b]], tbody)

            def ebody(eng, sig, mb=mb, c0=c0, w=w, ps=ps):
                if act is not None:
                    kw = {}
                    if bias_sb is not None:
                        kw["bias"] = bias_sb[:, mb:mb + 1]
                    ins = eng.activation(out=out_fn(mb, c0, w), in_=ps[:, 0:w], func=act, **kw)
                elif bias_sb is not None:
                    ins = eng.tensor_scalar(out_fn(mb, c0, w), ps[:, 0:w],
                                            bias_sb[:, mb:mb + 1], None, OP.add)
                else:
                    ins = eng.tensor_copy(out_fn(mb, c0, w), ps[:, 0:w])
                sig(ins)

            e_tok = p.sec("scalar" if act is not None else "vector", [m_tok], ebody)
            ps_free[b] = e_tok
            mm_toks.append(e_tok)
        prev[bi] = merge_deps(mm_toks)
        toks += mm_toks
    last = merge_deps(toks)
    p.close_scope(sc, last)
    return last


def emit_l2_ffn(p, deps, xn_fm, io, cst, single=False):
    caps = io["caps"]
    sc_g = p.scope()
    ghd = p.sbuf("l2f_ghd", [128, 8, T], BF16, ctx=sc_g)
    gd_tok = emit_gemm_fm_rowmajor(p, "l2fxd", deps, io["expert_down"], 1024, D,
                                   lambda kb: xn_fm[:, kb, :],
                                   lambda mb, c0, w: ghd[:, mb, c0:c0 + w],
                                   cst, act=AF.Gelu)

    sc_q = p.scope()
    q_fm = p.sbuf("l2f_q", [128, 16, T], BF16, ctx=sc_q)
    q_toks = emit_gemm_fm(p, "l2fq", deps, io["Wq"], D, 2048,
                          lambda kb: xn_fm[:, kb, :],
                          lambda mb, c0, w: q_fm[:, mb, c0:c0 + w],
                          bias_sb=io["bqT_sb"])
    stat = p.sbuf("l2f_stat", [128, 16], F32, ctx=sc_q)
    scr1 = p.sbuf("l2f_scr", [128, 2048], BF16, ctx=sc_q)
    st_toks = []
    for ti in range(16):
        def stb(eng, sig, ti=ti):
            eng.activation(out=scr1[:, :], in_=q_fm[:, ti, :], func=AF.Identity,
                           accum_out=stat[:, ti % 2:ti % 2 + 1] if False else stat[:, 4 + ti % 4: 5 + ti % 4])
            sig(eng.activation(out=scr1[:, :], in_=q_fm[:, ti, :], func=AF.Square,
                               accum_out=stat[:, 8 + ti % 4:9 + ti % 4]))
        pass
    def zb(eng, sig):
        sig(eng.memset(stat[:, 0:4], 0.0))

    z_tok = p.sec("vector", [q_toks], zb)
    prev_s = z_tok
    for ti in range(16):
        def stb(eng, sig, ti=ti):
            eng.activation(out=scr1[:, :], in_=q_fm[:, ti, :], func=AF.Identity,
                           accum_out=stat[:, 4:5])
            eng.drain()
            sig(eng.activation(out=scr1[:, :], in_=q_fm[:, ti, :], func=AF.Square,
                               accum_out=stat[:, 5:6]))

        a_tok = p.sec("scalar", [prev_s], stb)

        def acc(eng, sig, ti=ti):
            par = ti % 2
            eng.tensor_tensor(stat[:, par:par + 1], stat[:, par:par + 1], stat[:, 4:5], OP.add)
            eng.drain()
            sig(eng.tensor_tensor(stat[:, 2 + par:3 + par], stat[:, 2 + par:3 + par],
                                  stat[:, 5:6], OP.add))

        prev_s = p.sec("vector", [a_tok], acc)

    def tocc(eng, sig):
        sig(eng.dma_start(out=io["ccin"].ap(), in_=stat[:, 0:4]), dma=True)

    cc1 = p.sec("sync", [prev_s], tocc, dma=True)
    if single:
        def ccbody(eng, sig):
            sig(eng.dma_start(out=io["ccout"].ap(), in_=io["ccin"].ap()), dma=True)

        cc2 = p.sec("gpsimd", [cc1], ccbody, dma="sw")
    else:
        def ccbody(eng, sig):
            sig(eng.collective_compute("AllReduce", OP.add,
                                       replica_groups=[list(range(NCORES))],
                                       ins=[io["ccin"].ap().opt()],
                                       outs=[io["ccout"].ap().opt()]))

        cc2 = p.sec("gpsimd", [cc1], ccbody)

    def fromcc(eng, sig):
        sig(eng.dma_start(out=stat[:, 0:4], in_=io["ccout"].ap()), dma=True)

    cc3 = p.sec("sync", [cc2], fromcc, dma=True)

    NTOK = float(NCORES * T * 8)

    def affb(eng, sig):
        eng.tensor_scalar(stat[:, 6:8], stat[:, 0:2], 1.0 / NTOK, None, OP.mult)
        eng.tensor_scalar(stat[:, 8:10], stat[:, 2:4], 1.0 / NTOK, None, OP.mult)
        eng.drain()
        eng.tensor_tensor(stat[:, 10:12], stat[:, 6:8], stat[:, 6:8], OP.mult)
        eng.drain()
        sig(eng.scalar_tensor_tensor(stat[:, 10:12], stat[:, 8:10], 1e-5, stat[:, 10:12],
                                     OP.add, OP.subtract))

    af1 = p.sec("vector", [cc3], affb)

    def af2(eng, sig):
        eng.activation(out=stat[:, 12:14], in_=stat[:, 10:12], func=AF.Ln)
        eng.drain()
        sig(eng.activation(out=stat[:, 12:14], in_=stat[:, 12:14], func=AF.Exp, scale=-0.5))

    af2t = p.sec("scalar", [af1], af2)

    def af3(eng, sig):
        eng.tensor_tensor(stat[:, 12:14], stat[:, 12:14], io["bngT_sb"][:, :], OP.mult)
        eng.drain()
        eng.tensor_tensor(stat[:, 14:16], stat[:, 6:8], stat[:, 12:14], OP.mult)
        eng.drain()
        sig(eng.scalar_tensor_tensor(stat[:, 14:16], stat[:, 14:16], -1.0,
                                     io["bnbT_sb"][:, :], OP.mult, OP.add))

    af3t = p.sec("vector", [af2t], af3)

    bn_toks = []
    for ti in range(16):
        def bnb(eng, sig, ti=ti):
            par = ti % 2
            sig(eng.tensor_scalar(q_fm[:, ti, :], q_fm[:, ti, :], stat[:, 12 + par:13 + par],
                                  stat[:, 14 + par:15 + par], OP.mult, OP.add))

        bn_toks.append(p.sec("vector", [af3t], bnb))

    sc_n = p.scope()
    ps_ss = p.psum("l2f_ssps", [1, 512], F32, ctx=sc_n)
    ps_bc = p.psum("l2f_bcps", [128, 512], F32, ctx=sc_n)
    ssrow = p.sbuf("l2f_ssrow", [1, 2, T], F32, ctx=sc_n)
    nrm_toks = []
    prev_n = None
    for ti in range(16):
        def sqb(eng, sig, ti=ti):
            sig(eng.activation(out=scr1[:, :], in_=q_fm[:, ti, :], func=AF.Square))

        sq_tok = p.sec("scalar", [bn_toks, prev_n], sqb)
        cs_toks = []
        ev = None
        for cs in range(4):
            def smb(eng, sig, cs=cs):
                sig(eng.matmul(ps_ss[:, :], cst["ones128x1_b"][:, 0:1],
                               scr1[:, cs * 512:(cs + 1) * 512], start=True, stop=True))

            m_tok = p.sec("tensor", [sq_tok, ev], smb)

            def sev(eng, sig, cs=cs):
                sig(eng.tensor_scalar(ssrow[:, 0, cs * 512:(cs + 1) * 512], ps_ss[:, :],
                                      1e-24, None, OP.max))

            ev = p.sec("vector", [m_tok], sev)
            cs_toks.append(ev)

        def lnx(eng, sig):
            eng.activation(out=ssrow[:, 1, :], in_=ssrow[:, 0, :], func=AF.Ln)
            eng.drain()
            sig(eng.activation(out=ssrow[:, 0, :], in_=ssrow[:, 1, :], func=AF.Exp,
                               scale=-0.5))

        ln_tok = p.sec("scalar", [cs_toks], lnx)
        sub_toks = []
        for cs in range(4):
            def bcb(eng, sig, cs=cs):
                sig(eng.matmul(ps_bc[:, :], cst["ones1x128_f"][0:1, :],
                               ssrow[0:1, 0, cs * 512:(cs + 1) * 512], start=True, stop=True))

            b_tok = p.sec("tensor", [ln_tok, sub_toks[-1] if sub_toks else None], bcb)

            def nmul(eng, sig, ti=ti, cs=cs):
                sig(eng.tensor_tensor(q_fm[:, ti, cs * 512:(cs + 1) * 512],
                                      q_fm[:, ti, cs * 512:(cs + 1) * 512],
                                      ps_bc[:, :], OP.mult))

            sub_toks.append(p.sec("vector", [b_tok], nmul))
        prev_n = merge_deps(sub_toks)
        nrm_toks.append(prev_n)
    nrm_tok = merge_deps(nrm_toks)
    p.close_scope(sc_n, nrm_tok)

    sc_k = p.scope()
    krow = p.sbuf("l2f_krow", [32, 2, 128], F32, ctx=sc_k)
    kcol = p.sbuf("l2f_kcol", [32, 4], F32, ctx=sc_k)
    knT = p.sbuf("l2f_knT", [128, 2, 32], BF16, ctx=sc_k)
    ps_kT = p.psum("l2f_kTps", [128, 64], BF16, ctx=sc_k)

    def kload(eng, sig):
        sig(eng.dma_start(out=krow[:, 0, :], in_=io["sub_key0"]), dma=True)
        sig(eng.dma_start(out=krow[:, 1, :], in_=io["sub_key1"]), dma=True)

    kl_tok = p.sec("sync", [deps], kload, dma=True)

    def knorm(eng, sig):
        for j in range(2):
            eng.tensor_reduce(kcol[:, j:j + 1], krow[:, j, :], AX.X, OP.max,
                              apply_absolute_value=True) if False else None
        eng.tensor_tensor(krow[:, 0, :], krow[:, 0, :], krow[:, 0, :], OP.mult) if False else None
        sig(eng.memset(kcol[:, 0:4], 0.0))

    kn1 = p.sec("vector", [kl_tok], knorm)
    ksq = p.sbuf("l2f_ksq", [32, 128], F32, ctx=sc_k)
    kn_toks = []
    for j in range(2):
        def ksqb(eng, sig, j=j):
            sig(eng.activation(out=ksq[:, :], in_=krow[:, j, :], func=AF.Square))

        k1 = p.sec("scalar", [kn1, kn_toks[-1] if kn_toks else None], ksqb)

        def kred(eng, sig, j=j):
            eng.tensor_reduce(kcol[:, j:j + 1], ksq[:, :], AX.X, OP.add)
            eng.drain()
            sig(eng.tensor_scalar(kcol[:, j:j + 1], kcol[:, j:j + 1], 1e-24, None, OP.max))

        k2 = p.sec("vector", [k1], kred)

        def kln(eng, sig, j=j):
            eng.activation(out=kcol[:, 2 + j:3 + j], in_=kcol[:, j:j + 1], func=AF.Ln)
            eng.drain()
            sig(eng.activation(out=kcol[:, j:j + 1], in_=kcol[:, 2 + j:3 + j], func=AF.Exp,
                               scale=-0.5))

        k3 = p.sec("scalar", [kln], kln) if False else p.sec("scalar", [k2], kln)

        def kmul(eng, sig, j=j):
            sig(eng.tensor_scalar(krow[:, j, :], krow[:, j, :], kcol[:, j:j + 1], None,
                                  OP.mult))

        kn_toks.append(p.sec("vector", [k3], kmul))

    kt_pairs = [(krow[0:32, j, :], knT[:, j, :]) for j in range(2)]
    krowb = p.sbuf("l2f_krowb", [32, 2, 128], BF16, ctx=sc_k)

    def kcast(eng, sig):
        eng.tensor_copy(krowb[:, 0, :], krow[:, 0, :])
        sig(eng.tensor_copy(krowb[:, 1, :], krow[:, 1, :]))

    kc_tok = p.sec("vector", [kn_toks], kcast)
    kt_tok = emit_transpose_grid(p, [kc_tok],
                                 [(krowb[0:32, j, :], knT[:, j, :]) for j in range(2)],
                                 cst["identb"], BF16, tag="l2fkT")

    wm_dram = io["l2f_wm_dram"]
    wm_t = p.sbuf("l2f_wmt", [128, 8, 1024], BF16, ctx=sc_k)
    ue = p.sbuf("l2f_ue", [128, 2, 32], BF16, ctx=sc_k)
    uv = p.sbuf("l2f_uv", [128, 2, 32], BF16, ctx=sc_k)
    mc = p.sbuf("l2f_mc", [128, 8], F32, ctx=sc_k)
    mc8 = p.sbuf("l2f_mc8", [128, 16], F32, ctx=sc_k)
    ps_sc = p.psum("l2f_scps", [128, 64], F32, ctx=sc_k)
    prev_w = None
    prev_dma = None
    for tt in range(16):
        h_toks = []
        for h in range(8):
            def smm(eng, sig, tt=tt, h=h):
                eng.matmul(ps_sc[:, 0:32], q_fm[:, 2 * h, tt * 128:(tt + 1) * 128],
                           knT[:, 0, :], start=True, stop=True)
                sig(eng.matmul(ps_sc[:, 32:64], q_fm[:, 2 * h + 1, tt * 128:(tt + 1) * 128],
                               knT[:, 1, :], start=True, stop=True))

            s_tok = p.sec("tensor", [nrm_tok, kt_tok, prev_w, prev_dma], smm)

            def vbm(eng, sig):
                eng.max(out=mc8[:, 0:8], in_=ps_sc[:, 0:32])
                eng.drain()
                eng.max(out=mc8[:, 8:16], in_=ps_sc[:, 32:64])
                eng.drain()
                eng.tensor_scalar(mc[:, 0:1], mc8[:, 0:1], -1.0, None, OP.mult)
                sig(eng.tensor_scalar(mc[:, 2:3], mc8[:, 8:9], -1.0, None, OP.mult))

            m_tok = p.sec("vector", [s_tok], vbm)

            def eb(eng, sig):
                eng.activation(out=ue[:, 0, :], in_=ps_sc[:, 0:32], func=AF.Exp,
                               bias=mc[:, 0:1])
                sig(eng.activation(out=ue[:, 1, :], in_=ps_sc[:, 32:64], func=AF.Exp,
                                   bias=mc[:, 2:3]))

            e_tok = p.sec("scalar", [m_tok], eb)

            def vb2(eng, sig, tt=tt, h=h):
                eng.scalar_tensor_tensor(uv[:, 0, :], ps_sc[:, 0:32], mc8[:, 3:4],
                                         ue[:, 0, :], OP.is_ge, OP.mult,
                                         accum_out=mc[:, 4:5])
                eng.drain()
                eng.scalar_tensor_tensor(uv[:, 1, :], ps_sc[:, 32:64], mc8[:, 11:12],
                                         ue[:, 1, :], OP.is_ge, OP.mult,
                                         accum_out=mc[:, 5:6])
                eng.drain()
                eng.tensor_tensor(mc[:, 6:7], mc[:, 4:5], mc[:, 5:6], OP.mult)
                eng.drain()
                eng.reciprocal(mc[:, 7:8], mc[:, 6:7])
                eng.drain()
                eng.tensor_scalar(uv[:, 0, :], uv[:, 0, :], mc[:, 7:8], None, OP.mult)
                eng.drain()
                sig(eng.tensor_tensor(
                    wm_t[:, h, :].rearrange("p (a b) -> p a b", b=32),
                    uv[:, 0, :, None].to_broadcast([128, 32, 32]),
                    uv[:, 1, None, :].to_broadcast([128, 32, 32]), OP.mult))

            prev_w = p.sec("vector", [e_tok], vb2)
            h_toks.append(prev_w)

        def wdma(eng, sig, tt=tt):
            sig(eng.dma_start(out=wm_dram[tt, :, :, :], in_=wm_t[:, :, :]), dma=True)

        prev_dma = p.sec("sync", [h_toks, prev_dma], wdma, dma=True)
    wm_tok = prev_dma
    p.close_scope(sc_k, wm_tok)
    p.close_scope(sc_q, wm_tok)

    sc_o = p.scope()
    out_sb = p.sbuf("l2f_out", [128, 8, T], F32, ctx=sc_o)
    rhs_h = p.sbuf("l2f_rhs", [128, 8, T], BF16, ctx=sc_o)
    wmT_ld = p.sbuf("l2f_wmT", [128, T], BF16, ctx=sc_o)

    def zo(eng, sig):
        for mb in range(8):
            ins = eng.memset(out_sb[:, mb, :], 0.0)
        sig(ins)

    out_tok = p.sec("vector", [deps], zo)
    for h in range(8):
        prep_toks = []
        prev_t = None
        for kb in range(8):
            def tld(eng, sig, h=h, kb=kb):
                src = wm_dram.rearrange("tt p h e -> (tt p) h e")[:, h, kb * 128:(kb + 1) * 128]
                sig(eng.dma_start(out=wmT_ld[:, :], in_=src, transpose=True), dma=True)

            t_tok = p.sec("sync", [wm_tok, gd_tok, prev_t], tld, dma=True)

            def tmul(eng, sig, kb=kb):
                sig(eng.tensor_tensor(rhs_h[:, kb, :], wmT_ld[:, :], ghd[:, kb, :], OP.mult))

            prev_t = p.sec("vector", [t_tok], tmul)
            prep_toks.append(prev_t)

        up_toks = emit_gemm_fm(p, f"l2fup{h}", prep_toks, io["expert_up"], 1024, 1024,
                               lambda kb: rhs_h[:, kb, :], None,
                               resid_fm=out_sb, resid_dep=out_tok)
        out_tok = merge_deps(up_toks)

    ob = p.sbuf("l2f_ob", [128, T], BF16, ctx=sc_o)
    st_toks = []
    prev_st = None
    for mb in range(8):
        def cb(eng, sig, mb=mb):
            sig(eng.tensor_copy(ob[:, :], out_sb[:, mb, :]))

        c_tok = p.sec("vector", [out_tok, prev_st], cb)

        def sb_(eng, sig, mb=mb):
            sig(eng.dma_start(out=io["l2f_out_dram"][mb, :, :], in_=ob[:, :]), dma=True)

        prev_st = p.sec("sync", [c_tok], sb_, dma=True)
        st_toks.append(prev_st)
    last = merge_deps(st_toks)
    p.close_scope(sc_o, last)
    p.close_scope(sc_g, last)
    return last


def emit_fm_dram_add(p, deps, src_dram, y_out, cst, nMB=NFC, tag="fma"):
    sc = p.scope()
    ld = p.sbuf(f"{tag}_ld", [128, nMB, T], BF16, ctx=sc)
    s_tm = p.sbuf(f"{tag}_tm", [128, 16, nMB * 128], BF16, ctx=sc)

    def lb(eng, sig):
        sig(eng.dma_start(out=ld[:, :, :], in_=src_dram.rearrange("mb p t -> p mb t")),
            dma=True)

    l_tok = p.sec("sync", [deps], lb, dma=True)
    pairs = []
    for mb in range(nMB):
        for tc in range(16):
            pairs.append((ld[:, mb, tc * 128:(tc + 1) * 128],
                          s_tm[:, tc, mb * 128:(mb + 1) * 128]))
    t_tok = emit_transpose_grid(p, [l_tok], pairs, cst["identb"], BF16, tag=tag + "t")

    def ab(eng, sig):
        dst = y_out.rearrange("(tc p) d -> p tc d", p=128)
        sig(eng.dma_start(out=dst, in_=s_tm[:, :, :], accum_op=OP.add), dma=True)

    tok = p.sec("gpsimd", [t_tok], ab, dma="sw")
    p.close_scope(sc, tok)
    return tok


def emit_copy_x_to_y(p, deps, x_dram, y_dram):
    def body(eng, sig):
        sig(eng.dma_start(out=y_dram, in_=x_dram), dma=True)
    return p.sec("sync", [deps], body, dma=True)


def declare_io_l2(nc, stage, pref="", xap=None, yap=None):
    io = {"x": xap if xap is not None else
          nc.dram_tensor("x", [T, D], F32, kind="ExternalInput").ap(),
          "y": yap if yap is not None else
          nc.dram_tensor("y", [T, D], F32, kind="ExternalOutput").ap()}
    names = []
    if stage in ("attn", "full"):
        names += [("W_in", [D, 4096]), ("Wx_pad", [2048, 128]), ("Wdt_pad", [128, 2048]),
                  ("cwT", [128, 64]), ("cbT", [128, 16]), ("bdtT", [128, 16]),
                  ("AlogT", [128, 256]), ("DT", [128, 16]), ("W_out", [2048, D])]
    if stage in ("ffn", "full"):
        names += [("Wq", [D, 2048]), ("bqT", [128, 16]), ("bngT", [128, 2]),
                  ("bnbT", [128, 2]), ("sub_key0", [32, 128]), ("sub_key1", [32, 128]),
                  ("expert_down", [1024, 1024]), ("expert_up", [1024, 1024])]
    for nm, shp in names:
        io[nm] = nc.dram_tensor(pref + nm, shp, F32, kind="ExternalInput").ap()
    return io


def host_prep_l2(inp):
    d = {}
    W_x = np.asarray(inp["L2_attn_W_x"])
    wxp = np.zeros((2048, 128), np.float32)
    wxp[:, 0:16] = W_x[:, 64:80]
    wxp[:, 32:48] = W_x[:, 80:96]
    wxp[:, 64:128] = W_x[:, 0:64]
    d["Wx_pad"] = wxp
    wdtp = np.zeros((128, 2048), np.float32)
    wdtp[64:128, :] = np.asarray(inp["L2_attn_W_dt"])
    d["Wdt_pad"] = wdtp
    cw = np.asarray(inp["L2_attn_conv_w"])
    cwT = np.zeros((128, 64), np.float32)
    for k in range(4):
        cwT[:, k * 16:(k + 1) * 16] = cw[:, k].reshape(16, 128).T
    d["cwT"] = cwT
    d["cbT"] = _col128(np.asarray(inp["L2_attn_conv_b"]))
    d["bdtT"] = _col128(np.asarray(inp["L2_attn_b_dt"]))
    Alog = np.asarray(inp["L2_attn_A_log"])
    AT = np.zeros((128, 256), np.float32)
    for dc in range(16):
        for g in range(16):
            for pp in range(128):
                AT[pp, dc * 16 + g] = Alog[dc * 128 + 8 * g + pp // 16, pp % 16]
    d["AlogT"] = AT
    d["DT"] = _col128(np.asarray(inp["L2_attn_D"]))
    d["W_in"] = np.asarray(inp["L2_attn_W_in"])
    d["W_out"] = np.asarray(inp["L2_attn_W_out"])
    d["Wq"] = np.asarray(inp["L2_ffn_Wq"])
    d["bqT"] = _col128(np.asarray(inp["L2_ffn_bq"]))
    d["bngT"] = np.ascontiguousarray(np.asarray(inp["L2_ffn_bn_gamma"]).reshape(2, 128).T)
    d["bnbT"] = np.ascontiguousarray(np.asarray(inp["L2_ffn_bn_beta"]).reshape(2, 128).T)
    d["sub_key0"] = np.asarray(inp["L2_ffn_sub_key0"])
    d["sub_key1"] = np.asarray(inp["L2_ffn_sub_key1"])
    d["expert_down"] = np.asarray(inp["L2_ffn_expert_down"])
    d["expert_up"] = np.asarray(inp["L2_ffn_expert_up"])
    return d


def build_l2(stage="full", single=False):
    import ml_dtypes
    nc = bass.Bass("TRN2", target_bir_lowering=False)
    consts = make_consts()
    make_l2_consts(consts)
    caps = declare_consts(nc, consts)
    io = declare_io_l2(nc, stage)
    io["caps"] = caps
    do_attn = stage in ("attn", "full")
    do_ffn = stage in ("ffn", "full")
    if do_attn:
        io["l2_res_dram"] = nc.dram_tensor("l2res", [16, 128, T], BF16).ap()
        io["l2_y_dram"] = nc.dram_tensor("l2y", [16, 128, T], BF16).ap()
        io["l2_attnout_dram"] = nc.dram_tensor("l2ao", [8, 128, T], BF16).ap()
    if do_ffn:
        io["ccin"] = nc.dram_tensor("ccin", [128, 4], F32)
        io["ccout"] = nc.dram_tensor("ccout", [128, 4], F32)
        io["l2f_wm_dram"] = nc.dram_tensor("l2wm", [16, 128, 8, 1024], BF16).ap()
        io["l2f_out_dram"] = nc.dram_tensor("l2fo", [8, 128, T], BF16).ap()
    p = P(nc)
    small = []
    if do_attn:
        for nm in ("cwT", "cbT", "bdtT", "AlogT", "DT"):
            io[nm + "_sb"] = p.sbuf(nm + "S", list(io[nm].shape), F32)
            small.append((nm, io[nm + "_sb"][:, :]))
        cstn = {}
        for nm in ("l2_octmap", "l2_ymap", "l2_nmap"):
            t_ = p.sbuf(nm + "S", list(consts[nm].shape), BF16)
            cstn[nm + "_sb"] = t_
            small.append((nm, t_[:, :, :] if len(consts[nm].shape) == 3 else t_[:, :]))
        io["wdt_sb"] = p.sbuf("wdtS", [128, 2048], BF16)
    if do_ffn:
        for nm in ("bqT", "bngT", "bnbT"):
            io[nm + "_sb"] = p.sbuf(nm + "S", list(io[nm].shape), F32)
            small.append((nm, io[nm + "_sb"][:, :]))
    sm_tok = load_const_tiles(p, dict(list(caps.items()) + list(io.items())), small)
    if do_attn:
        def wdtl(eng, sig):
            sig(eng.dma_start(out=io["wdt_sb"][:, :], in_=io["Wdt_pad"]), dma=True)

        io["wdt_tok"] = p.sec("gpsimd", [], wdtl, dma="sw")

    xn_fm = p.sbuf("xn_fm", [128, NFC, T], BF16)
    scSX = p.scope()
    cst, x_fm, t_tok = emit_common_prelude(p, caps, io, xctx=scSX)
    if do_attn:
        cst.update(cstn)
    n_tok = emit_outer_ln(p, [t_tok, sm_tok], x_fm, xn_fm, cst)
    cp_tok = emit_copy_x_to_y(p, [t_tok], io["x"], io["y"])
    p.close_scope(scSX, merge_deps([n_tok, cp_tok]))
    a_tok = None
    f_tok = None
    if do_attn:
        at = emit_l2_attn(p, [n_tok, sm_tok, io["wdt_tok"]], xn_fm, io, cst, io["y"])
        a_tok = emit_fm_dram_add(p, [at, cp_tok], io["l2_attnout_dram"], io["y"], cst,
                                 nMB=8, tag="l2aadd")
    if do_ffn:
        ft = emit_l2_ffn(p, [n_tok, sm_tok], xn_fm, io, cst, single=single)
        f_tok = emit_fm_dram_add(p, [ft, cp_tok, a_tok], io["l2f_out_dram"], io["y"], cst,
                                 nMB=8, tag="l2fadd")
    p.barrier_close()
    return nc, consts



def make_l3_consts(c):
    import ml_dtypes
    i_ = np.arange(128)[:, None]
    j_ = np.arange(128)[None, :]
    c["l3_triu_incl"] = np.where(i_ <= j_, 1.0, 0.0).astype(ml_dtypes.bfloat16)
    c["l3_triu_strict"] = np.where(i_ < j_, 1.0, 0.0).astype(ml_dtypes.bfloat16)
    em = np.zeros((128, 16), np.float32)
    for pp in range(128):
        em[pp, pp % 16] = 1.0
    c["l3_emap"] = em
    c["l3_emapT"] = np.ascontiguousarray(em.T)
    tri8 = np.zeros((64, 8), np.float32)
    for base in (0, 32):
        for pp in range(8):
            for i in range(8):
                if pp < i:
                    tri8[base + pp, i] = 1.0
    c["l3_tri8"] = tri8
    c["l3_ones8"] = np.ones((64, 1), np.float32)
    sl = np.zeros((128, 8), np.float32)
    for pp in range(128):
        for st in range(8):
            sl[pp, st] = st * 128 + pp + 1
    c["l3f_slotiota"] = sl
    gid = np.zeros((64, T), np.float32)
    for base in (0, 32):
        for r in range(8):
            gid[base + r, :] = r * T + np.arange(T)
    c["l3f_gid"] = gid
    return c


def emit_lin_attn(p, deps, Qp, Kp, vchunk, o_dram, out_sb, cst, strict, tag):
    sc = p.scope()
    S_sb = p.sbuf(f"{tag}_S", [128, 16, 64], F32, ctx=sc)
    S_bf = p.sbuf(f"{tag}_Sb", [128, 16, 64], BF16, ctx=sc)
    sT = p.sbuf(f"{tag}_sT", [128, 128], BF16, ctx=sc)
    Vt = [p.sbuf(f"{tag}_Vt", [128, 1024], BF16, ctx=sc) for _ in range(2)]
    KT = [p.sbuf(f"{tag}_KT", [128, 1024], BF16, ctx=sc) for _ in range(2)]
    Ob = [p.sbuf(f"{tag}_Ob", [128, 1024], BF16, ctx=sc) for _ in range(2)] \
        if out_sb is None else None
    ps_sc = p.psum(f"{tag}_scps", [128, 128], F32, ctx=sc)
    ps_o = [p.psum(f"{tag}_ops", [128, 64], F32, ctx=sc) for _ in range(2)]
    ps_su = p.psum(f"{tag}_sups", [128, 64], F32, ctx=sc)
    mask = cst["l3_triu_strict_sb"] if strict else cst["l3_triu_incl_sb"]

    def zb(eng, sig):
        eng.memset(S_sb[:, :, :], 0.0)
        eng.drain()
        sig(eng.memset(S_bf[:, :, :], 0.0))

    z_tok = p.sec("vector", [deps], zb)
    toks = []
    prev_ev = None
    prev_o = [None, None]
    prev_sacc = [None] * 16
    su_prev = None
    v_prev = [None, None]
    k_prev = [None, None]
    fl_prev = [None, None]
    for cc in range(16):
        t0 = cc * 128
        b = cc % 2
        v_tok = vchunk(p, [deps, v_prev[b]], cc, Vt[b])
        kt_pairs = [(Kp[:, fc, t0:t0 + 128], KT[b][:, fc * 128:(fc + 1) * 128])
                    for fc in range(8)]
        kt_tok = emit_transpose_grid(p, [deps, k_prev[b]], kt_pairs, cst["identb"],
                                     BF16, tag=f"{tag}kt")
        ch_toks = []
        for h in range(16):
            fch, oh = h // 2, (h % 2) * 64

            def smm(eng, sig, fch=fch, oh=oh, t0=t0):
                sig(eng.matmul(ps_sc[:, :], Kp[:, fch, t0:t0 + 128][oh:oh + 64, :],
                               Qp[:, fch, t0:t0 + 128][oh:oh + 64, :],
                               start=True, stop=True))

            s_tok = p.sec("tensor", [z_tok, prev_ev], smm)

            def sev(eng, sig):
                sig(eng.tensor_tensor(sT[:, :], ps_sc[:, :], mask[:, :], OP.mult))

            prev_ev = p.sec("vector", [s_tok], sev)

            def pv(eng, sig, h=h, fch=fch, oh=oh, t0=t0, b=b):
                eng.matmul(ps_o[b][:, :], sT[:, :], Vt[b][:, h * 64:(h + 1) * 64],
                           start=True, stop=False)
                sig(eng.matmul(ps_o[b][:, :], Qp[:, fch, t0:t0 + 128][oh:oh + 64, :],
                               S_bf[oh:oh + 64, h, :], start=False, stop=True))

            pv_tok = p.sec("tensor", [prev_ev, v_tok, prev_o[b], prev_sacc[h]], pv)

            def oev(eng, sig, h=h, cc=cc, b=b):
                if out_sb is None:
                    ins = eng.tensor_copy(Ob[b][:, h * 64:(h + 1) * 64], ps_o[b][:, :])
                else:
                    ins = eng.tensor_copy(out_sb[:, cc, h * 64:(h + 1) * 64], ps_o[b][:, :])
                sig(ins)

            o_tok = p.sec("vector",
                          [pv_tok, fl_prev[b] if out_sb is None else None], oev)
            prev_o[b] = o_tok
            ch_toks.append(o_tok)

            if cc < 15:
                def supd(eng, sig, h=h, oh=oh, b=b):
                    sig(eng.matmul(ps_su[oh:oh + 64, :], KT[b][:, h * 64:(h + 1) * 64],
                                   Vt[b][:, h * 64:(h + 1) * 64], start=True, stop=True))

                su_tok = p.sec("tensor", [pv_tok, kt_tok, v_tok, su_prev], supd)

                def sacc(eng, sig, h=h, oh=oh):
                    eng.tensor_tensor(S_sb[oh:oh + 64, h, :], S_sb[oh:oh + 64, h, :],
                                      ps_su[oh:oh + 64, :], OP.add)
                    eng.drain()
                    sig(eng.tensor_copy(S_bf[oh:oh + 64, h, :], S_sb[oh:oh + 64, h, :]))

                prev_sacc[h] = p.sec("vector", [su_tok], sacc)
                su_prev = prev_sacc[h]
                ch_toks.append(prev_sacc[h])
        users = merge_deps(ch_toks)
        v_prev[b] = users
        k_prev[b] = users
        if out_sb is None:
            def fl(eng, sig, t0=t0, b=b):
                sig(eng.dma_start(out=o_dram[t0:t0 + 128, :], in_=Ob[b][:, :]), dma=True)

            fl_prev[b] = p.sec("sync", [users], fl, dma=True)
            toks.append(fl_prev[b])
        else:
            toks.append(users)
    last = merge_deps(toks)
    p.close_scope(sc, last)
    return last


def emit_l3_attn(p, deps, xn_fm, io, cst, single=False):
    sc1 = p.scope()
    Qp = p.sbuf("l3_Qp", [128, 8, T], BF16, ctx=sc1)
    Kb = p.sbuf("l3_K", [128, 8, T], BF16, ctx=sc1)
    sc1b = p.scope()
    Qb = p.sbuf("l3_Q", [128, 8, T], BF16, ctx=sc1b)
    zrow = p.sbuf("l3_zrow", [128, 8], BF16, ctx=sc1b)
    q_toks = emit_gemm_fm(p, "l3q", deps, io["q1_w"], D, D,
                          lambda kb: xn_fm[:, kb, :],
                          lambda mb, c0, w: Qb[:, mb, c0:c0 + w], bias_sb=io["q1bT_sb"])
    k_toks = emit_gemm_fm(p, "l3k", deps, io["k1_w"], D, D,
                          lambda kb: xn_fm[:, kb, :],
                          lambda mb, c0, w: Kb[:, mb, c0:c0 + w], bias_sb=io["k1bT_sb"])

    def vz(eng, sig):
        sig(eng.memset(zrow[:, :], 0.0))

    z1 = p.sec("vector", [deps], vz)

    def vzd(eng, sig):
        dst = io["l3_v_dram"][T:T + 1, :].rearrange("one (a b) -> a (one b)", a=128)
        sig(eng.dma_start(out=dst, in_=zrow[:, :]), dma=True)

    vzd_tok = p.sec("sync", [z1], vzd, dma=True)

    sc_g = p.scope()
    Gch = p.sbuf("l3_Gch", [128, T], F32, ctx=sc_g)
    Gc2 = p.sbuf("l3_Gc2", [128, T], F32, ctx=sc_g)
    Rch = p.sbuf("l3_Rch", [128, T], F32, ctx=sc_g)
    onesT = p.sbuf("l3_onesT", [128, T], F32, ctx=sc_g)
    ps_g = [p.psum("l3_gps", [128, 512], F32, ctx=sc_g) for _ in range(2)]

    def oz(eng, sig):
        sig(eng.memset(onesT[:, :], 1.0))

    oz_tok = p.sec("vector", [deps], oz)
    ap_toks = []
    prev_mm = [None, None]
    prev_iter = None
    for fc in range(NFC):
        g_toks = []
        for cs in range(4):
            b = cs % 2

            def gmm(eng, sig, fc=fc, cs=cs, b=b):
                sig(eng.matmul(ps_g[b][:, :], io["Wg2_sb"][:, :],
                               xn_fm[:, fc, cs * 512:(cs + 1) * 512],
                               start=True, stop=True))

            m1 = p.sec("tensor", [deps, io["wgs_tok"], prev_mm[b]], gmm)

            def gact(eng, sig, cs=cs, b=b):
                sig(eng.activation(out=Gch[:, cs * 512:(cs + 1) * 512], in_=ps_g[b][:, :],
                                   func=AF.Sigmoid, bias=io["gwb2_sb"][:, 0:1]))

            prev_mm[b] = p.sec("scalar", [m1, prev_iter], gact)
            g_toks.append(prev_mm[b])

        def lmx(eng, sig):
            sig(eng.tensor_scalar(Gch[:, :], Gch[:, :], 1e-6, None, OP.max))

        c1 = p.sec("vector", [g_toks, oz_tok], lmx)

        def lnn(eng, sig):
            sig(eng.activation(out=Gch[:, :], in_=Gch[:, :], func=AF.Ln))

        c2 = p.sec("scalar", [c1], lnn)

        def csc(eng, sig):
            eng.tensor_tensor_scan(Gch[:, :], onesT[:, :], Gch[:, :], 0.0,
                                   OP.mult, OP.add)
            eng.drain()
            sig(eng.tensor_scalar(Gch[:, :], Gch[:, :], -30.0, 30.0, OP.max, OP.min))

        c3 = p.sec("vector", [c2], csc)

        def gex(eng, sig):
            sig(eng.activation(out=Gc2[:, :], in_=Gch[:, :], func=AF.Exp))

        c4 = p.sec("scalar", [c3], gex)

        def gpl(eng, sig):
            eng.tensor_scalar(Gc2[:, :], Gc2[:, :], 1e-6, None, OP.add)
            eng.drain()
            sig(eng.reciprocal(Gch[:, :], Gc2[:, :]))

        c5 = p.sec("vector", [c4], gpl)

        def qap(eng, sig, fc=fc):
            sig(eng.tensor_tensor(Qp[:, fc, :], Qb[:, fc, :], Gc2[:, :], OP.mult))

        a1 = p.sec("vector", [c5, q_toks], qap)
        r_toks = []
        for cs in range(4):
            b = cs % 2

            def rmm(eng, sig, fc=fc, cs=cs, b=b):
                sig(eng.matmul(ps_g[b][:, :], io["Ws2_sb"][:, :],
                               Kb[:, fc, cs * 512:(cs + 1) * 512],
                               start=True, stop=True))

            m2 = p.sec("tensor", [k_toks, io["wgs_tok"], prev_mm[b], a1], rmm)

            def ract(eng, sig, cs=cs, b=b):
                sig(eng.activation(out=Rch[:, cs * 512:(cs + 1) * 512], in_=ps_g[b][:, :],
                                   func=AF.Silu, bias=io["swb2_sb"][:, 0:1]))

            prev_mm[b] = p.sec("scalar", [m2], ract)
            r_toks.append(prev_mm[b])

        def kap(eng, sig, fc=fc):
            eng.tensor_tensor(Rch[:, :], Rch[:, :], Gch[:, :], OP.mult)
            eng.drain()
            sig(eng.tensor_tensor(Kb[:, fc, :], Kb[:, fc, :], Rch[:, :], OP.mult))

        kap_tok = p.sec("vector", [r_toks, c5], kap)
        prev_iter = merge_deps([a1, kap_tok])
        ap_toks.append(prev_iter)
    f_tok = merge_deps(ap_toks)
    p.close_scope(sc_g, f_tok)

    scrA = p.sbuf("l3_scrA", [128, T], BF16, ctx=sc1b)
    q2_toks = []
    prev_q2 = None
    for fc in range(NFC):
        def lk(eng, sig, fc=fc):
            eng.tensor_scalar(scrA[:, :], Qb[:, fc, :], 0.125, None, OP.mult)
            eng.drain()
            eng.tensor_scalar(Qb[:, fc, :], scrA[:, :], 0.0, None, OP.min)
            eng.drain()
            eng.tensor_scalar(scrA[:, :], scrA[:, :], 0.0, None, OP.max)
            eng.drain()
            eng.scalar_tensor_tensor(Qb[:, fc, :], scrA[:, :], 0.02, Qb[:, fc, :],
                                     OP.mult, OP.add)
            eng.drain()
            eng.memset(scrA[:, 0:1], 0.0)
            eng.drain()
            sig(eng.tensor_copy(scrA[:, 1:T], Qb[:, fc, 0:T - 1]))

        l_tok = p.sec("vector", [f_tok, prev_q2], lk)

        def qsp(eng, sig, fc=fc):
            sig(eng.dma_start(out=io["l3_q2p_dram"][fc], in_=scrA[:, :]), dma=True)

        prev_q2 = p.sec("sync", [l_tok], qsp, dma=True)
        q2_toks.append(prev_q2)

    k2_toks = emit_gemm_fm(p, "l3k2", [deps, q2_toks], io["k2_w"], D, D,
                           lambda kb: xn_fm[:, kb, :],
                           lambda mb, c0, w: Qb[:, mb, c0:c0 + w],
                           bias_sb=io["k2bsT_sb"], act=AF.Sigmoid,
                           act_scale=0.02 / 32.0)

    def k2dma(eng, sig):
        sig(eng.dma_start(out=io["l3_k2_dram"].rearrange("mb p t -> p mb t"),
                          in_=Qb[:, :, :]), dma=True)

    k2d_tok = p.sec("sync", [k2_toks], k2dma, dma=True)
    p.close_scope(sc1b, merge_deps([k2d_tok, vzd_tok]))

    vd_prev = {0: None, 1: None}

    def vchunk1(pp, deps2, cc, dst):
        t0 = cc * 128
        pairs = [(xn_fm[:, fc, t0:t0 + 128], dst[:, fc * 128:(fc + 1) * 128])
                 for fc in range(NFC)]
        tt = emit_transpose_grid(pp, deps2 + [vd_prev[cc % 2]], pairs, cst["identb"],
                                 BF16, tag="l3vt")

        def vd(eng, sig, t0=t0):
            sig(eng.dma_start(out=io["l3_v_dram"][t0:t0 + 128, :], in_=dst[:, :]),
                dma=True)

        vd_prev[cc % 2] = pp.sec("sync", [tt], vd, dma=True)
        return tt

    o1_tok = emit_lin_attn(p, merge_deps([f_tok, deps]), Qp, Kb, vchunk1,
                           io["l3_o1_dram"], None, cst, strict=False, tag="l3o1")
    o1_done = fence(p, merge_deps([o1_tok, vd_prev[0], vd_prev[1]]))
    p.close_scope(sc1, o1_done)

    sc3 = p.scope()
    O2_tm = p.sbuf("l3_O2", [128, 16, 1024], BF16, ctx=sc3)
    sc3b = p.scope()
    q2p_fm = p.sbuf("l3_q2pfm", [128, 8, T], BF16, ctx=sc3b)
    k2_fm = p.sbuf("l3_k2fm", [128, 8, T], BF16, ctx=sc3b)
    e_scr = [p.sbuf("l3_escr", [128, 1024], BF16, ctx=sc3b) for _ in range(2)]

    def rl1(eng, sig):
        sig(eng.dma_start(out=q2p_fm[:, :, :],
                          in_=io["l3_q2p_dram"].rearrange("mb p t -> p mb t")), dma=True)
        sig(eng.dma_start(out=k2_fm[:, :, :],
                          in_=io["l3_k2_dram"].rearrange("mb p t -> p mb t")), dma=True)

    rl0_tok = p.sec("sync", [prev_q2, k2d_tok], rl1, dma=True)

    def q0z(eng, sig):
        sig(eng.memset(q2p_fm[:, :, 0:1], 0.0))

    rl_tok = p.sec("vector", [rl0_tok], q0z)
    e_prev = {0: None, 1: None}

    def vchunk2(pp, deps2, cc, dst):
        t0 = cc * 128
        b = cc % 2

        def ld(eng, sig, t0=t0, b=b):
            sig(eng.dma_start(out=e_scr[b][:, :], in_=io["l3_o1_dram"][t0:t0 + 128, :]),
                dma=True)
            sig(eng.dma_start(out=dst[:, :], in_=io["l3_v_dram"][t0 + 1:t0 + 129, :]),
                dma=True)

        l_tok = pp.sec("sync", deps2 + [o1_done, e_prev[b]], ld, dma=True)

        def sub(eng, sig, b=b):
            sig(eng.tensor_tensor(dst[:, :], dst[:, :], e_scr[b][:, :], OP.subtract))

        e_prev[b] = pp.sec("vector", [l_tok], sub)
        return e_prev[b]

    o2_tok = emit_lin_attn(p, merge_deps([rl_tok, o1_done]), q2p_fm, k2_fm, vchunk2,
                           None, O2_tm, cst, strict=True, tag="l3o2")
    p.close_scope(sc3b, o2_tok)

    sc3c = p.scope()
    O1ld = p.sbuf("l3_O1ld", [128, 2, 1024], BF16, ctx=sc3c)
    O12_fm = p.sbuf("l3_O12", [128, 8, T], BF16, ctx=sc3c)
    add_toks = []
    prev_ld = [None, None]
    for cc in range(16):
        b = cc % 2

        def ol(eng, sig, cc=cc, b=b):
            sig(eng.dma_start(out=O1ld[:, b, :],
                              in_=io["l3_o1_dram"][cc * 128:(cc + 1) * 128, :]), dma=True)

        ol_tok = p.sec("sync", [o2_tok, prev_ld[b]], ol, dma=True)

        def oa(eng, sig, cc=cc, b=b):
            sig(eng.tensor_tensor(O2_tm[:, cc, :], O2_tm[:, cc, :], O1ld[:, b, :],
                                  OP.add))

        prev_ld[b] = p.sec("vector", [ol_tok], oa)
        add_toks.append(prev_ld[b])
    oa_tok = merge_deps(add_toks)
    pairs_o = []
    for fc in range(NFC):
        for tc in range(16):
            pairs_o.append((O2_tm[:, tc, fc * 128:(fc + 1) * 128],
                            O12_fm[:, fc, tc * 128:(tc + 1) * 128]))
    of_tok = emit_transpose_grid(p, [oa_tok], pairs_o, cst["identb"], BF16, tag="l3of")
    out_toks = emit_gemm_to_dram(p, "l3cp", [of_tok], io["cp_w"], D, D,
                                 lambda kb: O12_fm[:, kb, :], io["l3a_out_dram"],
                                 bias_sb=io["cpbT_sb"])
    p.close_scope(sc3c, out_toks)
    p.close_scope(sc3, out_toks)
    return out_toks


def emit_ag(p, deps, gin, gout, single, tag):
    if single:
        rows = gin.shape[0]

        def body(eng, sig):
            for r in range(NCORES):
                sig(eng.dma_start(out=gout[r * rows:(r + 1) * rows], in_=gin[:]), dma=True)

        return fence(p, p.sec("gpsimd", [deps], body, dma="sw"))

    def body(eng, sig):
        sig(eng.collective_compute("AllGather", OP.bypass,
                                   replica_groups=[list(range(NCORES))],
                                   ins=[gin.opt()], outs=[gout.opt()]))

    return p.sec("gpsimd", [deps], body)


def emit_l3_ffn(p, deps, xn_fm, io, cst, single=False):
    caps = io["caps"]
    sc_a = p.scope()
    xr_fm = p.sbuf("l3f_xr", [128, NFC, T], BF16, ctx=sc_a)
    sc_a1 = p.scope()
    sq = p.sbuf("l3f_sq", [128, 512], BF16, ctx=sc_a1)
    srow = p.sbuf("l3f_srow", [1, 2, T], F32, ctx=sc_a1)
    ps_s = p.psum("l3f_sps", [1, 512], F32, ctx=sc_a1)
    ps_b = p.psum("l3f_bps", [128, 512], F32, ctx=sc_a1)
    st_toks = []
    mm_tok = None
    prev_sev = None
    for cs in range(4):
        for fc in range(NFC):
            def sqb(eng, sig, fc=fc, cs=cs):
                sig(eng.activation(out=sq[:, :], in_=xn_fm[:, fc, cs * 512:(cs + 1) * 512],
                                   func=AF.Square))

            c_tok = p.sec("scalar", [deps, mm_tok], sqb)

            def smb(eng, sig, fc=fc):
                sig(eng.matmul(ps_s[:, :], cst["ones128x1_b"][:, 0:1], sq[:, :],
                               start=(fc == 0), stop=(fc == NFC - 1)))

            mm_tok = p.sec("tensor", [c_tok, prev_sev], smb)

        def sev(eng, sig, cs=cs):
            sig(eng.tensor_scalar(srow[:, 0, cs * 512:(cs + 1) * 512], ps_s[:, :],
                                  1.0 / D, 1e-5, OP.mult, OP.add))

        prev_sev = p.sec("vector", [mm_tok], sev)
        st_toks.append(prev_sev)

    def rstd(eng, sig):
        eng.activation(out=srow[:, 1, :], in_=srow[:, 0, :], func=AF.Ln)
        eng.drain()
        sig(eng.activation(out=srow[:, 0, :], in_=srow[:, 1, :], func=AF.Exp, scale=-0.5))

    r_tok = p.sec("scalar", [st_toks], rstd)
    xr_toks = []
    bc_prev = None
    for cs in range(4):
        def bcb(eng, sig, cs=cs):
            sig(eng.matmul(ps_b[:, :], cst["ones1x128_f"][0:1, :],
                           srow[0:1, 0, cs * 512:(cs + 1) * 512], start=True, stop=True))

        b_tok = p.sec("tensor", [r_tok, bc_prev], bcb)

        def xrb(eng, sig, cs=cs):
            for fc in range(NFC):
                sl = slice(cs * 512, (cs + 1) * 512)
                eng.tensor_tensor(xr_fm[:, fc, sl], xn_fm[:, fc, sl], ps_b[:, :], OP.mult)
                eng.drain()
                ins = eng.tensor_scalar(xr_fm[:, fc, sl], xr_fm[:, fc, sl],
                                        io["nwT_sb"][:, fc:fc + 1], None, OP.mult)
                if fc < NFC - 1:
                    eng.drain()
            sig(ins)

        bc_prev = p.sec("vector", [b_tok], xrb)
        xr_toks.append(bc_prev)
    xr_tok = merge_deps(xr_toks)
    p.close_scope(sc_a1, xr_tok)
    sc_a2 = p.scope()
    xr_tm = p.sbuf("l3f_xrtm", [128, 16, D], BF16, ctx=sc_a2)
    t_tok = emit_fm_to_tm(p, [xr_tok], xr_fm, xr_tm, cst["identb"], BF16, tag="l3fxt")

    def xd(eng, sig):
        sig(eng.dma_start(out=io["l3f_xr_dram"].rearrange("(tc p) d -> p tc d", p=128),
                          in_=xr_tm[:, :, :]), dma=True)

    xrd_tok = p.sec("sync", [t_tok], xd, dma=True)
    p.close_scope(sc_a2, xrd_tok)

    sc_b = p.scope()
    lg_row = p.sbuf("l3f_lg", [16, T], F32, ctx=sc_b)
    rw_sb = p.sbuf("l3f_rw", [128, 8, 16], BF16, ctx=sc_b)
    ps_l = p.psum("l3f_lps", [16, 512], F32, ctx=sc_b)

    def rwl(eng, sig):
        sig(eng.dma_start(out=rw_sb[:, :, :],
                          in_=io["router_wT"].rearrange("(kb p) e -> p kb e", p=128)),
            dma=True)

    rw_tok = p.sec("gpsimd", [deps], rwl, dma="sw")
    lg_toks = []
    ev = None
    for cs in range(4):
        def lmm(eng, sig, cs=cs):
            for kb in range(8):
                ins = eng.matmul(ps_l[:, :], rw_sb[:, kb, :],
                                 xr_fm[:, kb, cs * 512:(cs + 1) * 512],
                                 start=(kb == 0), stop=(kb == 7))
            sig(ins)

        m_tok = p.sec("tensor", [xr_tok, rw_tok, ev], lmm)

        def lev(eng, sig, cs=cs):
            sig(eng.tensor_scalar(lg_row[:, cs * 512:(cs + 1) * 512], ps_l[:, :],
                                  io["rbT_sb"][0:16, 0:1], None, OP.add))

        ev = p.sec("vector", [m_tok], lev)
        lg_toks.append(ev)
    lg_tok = merge_deps(lg_toks)

    lg_tm = p.sbuf("l3f_lgtm", [128, 16, 16], F32, ctx=sc_b)
    pr_tm = p.sbuf("l3f_prtm", [128, 16, 16], F32, ctx=sc_b)
    mcl = p.sbuf("l3f_mcl", [128, 4], F32, ctx=sc_b)
    lgb = p.sbuf("l3f_lgb", [16, T], BF16, ctx=sc_b)

    def lgc(eng, sig):
        sig(eng.tensor_copy(lgb[:, :], lg_row[:, :]))

    lgc_tok = p.sec("vector", [lg_tok], lgc)
    tp_pairs = [(lgb[0:16, tc * 128:(tc + 1) * 128], lg_tm[:, tc, :].rearrange("p e -> p e"))
                for tc in range(16)]
    lg_tmb = p.sbuf("l3f_lgtmb", [128, 16, 16], BF16, ctx=sc_b)
    tpt = emit_transpose_grid(p, [lgc_tok],
                              [(lgb[0:16, tc * 128:(tc + 1) * 128], lg_tmb[:, tc, :])
                               for tc in range(16)], cst["identb"], BF16, tag="l3flgt")
    pr_toks = []
    prev_pr = None
    for tc in range(16):
        def smx(eng, sig, tc=tc):
            eng.tensor_reduce(mcl[:, 0:1], lg_tmb[:, tc, :], AX.X, OP.max)
            eng.drain()
            sig(eng.tensor_scalar(mcl[:, 1:2], mcl[:, 0:1], -1.0, None, OP.mult))

        m1 = p.sec("vector", [tpt, prev_pr], smx)

        def sex(eng, sig, tc=tc):
            sig(eng.activation(out=pr_tm[:, tc, :], in_=lg_tmb[:, tc, :], func=AF.Exp,
                               bias=mcl[:, 1:2], accum_out=mcl[:, 2:3]))

        m2 = p.sec("scalar", [m1], sex)

        def snr(eng, sig, tc=tc):
            eng.reciprocal(mcl[:, 3:4], mcl[:, 2:3])
            eng.drain()
            sig(eng.tensor_scalar(pr_tm[:, tc, :], pr_tm[:, tc, :], mcl[:, 3:4], None,
                                  OP.mult))

        prev_pr = p.sec("vector", [m2], snr)
        pr_toks.append(prev_pr)

    def prd(eng, sig):
        sig(eng.dma_start(out=io["l3f_pr_dram"].rearrange("(tc p) e -> p tc e", p=128),
                          in_=pr_tm[:, :, :]), dma=True)
        sig(eng.dma_start(out=io["l3f_lgT_dram"], in_=lg_row[:, :]), dma=True)

    prd_tok = p.sec("sync", [pr_toks, lg_tok], prd, dma=True)
    p.close_scope(sc_b, prd_tok)

    ag1 = emit_ag(p, [prd_tok], io["l3f_lgT_dram"], io["l3f_lgall_dram"], single, "aglg")
    ag2 = emit_ag(p, [prd_tok], io["l3f_pr_dram"], io["l3f_prall_dram"], single, "agpr")
    ag3 = emit_ag(p, [xrd_tok], io["l3f_xr_dram"], io["l3f_xrall_dram"], single, "agxr")

    sc_c = p.scope()
    GL = p.sbuf("l3f_GL", [128, T], F32, ctx=sc_c)
    cmp_scr = p.sbuf("l3f_cmp", [128, T], BF16, ctx=sc_c)
    bis = p.sbuf("l3f_bis", [128, 8], F32, ctx=sc_c)
    b16 = p.sbuf("l3f_b16", [16, 8], F32, ctx=sc_c)
    ps_c = p.psum("l3f_cps", [16, 8], F32, ctx=sc_c)
    ps_cb = p.psum("l3f_cbps", [128, 8], F32, ctx=sc_c)

    def gll(eng, sig):
        sig(eng.dma_start(out=GL[:, :], in_=io["l3f_lgall_dram"]), dma=True)

    gl_tok = p.sec("sync", [ag1], gll, dma=True)

    def mnmx(eng, sig):
        eng.tensor_reduce(bis[:, 2:3], GL[:, :], AX.X, OP.min)
        eng.drain()
        eng.tensor_reduce(bis[:, 3:4], GL[:, :], AX.X, OP.max)
        eng.drain()
        eng.tensor_scalar(bis[:, 4:5], bis[:, 2:3], -1.0, None, OP.mult)
        eng.drain()
        sig(eng.tensor_copy(bisb[:, 0:3], bis[:, 2:5]))

    bisb = p.sbuf("l3f_bisb", [128, 4], BF16, ctx=sc_c)
    rowr = p.sbuf("l3f_rowr", [4, 128], BF16, ctx=sc_c)
    rowcol = p.sbuf("l3f_rowc", [4, 4], BF16, ctx=sc_c)
    rowflat = p.sbuf("l3f_rowf", [1, 2, 4], BF16, ctx=sc_c)
    rowf32 = p.sbuf("l3f_rf32", [1, 4], F32, ctx=sc_c)
    mm0 = p.sec("vector", [gl_tok], mnmx)
    tr_tok = emit_transpose_grid(p, [mm0], [(bisb[:, 0:3], rowr[0:3, :])],
                                 cst["identb"], BF16, tag="l3fbt")

    def rowred(eng, sig):
        sig(eng.tensor_reduce(rowcol[0:3, 0:1], rowr[0:3, :], AX.X, OP.max))

    rr_tok = p.sec("vector", [tr_tok], rowred)
    tr2_tok = emit_transpose_grid(p, [rr_tok], [(rowcol[0:3, 0:1], rowflat[0:1, 0, 0:3])],
                                  cst["identb"], BF16, tag="l3fbt2")

    def rcast(eng, sig):
        eng.tensor_copy(rowf32[:, 0:3], rowflat[:, 0, 0:3])
        eng.drain()
        eng.tensor_scalar(rowf32[:, 2:3], rowf32[:, 2:3], -1.0, None, OP.mult)
        eng.drain()
        eng.tensor_tensor(rowf32[:, 3:4], rowf32[:, 2:3], rowf32[:, 2:3], OP.abs_max) if False else None
        eng.tensor_scalar(rowf32[:, 0:1], rowf32[:, 2:3], 1.0, -1.0, OP.mult, OP.add)
        eng.drain()
        sig(eng.tensor_scalar(rowf32[:, 1:2], rowf32[:, 1:2], 1.0, 1.0, OP.mult, OP.add))

    rc_tok = p.sec("vector", [tr2_tok], rcast)
    ps_bb = p.psum("l3f_bbps", [128, 8], F32, ctx=sc_c)

    def bbc(eng, sig):
        sig(eng.matmul(ps_bb[:, 0:2], cst["ones1x128_f"][0:1, :], rowf32[0:1, 0:2],
                       start=True, stop=True))

    bb_tok = p.sec("tensor", [rc_tok], bbc)

    def binit2(eng, sig):
        eng.tensor_copy(bis[:, 2:3], ps_bb[:, 0:1])
        eng.drain()
        sig(eng.tensor_copy(bis[:, 3:4], ps_bb[:, 1:2]))

    bi_tok = p.sec("vector", [bb_tok], binit2)

    it_tok = bi_tok
    for it in range(42):
        def bit(eng, sig):
            eng.tensor_tensor(bis[:, 0:1], bis[:, 2:3], bis[:, 3:4], OP.add)
            eng.drain()
            eng.tensor_scalar(bis[:, 0:1], bis[:, 0:1], 0.5, None, OP.mult)
            eng.drain()
            sig(eng.tensor_scalar(cmp_scr[:, :], GL[:, :], bis[:, 0:1], 0.0, OP.is_gt,
                                  OP.add, accum_out=bis[:, 1:2]))

        t1 = p.sec("vector", [it_tok], bit)

        def csum(eng, sig):
            sig(eng.matmul(ps_c[:, 0:1], cst["l3_emap_sb"][:, 0:16], bis[:, 1:2],
                           start=True, stop=True))

        t2 = p.sec("tensor", [t1], csum)

        def cbc(eng, sig):
            sig(eng.matmul(ps_cb[:, 0:1], cst["l3_emapT_sb"][0:16, :], ps_c[0:16, 0:1],
                           start=True, stop=True))

        def cev(eng, sig):
            sig(eng.tensor_copy(b16[:, 0:1], ps_c[0:16, 0:1]))

        t3 = p.sec("vector", [t2], cev)

        def cbc2(eng, sig):
            sig(eng.matmul(ps_cb[:, 0:1], cst["l3_emapT_sb"][0:16, :], b16[0:16, 0:1],
                           start=True, stop=True))

        t4 = p.sec("tensor", [t3], cbc2)

        def upd(eng, sig):
            eng.tensor_scalar(bis[:, 4:5], ps_cb[:, 0:1], 1023.5, None, OP.is_ge)
            eng.drain()
            eng.tensor_tensor(bis[:, 5:6], bis[:, 0:1], bis[:, 2:3], OP.subtract)
            eng.drain()
            eng.scalar_tensor_tensor(bis[:, 5:6], bis[:, 5:6], 1.0, bis[:, 4:5],
                                     OP.mult, OP.mult)
            eng.drain()
            eng.tensor_tensor(bis[:, 2:3], bis[:, 2:3], bis[:, 5:6], OP.add)
            eng.drain()
            eng.tensor_tensor(bis[:, 5:6], bis[:, 0:1], bis[:, 3:4], OP.subtract)
            eng.drain()
            eng.tensor_scalar(bis[:, 6:7], bis[:, 4:5], 1.0, -1.0, OP.subtract, OP.mult)
            eng.drain()
            eng.tensor_tensor(bis[:, 5:6], bis[:, 5:6], bis[:, 6:7], OP.mult)
            eng.drain()
            sig(eng.tensor_tensor(bis[:, 3:4], bis[:, 3:4], bis[:, 5:6], OP.add))

        it_tok = p.sec("vector", [t4], upd)

    M_dram = io["l3f_m_dram"]

    def mfin(eng, sig):
        sig(eng.tensor_scalar(GL[:, :], GL[:, :], bis[:, 2:3], None, OP.is_gt))

    mf_tok = p.sec("vector", [it_tok], mfin)

    def mstore(eng, sig):
        sig(eng.dma_start(out=M_dram, in_=GL[:, :]), dma=True)

    ms_tok = p.sec("sync", [mf_tok], mstore, dma=True)
    p.close_scope(sc_c, ms_tok)
    p.close_scope(sc_a, merge_deps([ms_tok, lg_tok]))
    return ms_tok, ag2, ag3


def emit_l3_ffn_part2(p, io, cst, ms_tok, ag2, ag3, single, y_out, extra_dep=None):
    sc_f = p.scope()
    own = p.sbuf("l3f_own", [64, 3, T], F32, ctx=sc_f)
    idxf = p.sbuf("l3f_idxcf", [128, 16], F32, ctx=sc_f)
    idxu = p.sbuf("l3f_idxcu", [128, 16], U32, ctx=sc_f)
    idxfT = p.sbuf("l3f_idxft", [16, 128], F32, ctx=sc_f)
    tcol = p.sbuf("l3f_tcol", [64, 1], F32, ctx=sc_f)
    cntb = p.sbuf("l3f_cntb", [128, 2], F32, ctx=sc_f)
    trow = p.sbuf("l3f_trow", [1, 2], F32, ctx=sc_f)
    ownidx = p.sbuf("l3f_oidx", [128, 1], U32, ctx=sc_f)
    rowq = p.sbuf("l3f_rowq", [1, 2, 512], F32, ctx=sc_f)
    posq = p.sbuf("l3f_posq", [128, 4096], F32, ctx=sc_f)
    tokq = p.sbuf("l3f_tokq", [128, 4096], F32, ctx=sc_f)
    sink = p.sbuf("l3f_sink", [128, 4096], F32, ctx=sc_f)
    part = p.sbuf("l3f_part", [128, 2], F32, ctx=sc_f)
    ps_p = p.psum("l3f_pps", [64, 1], F32, ctx=sc_f)
    ps_t = p.psum("l3f_tps", [1, 2], F32, ctx=sc_f)
    ps_tb = p.psum("l3f_tbps", [128, 2], F32, ctx=sc_f)
    ps_qa = p.psum("l3f_qpsa", [128, 512], F32, ctx=sc_f)
    ps_qb = p.psum("l3f_qpsb", [128, 512], F32, ctx=sc_f)

    def oil(eng, sig):
        sig(eng.dma_start(out=ownidx[:, :], in_=io["own_rows"]), dma=True)

    oi_tok = p.sec("sync", [], oil, dma=True)

    def omz(eng, sig):
        sig(eng.memset(own[:, 0, :], 0.0))

    oz_tok = p.sec("vector", [], omz)

    def ogather(eng, sig):
        sig(ind_dma(eng, lambda: eng.indirect_dma_start(
            out=own[0:64, 0, :], out_offset=None, in_=io["l3f_m_dram"],
            in_offset=IOA(ap=ownidx[0:64, :], axis=0),
            bounds_check=127, oob_is_err=False)), dma=True)

    og_tok = p.sec("gpsimd", [oz_tok, oi_tok, ms_tok], ogather, dma="sw")

    comp_toks = []
    pt_dma = []
    for j in range(2):
        base = 32 * j

        def posb(eng, sig, base=base):
            eng.memset(own[base:base + 8, 1, :], 1.0)
            eng.drain()
            eng.tensor_tensor_scan(own[base:base + 8, 1, :], own[base:base + 8, 1, :],
                                   own[base:base + 8, 0, :], 0.0, OP.mult, OP.add)
            eng.drain()
            sig(eng.tensor_tensor(own[base:base + 8, 2, :], own[base:base + 8, 0, :],
                                  cst["l3f_gid_sb"][base:base + 8, :], OP.mult))

        p1 = p.sec("vector", [og_tok, comp_toks[-1] if comp_toks else None], posb)

        def pref(eng, sig, base=base):
            sig(eng.matmul(ps_p[base:base + 8, 0:1], cst["l3_tri8_sb"][base:base + 8, 0:8],
                           own[base:base + 8, 1, T - 1:T], start=True, stop=True))

        p2 = p.sec("tensor", [p1], pref)

        def totm(eng, sig, base=base, j=j):
            sig(eng.matmul(ps_t[0:1, j:j + 1], cst["l3_ones8_sb"][base:base + 8, 0:1],
                           own[base:base + 8, 1, T - 1:T], start=True, stop=True))

        p2b = p.sec("tensor", [p2], totm)

        def pos2(eng, sig, base=base):
            eng.tensor_copy(tcol[base:base + 8, 0:1], ps_p[base:base + 8, 0:1])
            eng.drain()
            sig(eng.tensor_scalar(own[base:base + 8, 1, :], own[base:base + 8, 1, :],
                                  tcol[base:base + 8, 0:1], None, OP.add))

        p3 = p.sec("vector", [p2, p2b], pos2)
        comp_toks.append(p3)

        def pd(eng, sig, base=base, j=j):
            sig(eng.dma_start(out=io["l3f_pos_dram"][j], in_=own[base:base + 8, 1, :]),
                dma=True)
            sig(eng.dma_start(out=io["l3f_tok_dram"][j], in_=own[base:base + 8, 2, :]),
                dma=True)

        pt_dma.append(fence(p, p.sec("sync", [p3], pd, dma=True)))

    def totev(eng, sig):
        sig(eng.tensor_copy(trow[0:1, 0:2], ps_t[0:1, 0:2]))

    te_tok = p.sec("vector", [comp_toks], totev)

    def totbc(eng, sig):
        sig(eng.matmul(ps_tb[:, 0:2], cst["ones1x128_f"][0:1, :], trow[0:1, 0:2],
                       start=True, stop=True))

    tb_tok = p.sec("tensor", [te_tok], totbc)

    def totcp(eng, sig):
        sig(eng.tensor_copy(cntb[:, 0:2], ps_tb[:, 0:2]))

    tc_tok = p.sec("vector", [tb_tok], totcp)

    ex_toks = []
    stt_last = None
    ev_prev = None
    for j in range(2):
        def zidx(eng, sig, j=j):
            sig(eng.memset(idxf[:, j * 8:(j + 1) * 8], 0.0))

        z_tok = p.sec("vector", [stt_last], zidx)
        for q in range(4):
            ev_toks = []
            for nh in range(8):
                def rl(eng, sig, j=j, q=q, nh=nh):
                    b0 = q * 4096 + nh * 512
                    srcp = io["l3f_pos_dram"][j].rearrange("r t -> (r t)")
                    srct = io["l3f_tok_dram"][j].rearrange("r t -> (r t)")
                    sig(eng.dma_start(out=rowq[0:1, 0, :], in_=srcp[None, b0:b0 + 512]),
                        dma=True)
                    sig(eng.dma_start(out=rowq[0:1, 1, :], in_=srct[None, b0:b0 + 512]),
                        dma=True)

                ld = p.sec("sync", [pt_dma[j], ev_prev, stt_last], rl, dma=True)

                def bmm(eng, sig):
                    eng.matmul(ps_qa[:, :], cst["ones1x128_f"][0:1, :], rowq[0:1, 0, :],
                               start=True, stop=True)
                    sig(eng.matmul(ps_qb[:, :], cst["ones1x128_f"][0:1, :],
                                   rowq[0:1, 1, :], start=True, stop=True))

                b1 = p.sec("tensor", [ld, ev_prev], bmm)

                def ev(eng, sig, nh=nh):
                    eng.tensor_copy(posq[:, nh * 512:(nh + 1) * 512], ps_qa[:, :])
                    eng.drain()
                    sig(eng.tensor_copy(tokq[:, nh * 512:(nh + 1) * 512], ps_qb[:, :]))

                ev_prev = p.sec("vector", [b1], ev)
                ev_toks.append(ev_prev)
            stp = None
            for st in range(8):
                def ext(eng, sig, st=st):
                    sig(eng.scalar_tensor_tensor(sink[:, :], posq[:, :],
                                                 cst["l3f_slotiota_sb"][:, st:st + 1],
                                                 tokq[:, :], OP.is_equal, OP.mult,
                                                 accum_out=part[:, 0:1]))

                e1 = p.sec("vector", [ev_toks, stp], ext)

                def eacc(eng, sig, j=j, st=st):
                    c = j * 8 + st
                    sig(eng.tensor_tensor(idxf[:, c:c + 1], idxf[:, c:c + 1],
                                          part[:, 0:1], OP.add))

                stp = p.sec("vector", [e1, z_tok], eacc)
            stt_last = stp

        def oobf(eng, sig, j=j):
            for st in range(8):
                eng.scalar_tensor_tensor(part[:, 1:2], cst["l3f_slotiota_sb"][:, st:st + 1],
                                         cntb[:, j:j + 1],
                                         cst["l3f_slotiota_sb"][:, st:st + 1], OP.is_gt,
                                         OP.bypass)
                eng.drain()
                c = j * 8 + st
                ins = eng.scalar_tensor_tensor(idxf[:, c:c + 1], part[:, 1:2], 2e9,
                                               idxf[:, c:c + 1], OP.mult, OP.add)
                if st < 7:
                    eng.drain()
            sig(ins)

        stt_last = p.sec("vector", [stt_last, tc_tok], oobf)
        ex_toks.append(stt_last)
    ex_tok = merge_deps(ex_toks)

    def ucast(eng, sig):
        sig(eng.tensor_copy(idxu[:, :], idxf[:, :]))

    uc_tok = p.sec("vector", [ex_tok], ucast)
    tf_tok = emit_transpose_grid(p, [ex_tok], [(idxf[:, 0:16], idxfT[0:16, :])],
                                 cst["ident"][:, :], F32, tag="l3fit")

    def idma(eng, sig):
        sig(eng.dma_start(out=io["l3f_idx_dram"], in_=idxu[:, :]), dma=True)
        dst = io["l3f_idxf_dram"].rearrange("(js pp) c -> js (pp c)", pp=128)
        sig(eng.dma_start(out=dst, in_=idxfT[0:16, :]), dma=True)

    id_tok = fence(p, p.sec("sync", [uc_tok, tf_tok], idma, dma=True))
    p.close_scope(sc_f, id_tok)

    sc_g2 = p.scope()
    idxu2 = p.sbuf("l3f_idxu2", [128, 16], U32, ctx=sc_g2)

    def irld(eng, sig):
        sig(eng.dma_start(out=idxu2[:, :], in_=io["l3f_idx_dram"]), dma=True)

    ir_tok = fence(p, p.sec("sync", [id_tok], irld, dma=True))
    con_toks = []
    for j in range(2):
        sc_e = p.scope()
        gtile = p.sbuf("l3f_gt", [128, 16], F32, ctx=sc_e)
        gcol = p.sbuf("l3f_gcol", [128, 8], F32, ctx=sc_e)
        xe_tm = p.sbuf("l3f_xetm", [128, 1024], BF16, ctx=sc_e)
        xeT = p.sbuf("l3f_xeT", [128, 8, 1024], BF16, ctx=sc_e)
        prev_g = None
        for st in range(8):
            def gg(eng, sig, j=j, st=st):
                sig(ind_dma(eng, lambda: eng.indirect_dma_start(
                    out=gtile[:, :], out_offset=None, in_=io["l3f_prall_dram"],
                    in_offset=IOA(ap=idxu2[:, j * 8 + st:j * 8 + st + 1], axis=0),
                    bounds_check=16383, oob_is_err=False)), dma=True)

            g1 = p.sec("gpsimd", [ir_tok, ag2, prev_g], gg, dma="sw")

            def gdot(eng, sig, j=j, st=st):
                eng.tensor_tensor(gtile[:, :], gtile[:, :],
                                  io["ehot_sb"][:, j * 16:(j + 1) * 16], OP.mult)
                eng.drain()
                sig(eng.tensor_reduce(gcol[:, st:st + 1], gtile[:, :], AX.X, OP.add))

            prev_g = p.sec("vector", [g1], gdot)

            def xg(eng, sig, j=j, st=st):
                sig(ind_dma(eng, lambda: eng.indirect_dma_start(
                    out=xe_tm[:, :], out_offset=None, in_=io["l3f_xrall_dram"],
                    in_offset=IOA(ap=idxu2[:, j * 8 + st:j * 8 + st + 1], axis=0),
                    bounds_check=16383, oob_is_err=False)), dma=True)

            x1 = p.sec("gpsimd", [ir_tok, ag3, prev_g], xg, dma="sw")
            tp = [(xe_tm[:, kb * 128:(kb + 1) * 128], xeT[:, kb, st * 128:(st + 1) * 128])
                  for kb in range(8)]
            prev_g = emit_transpose_grid(p, [x1], tp, cst["identb"], BF16,
                                         tag=f"l3fxe{j}")
        ge_tok = prev_g

        h_fm = p.sbuf("l3f_h", [128, 32, 1024], BF16, ctx=sc_e)
        f1_toks = emit_gemm_fm_rowmajor(p, f"l3fc1_{j}", ge_tok, io["fc1w"][j], 4096, D,
                                        lambda kb: xeT[:, kb, :],
                                        lambda mb, c0, w: h_fm[:, mb, c0:c0 + w],
                                        cst, act=AF.Gelu, Tlen=1024,
                                        bias_sb=io["fc1bT_sb"][:, j * 32:(j + 1) * 32])
        y_fm = p.sbuf("l3f_yfm", [128, 8, 1024], BF16, ctx=sc_e)
        f2_toks = emit_gemm_fm_rowmajor(p, f"l3fc2_{j}", f1_toks, io["fc2w"][j], 1024,
                                        4096, lambda kb: h_fm[:, kb, :],
                                        lambda mb, c0, w: y_fm[:, mb, c0:c0 + w],
                                        cst, Tlen=1024,
                                        bias_sb=io["fc2bT_sb"][:, j * 8:(j + 1) * 8])
        grow2 = p.sbuf("l3f_grow2", [1, 1024], BF16, ctx=sc_e)
        gcolT = p.sbuf("l3f_gcolT", [8, 128], F32, ctx=sc_e)
        ps_gb = p.psum("l3f_gbps", [128, 512], F32, ctx=sc_e)
        gt_tok = emit_transpose_grid(p, [ge_tok], [(gcol[:, 0:8], gcolT[0:8, :])],
                                     cst["ident"][:, :], F32, tag=f"l3fgt{j}")

        def gwr(eng, sig, j=j):
            sig(eng.dma_start(out=io["l3f_g_dram"][j], in_=gcolT[0:8, :]), dma=True)

        gw_tok = p.sec("sync", [gt_tok], gwr, dma=True)

        def grd(eng, sig, j=j):
            src = io["l3f_g_dram"][j].rearrange("st pp -> (st pp)")
            sig(eng.dma_start(out=grow2[:, :], in_=src[None, :]), dma=True)

        gr_tok = p.sec("gpsimd", [gw_tok], grd, dma="sw")
        gm_toks = []
        ev3 = None
        for cs in range(2):
            def gbc(eng, sig, cs=cs):
                sig(eng.matmul(ps_gb[:, :], cst["ones1x128_b"][0:1, :],
                               grow2[0:1, cs * 512:(cs + 1) * 512], start=True, stop=True))

            b1 = p.sec("tensor", [gr_tok, ev3], gbc)

            def gmul(eng, sig, cs=cs):
                for mb in range(8):
                    ins = eng.tensor_tensor(y_fm[:, mb, cs * 512:(cs + 1) * 512],
                                            y_fm[:, mb, cs * 512:(cs + 1) * 512],
                                            ps_gb[:, :], OP.mult)
                    if mb < 7:
                        eng.drain()
                sig(ins)

            ev3 = p.sec("vector", [b1, f2_toks], gmul)
            gm_toks.append(ev3)
        gm_tok = merge_deps(gm_toks)

        crow = p.sbuf("l3f_crow", [128, 8, 1032], BF16, ctx=sc_e)
        tpb = [(y_fm[:, mb, st * 128:(st + 1) * 128], crow[:, st, mb * 128:(mb + 1) * 128])
               for mb in range(8) for st in range(8)]
        ct_tok = emit_transpose_grid(p, [gm_tok], tpb, cst["identb"], BF16,
                                     tag=f"l3fct{j}")

        def gapp(eng, sig, j=j):
            eng.memset(crow[:, :, 1025:1032], 0.0)
            eng.drain()
            for st in range(8):
                ins = eng.tensor_copy(crow[:, st, 1024:1025], gcol[:, st:st + 1])
                if st < 7:
                    eng.drain()
            sig(ins)

        ga_tok = p.sec("vector", [ct_tok, ge_tok], gapp)

        def cdma(eng, sig, j=j):
            dst = io["l3f_con_dram"][j * 1024:(j + 1) * 1024, :]
            sig(eng.dma_start(out=dst.rearrange("(st pp) c -> pp st c", pp=128),
                              in_=crow[:, :, :]), dma=True)

        con_toks.append(p.sec("sync", [ga_tok], cdma, dma=True))
        p.close_scope(sc_e, con_toks[-1])
    con_tok = merge_deps(con_toks)
    p.close_scope(sc_g2, con_tok)

    ag4 = emit_ag(p, [con_tok], io["l3f_con_dram"], io["l3f_conall_dram"], single,
                  "agcon")
    ag5 = emit_ag(p, [id_tok], io["l3f_idxf_dram"], io["l3f_idxall_dram"], single,
                  "agidx")

    sc_s = p.scope()
    zt = p.sbuf("l3f_zt", [128, 16, 1032], F32, ctx=sc_s)
    crld = p.sbuf("l3f_crld", [128, 1032], BF16, ctx=sc_s)
    crf = p.sbuf("l3f_crf", [128, 1032], F32, ctx=sc_s)
    lidxf = p.sbuf("l3f_lidxf", [128, 1], F32, ctx=sc_s)
    lidxu = p.sbuf("l3f_lidxu", [128, 1], U32, ctx=sc_s)
    negf = p.sbuf("l3f_negf", [128, 1], F32, ctx=sc_s)
    icol = p.sbuf("l3f_icol", [128, 1], F32, ctx=sc_s)

    def zz(eng, sig):
        sig(eng.memset(zt[:, :, :], 0.0))

    z1 = p.sec("vector", [], zz)

    def zd(eng, sig):
        sig(eng.dma_start(out=io["l3f_acc_dram"].rearrange("(tc pp) c -> pp tc c", pp=128),
                          in_=zt[:, :, :]), dma=True)

    z2 = fence(p, p.sec("sync", [z1], zd, dma=True))

    prev_sc = z2
    for e in range(16):
        for st in range(8):
            def crl(eng, sig, e=e, st=st):
                r0 = e * 1024 + st * 128
                sig(eng.dma_start(out=crld[:, :],
                                  in_=io["l3f_conall_dram"][r0:r0 + 128, :]), dma=True)
                sig(eng.dma_start(out=icol[:, :],
                                  in_=io["l3f_idxall_dram"][r0:r0 + 128, :]), dma=True)

            c1 = p.sec("sync", [ag4, ag5, prev_sc], crl, dma=True)

            def lix(eng, sig, e=e, st=st):
                eng.tensor_copy(crf[:, :], crld[:, :])
                eng.drain()
                eng.tensor_copy(lidxf[:, :], icol[:, :])
                eng.drain()
                eng.tensor_scalar(lidxf[:, :], lidxf[:, :], io["mylo_sb"][:, 0:1], None,
                                  OP.subtract)
                eng.drain()
                eng.tensor_scalar(negf[:, :], lidxf[:, :], 0.0, 4e6, OP.is_lt, OP.mult)
                eng.drain()
                eng.tensor_tensor(lidxf[:, :], lidxf[:, :], negf[:, :], OP.add)
                eng.drain()
                sig(eng.tensor_copy(lidxu[:, :], lidxf[:, :]))

            l1 = p.sec("vector", [c1], lix)

            def sca(eng, sig):
                sig(ind_dma(eng, lambda: eng.indirect_dma_start(
                    out=io["l3f_acc_dram"], out_offset=IOA(ap=lidxu[:, :], axis=0),
                    in_=crf[:, :], in_offset=None, bounds_check=T - 1,
                    oob_is_err=False, compute_op=OP.add)), dma=True)

            prev_sc = p.sec("gpsimd", [l1], sca, dma="sw")
    sc_tok = prev_sc

    acct = p.sbuf("l3f_acct", [128, 2, 1034], F32, ctx=sc_s)
    outb = p.sbuf("l3f_outb", [128, 2, 1024], BF16, ctx=sc_s)
    fin_toks = []
    prev_f = [None, None]
    for tc in range(16):
        b = tc % 2

        def al(eng, sig, tc=tc, b=b):
            sig(eng.dma_start(out=acct[:, b, 0:1032],
                              in_=io["l3f_acc_dram"][tc * 128:(tc + 1) * 128, :]),
                dma=True)

        a1 = p.sec("sync", [sc_tok, prev_f[b], extra_dep], al, dma=True)

        def fz(eng, sig, tc=tc, b=b):
            eng.tensor_scalar(acct[:, b, 1033:1034], acct[:, b, 1024:1025], 0.0, None,
                              OP.is_le)
            eng.drain()
            eng.tensor_tensor(acct[:, b, 1024:1025], acct[:, b, 1024:1025],
                              acct[:, b, 1033:1034], OP.add)
            eng.drain()
            eng.reciprocal(acct[:, b, 1033:1034], acct[:, b, 1024:1025])
            eng.drain()
            sig(eng.tensor_scalar(outb[:, b, :], acct[:, b, 0:1024],
                                  acct[:, b, 1033:1034], None, OP.mult))

        f1 = p.sec("vector", [a1], fz)

        def fd(eng, sig, tc=tc, b=b):
            dst = y_out.rearrange("(tc pp) d -> pp tc d", pp=128)[:, tc, :]
            sig(eng.dma_start(out=dst, in_=outb[:, b, :], accum_op=OP.add), dma=True)

        prev_f[b] = p.sec("gpsimd", [f1], fd, dma="sw")
        fin_toks.append(prev_f[b])
    last = merge_deps(fin_toks)
    p.close_scope(sc_s, last)
    return last


def declare_io_l3(nc, stage, pref="", xap=None, yap=None):
    io = {"x": xap if xap is not None else
          nc.dram_tensor("x", [T, D], F32, kind="ExternalInput").ap(),
          "y": yap if yap is not None else
          nc.dram_tensor("y", [T, D], F32, kind="ExternalOutput").ap()}
    names = []
    if stage in ("attn", "full"):
        names += [("q1_w", [D, D]), ("q1bT", [128, 8]), ("k1_w", [D, D]),
                  ("k1bT", [128, 8]), ("k2_w", [D, D]), ("k2bsT", [128, 8]),
                  ("Wg2", [128, 128]), ("gwb2", [128, 1]), ("Ws2", [128, 128]),
                  ("swb2", [128, 1]), ("cp_w", [D, D]), ("cpbT", [128, 8])]
    if stage in ("ffn", "full"):
        names += [("nwT", [128, 8]), ("router_wT", [D, 16]), ("rbT", [16, 1]),
                  ("fc1bT", [128, 64]), ("fc2bT", [128, 16]), ("ehot", [128, 32]),
                  ("mylo", [128, 1]), ("fc1w", [2, 4096, D]), ("fc2w", [2, D, 4096])]
    for nm, shp in names:
        io[nm] = nc.dram_tensor(pref + nm, shp, F32, kind="ExternalInput").ap()
    if stage in ("ffn", "full"):
        io["own_rows"] = nc.dram_tensor(pref + "own_rows", [128, 1], U32,
                                        kind="ExternalInput").ap()
    return io


def host_prep_l3(inp, cid):
    d = {}
    d["q1_w"] = np.asarray(inp["L3_attn_q1_w"], np.float32)
    d["q1bT"] = _col128(inp["L3_attn_q1_b"])
    d["k1_w"] = np.asarray(inp["L3_attn_k1_w"], np.float32)
    d["k1bT"] = _col128(inp["L3_attn_k1_b"])
    d["k2_w"] = np.asarray(inp["L3_attn_k2_w"], np.float32)
    d["k2bsT"] = _col128(np.asarray(inp["L3_attn_k2_b"], np.float32) * (0.02 / 32.0))
    for nm, bnm, wk, bk in (("Wg2", "gwb2", "L3_attn_gw_w", "L3_attn_gw_b"),
                            ("Ws2", "swb2", "L3_attn_sw_w", "L3_attn_sw_b")):
        w = np.broadcast_to(np.asarray(inp[wk], np.float32), (64, 64))
        b = np.broadcast_to(np.asarray(inp[bk], np.float32), (64,))
        W2 = np.zeros((128, 128), np.float32)
        W2[0:64, 0:64] = w
        W2[64:128, 64:128] = w
        d[nm] = W2
        d[bnm] = np.tile(b, 2)[:, None].copy()
    d["cp_w"] = np.asarray(inp["L3_attn_cp_w"], np.float32)
    d["cpbT"] = _col128(inp["L3_attn_cp_b"])
    d["nwT"] = _col128(inp["L3_ffn_norm_weight"])
    d["router_wT"] = np.ascontiguousarray(
        np.asarray(inp["L3_ffn_router_weight"], np.float32).T)
    d["rbT"] = np.asarray(inp["L3_ffn_router_bias"], np.float32)[:, None]
    f1b = np.asarray(inp["L3_ffn_fc1_bias"], np.float32)
    f2b = np.asarray(inp["L3_ffn_fc2_bias"], np.float32)
    fc1bT = np.zeros((128, 64), np.float32)
    fc2bT = np.zeros((128, 16), np.float32)
    for j in range(2):
        e = 2 * cid + j
        fc1bT[:, j * 32:(j + 1) * 32] = f1b[e].reshape(32, 128).T
        fc2bT[:, j * 8:(j + 1) * 8] = f2b[e].reshape(8, 128).T
    d["fc1bT"] = fc1bT
    d["fc2bT"] = fc2bT
    d["fc1w"] = np.ascontiguousarray(
        np.asarray(inp["L3_ffn_fc1_weight"], np.float32)[2 * cid:2 * cid + 2])
    d["fc2w"] = np.ascontiguousarray(
        np.asarray(inp["L3_ffn_fc2_weight"], np.float32)[2 * cid:2 * cid + 2])
    orows = np.full((128, 1), 4000000000, np.uint32)
    for j in range(2):
        e = 2 * cid + j
        orows[32 * j:32 * j + 8, 0] = np.arange(8) * 16 + e
    d["own_rows"] = orows
    eh = np.zeros((128, 32), np.float32)
    eh[:, 2 * cid] = 1.0
    eh[:, 16 + 2 * cid + 1] = 1.0
    d["ehot"] = eh
    d["mylo"] = np.full((128, 1), cid * T, np.float32)
    return d


def build_l3(stage="full", single=False):
    nc = bass.Bass("TRN2", target_bir_lowering=False)
    consts = make_consts()
    make_l3_consts(consts)
    caps = declare_consts(nc, consts)
    io = declare_io_l3(nc, stage)
    io["caps"] = caps
    do_attn = stage in ("attn", "full")
    do_ffn = stage in ("ffn", "full")
    if do_attn:
        io["l3_q2p_dram"] = nc.dram_tensor("l3q2p", [8, 128, T], BF16).ap()
        io["l3_k2_dram"] = nc.dram_tensor("l3k2d", [8, 128, T], BF16).ap()
        io["l3_v_dram"] = nc.dram_tensor("l3vd", [T + 1, D], BF16).ap()
        io["l3_o1_dram"] = nc.dram_tensor("l3o1", [T, D], BF16).ap()
        io["l3a_out_dram"] = nc.dram_tensor("l3ao", [8, 128, T], BF16).ap()
    if do_ffn:
        io["l3f_xr_dram"] = nc.dram_tensor("l3fxr", [T, D], BF16).ap()
        io["l3f_xrall_dram"] = nc.dram_tensor("l3fxra", [8 * T, D], BF16).ap()
        io["l3f_pr_dram"] = nc.dram_tensor("l3fpr", [T, 16], F32).ap()
        io["l3f_prall_dram"] = nc.dram_tensor("l3fpra", [8 * T, 16], F32).ap()
        io["l3f_lgT_dram"] = nc.dram_tensor("l3flg", [16, T], F32).ap()
        io["l3f_lgall_dram"] = nc.dram_tensor("l3flga", [128, T], F32).ap()
        io["l3f_m_dram"] = nc.dram_tensor("l3fm", [128, T], F32).ap()
        io["l3f_pos_dram"] = nc.dram_tensor("l3fpos", [2, 8, T], F32).ap()
        io["l3f_tok_dram"] = nc.dram_tensor("l3ftok", [2, 8, T], F32).ap()
        io["l3f_idx_dram"] = nc.dram_tensor("l3fidx", [128, 16], U32).ap()
        io["l3f_idxf_dram"] = nc.dram_tensor("l3fidxf", [2048, 1], F32).ap()
        io["l3f_idxall_dram"] = nc.dram_tensor("l3fidxa", [16384, 1], F32).ap()
        io["l3f_g_dram"] = nc.dram_tensor("l3fgd", [2, 128, 8], F32).ap()
        io["l3f_con_dram"] = nc.dram_tensor("l3fcon", [2048, 1032], BF16).ap()
        io["l3f_conall_dram"] = nc.dram_tensor("l3fcona", [16384, 1032], BF16).ap()
        io["l3f_acc_dram"] = nc.dram_tensor("l3facc", [T, 1032], F32).ap()
    p = P(nc)
    small = []
    cstn = {}
    cnames = []
    if do_attn:
        cnames += ["l3_triu_incl", "l3_triu_strict"]
    if do_ffn:
        cnames += ["l3_emap", "l3_emapT", "l3_tri8", "l3_ones8", "l3f_slotiota",
                   "l3f_gid", "ones1x128_b"]
    for nm in cnames:
        import ml_dtypes
        dt = BF16 if consts[nm].dtype == ml_dtypes.bfloat16 else F32
        t_ = p.sbuf(nm + "S", list(consts[nm].shape), dt)
        cstn[nm + "_sb" if nm.startswith("l3") else nm] = t_
        small.append((nm, t_[:, :]))
    if do_attn:
        for nm in ("q1bT", "k1bT", "k2bsT", "gwb2", "swb2", "cpbT"):
            io[nm + "_sb"] = p.sbuf(nm + "S", list(io[nm].shape), F32)
            small.append((nm, io[nm + "_sb"][:, :]))
        io["Wg2_sb"] = p.sbuf("Wg2S", [128, 128], BF16)
        io["Ws2_sb"] = p.sbuf("Ws2S", [128, 128], BF16)
    if do_ffn:
        for nm in ("nwT", "rbT", "fc1bT", "fc2bT", "ehot", "mylo"):
            io[nm + "_sb"] = p.sbuf(nm + "S", list(io[nm].shape), F32)
            small.append((nm, io[nm + "_sb"][:, :]))
    sm_tok = load_const_tiles(p, dict(list(caps.items()) + list(io.items())), small)
    if do_attn:
        def wgl(eng, sig):
            sig(eng.dma_start(out=io["Wg2_sb"][:, :], in_=io["Wg2"]), dma=True)
            sig(eng.dma_start(out=io["Ws2_sb"][:, :], in_=io["Ws2"]), dma=True)

        io["wgs_tok"] = p.sec("gpsimd", [], wgl, dma="sw")
    xn_fm = p.sbuf("xn_fm", [128, NFC, T], BF16)
    scSX = p.scope()
    cst, x_fm, t_tok = emit_common_prelude(p, caps, io, xctx=scSX)
    cst.update(cstn)
    n_tok = emit_outer_ln(p, [t_tok, sm_tok], x_fm, xn_fm, cst)
    cp_tok = emit_copy_x_to_y(p, [t_tok], io["x"], io["y"])
    p.close_scope(scSX, merge_deps([n_tok, cp_tok]))
    a_tok = None
    if do_attn:
        at = emit_l3_attn(p, [n_tok, sm_tok], xn_fm, io, cst, single=single)
        a_tok = emit_fm_dram_add(p, [at, cp_tok], io["l3a_out_dram"], io["y"], cst,
                                 nMB=8, tag="l3aadd")
    if do_ffn:
        ms_tok, ag2, ag3 = emit_l3_ffn(p, [n_tok, sm_tok], xn_fm, io, cst,
                                       single=single)
        emit_l3_ffn_part2(p, io, cst, ms_tok, ag2, ag3, single, io["y"],
                          extra_dep=merge_deps([a_tok, cp_tok]))
    p.barrier_close()
    return nc, consts



def host_prep_l0(inp):
    perm = l0_qkv_perm()
    Wqkv = np.asarray(inp["L0_attn_W_qkv"], np.float32)
    bqkv = np.asarray(inp["L0_attn_b_qkv"], np.float32)
    return {"Wqkv": np.ascontiguousarray(Wqkv[:, perm]),
            "bqkvT": _col128(bqkv[perm]),
            "Wout": np.asarray(inp["L0_attn_W_out"], np.float32),
            "boutT": _col128(np.asarray(inp["L0_attn_b_out"], np.float32)),
            "wg": np.asarray(inp["L0_ffn_w_gate"], np.float32),
            "wu": np.asarray(inp["L0_ffn_w_up"], np.float32),
            "wd": np.asarray(inp["L0_ffn_w_down"], np.float32)}


def kernel_unfused(**inputs):
    x = np.asarray(inputs["L0_attn_x"], np.float32)
    acts = [np.ascontiguousarray(x[c]) for c in range(NCORES)]
    for build, prep in ((build_l0, host_prep_l0), (build_l1, host_prep_l1),
                        (build_l2, host_prep_l2)):
        nc, consts = build("full")
        shared = prep(inputs)
        inmaps = []
        for c in range(NCORES):
            im = dict(consts)
            im.update(shared)
            im["x"] = acts[c]
            inmaps.append(im)
        res = run_bass_kernel_spmd(nc, inmaps, core_ids=list(range(NCORES)))
        acts = [np.asarray(res.results[c]["y"], np.float32) for c in range(NCORES)]
    nc, consts = build_l3("full")
    inmaps = []
    for c in range(NCORES):
        im = dict(consts)
        im.update(host_prep_l3(inputs, c))
        im["x"] = acts[c]
        inmaps.append(im)
    res = run_bass_kernel_spmd(nc, inmaps, core_ids=list(range(NCORES)))
    return np.stack([np.asarray(res.results[c]["y"], np.float32)
                     for c in range(NCORES)])



def build_full():
    import ml_dtypes
    nc = bass.Bass("TRN2", target_bir_lowering=False)
    consts = make_consts()
    make_l1_consts(consts)
    make_l1f_consts(consts)
    consts["l1_masks"] = consts["l1_masks"].astype(ml_dtypes.bfloat16)
    consts["l1f_mask"] = consts["l1f_mask"].astype(ml_dtypes.bfloat16)
    blk = np.zeros((128, 2), np.float32)
    blk[0:64, 0] = 1.0
    blk[64:128, 1] = 1.0
    consts["blk64T_b"] = blk.astype(ml_dtypes.bfloat16)
    consts["blk64_f"] = np.ascontiguousarray(blk.T)
    make_l2_consts(consts)
    make_l3_consts(consts)
    caps = declare_consts(nc, consts)
    x_in = nc.dram_tensor("x", [T, D], F32, kind="ExternalInput").ap()
    y_out = nc.dram_tensor("y", [T, D], F32, kind="ExternalOutput").ap()
    act0 = nc.dram_tensor("act0", [T, D], F32).ap()
    act1 = nc.dram_tensor("act1", [T, D], F32).ap()
    act2 = nc.dram_tensor("act2", [T, D], F32).ap()
    io0 = declare_io_l0(nc, "full", pref="a_", xap=x_in, yap=act0)
    io1 = declare_io_l1(nc, "full", pref="b_", xap=act0, yap=act1)
    io2d = declare_io_l2(nc, "full", pref="c_", xap=act1, yap=act2)
    io3 = declare_io_l3(nc, "full", pref="d_", xap=act2, yap=y_out)
    p = P(nc)

    scL = p.scope()
    cst, x_fm, t_tok = emit_common_prelude(p, caps, io0, xctx=scL, cctx=scL)
    xq_fm = p.sbuf("l0xq_fm", [128, NFC, T], BF16, ctx=scL)
    ioA = {"Wqkv": io0["Wqkv"], "Wout": io0["Wout"]}
    ioA["bqkvT_sb"] = p.sbuf("bqkvT_sb", [128, 24], F32, ctx=scL)
    ioA["boutT_sb"] = p.sbuf("boutT_sb", [128, 8], F32, ctx=scL)
    b_tok = load_const_tiles(p, {"bqkvT": io0["bqkvT"], "boutT": io0["boutT"]},
                             [("bqkvT", ioA["bqkvT_sb"][:, :]),
                              ("boutT", ioA["boutT_sb"][:, :])])
    scA = p.scope()
    xn_fm = p.sbuf("xn_fm", [128, NFC, T], BF16, ctx=scA)
    scB = p.scope()
    mean_sb = p.sbuf("ln_mean", [128, T], F32, ctx=scB)
    rstd_sb = p.sbuf("ln_rstd", [128, T], F32, ctx=scB)
    n_tok = emit_outer_ln(p, [t_tok], x_fm, xn_fm, cst, mean_sb, rstd_sb)
    q_tok = emit_quant_x_fm(p, [n_tok], x_fm, mean_sb, rstd_sb, xq_fm)
    p.close_scope(scB, q_tok)
    x_tok = emit_l0_attn(p, [n_tok, b_tok], xn_fm, x_fm, ioA, cst, n_tok)
    p.close_scope(scA, merge_deps([x_tok, q_tok]))
    io0["wgq"] = nc.dram_tensor("wgq", [D, 2816], BF16).ap()
    io0["wuq"] = nc.dram_tensor("wuq", [D, 2816], BF16).ap()
    io0["wdq"] = nc.dram_tensor("wdq", [2816, D], BF16).ap()
    x_tok = emit_l0_ffn(p, [q_tok], xq_fm, x_fm, io0, cst, x_tok)
    st_tok = emit_store_y(p, [x_tok], x_fm, io0["y"], cst)
    p.close_scope(scL, st_tok)
    p.mem_tok = merge_deps([fence(p, [p.mem_tok, st_tok])])

    scL = p.scope()
    y0_dram = nc.dram_tensor("y0scr", [NFC, 128, T], F32).ap()
    io1["wqkv"] = [io1["wqkv0"], io1["wqkv1"]]
    io1["wo"] = [io1["wo0"], io1["wo1"]]
    io1["w1p"] = [io1["w1p0"], io1["w1p1"]]
    io1["w2p"] = [io1["w2p0"], io1["w2p1"]]
    io1["caps"] = caps
    small = []
    for nm in ("bqT", "bkT", "bvT", "boT", "hwT"):
        io1[nm + "_sb"] = p.sbuf(nm + "S", list(io1[nm].shape), F32, ctx=scL)
        small.append((nm, io1[nm + "_sb"][:, :]))
    for nm in ("qsT", "ksT", "lngT", "lnbT", "b1gT", "b1aT", "b2T"):
        t_ = p.sbuf(nm + "S", list(io1[nm].shape), F32, ctx=scL)
        cols = io1[nm].shape[1] // 2
        io1[nm + "_sb"] = [t_[:, il * cols:(il + 1) * cols] for il in range(2)]
        small.append((nm, t_[:, :]))
    cst_pre = {}
    cst_pre["blk64T_b"] = p.sbuf("blk64T", [128, 2], BF16, ctx=scL)
    cst_pre["blk64_f"] = p.sbuf("blk64f", [2, 128], F32, ctx=scL)
    cst_pre["l1_rot_sb"] = p.sbuf("l1rot", [128, 128], BF16, ctx=scL)
    small += [("blk64T_b", cst_pre["blk64T_b"][:, :]),
              ("blk64_f", cst_pre["blk64_f"][:, :]),
              ("l1_rot", cst_pre["l1_rot_sb"][:, :])]
    scSX = p.scope()
    cst, x_fm, t_tok = emit_common_prelude(p, caps, io1, xctx=scSX, cctx=scL)
    cst.update(cst_pre)
    sm_tok = load_const_tiles(p, dict(list(caps.items()) + list(io1.items())), small)
    scX = p.scope()
    xn_fm = p.sbuf("xn_fm", [128, NFC, T], BF16, ctx=scX)
    scMR = p.scope()
    mean_sb = p.sbuf("ln_mean", [128, T], F32, ctx=scMR)
    rstd_sb = p.sbuf("ln_rstd", [128, T], F32, ctx=scMR)
    n_tok = emit_outer_ln(p, [t_tok, sm_tok], x_fm, xn_fm, cst, mean_sb, rstd_sb)
    y0_tok = emit_y0_to_dram(p, [n_tok], x_fm, mean_sb, rstd_sb, y0_dram)
    p.close_scope(scMR, y0_tok)
    scATT = p.scope()
    cst["l1_cos"] = p.sbuf("l1cos", [128, T], BF16, ctx=scATT)
    cst["l1_sin"] = p.sbuf("l1sin", [128, T], BF16, ctx=scATT)
    cst["l1_masks_sb"] = p.sbuf("l1mask", [128, 8, 128], BF16, ctx=scATT)
    at_tok = load_const_tiles(p, caps, [("l1_cos", cst["l1_cos"][:, :]),
                                        ("l1_sin", cst["l1_sin"][:, :]),
                                        ("l1_masks", cst["l1_masks_sb"][:, :, :])])
    x_tok = emit_l1_attn(p, [n_tok, sm_tok, at_tok], xn_fm, x_fm, io1, cst, n_tok)
    p.close_scope(scATT, x_tok)
    p.close_scope(scX, merge_deps([x_tok, y0_tok]))
    st_tok = emit_store_y(p, [x_tok], x_fm, io1["y"], cst)
    p.close_scope(scSX, st_tok)
    scF = p.scope()
    cst2 = dict(cst_pre)
    for nm in ("identb", "identb_hi", "ones1x128_f", "ones128x1_b", "ident", "regs"):
        cst2[nm] = cst[nm]
    cst2["l1f_mask_sb"] = p.sbuf("l1fmask", [128, 4, 1024], BF16, ctx=scF)
    f_tok = load_const_tiles(p, caps, [("l1f_mask", cst2["l1f_mask_sb"][:, :, :])])
    f1_tok = emit_l1_ffn(p, [y0_tok, sm_tok, f_tok, st_tok], io1["y"], y0_dram,
                         io1, cst2, cst)
    p.close_scope(scF, f1_tok)
    p.close_scope(scL, f1_tok)
    p.mem_tok = merge_deps([fence(p, [p.mem_tok, f1_tok, st_tok])])

    scL = p.scope()
    io2d["caps"] = caps
    io2d["l2_res_dram"] = nc.dram_tensor("l2res", [16, 128, T], BF16).ap()
    io2d["l2_y_dram"] = nc.dram_tensor("l2y", [16, 128, T], BF16).ap()
    io2d["l2_attnout_dram"] = nc.dram_tensor("l2ao", [8, 128, T], BF16).ap()
    io2d["ccin"] = nc.dram_tensor("ccin", [128, 4], F32)
    io2d["ccout"] = nc.dram_tensor("ccout", [128, 4], F32)
    io2d["l2f_wm_dram"] = nc.dram_tensor("l2wm", [16, 128, 8, 1024], BF16).ap()
    io2d["l2f_out_dram"] = nc.dram_tensor("l2fo", [8, 128, T], BF16).ap()
    small = []
    for nm in ("cwT", "cbT", "bdtT", "AlogT", "DT"):
        io2d[nm + "_sb"] = p.sbuf(nm + "S", list(io2d[nm].shape), F32, ctx=scL)
        small.append((nm, io2d[nm + "_sb"][:, :]))
    cstn = {}
    for nm in ("l2_octmap", "l2_ymap", "l2_nmap"):
        t_ = p.sbuf(nm + "S", list(consts[nm].shape), BF16, ctx=scL)
        cstn[nm + "_sb"] = t_
        small.append((nm, t_[:, :, :] if len(consts[nm].shape) == 3 else t_[:, :]))
    io2d["wdt_sb"] = p.sbuf("wdtS", [128, 2048], BF16, ctx=scL)
    for nm in ("bqT", "bngT", "bnbT"):
        io2d[nm + "_sb"] = p.sbuf(nm + "S2", list(io2d[nm].shape), F32, ctx=scL)
        small.append((nm, io2d[nm + "_sb"][:, :]))
    sm_tok = load_const_tiles(p, dict(list(caps.items()) + list(io2d.items())), small)

    def wdtl(eng, sig):
        sig(eng.dma_start(out=io2d["wdt_sb"][:, :], in_=io2d["Wdt_pad"]), dma=True)

    io2d["wdt_tok"] = p.sec("gpsimd", [], wdtl, dma="sw")
    xn_fm = p.sbuf("xn_fm", [128, NFC, T], BF16, ctx=scL)
    scSX = p.scope()
    cst, x_fm, t_tok = emit_common_prelude(p, caps, io2d, xctx=scSX, cctx=scL)
    cst.update(cstn)
    n_tok = emit_outer_ln(p, [t_tok, sm_tok], x_fm, xn_fm, cst)
    cp_tok = emit_copy_x_to_y(p, [t_tok], io2d["x"], io2d["y"])
    p.close_scope(scSX, merge_deps([n_tok, cp_tok]))
    at = emit_l2_attn(p, [n_tok, sm_tok, io2d["wdt_tok"]], xn_fm, io2d, cst,
                      io2d["y"])
    a_tok = emit_fm_dram_add(p, [at, cp_tok], io2d["l2_attnout_dram"], io2d["y"],
                             cst, nMB=8, tag="l2aadd")
    ft = emit_l2_ffn(p, [n_tok, sm_tok], xn_fm, io2d, cst, single=False)
    f_tok = emit_fm_dram_add(p, [ft, cp_tok, a_tok], io2d["l2f_out_dram"],
                             io2d["y"], cst, nMB=8, tag="l2fadd")
    p.close_scope(scL, merge_deps([f_tok, a_tok]))
    p.mem_tok = merge_deps([fence(p, [p.mem_tok, f_tok, a_tok])])

    scL = p.scope()
    io3["caps"] = caps
    io3["l3_q2p_dram"] = nc.dram_tensor("l3q2p", [8, 128, T], BF16).ap()
    io3["l3_k2_dram"] = nc.dram_tensor("l3k2d", [8, 128, T], BF16).ap()
    io3["l3_v_dram"] = nc.dram_tensor("l3vd", [T + 1, D], BF16).ap()
    io3["l3_o1_dram"] = nc.dram_tensor("l3o1", [T, D], BF16).ap()
    io3["l3a_out_dram"] = nc.dram_tensor("l3ao", [8, 128, T], BF16).ap()
    io3["l3f_xr_dram"] = nc.dram_tensor("l3fxr", [T, D], BF16).ap()
    io3["l3f_xrall_dram"] = nc.dram_tensor("l3fxra", [8 * T, D], BF16).ap()
    io3["l3f_pr_dram"] = nc.dram_tensor("l3fpr", [T, 16], F32).ap()
    io3["l3f_prall_dram"] = nc.dram_tensor("l3fpra", [8 * T, 16], F32).ap()
    io3["l3f_lgT_dram"] = nc.dram_tensor("l3flg", [16, T], F32).ap()
    io3["l3f_lgall_dram"] = nc.dram_tensor("l3flga", [128, T], F32).ap()
    io3["l3f_m_dram"] = nc.dram_tensor("l3fm", [128, T], F32).ap()
    io3["l3f_pos_dram"] = nc.dram_tensor("l3fpos", [2, 8, T], F32).ap()
    io3["l3f_tok_dram"] = nc.dram_tensor("l3ftok", [2, 8, T], F32).ap()
    io3["l3f_idx_dram"] = nc.dram_tensor("l3fidx", [128, 16], U32).ap()
    io3["l3f_idxf_dram"] = nc.dram_tensor("l3fidxf", [2048, 1], F32).ap()
    io3["l3f_idxall_dram"] = nc.dram_tensor("l3fidxa", [16384, 1], F32).ap()
    io3["l3f_g_dram"] = nc.dram_tensor("l3fgd", [2, 8, 128], F32).ap()
    io3["l3f_con_dram"] = nc.dram_tensor("l3fcon", [2048, 1032], BF16).ap()
    io3["l3f_conall_dram"] = nc.dram_tensor("l3fcona", [16384, 1032], BF16).ap()
    io3["l3f_acc_dram"] = nc.dram_tensor("l3facc", [T, 1032], F32).ap()
    small = []
    cstn = {}
    for nm in ("l3_triu_incl", "l3_triu_strict", "l3_emap", "l3_emapT", "l3_tri8",
               "l3_ones8", "l3f_slotiota", "l3f_gid", "ones1x128_b"):
        dt = BF16 if consts[nm].dtype == ml_dtypes.bfloat16 else F32
        t_ = p.sbuf(nm + "S", list(consts[nm].shape), dt, ctx=scL)
        cstn[nm + "_sb" if nm.startswith("l3") else nm] = t_
        small.append((nm, t_[:, :]))
    for nm in ("q1bT", "k1bT", "k2bsT", "gwb2", "swb2", "cpbT"):
        io3[nm + "_sb"] = p.sbuf(nm + "S", list(io3[nm].shape), F32, ctx=scL)
        small.append((nm, io3[nm + "_sb"][:, :]))
    io3["Wg2_sb"] = p.sbuf("Wg2S", [128, 128], BF16, ctx=scL)
    io3["Ws2_sb"] = p.sbuf("Ws2S", [128, 128], BF16, ctx=scL)
    for nm in ("nwT", "rbT", "fc1bT", "fc2bT", "ehot", "mylo"):
        io3[nm + "_sb"] = p.sbuf(nm + "S", list(io3[nm].shape), F32, ctx=scL)
        small.append((nm, io3[nm + "_sb"][:, :]))
    sm_tok = load_const_tiles(p, dict(list(caps.items()) + list(io3.items())), small)

    def wgl(eng, sig):
        sig(eng.dma_start(out=io3["Wg2_sb"][:, :], in_=io3["Wg2"]), dma=True)
        sig(eng.dma_start(out=io3["Ws2_sb"][:, :], in_=io3["Ws2"]), dma=True)

    io3["wgs_tok"] = p.sec("gpsimd", [], wgl, dma="sw")
    xn_fm = p.sbuf("xn_fm", [128, NFC, T], BF16, ctx=scL)
    scSX = p.scope()
    cst, x_fm, t_tok = emit_common_prelude(p, caps, io3, xctx=scSX, cctx=scL)
    cst.update(cstn)
    n_tok = emit_outer_ln(p, [t_tok, sm_tok], x_fm, xn_fm, cst)
    cp_tok = emit_copy_x_to_y(p, [t_tok], io3["x"], io3["y"])
    p.close_scope(scSX, merge_deps([n_tok, cp_tok]))
    at = emit_l3_attn(p, [n_tok, sm_tok], xn_fm, io3, cst, single=False)
    a_tok = emit_fm_dram_add(p, [at, cp_tok], io3["l3a_out_dram"], io3["y"], cst,
                             nMB=8, tag="l3aadd")
    ms_tok, ag2, ag3 = emit_l3_ffn(p, [n_tok, sm_tok], xn_fm, io3, cst,
                                   single=False)
    emit_l3_ffn_part2(p, io3, cst, ms_tok, ag2, ag3, False, io3["y"],
                      extra_dep=merge_deps([a_tok, cp_tok]))
    p.barrier_close()
    return nc, consts


def kernel_fused(**inputs):
    x = np.asarray(inputs["L0_attn_x"], np.float32)
    nc, consts = build_full()
    h0 = host_prep_l0(inputs)
    h1 = host_prep_l1(inputs)
    h2 = host_prep_l2(inputs)
    inmaps = []
    for c in range(NCORES):
        im = dict(consts)
        for pref, hp in (("a_", h0), ("b_", h1), ("c_", h2),
                         ("d_", host_prep_l3(inputs, c))):
            for k, v in hp.items():
                im[pref + k] = v
        im["x"] = np.ascontiguousarray(x[c])
        inmaps.append(im)
    res = run_bass_kernel_spmd(nc, inmaps, core_ids=list(range(NCORES)))
    return np.stack([np.asarray(res.results[c]["y"], np.float32)
                     for c in range(NCORES)])


kernel = kernel_fused


# version 05
# speedup vs baseline: 1.0053x; 1.0053x over previous
import numpy as np
from contextlib import ExitStack

import concourse.bass as bass
import concourse.mybir as mybir
from concourse.bass_utils import run_bass_kernel_spmd
from concourse.alu_op_type import AluOpType as OP

F32 = mybir.dt.float32
BF16 = mybir.dt.bfloat16
U32 = mybir.dt.uint32
U8 = mybir.dt.uint8
AF = mybir.ActivationFunctionType
AX = mybir.AxisListType
IOA = bass.IndirectOffsetOnAxis

NCORES = 8
T = 2048
D = 1024
NFC = D // 128



def merge_deps(deps):
    best = {}

    def add(d):
        if d is None:
            return
        if isinstance(d, list):
            for x in d:
                add(x)
        else:
            j, v = d
            if v > best.get(j, 0):
                best[j] = v
    add(deps)
    return [(j, v) for j, v in best.items() if v > 0]


class Sig:
    def __init__(self, p, i):
        self.p, self.i, self.n = p, i, 0

    def __call__(self, instr, dma=False):
        n = 16 if dma else 1
        instr.then_inc(self.p.sems[self.i], n)
        self.p.val[self.i] += n
        self.n += n
        return instr


class P:
    def __init__(self, nc, nsem=96, ndma=16, nsw=16):
        self.nc = nc
        self.ctx = ExitStack()
        self.sems = [self.ctx.enter_context(nc.semaphore(f"sm{i}")) for i in range(nsem)]
        self.val = [0] * nsem
        self.block = self.ctx.enter_context(nc.Block(no_gpsimd_drain=True))
        self.ncomp = nsem - ndma - nsw
        self.ndma = ndma
        self.nsw = nsw
        self.nsem = nsem
        self.rr = 0
        self.rrd = 0
        self.rrs = 0
        self.mem_tok = None
        self._names = {}

    def _uname(self, name):
        k = self._names.get(name, 0)
        self._names[name] = k + 1
        return name if k == 0 else f"{name}_{k}"

    def sbuf(self, name, shape, dt=F32, ctx=None):
        return (ctx or self.ctx).enter_context(
            self.nc.sbuf_tensor(self._uname(name), shape, dt))

    def psum(self, name, shape, dt=F32, ctx=None):
        return (ctx or self.ctx).enter_context(
            self.nc.psum_tensor(self._uname(name), shape, dt))

    def scope(self):
        return ExitStack()

    def close_scope(self, sc, tok):
        sc.close()
        self.mem_tok = merge_deps([self.mem_tok, tok])
        if len(self.mem_tok) > 1:
            def body(eng, sig):
                sig(eng.nop())

            self.mem_tok = [self.sec("sync", [], body)]

    def sec(self, eng_name, deps, body, dma=False):
        if dma == "sw":
            i = self.ncomp + self.ndma + self.rrs
            self.rrs = (self.rrs + 1) % self.nsw
        elif dma:
            i = self.ncomp + self.rrd
            self.rrd = (self.rrd + 1) % self.ndma
        else:
            i = self.rr
            self.rr = (self.rr + 1) % self.ncomp
        deps = merge_deps([deps, self.mem_tok])
        sig = Sig(self, i)
        pp = self

        def fn(eng):
            for (j, v) in deps:
                eng.wait_ge(pp.sems[j], v)
            body(eng, sig)

        getattr(self.block, eng_name)(fn)
        assert sig.n > 0, f"section on {eng_name} signed nothing"
        return (i, self.val[i])

    def barrier_close(self):
        pp = self

        def fn(eng):
            for j, v in enumerate(pp.val):
                if v:
                    eng.wait_ge(pp.sems[j], v)

        self.block.sync(fn)
        self.ctx.close()





def fence(p, deps, eng="sync"):
    def body(eng_, sig):
        sig(eng_.nop())
    return p.sec(eng, [deps], body)

def ind_dma(eng, fn):
    h = eng.alloc_register()
    nm, engt = h.name, h.engine
    eng.free_register(h)
    ins = fn()
    try:
        eng.free_register(bass.RegisterHandle(nm, engt))
    except Exception:
        pass
    return ins

def emit_transpose_grid(p, deps, pairs, ident_ap, dt, evac_eng="vector", tag="tp"):
    group = 8 if dt == BF16 else 4
    sc = p.scope()
    ps = [p.psum(tag, [128, group * 128], dt, ctx=sc) for _ in range(2)]
    evac_tok = [None, None]
    last = None
    for g0 in range(0, len(pairs), group):
        chunk = pairs[g0:g0 + group]
        b = (g0 // group) % 2

        def tbody(eng, sig, chunk=chunk, b=b):
            for k, pair in enumerate(chunk):
                src, dst = pair[0], pair[1]
                ident = pair[2] if len(pair) > 2 else ident_ap[0:src.shape[0], 0:src.shape[0]]
                pa, fb = src.shape[0], src.shape[-1]
                ins = eng.transpose(ps[b][0:fb, k * 128:k * 128 + pa], src, ident)
            sig(ins)

        t_tok = p.sec("tensor", [deps, evac_tok[b]], tbody)

        def ebody(eng, sig, chunk=chunk, b=b):
            for k, pair in enumerate(chunk):
                src, dst = pair[0], pair[1]
                pa, fb = src.shape[0], src.shape[-1]
                ins = eng.tensor_copy(dst, ps[b][0:fb, k * 128:k * 128 + pa])
            sig(ins)

        last = p.sec(evac_eng, [t_tok], ebody)
        evac_tok[b] = last
    p.close_scope(sc, [evac_tok[0], evac_tok[1]])
    return last


def load_weight_panel(p, w_dram, K, m0, m1, wbuf, deps):
    kb = (K + 127) // 128

    def body(eng, sig):
        src = w_dram[0:K, m0:m1].rearrange("(kb p) m -> p kb m", p=128)
        sig(eng.dma_start(out=wbuf[:, 0:kb, 0:m1 - m0], in_=src), dma=True)

    return p.sec("gpsimd", deps, body, dma="sw")


def emit_gemm_fm(p, name, deps, w_src, K, Mtot, rhs_tiles, out_fn, *,
                 bias_sb=None, act=None, act_scale=None, Tlen=T, nch=512, panel=512,
                 resid_fm=None, resid_dep=None, w_tiles=None, resid_off=0, w_bf16=False):
    kb_n = (K + 127) // 128
    sc = p.scope()
    stream = w_src is not None
    if stream:
        wbufs = [p.sbuf(f"{name}_w", [128, kb_n, panel], BF16, ctx=sc) for _ in range(2)]
    pss = [p.psum(f"{name}_ps", [128, nch], F32, ctx=sc) for _ in range(2)]
    npanels = (Mtot + panel - 1) // panel if stream else 1
    load_toks = [None] * max(npanels, 1)
    comp_done = [[] for _ in range(max(npanels, 1))]
    if stream:
        load_toks[0] = load_weight_panel(p, w_src, K, 0, min(Mtot, panel), wbufs[0], [deps])
    ps_free = [None, None]
    psi = 0
    toks = []
    mb_list = [(mb * 128, min(128, Mtot - mb * 128)) for mb in range((Mtot + 127) // 128)]
    last_pa = -1
    for mb, (mof, mw) in enumerate(mb_list):
        pa = mof // panel if stream else 0
        if stream and pa != last_pa:
            last_pa = pa
            if pa + 1 < npanels:
                m0, m1 = (pa + 1) * panel, min(Mtot, (pa + 2) * panel)
                dep = [deps] if pa == 0 else [deps, comp_done[pa - 1]]
                load_toks[pa + 1] = load_weight_panel(p, w_src, K, m0, m1, wbufs[(pa + 1) % 2], dep)
        for c0 in range(0, Tlen, nch):
            w = min(nch, Tlen - c0)
            b = psi % 2
            psi += 1
            ps = pss[b]

            def tbody(eng, sig, mb=mb, mof=mof, mw=mw, c0=c0, w=w, ps=ps, pa=pa):
                for k in range(kb_n):
                    kwd = min(128, K - k * 128)
                    if stream:
                        lhsT = wbufs[pa % 2][0:kwd, k, mof - pa * panel:mof - pa * panel + mw]
                    else:
                        lhsT = w_tiles(k, mb)
                    ins = eng.matmul(ps[0:mw, 0:w], lhsT,
                                     rhs_tiles(k)[0:kwd, c0:c0 + w],
                                     start=(k == 0), stop=(k == kb_n - 1))
                sig(ins)

            t_tok = p.sec("tensor", [load_toks[pa] if stream else deps, ps_free[b]], tbody)
            if stream:
                comp_done[pa].append(t_tok)

            def ebody(eng, sig, mb=mb, mw=mw, c0=c0, w=w, ps=ps):
                if resid_fm is not None:
                    dst = resid_fm[:, mb, resid_off + c0:resid_off + c0 + w]
                    if bias_sb is not None:
                        ins = eng.scalar_tensor_tensor(dst, ps[0:mw, 0:w], bias_sb[:, mb:mb + 1],
                                                       dst, OP.add, OP.add)
                    else:
                        ins = eng.tensor_tensor(dst, ps[0:mw, 0:w], dst, OP.add)
                elif act is not None:
                    kw = {}
                    if bias_sb is not None:
                        kw["bias"] = bias_sb[:, mb:mb + 1]
                    if act_scale is not None:
                        kw["scale"] = act_scale
                    ins = eng.activation(out=out_fn(mb, c0, w), in_=ps[0:mw, 0:w], func=act, **kw)
                else:
                    if bias_sb is not None:
                        ins = eng.tensor_scalar(out_fn(mb, c0, w), ps[0:mw, 0:w],
                                                bias_sb[:, mb:mb + 1], None, OP.add)
                    else:
                        ins = eng.tensor_copy(out_fn(mb, c0, w), ps[0:mw, 0:w])
                sig(ins)

            eng_name = "scalar" if (act is not None and resid_fm is None) else "vector"
            e_tok = p.sec(eng_name, [t_tok, resid_dep], ebody)
            ps_free[b] = e_tok
            toks.append(e_tok)
    p.close_scope(sc, toks)
    return toks



def emit_outer_ln(p, deps, x_fm, xn_fm, cst, mean_sb=None, rstd_sb=None,
                  gain_col=None, bias_col=None):
    sc = p.scope()
    ps_sx = p.psum("ln_sx", [1, 512], F32, ctx=sc)
    ps_sq = p.psum("ln_sq", [1, 512], F32, ctx=sc)
    ps_mb = p.psum("ln_mb", [128, 512], F32, ctx=sc)
    ps_rb = p.psum("ln_rb", [128, 512], F32, ctx=sc)
    xb = p.sbuf("ln_xb", [128, 2, 512], BF16, ctx=sc)
    xq = p.sbuf("ln_xq", [128, 2, 512], BF16, ctx=sc)
    stats = p.sbuf("ln_st", [1, 4, T], F32, ctx=sc)
    own_bc = mean_sb is None
    if own_bc:
        mean_sb = p.sbuf("ln_meanb", [128, T], F32, ctx=sc)
        rstd_sb = p.sbuf("ln_rstdb", [128, T], F32, ctx=sc)
    ones1x128_f = cst["ones1x128_f"]
    ones128x1_b = cst["ones128x1_b"]
    mean = stats[:, 0, :]
    ex2 = stats[:, 1, :]
    msq = stats[:, 2, :]
    rstd = stats[:, 3, :]
    st_toks = []
    for ci, c0 in enumerate(range(0, T, 512)):
        mm_tok = None
        cast_toks = []
        for fc in range(NFC):
            pb = fc % 2

            def cbody(eng, sig, fc=fc, c0=c0, pb=pb):
                eng.activation(out=xb[:, pb, :], in_=x_fm[:, fc, c0:c0 + 512], func=AF.Identity)
                sig(eng.activation(out=xq[:, pb, :], in_=x_fm[:, fc, c0:c0 + 512], func=AF.Square))

            prev_mm = mm_tok if fc >= 2 else None
            c_tok = p.sec("scalar", [deps, cast_toks[fc - 2] if fc >= 2 else None, prev_mm,
                                     st_toks[-1] if st_toks else None], cbody)
            cast_toks.append(c_tok)

            def mbody(eng, sig, fc=fc, pb=pb):
                eng.matmul(ps_sx[:, :], ones128x1_b[:, 0:1], xb[:, pb, :],
                           start=(fc == 0), stop=(fc == NFC - 1))
                sig(eng.matmul(ps_sq[:, :], ones128x1_b[:, 0:1], xq[:, pb, :],
                               start=(fc == 0), stop=(fc == NFC - 1)))

            mm_tok = p.sec("tensor", [c_tok], mbody)

        def sbody(eng, sig, c0=c0):
            eng.tensor_scalar(mean[:, c0:c0 + 512], ps_sx[:, :], 1.0 / D, None, OP.mult)
            eng.tensor_scalar(ex2[:, c0:c0 + 512], ps_sq[:, :], 1.0 / D, None, OP.mult)
            eng.drain()
            eng.tensor_tensor(msq[:, c0:c0 + 512], mean[:, c0:c0 + 512],
                              mean[:, c0:c0 + 512], OP.mult)
            eng.drain()
            sig(eng.scalar_tensor_tensor(ex2[:, c0:c0 + 512], ex2[:, c0:c0 + 512], 1e-5,
                                         msq[:, c0:c0 + 512], OP.add, OP.subtract))

        st_toks.append(p.sec("vector", [mm_tok], sbody))

    def abody(eng, sig):
        eng.activation(out=msq, in_=ex2, func=AF.Ln)
        eng.drain()
        sig(eng.activation(out=rstd, in_=msq, func=AF.Exp, scale=-0.5))

    a_tok = p.sec("scalar", [st_toks], abody)

    bc_toks = []
    ev_toks = []
    for ci, c0 in enumerate(range(0, T, 512)):
        def bcast(eng, sig, c0=c0):
            eng.matmul(ps_mb[:, :], ones1x128_f[0:1, :], mean[0:1, c0:c0 + 512],
                       start=True, stop=True)
            sig(eng.matmul(ps_rb[:, :], ones1x128_f[0:1, :], rstd[0:1, c0:c0 + 512],
                           start=True, stop=True))

        b_tok = p.sec("tensor", [a_tok, ev_toks[-1] if ev_toks else None], bcast)
        bc_toks.append(b_tok)

        def ebody(eng, sig, c0=c0):
            eng.tensor_copy(mean_sb[:, c0:c0 + 512], ps_mb[:, :])
            sig(eng.tensor_copy(rstd_sb[:, c0:c0 + 512], ps_rb[:, :]))

        ev_toks.append(p.sec("vector", [b_tok], ebody))

    n_toks = []
    for fc in range(NFC):
        def nbody(eng, sig, fc=fc):
            for c0 in range(0, T, 512):
                pb = (c0 // 512) % 2
                tmp = xb[:, pb, :]
                eng.tensor_tensor(tmp, x_fm[:, fc, c0:c0 + 512], mean_sb[:, c0:c0 + 512],
                                  OP.subtract)
                eng.drain()
                ins = eng.tensor_tensor(xn_fm[:, fc, c0:c0 + 512], tmp,
                                        rstd_sb[:, c0:c0 + 512], OP.mult)
                if gain_col is not None:
                    eng.drain()
                    ins = eng.tensor_scalar(xn_fm[:, fc, c0:c0 + 512],
                                            xn_fm[:, fc, c0:c0 + 512],
                                            gain_col[:, fc:fc + 1], bias_col[:, fc:fc + 1],
                                            OP.mult, OP.add)
            sig(ins)

        n_toks.append(p.sec("vector", [ev_toks], nbody))
    last = merge_deps(n_toks)
    p.close_scope(sc, last)
    return last



def _col128(v):
    v = np.asarray(v, np.float32)
    n = v.shape[0]
    assert n % 128 == 0
    return np.ascontiguousarray(v.reshape(n // 128, 128).T)


def make_consts():
    c = {}
    c["ident"] = np.eye(128, dtype=np.float32)
    import ml_dtypes
    c["identb"] = np.eye(128, dtype=np.float32).astype(ml_dtypes.bfloat16)
    c["ones1x128_b"] = np.ones((1, 128), np.float32).astype(ml_dtypes.bfloat16)
    c["ones128x1_b"] = np.ones((128, 1), np.float32).astype(ml_dtypes.bfloat16)
    c["ones1x128_f"] = np.ones((1, 128), np.float32)
    hi = np.zeros((128, 64), np.float32)
    hi[64:128, :] = np.eye(64)
    c["identb_hi"] = hi.astype(ml_dtypes.bfloat16)
    return c


def declare_consts(nc, c):
    aps = {}
    for k, v in c.items():
        dt = {np.dtype("float32"): F32}.get(v.dtype, None)
        if dt is None:
            import ml_dtypes
            dt = BF16 if v.dtype == ml_dtypes.bfloat16 else U32
        aps[k] = nc.dram_tensor(k, list(v.shape), dt, kind="ExternalInput").ap()
    return aps


def load_const_tiles(p, aps, specs):
    def body(eng, sig):
        for name, tile in specs:
            sig(eng.dma_start(out=tile, in_=aps[name]), dma=True)
    return p.sec("sync", [], body, dma=True)



def emit_load_x(p, x_dram, x_tm):
    def body(eng, sig):
        src = x_dram.rearrange("(tc p) d -> p tc d", p=128)
        sig(eng.dma_start(out=x_tm[:, :, :], in_=src), dma=True)
    return p.sec("sync", [], body, dma=True)


def emit_tm_to_fm(p, deps, tm, fm, ident, dt, nI=16, nJ=NFC, tag="t2f"):
    pairs = []
    for j in range(nJ):
        for i in range(nI):
            pairs.append((tm[:, i, j * 128:(j + 1) * 128], fm[:, j, i * 128:(i + 1) * 128]))
    return emit_transpose_grid(p, deps, pairs, ident, dt, tag=tag)


def emit_fm_to_tm(p, deps, fm, tm, ident, dt, nJ=NFC, nI=16, tag="f2t"):
    pairs = []
    for j in range(nJ):
        for i in range(nI):
            pairs.append((fm[:, j, i * 128:(i + 1) * 128], tm[:, i, j * 128:(j + 1) * 128]))
    return emit_transpose_grid(p, deps, pairs, ident, dt, tag=tag)


def emit_store_x(p, deps, x_fm, y_dram, ident, scratch_tm):
    t_tok = emit_fm_to_tm(p, deps, x_fm, scratch_tm, ident, F32, tag="sto")

    def body(eng, sig):
        dst = y_dram.rearrange("(tc p) d -> p tc d", p=128)
        sig(eng.dma_start(out=dst, in_=scratch_tm[:, :, :]), dma=True)
    return p.sec("sync", [t_tok], body)



def l0_qkv_perm():
    perm = []
    for g in range(4):
        for pair in range(2):
            for part in (0, 64, 128):
                for jj in (0, 1):
                    h = g * 4 + pair * 2 + jj
                    base = h * 192 + part
                    perm.extend(range(base, base + 64))
    return np.array(perm)


def emit_l0_attn(p, deps, xn_fm, x_fm, io, cst, x_dep):
    sc_all = p.scope()
    qkvg = p.sbuf("l0_qkvg", [128, 6, T], BF16, ctx=sc_all)
    v_tm = p.sbuf("l0_vtm", [128, 16, 256], BF16, ctx=sc_all)
    values = p.sbuf("l0_val", [128, 2, T], BF16, ctx=sc_all)
    S_sb = p.sbuf("l0_S", [128, T], F32, ctx=sc_all)
    P_sb = p.sbuf("l0_P", [128, T], BF16, ctx=sc_all)
    PT_sb = p.sbuf("l0_PT", [128, 16, 128], BF16, ctx=sc_all)
    mcol = p.sbuf("l0_m", [128, 4], F32, ctx=sc_all)
    identb = cst["identb"]
    x_tok = x_dep
    for g in range(4):
        def qout(mb, c0, w, qkvg=qkvg):
            return qkvg[:, mb, c0:c0 + w]
        gdeps = [deps] if g == 0 else [deps, attn_done]
        qkv_toks = emit_gemm_fm(
            p, "l0qkv", gdeps, io["Wqkv"][:, g * 768:(g + 1) * 768], D, 768,
            lambda kb: xn_fm[:, kb, :], qout,
            bias_sb=io["bqkvT_sb"][:, g * 6:(g + 1) * 6])
        pairs = []
        for j in range(4):
            tv, ov = 3 * (j // 2) + 2, 64 * (j % 2)
            for tc in range(16):
                idn = cst["identb"][0:64, 0:64] if ov == 0 else cst["identb_hi"][64:128, 0:64]
                pairs.append((qkvg[:, tv, tc * 128:(tc + 1) * 128][ov:ov + 64, :],
                              v_tm[:, tc, j * 64:(j + 1) * 64], idn))
        vt_tok = emit_transpose_grid(p, qkv_toks, pairs, identb, BF16, tag="l0vt")

        sc = p.scope()
        ps_s = [p.psum("l0_sps", [128, 512], F32, ctx=sc) for _ in range(4)]
        ps_o = p.psum("l0_ops", [128, 128], F32, ctx=sc)
        prev_cycle = None if g == 0 else attn_done
        prev_pv = None if g == 0 else attn_done
        attn_toks = []
        for j in range(4):
            tq, oq = 3 * (j // 2), 64 * (j % 2)
            tk, ok = 3 * (j // 2) + 1, 64 * (j % 2)
            for qb in range(16):
                kvext = (qb + 1) * 128
                nkc = (kvext + 511) // 512

                def sbody(eng, sig, tq=tq, oq=oq, tk=tk, ok=ok, qb=qb, kvext=kvext, nkc=nkc):
                    for c in range(nkc):
                        w = min(512, kvext - c * 512)
                        ins = eng.matmul(ps_s[c][:, 0:w],
                                         qkvg[:, tq, qb * 128:(qb + 1) * 128][oq:oq + 64, :],
                                         qkvg[:, tk, c * 512:c * 512 + w][ok:ok + 64, :],
                                         start=True, stop=True)
                    sig(ins)

                s_tok = p.sec("tensor", [vt_tok, prev_cycle], sbody)

                def evbody(eng, sig, kvext=kvext, nkc=nkc):
                    for c in range(nkc):
                        w = min(512, kvext - c * 512)
                        ins = eng.copy(out=S_sb[:, c * 512:c * 512 + w], in_=ps_s[c][:, 0:w])
                    sig(ins)

                ev_tok = p.sec("scalar", [s_tok], evbody)

                def mkbody(eng, sig, qb=qb):
                    sig(eng.affine_select(S_sb[:, qb * 128:(qb + 1) * 128],
                                          S_sb[:, qb * 128:(qb + 1) * 128],
                                          pattern=[[-1, 128]], compare_op=OP.is_ge,
                                          fill=cst["regs"]["neg1e9"], base=0,
                                          channel_multiplier=1))

                mk_tok = p.sec("gpsimd", [ev_tok], mkbody)

                def mxbody(eng, sig, kvext=kvext):
                    eng.tensor_reduce(mcol[:, 0:1], S_sb[:, 0:kvext], AX.X, OP.max)
                    eng.drain()
                    sig(eng.tensor_scalar(mcol[:, 1:2], mcol[:, 0:1], -0.125, None, OP.mult))

                mx_tok = p.sec("vector", [mk_tok], mxbody)

                def expbody(eng, sig, kvext=kvext):
                    sig(eng.activation(out=P_sb[:, 0:kvext], in_=S_sb[:, 0:kvext], func=AF.Exp,
                                       scale=0.125, bias=mcol[:, 1:2], accum_out=mcol[:, 2:3]))

                ex_tok = p.sec("scalar", [mx_tok], expbody)

                def nrmbody(eng, sig, kvext=kvext):
                    eng.reciprocal(mcol[:, 3:4], mcol[:, 2:3])
                    eng.drain()
                    sig(eng.tensor_scalar(P_sb[:, 0:kvext], P_sb[:, 0:kvext],
                                          mcol[:, 3:4], None, OP.mult))

                nr_tok = p.sec("vector", [ex_tok], nrmbody)

                tp_pairs = [(P_sb[:, c * 128:(c + 1) * 128], PT_sb[:, c, :])
                            for c in range(qb + 1)]
                pt_tok = emit_transpose_grid(p, [nr_tok], tp_pairs, identb, BF16, tag="l0pt")
                prev_cycle = pt_tok

                def pvbody(eng, sig, j=j, qb=qb):
                    ob = (j % 2) * 64
                    for c in range(qb + 1):
                        ins = eng.matmul(ps_o[ob:ob + 64, :],
                                         v_tm[:, c, j * 64:(j + 1) * 64],
                                         PT_sb[:, c, :],
                                         start=(c == 0), stop=(c == qb))
                    sig(ins)

                pv_tok = p.sec("tensor", [pt_tok, prev_pv], pvbody)

                def vebody(eng, sig, j=j, qb=qb):
                    ob = (j % 2) * 64
                    sig(eng.tensor_copy(values[:, j // 2, qb * 128:(qb + 1) * 128][ob:ob + 64, :],
                                        ps_o[ob:ob + 64, :]))

                ve_tok = p.sec("vector", [pv_tok], vebody)
                prev_pv = ve_tok
                attn_toks.append(ve_tok)
        p.close_scope(sc, merge_deps(attn_toks + [prev_cycle]))

        op_toks = emit_gemm_fm(
            p, "l0op", attn_toks, io["Wout"][g * 256:(g + 1) * 256, :], 256, D,
            lambda kb: values[:, kb, :], None,
            bias_sb=io["boutT_sb"] if g == 0 else None,
            resid_fm=x_fm, resid_dep=x_tok)
        x_tok = op_toks
        attn_done = merge_deps(op_toks)
    p.close_scope(sc_all, attn_done)
    return merge_deps(x_tok)



def declare_io_l0(nc, stage, pref="", xap=None, yap=None):
    io = {}
    io["x"] = xap if xap is not None else \
        nc.dram_tensor("x", [T, D], F32, kind="ExternalInput").ap()
    io["y"] = yap if yap is not None else \
        nc.dram_tensor("y", [T, D], F32, kind="ExternalOutput").ap()
    names = []
    if stage in ("attn", "full"):
        names += [("Wqkv", [D, 3072]), ("bqkvT", [128, 24]), ("Wout", [D, D]),
                  ("boutT", [128, 8])]
    if stage in ("ffn", "full"):
        names += [("wg", [2816, D]), ("wu", [2816, D]), ("wd", [D, 2816])]
    for nm, shp in names:
        io[nm] = nc.dram_tensor(pref + nm, shp, F32, kind="ExternalInput").ap()
    return io


def emit_common_prelude(p, caps, io, xctx=None, cctx=None):
    cst = {}
    cst["ident"] = p.sbuf("c_ident", [128, 128], F32, ctx=cctx)
    cst["identb"] = p.sbuf("c_identb", [128, 128], BF16, ctx=cctx)
    cst["ones1x128_f"] = p.sbuf("c_ones1f", [1, 128], F32, ctx=cctx)
    cst["ones128x1_b"] = p.sbuf("c_onescb", [128, 1], BF16, ctx=cctx)
    cst["identb_hi"] = p.sbuf("c_identbh", [128, 64], BF16, ctx=cctx)
    c_tok = load_const_tiles(p, caps, [("ident", cst["ident"][:, :]),
                                       ("identb", cst["identb"][:, :]),
                                       ("ones1x128_f", cst["ones1x128_f"][:, :]),
                                       ("ones128x1_b", cst["ones128x1_b"][:, :]),
                                       ("identb_hi", cst["identb_hi"][:, :])])
    regs = {}
    scratch1 = p.sbuf("c_scr1", [1, 4], F32, ctx=cctx)

    def rbody(eng, sig):
        regs["neg1e9"] = eng.to_reg(-1e9)
        sig(eng.memset(scratch1[0:1, 0:4], 0.0))

    p.sec("gpsimd", [], rbody)
    cst["regs"] = regs
    x_fm = p.sbuf("x_fm", [128, NFC, T], F32, ctx=xctx)
    sc0 = p.scope()
    x_tm = p.sbuf("x_tm", [128, 16, D], F32, ctx=sc0)
    l_tok = emit_load_x(p, io["x"], x_tm)
    t_tok = emit_tm_to_fm(p, [c_tok, l_tok], x_tm, x_fm, cst["ident"][:, :], F32)
    p.close_scope(sc0, t_tok)
    return cst, x_fm, t_tok


def emit_store_y(p, deps, x_fm, y_dram, cst):
    sc = p.scope()
    s_tm = p.sbuf("sto_tm", [128, 16, D], F32, ctx=sc)
    t_tok = emit_fm_to_tm(p, deps, x_fm, s_tm, cst["ident"][:, :], F32, tag="sto")

    def body(eng, sig):
        dst = y_dram.rearrange("(tc p) d -> p tc d", p=128)
        sig(eng.dma_start(out=dst, in_=s_tm[:, :, :]), dma=True)

    tok = p.sec("sync", [t_tok], body, dma=True)
    p.close_scope(sc, tok)
    return tok


def build_l0(stage="attn"):
    nc = bass.Bass("TRN2", target_bir_lowering=False)
    consts = make_consts()
    caps = declare_consts(nc, consts)
    io = declare_io_l0(nc, stage)
    p = P(nc)
    cst, x_fm, t_tok = emit_common_prelude(p, caps, io)
    do_ffn = stage in ("ffn", "full")
    do_attn = stage in ("attn", "full")
    xq_fm = p.sbuf("xq_fm", [128, NFC, T], BF16) if do_ffn else None
    if do_attn:
        io2 = {"Wqkv": io["Wqkv"], "Wout": io["Wout"]}
        io2["bqkvT_sb"] = p.sbuf("bqkvT_sb", [128, 24], F32)
        io2["boutT_sb"] = p.sbuf("boutT_sb", [128, 8], F32)
        b_tok = load_const_tiles(p, {"bqkvT": io["bqkvT"], "boutT": io["boutT"]},
                                 [("bqkvT", io2["bqkvT_sb"][:, :]),
                                  ("boutT", io2["boutT_sb"][:, :])])
    scA = p.scope()
    xn_fm = p.sbuf("xn_fm", [128, NFC, T], BF16, ctx=scA)
    scB = p.scope()
    mean_sb = p.sbuf("ln_mean", [128, T], F32, ctx=scB)
    rstd_sb = p.sbuf("ln_rstd", [128, T], F32, ctx=scB)
    n_tok = emit_outer_ln(p, [t_tok], x_fm, xn_fm, cst, mean_sb, rstd_sb)
    x_tok = n_tok
    if do_ffn:
        q_tok = emit_quant_x_fm(p, [n_tok], x_fm, mean_sb, rstd_sb, xq_fm)
        p.close_scope(scB, q_tok)
    else:
        p.close_scope(scB, n_tok)
    if do_attn:
        x_tok = emit_l0_attn(p, [n_tok, b_tok], xn_fm, x_fm, io2, cst, x_tok)
    p.close_scope(scA, merge_deps([x_tok, q_tok if do_ffn else None]))
    if do_ffn:
        io["wgq"] = nc.dram_tensor("wgq", [D, 2816], BF16).ap()
        io["wuq"] = nc.dram_tensor("wuq", [D, 2816], BF16).ap()
        io["wdq"] = nc.dram_tensor("wdq", [2816, D], BF16).ap()
        x_tok = emit_l0_ffn(p, [q_tok], xq_fm, x_fm, io, cst, x_tok)
    emit_store_y(p, [x_tok], x_fm, io["y"], cst)
    p.barrier_close()
    return nc, consts



def emit_fp4_quant(p, deps, src, dst, scr, W):
    G = W // 16

    def abody(eng, sig):
        sig(eng.activation(out=scr["a"][:, 0:W], in_=src, func=AF.Abs))

    a_tok = p.sec("scalar", [deps, scr.get("last")], abody)

    def vbody(eng, sig):
        a = scr["a"][:, 0:W]
        a3 = scr["a"][:, 0:W].rearrange("p (g k) -> p g k", k=16)
        q = scr["q"][:, 0:W]
        r = scr["r"][:, 0:W]
        m = scr["m"][:, 0:G]
        s = scr["s"][:, 0:G]
        eng.tensor_reduce(m, a3, AX.X, OP.max)
        eng.drain()
        eng.tensor_scalar(s, m, 1.0 / 6.0, 1e-12, OP.mult, OP.max)
        eng.drain()
        eng.reciprocal(m, s)
        eng.drain()
        eng.tensor_tensor(a3, a3, scr["m"][:, 0:G, None].to_broadcast([128, G, 16]), OP.mult)
        eng.drain()
        eng.tensor_scalar(q, a, 0.25, None, OP.is_gt)
        eng.drain()
        for th in (0.75, 1.25, 1.75):
            eng.scalar_tensor_tensor(q, a, th, q, OP.is_gt, OP.add)
            eng.drain()
        eng.tensor_scalar(r, a, 2.5, None, OP.is_gt)
        eng.drain()
        eng.scalar_tensor_tensor(r, a, 3.5, r, OP.is_gt, OP.add)
        eng.drain()
        eng.scalar_tensor_tensor(q, q, 0.5, r, OP.mult, OP.add)
        eng.drain()
        sig(eng.tensor_scalar(r, a, 5.0, 2.0, OP.is_gt, OP.mult))

    v_tok = p.sec("vector", [a_tok], vbody)

    def sbody(eng, sig):
        sig(eng.activation(out=scr["a"][:, 0:W], in_=src, func=AF.Sign))

    s_tok = p.sec("scalar", [v_tok], sbody)

    def v2body(eng, sig):
        a = scr["a"][:, 0:W]
        q = scr["q"][:, 0:W]
        r = scr["r"][:, 0:W]
        eng.tensor_tensor(q, q, r, OP.add)
        eng.drain()
        eng.tensor_tensor(q, q, a, OP.mult)
        eng.drain()
        q3 = scr["q"][:, 0:W].rearrange("p (g k) -> p g k", k=16)
        d3 = dst.rearrange("p (g k) -> p g k", k=16)
        sig(eng.tensor_tensor(d3, q3, scr["s"][:, 0:G, None].to_broadcast([128, G, 16]), OP.mult))

    tok = p.sec("vector", [s_tok], v2body)
    scr["last"] = tok
    return tok


def emit_quant_fm_via_vt(p, deps, src_fm_slice, dst_fm_slice, scr, vt_scr):
    def t1(eng, sig):
        sig(eng.transpose(vt_scr[:, 0:T], src_fm_slice))

    tok1 = p.sec("vector", deps, t1)
    tok2 = emit_fp4_quant(p, [tok1], vt_scr[:, 0:T], vt_scr[:, T:2 * T], scr, T)

    def t2(eng, sig):
        sig(eng.transpose(dst_fm_slice, vt_scr[:, T:2 * T]))

    return p.sec("vector", [tok2], t2)



def mk_quant_scr(p, sc, maxW=2816):
    return {
        "m": p.sbuf("q_m", [128, maxW // 16], F32, ctx=sc),
        "s": p.sbuf("q_s", [128, maxW // 16], F32, ctx=sc),
        "a": p.sbuf("q_a", [128, maxW], F32, ctx=sc),
        "q": p.sbuf("q_q", [128, maxW], BF16, ctx=sc),
        "r": p.sbuf("q_r", [128, maxW], BF16, ctx=sc),
    }


def emit_quant_x_fm(p, deps, x_fm, mean_sb, rstd_sb, xq_fm):
    sc = p.scope()
    scr = mk_quant_scr(p, sc, maxW=2048)
    xnf = p.sbuf("qx_f", [128, T], F32, ctx=sc)
    vt = p.sbuf("qx_vt", [128, T], F32, ctx=sc)
    vtq = p.sbuf("qx_vq", [128, T], BF16, ctx=sc)
    tok = merge_deps([deps])
    for fc in range(NFC):
        def fbody(eng, sig, fc=fc):
            eng.tensor_tensor(xnf[:, :], x_fm[:, fc, :], mean_sb[:, :], OP.subtract)
            eng.drain()
            eng.tensor_tensor(xnf[:, :], xnf[:, :], rstd_sb[:, :], OP.mult)
            eng.drain()
            sig(eng.transpose(vt[:, :], xnf[:, :]))

        f_tok = p.sec("vector", [tok], fbody)
        q_tok = emit_fp4_quant(p, [f_tok], vt[:, :], vtq[:, :], scr, T)

        def tbody(eng, sig, fc=fc):
            sig(eng.transpose(xq_fm[:, fc, :], vtq[:, :]))

        tok = p.sec("vector", [q_tok], tbody)
    p.close_scope(sc, tok)
    return tok


def emit_quant_w_to_dram(p, deps, w_dram, Mrows, K, wq_dram, cst, tag):
    sc = p.scope()
    scr = mk_quant_scr(p, sc, maxW=K)
    wrow = [p.sbuf(f"{tag}_wr", [128, K], F32, ctx=sc) for _ in range(2)]
    wq = [p.sbuf(f"{tag}_wq", [128, K], BF16, ctx=sc) for _ in range(2)]
    wT = [p.sbuf(f"{tag}_wt", [128, K // 128, 128], BF16, ctx=sc) for _ in range(2)]
    identb = cst["identb"]
    prev = [None, None]
    toks = []
    for mb in range(Mrows // 128):
        bi = mb % 2

        def lbody(eng, sig, mb=mb, bi=bi):
            sig(eng.dma_start(out=wrow[bi][:, :], in_=w_dram[mb * 128:(mb + 1) * 128, 0:K]),
                dma=True)

        l_tok = p.sec("sync", [deps, prev[bi]], lbody, dma=True)
        q_tok = emit_fp4_quant(p, [l_tok], wrow[bi][:, :], wq[bi][:, :], scr, K)
        pairs = [(wq[bi][:, kb * 128:(kb + 1) * 128], wT[bi][:, kb, :])
                 for kb in range(K // 128)]
        t_tok = emit_transpose_grid(p, [q_tok], pairs, identb, BF16, tag=f"{tag}_t")

        def obody(eng, sig, mb=mb, bi=bi):
            dst = wq_dram[:, mb * 128:(mb + 1) * 128].rearrange("(kb p) m -> p kb m", p=128)
            sig(eng.dma_start(out=dst, in_=wT[bi][:, :, :]), dma=True)

        o_tok = p.sec("sync", [t_tok], obody, dma=True)
        prev[bi] = o_tok
        toks.append(o_tok)
    last = merge_deps(toks)
    p.close_scope(sc, last)
    return last


def emit_l0_ffn(p, deps, xq_fm, x_fm, io, cst, x_dep):
    IFF = 2816
    IB = IFF // 128
    wq_tok = emit_quant_w_to_dram(p, deps, io["wg"], IFF, D, io["wgq"], cst, "l0g")
    wq_tok = merge_deps([wq_tok, emit_quant_w_to_dram(p, [wq_tok], io["wu"], IFF, D,
                                                      io["wuq"], cst, "l0u")])
    wq_tok = merge_deps([wq_tok, emit_quant_w_to_dram(p, [wq_tok], io["wd"], D, IFF,
                                                      io["wdq"], cst, "l0d")])

    sc_all = p.scope()
    hq = p.sbuf("l0f_hq", [128, IB, 1024], BF16, ctx=sc_all)
    x_tok = x_dep
    for tc in range(2):
        c_base = tc * 1024
        sc = p.scope()
        scr = mk_quant_scr(p, sc, maxW=1024)
        gpan = [p.sbuf("l0f_gp", [128, 8, 512], BF16, ctx=sc) for _ in range(2)]
        upan = [p.sbuf("l0f_up", [128, 8, 512], BF16, ctx=sc) for _ in range(2)]
        gu0 = p.sbuf("l0f_g0", [128, 1024], F32, ctx=sc)
        gu1 = p.sbuf("l0f_g1", [128, 1024], F32, ctx=sc)
        vt = p.sbuf("l0f_vt", [128, 1024], F32, ctx=sc)
        vtq = p.sbuf("l0f_vq", [128, 1024], BF16, ctx=sc)
        ps = [p.psum("l0f_ps", [128, 512], F32, ctx=sc) for _ in range(2)]
        hq_toks = []
        pan_tok = [None, None]
        pan_guard = [None, None]
        for pa in range(6):
            m0 = pa * 512
            m1 = min(IFF, m0 + 512)
            bi = pa % 2

            def lpan(eng, sig, m0=m0, m1=m1, bi=bi):
                for wsrc, buf in ((io["wgq"], gpan), (io["wuq"], upan)):
                    src = wsrc[:, m0:m1].rearrange("(kb p) m -> p kb m", p=128)
                    sig(eng.dma_start(out=buf[bi][:, :, 0:m1 - m0], in_=src), dma=True)

            lp_tok = p.sec("gpsimd", [wq_tok, pan_guard[bi]], lpan, dma="sw")
            pan_tok[bi] = lp_tok
            for mi in range((m1 - m0) // 128):
                mb = pa * 4 + mi
                mm_toks = []
                for half, (pan_, dst_, actf) in enumerate(
                        ((gpan, gu0, AF.Silu), (upan, gu1, None))):
                    for nc2 in range(2):
                        def gmm(eng, sig, pan_=pan_, mi=mi, nc2=nc2, bi=bi, c_base=c_base):
                            for kb in range(8):
                                ins = eng.matmul(
                                    ps[nc2][:, :], pan_[bi][:, kb, mi * 128:(mi + 1) * 128],
                                    xq_fm[:, kb, c_base + nc2 * 512:c_base + (nc2 + 1) * 512],
                                    start=(kb == 0), stop=(kb == 7))
                            sig(ins)

                        mmt = p.sec("tensor", [lp_tok, hq_toks[-1] if hq_toks else None,
                                               mm_toks[-1] if mm_toks else None], gmm)

                        if actf is not None:
                            def gev(eng, sig, dst_=dst_, nc2=nc2):
                                sig(eng.activation(out=dst_[:, nc2 * 512:(nc2 + 1) * 512],
                                                   in_=ps[nc2][:, :], func=actf))
                            ev = p.sec("scalar", [mmt], gev)
                        else:
                            def uev(eng, sig, dst_=dst_, nc2=nc2):
                                sig(eng.tensor_copy(dst_[:, nc2 * 512:(nc2 + 1) * 512],
                                                    ps[nc2][:, :]))
                            ev = p.sec("vector", [mmt], uev)
                        mm_toks.append(ev)

                def hbody(eng, sig):
                    eng.tensor_tensor(gu0[:, :], gu0[:, :], gu1[:, :], OP.mult)
                    eng.drain()
                    sig(eng.transpose(vt[:, :], gu0[:, :]))

                h_tok = p.sec("vector", [mm_toks], hbody)
                q_tok = emit_fp4_quant(p, [h_tok], vt[:, :], vtq[:, :], scr, 1024)

                def hv2(eng, sig, mb=mb):
                    sig(eng.transpose(hq[:, mb, :], vtq[:, :]))

                hq_tok = p.sec("vector", [q_tok], hv2)
                hq_toks.append(hq_tok)
            pan_guard[bi] = merge_deps(hq_toks)
        dn_toks = emit_gemm_fm(p, "l0dn", hq_toks, io["wdq"], IFF, D,
                               lambda kb: hq[:, kb, :], None, Tlen=1024,
                               resid_fm=x_fm, resid_dep=x_tok, resid_off=c_base)
        x_tok = merge_deps([dn_toks])
        p.close_scope(sc, merge_deps([x_tok, pan_tok[0], pan_tok[1]]))
    p.close_scope(sc_all, x_tok)
    return x_tok



def build_l0(stage="attn"):
    nc = bass.Bass("TRN2", target_bir_lowering=False)
    consts = make_consts()
    caps = declare_consts(nc, consts)
    io = declare_io_l0(nc, stage)
    p = P(nc)
    cst, x_fm, t_tok = emit_common_prelude(p, caps, io)
    do_ffn = stage in ("ffn", "full")
    do_attn = stage in ("attn", "full")
    xq_fm = p.sbuf("xq_fm", [128, NFC, T], BF16) if do_ffn else None
    if do_attn:
        io2 = {"Wqkv": io["Wqkv"], "Wout": io["Wout"]}
        io2["bqkvT_sb"] = p.sbuf("bqkvT_sb", [128, 24], F32)
        io2["boutT_sb"] = p.sbuf("boutT_sb", [128, 8], F32)
        b_tok = load_const_tiles(p, {"bqkvT": io["bqkvT"], "boutT": io["boutT"]},
                                 [("bqkvT", io2["bqkvT_sb"][:, :]),
                                  ("boutT", io2["boutT_sb"][:, :])])
    scA = p.scope()
    xn_fm = p.sbuf("xn_fm", [128, NFC, T], BF16, ctx=scA)
    scB = p.scope()
    mean_sb = p.sbuf("ln_mean", [128, T], F32, ctx=scB)
    rstd_sb = p.sbuf("ln_rstd", [128, T], F32, ctx=scB)
    n_tok = emit_outer_ln(p, [t_tok], x_fm, xn_fm, cst, mean_sb, rstd_sb)
    x_tok = n_tok
    if do_ffn:
        q_tok = emit_quant_x_fm(p, [n_tok], x_fm, mean_sb, rstd_sb, xq_fm)
        p.close_scope(scB, q_tok)
    else:
        p.close_scope(scB, n_tok)
    if do_attn:
        x_tok = emit_l0_attn(p, [n_tok, b_tok], xn_fm, x_fm, io2, cst, x_tok)
    p.close_scope(scA, merge_deps([x_tok, q_tok if do_ffn else None]))
    if do_ffn:
        io["wgq"] = nc.dram_tensor("wgq", [D, 2816], BF16).ap()
        io["wuq"] = nc.dram_tensor("wuq", [D, 2816], BF16).ap()
        io["wdq"] = nc.dram_tensor("wdq", [2816, D], BF16).ap()
        x_tok = emit_l0_ffn(p, [q_tok], xq_fm, x_fm, io, cst, x_tok)
    emit_store_y(p, [x_tok], x_fm, io["y"], cst)
    p.barrier_close()
    return nc, consts



def emit_fp4_quant(p, deps, src, dst, scr, W):
    G = W // 16

    def abody(eng, sig):
        sig(eng.activation(out=scr["a"][:, 0:W], in_=src, func=AF.Abs))

    a_tok = p.sec("scalar", [deps, scr.get("last")], abody)

    def vbody(eng, sig):
        a = scr["a"][:, 0:W]
        a3 = scr["a"][:, 0:W].rearrange("p (g k) -> p g k", k=16)
        q = scr["q"][:, 0:W]
        r = scr["r"][:, 0:W]
        m = scr["m"][:, 0:G]
        s = scr["s"][:, 0:G]
        eng.tensor_reduce(m, a3, AX.X, OP.max)
        eng.drain()
        eng.tensor_scalar(s, m, 1.0 / 6.0, 1e-12, OP.mult, OP.max)
        eng.drain()
        eng.reciprocal(m, s)
        eng.drain()
        eng.tensor_tensor(a3, a3, scr["m"][:, 0:G, None].to_broadcast([128, G, 16]), OP.mult)
        eng.drain()
        eng.tensor_scalar(q, a, 0.25, None, OP.is_gt)
        eng.drain()
        for th in (0.75, 1.25, 1.75):
            eng.scalar_tensor_tensor(q, a, th, q, OP.is_gt, OP.add)
            eng.drain()
        eng.tensor_scalar(r, a, 2.5, None, OP.is_gt)
        eng.drain()
        eng.scalar_tensor_tensor(r, a, 3.5, r, OP.is_gt, OP.add)
        eng.drain()
        eng.scalar_tensor_tensor(q, q, 0.5, r, OP.mult, OP.add)
        eng.drain()
        sig(eng.tensor_scalar(r, a, 5.0, 2.0, OP.is_gt, OP.mult))

    v_tok = p.sec("vector", [a_tok], vbody)

    def sbody(eng, sig):
        sig(eng.activation(out=scr["a"][:, 0:W], in_=src, func=AF.Sign))

    s_tok = p.sec("scalar", [v_tok], sbody)

    def v2body(eng, sig):
        a = scr["a"][:, 0:W]
        q = scr["q"][:, 0:W]
        r = scr["r"][:, 0:W]
        eng.tensor_tensor(q, q, r, OP.add)
        eng.drain()
        eng.tensor_tensor(q, q, a, OP.mult)
        eng.drain()
        q3 = scr["q"][:, 0:W].rearrange("p (g k) -> p g k", k=16)
        d3 = dst.rearrange("p (g k) -> p g k", k=16)
        sig(eng.tensor_tensor(d3, q3, scr["s"][:, 0:G, None].to_broadcast([128, G, 16]), OP.mult))

    tok = p.sec("vector", [s_tok], v2body)
    scr["last"] = tok
    return tok


def emit_quant_fm_via_vt(p, deps, src_fm_slice, dst_fm_slice, scr, vt_scr):
    def t1(eng, sig):
        sig(eng.transpose(vt_scr[:, 0:T], src_fm_slice))

    tok1 = p.sec("vector", deps, t1)
    tok2 = emit_fp4_quant(p, [tok1], vt_scr[:, 0:T], vt_scr[:, T:2 * T], scr, T)

    def t2(eng, sig):
        sig(eng.transpose(dst_fm_slice, vt_scr[:, T:2 * T]))

    return p.sec("vector", [tok2], t2)



def mk_quant_scr(p, sc, maxW=2816):
    return {
        "m": p.sbuf("q_m", [128, maxW // 16], F32, ctx=sc),
        "s": p.sbuf("q_s", [128, maxW // 16], F32, ctx=sc),
        "a": p.sbuf("q_a", [128, maxW], F32, ctx=sc),
        "q": p.sbuf("q_q", [128, maxW], BF16, ctx=sc),
        "r": p.sbuf("q_r", [128, maxW], BF16, ctx=sc),
    }


def emit_quant_x_fm(p, deps, x_fm, mean_sb, rstd_sb, xq_fm):
    sc = p.scope()
    scr = mk_quant_scr(p, sc, maxW=2048)
    xnf = p.sbuf("qx_f", [128, T], F32, ctx=sc)
    vt = p.sbuf("qx_vt", [128, T], F32, ctx=sc)
    vtq = p.sbuf("qx_vq", [128, T], BF16, ctx=sc)
    tok = merge_deps([deps])
    for fc in range(NFC):
        def fbody(eng, sig, fc=fc):
            eng.tensor_tensor(xnf[:, :], x_fm[:, fc, :], mean_sb[:, :], OP.subtract)
            eng.drain()
            eng.tensor_tensor(xnf[:, :], xnf[:, :], rstd_sb[:, :], OP.mult)
            eng.drain()
            sig(eng.transpose(vt[:, :], xnf[:, :]))

        f_tok = p.sec("vector", [tok], fbody)
        q_tok = emit_fp4_quant(p, [f_tok], vt[:, :], vtq[:, :], scr, T)

        def tbody(eng, sig, fc=fc):
            sig(eng.transpose(xq_fm[:, fc, :], vtq[:, :]))

        tok = p.sec("vector", [q_tok], tbody)
    p.close_scope(sc, tok)
    return tok


def emit_l0_ffn(p, deps, xq_fm, x_fm, io, cst, x_dep):
    sc_all = p.scope()
    scr = mk_quant_scr(p, sc_all, maxW=2816)
    vt_scr = p.sbuf("l0f_vt", [128, 2 * T], BF16, ctx=sc_all)
    xq_tok = merge_deps([deps])

    IFF = 2816
    IB = IFF // 128
    hq = p.sbuf("l0f_hq", [128, IB, 1024], BF16, ctx=sc_all)
    wrow = [p.sbuf("l0f_wr", [128, 2816], BF16, ctx=sc_all) for _ in range(2)]
    wq = [p.sbuf("l0f_wq", [128, 2816], BF16, ctx=sc_all) for _ in range(2)]
    wT = [p.sbuf("l0f_wT", [128, 22, 128], BF16, ctx=sc_all) for _ in range(2)]
    gu = [vt_scr[:, 1024:2048], vt_scr[:, T + 1024:2 * T]]
    ps = [p.psum("l0f_ps", [128, 512], F32, ctx=sc_all) for _ in range(2)]
    identb = cst["identb"]
    x_tok = x_dep

    def stream_wq(tag, w_dram, mb, K, buf_i, deps2):
        def lbody(eng, sig):
            src = w_dram[mb * 128:(mb + 1) * 128, 0:K].rearrange("p m -> p m")
            sig(eng.dma_start(out=wrow[buf_i][:, 0:K], in_=src), dma=True)

        l_tok = p.sec("gpsimd", deps2, lbody, dma="sw")
        q_tok = emit_fp4_quant(p, [l_tok], wrow[buf_i][:, 0:K], wq[buf_i][:, 0:K], scr, K)
        pairs = [(wq[buf_i][:, kb * 128:(kb + 1) * 128], wT[buf_i][:, kb, :])
                 for kb in range(K // 128)]
        return emit_transpose_grid(p, [q_tok], pairs, identb, BF16, tag=f"l0f_{tag}")

    carry = [None, None]
    for tc in range(2):
        c_base = tc * 1024
        hq_toks = []
        prev = list(carry)
        for mb in range(IB):
            bi = mb % 2
            wg_tok = stream_wq("g", io["wg"], mb, D, bi, [xq_tok, prev[bi]])
            g_toks = []
            for nc_ in range(2):
                def gmm(eng, sig, bi=bi, nc_=nc_, c_base=c_base):
                    for kb in range(8):
                        ins = eng.matmul(ps[nc_][:, :], wT[bi][:, kb, :],
                                         xq_fm[:, kb, c_base + nc_ * 512:c_base + (nc_ + 1) * 512],
                                         start=(kb == 0), stop=(kb == 7))
                    sig(ins)

                mm_tok = p.sec("tensor", [wg_tok, prev[bi]], gmm)

                def gev(eng, sig, nc_=nc_):
                    sig(eng.activation(out=gu[0][:, nc_ * 512:(nc_ + 1) * 512],
                                       in_=ps[nc_][:, :], func=AF.Silu))

                g_toks.append(p.sec("scalar", [mm_tok], gev))
            wu_tok = stream_wq("u", io["wu"], mb, D, bi, [g_toks])
            u_toks = []
            for nc_ in range(2):
                def umm(eng, sig, bi=bi, nc_=nc_, c_base=c_base):
                    for kb in range(8):
                        ins = eng.matmul(ps[nc_][:, :], wT[bi][:, kb, :],
                                         xq_fm[:, kb, c_base + nc_ * 512:c_base + (nc_ + 1) * 512],
                                         start=(kb == 0), stop=(kb == 7))
                    sig(ins)

                mm_tok = p.sec("tensor", [wu_tok, g_toks], umm)

                def uev(eng, sig, nc_=nc_):
                    sig(eng.tensor_copy(gu[1][:, nc_ * 512:(nc_ + 1) * 512], ps[nc_][:, :]))

                u_toks.append(p.sec("vector", [mm_tok], uev))

            def hbody(eng, sig):
                sig(eng.tensor_tensor(gu[0][:, :], gu[0][:, :], gu[1][:, :], OP.mult))

            h_tok = p.sec("vector", [u_toks], hbody)

            def hv1(eng, sig):
                sig(eng.transpose(vt_scr[:, 0:1024], gu[0][:, :]))

            hv1_tok = p.sec("vector", [h_tok], hv1)
            hq_q = emit_fp4_quant(p, [hv1_tok], vt_scr[:, 0:1024], vt_scr[:, T:T + 1024],
                                  scr, 1024)

            def hv2(eng, sig, mb=mb):
                sig(eng.transpose(hq[:, mb, :], vt_scr[:, T:T + 1024]))

            hq_tok = p.sec("vector", [hq_q], hv2)
            hq_toks.append(hq_tok)
            prev[bi] = hq_tok

        prev_d = list(prev)
        for ob in range(NFC):
            bi = ob % 2
            wd_tok = stream_wq("d", io["wd"], ob, IFF, bi, [hq_toks, prev_d[bi]])
            for nc_ in range(2):
                def dmm(eng, sig, bi=bi, nc_=nc_, c_base=c_base):
                    for kb in range(22):
                        ins = eng.matmul(ps[nc_][:, :], wT[bi][:, kb, :],
                                         hq[:, kb, nc_ * 512:(nc_ + 1) * 512],
                                         start=(kb == 0), stop=(kb == 21))
                    sig(ins)

                mm_tok = p.sec("tensor", [wd_tok, prev_d[bi]], dmm)

                def dev(eng, sig, ob=ob, nc_=nc_, c_base=c_base):
                    dst = x_fm[:, ob, c_base + nc_ * 512:c_base + (nc_ + 1) * 512]
                    sig(eng.tensor_tensor(dst, ps[nc_][:, :], dst, OP.add))

                e_tok = p.sec("vector", [mm_tok, x_tok], dev)
                prev_d[bi] = e_tok
        x_tok = merge_deps([x_tok, prev_d[0], prev_d[1]])
        carry = list(prev_d)
    p.close_scope(sc_all, x_tok)
    return x_tok



def l1_q_slotmap():
    slots = [None] * 16
    even = [s for s in range(16) if s % 2 == 0]
    odd = [s for s in range(16) if s % 2 == 1]
    for h in range(16):
        if (h // 4) % 2 == 0:
            slots[h] = even.pop(0)
        else:
            slots[h] = odd.pop(0)
    return slots


def make_l1_consts(c):
    import ml_dtypes
    HD = 64
    nb = HD // 4
    inv = 1.0 / (10000.0 ** (np.arange(nb) / nb))
    ang = np.arange(T)[:, None] * inv[None, :]
    sin_h = np.repeat(np.sin(ang), 2, axis=-1)
    cos_h = np.repeat(np.cos(ang), 2, axis=-1)
    sin = np.concatenate([sin_h, sin_h], -1)
    cos = np.concatenate([cos_h, cos_h], -1)
    c["l1_cos"] = np.ascontiguousarray(np.tile(cos.T, (2, 1))).astype(ml_dtypes.bfloat16)
    c["l1_sin"] = np.ascontiguousarray(np.tile(sin.T, (2, 1))).astype(ml_dtypes.bfloat16)
    R = np.zeros((64, 64), np.float32)
    for i in range(32):
        R[i + 32, i] = -1.0
    for i in range(32, 64):
        R[i - 32, i] = 1.0
    RB = np.zeros((128, 128), np.float32)
    RB[0:64, 0:64] = R
    RB[64:128, 64:128] = R
    c["l1_rot"] = RB.astype(ml_dtypes.bfloat16)
    i_ = np.arange(128)[:, None]
    j_ = np.arange(128)[None, :]
    NEG = -1e9
    Z = np.zeros((128, 128), np.float32)
    D_ = np.where(np.abs(i_ - j_) <= 12, 0.0, NEG) + Z
    L_ = np.where(np.abs(i_ - j_ + 128) <= 12, 0.0, NEG) + Z
    U_ = np.where(np.abs(i_ - j_ - 128) <= 12, 0.0, NEG) + Z
    G_ = np.where(j_ < 8, 0.0, NEG) + Z
    R0 = np.where(i_ < 8, 0.0, NEG) + Z
    combos = [np.maximum(np.maximum(R0, D_), G_),
              np.maximum(R0, U_),
              R0,
              np.maximum(L_, G_),
              G_,
              L_,
              D_,
              U_]
    c["l1_masks"] = np.stack(combos, 1).astype(np.float32)
    return c


def emit_l1_attn(p, deps, xn_fm, x_fm, io, cst, x_dep):
    slots = l1_q_slotmap()
    head_of_slot = [None] * 16
    for h in range(16):
        head_of_slot[slots[h]] = h
    sc_kv = p.scope()
    k_fm = p.sbuf("l1_k", [128, 2, T], BF16, ctx=sc_kv)
    v_fm = p.sbuf("l1_v", [128, 2, T], BF16, ctx=sc_kv)
    k_toks = emit_gemm_fm(p, "l1k", deps, io["Wk"], D, 256,
                          lambda kb: xn_fm[:, kb, :],
                          lambda mb, c0, w: k_fm[:, mb, c0:c0 + w],
                          bias_sb=io["bkT_sb"])
    v_toks = emit_gemm_fm(p, "l1v", deps, io["Wv"], D, 256,
                          lambda kb: xn_fm[:, kb, :],
                          lambda mb, c0, w: v_fm[:, mb, c0:c0 + w],
                          bias_sb=io["bvT_sb"])
    v_tm = p.sbuf("l1_vtm", [128, 16, 256], BF16, ctx=sc_kv)
    pairs = []
    for kvh in range(4):
        tv, ov = kvh // 2, (kvh % 2) * 64
        for tcch in range(16):
            idn = cst["identb"][0:64, 0:64] if ov == 0 else cst["identb_hi"][64:128, 0:64]
            pairs.append((v_fm[:, tv, tcch * 128:(tcch + 1) * 128][ov:ov + 64, :],
                          v_tm[:, tcch, kvh * 64:(kvh + 1) * 64], idn))
    vt_tok = emit_transpose_grid(p, merge_deps(v_toks), pairs, cst["identb"], BF16,
                                 tag="l1vt")

    def emit_rope(tiles, ti, is_q, base_tok, prev_ev, ps_rot, rope_scr, hw_col):
        toks = []
        for c0 in range(0, T, 512):
            def rbody(eng, sig, ti=ti, c0=c0):
                sig(eng.matmul(ps_rot[:, :], cst["l1_rot_sb"][:, :], tiles[:, ti, c0:c0 + 512],
                               start=True, stop=True))

            r_tok = p.sec("tensor", [base_tok, prev_ev], rbody)

            def vbody(eng, sig, ti=ti, c0=c0):
                eng.tensor_tensor(rope_scr[:, :], ps_rot[:, :],
                                  cst["l1_sin"][:, c0:c0 + 512], OP.mult)
                eng.drain()
                eng.tensor_tensor(tiles[:, ti, c0:c0 + 512], tiles[:, ti, c0:c0 + 512],
                                  cst["l1_cos"][:, c0:c0 + 512], OP.mult)
                eng.drain()
                if is_q:
                    eng.tensor_tensor(tiles[:, ti, c0:c0 + 512],
                                      tiles[:, ti, c0:c0 + 512], rope_scr[:, :], OP.add)
                    eng.drain()
                    ins = eng.tensor_scalar(tiles[:, ti, c0:c0 + 512],
                                            tiles[:, ti, c0:c0 + 512],
                                            hw_col, None, OP.mult)
                else:
                    ins = eng.tensor_tensor(tiles[:, ti, c0:c0 + 512],
                                            tiles[:, ti, c0:c0 + 512], rope_scr[:, :],
                                            OP.add)
                sig(ins)

            prev_ev = p.sec("vector", [r_tok], vbody)
            toks.append(prev_ev)
        return merge_deps(toks), prev_ev

    sc_r = p.scope()
    ps_rot = p.psum("l1_rot_ps", [128, 512], F32, ctx=sc_r)
    rope_scr = p.sbuf("l1_ropes", [128, 512], BF16, ctx=sc_r)
    prev_ev = None
    k_rope_toks = []
    for ti in range(2):
        tk_, prev_ev = emit_rope(k_fm, ti, False, merge_deps(k_toks), prev_ev,
                                 ps_rot, rope_scr, None)
        k_rope_toks.append(tk_)
    k_rope = merge_deps(k_rope_toks)
    p.close_scope(sc_r, k_rope)

    masks = cst["l1_masks_sb"]
    x_tok = x_dep
    gdone = None
    for g in range(2):
        sc_g = p.scope()
        q_fm = p.sbuf("l1_q", [128, 4, T], BF16, ctx=sc_g)
        gslots = list(range(g * 8, (g + 1) * 8))
        q_toks = emit_gemm_fm(p, "l1q", [deps, gdone], io["Wq"][:, g * 512:(g + 1) * 512],
                              D, 512, lambda kb: xn_fm[:, kb, :],
                              lambda mb, c0, w: q_fm[:, mb, c0:c0 + w],
                              bias_sb=io["bqT_sb"][:, g * 4:(g + 1) * 4])
        sc_r2 = p.scope()
        ps_rot2 = p.psum("l1_rot_ps2", [128, 512], F32, ctx=sc_r2)
        rope_scr2 = p.sbuf("l1_ropes2", [128, 512], BF16, ctx=sc_r2)
        prev_ev2 = None
        q_rope_toks = []
        for ti in range(4):
            tq_, prev_ev2 = emit_rope(q_fm, ti, True, merge_deps(q_toks), prev_ev2,
                                      ps_rot2, rope_scr2,
                                      io["hwT_sb"][:, g * 4 + ti:g * 4 + ti + 1])
            q_rope_toks.append(tq_)
        q_rope = merge_deps(q_rope_toks)
        p.close_scope(sc_r2, q_rope)

        sc_a = p.scope()
        S_sb = p.sbuf("l1_S", [128, T], F32, ctx=sc_a)
        P_sb = p.sbuf("l1_P", [128, T], BF16, ctx=sc_a)
        PT_sb = p.sbuf("l1_PT", [128, 16, 128], BF16, ctx=sc_a)
        mcol = p.sbuf("l1_m", [128, 4], F32, ctx=sc_a)
        values = p.sbuf("l1_val", [128, 4, T], BF16, ctx=sc_a)
        ps_s = [p.psum("l1_sps", [128, 512], F32, ctx=sc_a) for _ in range(4)]
        ps_o = p.psum("l1_ops", [128, 128], F32, ctx=sc_a)
        prev_cycle = None
        prev_pv = None
        attn_toks = []
        for si in range(8):
            s_ = gslots[si]
            h = head_of_slot[s_]
            kvh = h // 4
            tq, oq = si // 2, (si % 2) * 64
            tk, ok = kvh // 2, (kvh % 2) * 64
            assert oq == ok, (s_, h, kvh)
            for rb in range(16):
                if rb == 0:
                    cbs = list(range(16))
                else:
                    cbs = sorted(set(c for c in (0, rb - 1, rb, rb + 1) if 0 <= c <= 15))

                def mclass(rb, cb):
                    if rb == 0:
                        return 0 if cb == 0 else (1 if cb == 1 else 2)
                    if cb == 0:
                        return 3 if rb == 1 else 4
                    return {rb - 1: 5, rb: 6, rb + 1: 7}[cb]

                def sbody(eng, sig, tq=tq, oq=oq, tk=tk, ok=ok, rb=rb, cbs=cbs):
                    for ci, cb in enumerate(cbs):
                        ins = eng.matmul(
                            ps_s[ci // 4][:, (ci % 4) * 128:(ci % 4 + 1) * 128],
                            q_fm[:, tq, rb * 128:(rb + 1) * 128][oq:oq + 64, :],
                            k_fm[:, tk, cb * 128:(cb + 1) * 128][ok:ok + 64, :],
                            start=True, stop=True)
                    sig(ins)

                s_tok = p.sec("tensor", [q_rope, k_rope, vt_tok, prev_cycle], sbody)

                def evbody(eng, sig, rb=rb, cbs=cbs):
                    for ci, cb in enumerate(cbs):
                        ins = eng.tensor_tensor(
                            S_sb[:, ci * 128:(ci + 1) * 128],
                            ps_s[ci // 4][:, (ci % 4) * 128:(ci % 4 + 1) * 128],
                            masks[:, mclass(rb, cb), :], OP.add)
                    sig(ins)

                ev_tok = p.sec("vector", [s_tok], evbody)
                kvext = len(cbs) * 128

                def mxbody(eng, sig, kvext=kvext):
                    eng.tensor_reduce(mcol[:, 0:1], S_sb[:, 0:kvext], AX.X, OP.max)
                    eng.drain()
                    sig(eng.tensor_scalar(mcol[:, 1:2], mcol[:, 0:1], -0.125, None, OP.mult))

                mx_tok = p.sec("vector", [ev_tok], mxbody)

                def expbody(eng, sig, kvext=kvext):
                    sig(eng.activation(out=P_sb[:, 0:kvext], in_=S_sb[:, 0:kvext],
                                       func=AF.Exp, scale=0.125, bias=mcol[:, 1:2],
                                       accum_out=mcol[:, 2:3]))

                ex_tok = p.sec("scalar", [mx_tok], expbody)

                def nrmbody(eng, sig, kvext=kvext):
                    eng.reciprocal(mcol[:, 3:4], mcol[:, 2:3])
                    eng.drain()
                    sig(eng.tensor_scalar(P_sb[:, 0:kvext], P_sb[:, 0:kvext],
                                          mcol[:, 3:4], None, OP.mult))

                nr_tok = p.sec("vector", [ex_tok], nrmbody)
                tp_pairs = [(P_sb[:, ci * 128:(ci + 1) * 128], PT_sb[:, ci, :])
                            for ci in range(len(cbs))]
                pt_tok = emit_transpose_grid(p, [nr_tok], tp_pairs, cst["identb"], BF16,
                                             tag="l1pt")
                prev_cycle = pt_tok

                def pvbody(eng, sig, si=si, kvh=kvh, cbs=cbs):
                    ob = (si % 2) * 64
                    for ci, cb in enumerate(cbs):
                        ins = eng.matmul(ps_o[ob:ob + 64, :],
                                         v_tm[:, cb, kvh * 64:(kvh + 1) * 64],
                                         PT_sb[:, ci, :],
                                         start=(ci == 0), stop=(ci == len(cbs) - 1))
                    sig(ins)

                pv_tok = p.sec("tensor", [pt_tok, prev_pv], pvbody)

                def vebody(eng, sig, si=si, rb=rb):
                    ob = (si % 2) * 64
                    sig(eng.tensor_copy(
                        values[:, si // 2, rb * 128:(rb + 1) * 128][ob:ob + 64, :],
                        ps_o[ob:ob + 64, :]))

                ve_tok = p.sec("vector", [pv_tok], vebody)
                prev_pv = ve_tok
                attn_toks.append(ve_tok)

        op_toks = emit_gemm_fm(p, "l1op", attn_toks, io["Wo"][g * 512:(g + 1) * 512, :],
                               512, D, lambda kb: values[:, kb, :], None,
                               bias_sb=io["boT_sb"] if g == 0 else None,
                               resid_fm=x_fm, resid_dep=x_tok)
        x_tok = merge_deps(op_toks)
        gdone = x_tok
        p.close_scope(sc_a, x_tok)
        p.close_scope(sc_g, x_tok)
    p.close_scope(sc_kv, x_tok)
    return x_tok



def make_l1f_consts(c):
    import ml_dtypes
    DIM_HEAD, WINDOW = 64, 512
    half_idx = np.arange(0, DIM_HEAD, 2, dtype=np.float64)
    inv_freq = 1.0 / 10000.0 ** (half_idx / DIM_HEAD)
    t = np.arange(T, dtype=np.float64)
    freqs = t[:, None] * inv_freq[None, :]
    freqs = np.concatenate([freqs, freqs], -1)
    base = (half_idx + 0.4 * DIM_HEAD) / (1.4 * DIM_HEAD)
    power = (t - T // 2) / float(WINDOW // 2)
    scale = base[None, :] ** power[:, None]
    scale = np.concatenate([scale, scale], -1)
    cos, sin = np.cos(freqs), np.sin(freqs)
    for nm, arr in (("l1f_cs", cos * scale), ("l1f_ss", sin * scale),
                    ("l1f_ci", cos / scale), ("l1f_si", sin / scale)):
        c[nm] = np.ascontiguousarray(np.tile(arr.T, (2, 1))).astype(ml_dtypes.bfloat16)
    NEG = np.float32(-1e30)
    m = np.zeros((128, 4, 1024), np.float32)
    for s in range(4):
        for r in range(128):
            i_rel = 128 * s + r
            for cchunk in range(8):
                pass
    r_ = np.arange(128)[:, None]
    cloc = np.arange(1024)[None, :]
    for s in range(4):
        i_rel = 128 * s + r_
        j_rel = cloc - 512
        ok = (i_rel - j_rel >= 0) & (i_rel - j_rel <= 512)
        m[:, s, :] = np.where(ok, 0.0, NEG)
    c["l1f_mask"] = m
    return c


def emit_inner_ln(p, deps, src_fm, dst_fm, cst, g_col, b_col, src_dt=BF16):
    sc = p.scope()
    mean_sb = p.sbuf("iln_mean", [128, T], F32, ctx=sc)
    rstd_sb = p.sbuf("iln_rstd", [128, T], F32, ctx=sc)
    tok = emit_outer_ln(p, deps, src_fm, dst_fm, cst, mean_sb, rstd_sb,
                        gain_col=g_col, bias_col=b_col)
    p.close_scope(sc, tok)
    return tok


def emit_y0_to_dram(p, deps, x_fm, mean_sb, rstd_sb, y0_dram):
    sc = p.scope()
    scr = p.sbuf("y0_scr", [128, T], F32, ctx=sc)
    toks = []
    prev = None
    for fc in range(NFC):
        def yb(eng, sig, fc=fc):
            eng.tensor_tensor(scr[:, :], x_fm[:, fc, :], mean_sb[:, :], OP.subtract)
            eng.drain()
            sig(eng.tensor_tensor(scr[:, :], scr[:, :], rstd_sb[:, :], OP.mult))

        v_tok = p.sec("vector", [deps, prev], yb)

        def db(eng, sig, fc=fc):
            sig(eng.dma_start(out=y0_dram[fc, :, :], in_=scr[:, :]), dma=True)

        prev = p.sec("sync", [v_tok], db, dma=True)
        toks.append(prev)
    last = merge_deps(toks)
    p.close_scope(sc, last)
    return last


def emit_l1_ffn(p, deps, y_out_dram, y0_dram, io, cst, cst_store):
    sc_y = p.scope()
    y_fm = p.sbuf("l1f_y", [128, NFC, T], F32, ctx=sc_y)

    def ldy(eng, sig):
        sig(eng.dma_start(out=y_fm[:, :, :],
                          in_=y0_dram.rearrange("fc p t -> p fc t")), dma=True)

    y_tok = p.sec("sync", [deps], ldy, dma=True)

    for il in range(2):
        y_tok = emit_l1f_layer(p, y_tok, y_fm, io, cst, il)

    sc_s = p.scope()
    s_tm = p.sbuf("l1f_stm", [128, 16, D], F32, ctx=sc_s)
    t_tok = emit_fm_to_tm(p, [y_tok], y_fm, s_tm, cst_store["ident"][:, :], F32, tag="l1fst")

    def acc(eng, sig):
        dst = y_out_dram.rearrange("(tc p) d -> p tc d", p=128)
        sig(eng.dma_start(out=dst, in_=s_tm[:, :, :], accum_op=OP.add), dma=True)

    out = p.sec("gpsimd", [t_tok], acc, dma="sw")
    p.close_scope(sc_s, out)
    p.close_scope(sc_y, out)
    return out


def emit_shift_cast(p, deps, y_fm, h_bf, tag):
    toks = []
    for fc in range(NFC):
        if fc < 4:
            def sb(eng, sig, fc=fc):
                sig(eng.activation(out=h_bf[:, fc, :], in_=y_fm[:, fc, :], func=AF.Identity))

            toks.append(p.sec("scalar", [deps], sb))
        else:
            def vb(eng, sig, fc=fc):
                eng.memset(h_bf[:, fc, 0:1], 0.0)
                sig(eng.tensor_copy(h_bf[:, fc, 1:T], y_fm[:, fc, 0:T - 1]))

            toks.append(p.sec("vector", [deps], vb))
    return merge_deps(toks)


def emit_l1f_layer(p, y_tok, y_fm, io, cst, il):
    INNER = 512
    sc1 = p.scope()
    h_bf = p.sbuf("l1f_h1", [128, NFC, T], BF16, ctx=sc1)
    sh_tok = emit_shift_cast(p, [y_tok], y_fm, h_bf, f"l1fs{il}")

    qkv = p.sbuf("l1f_qkv", [128, 12, T], BF16, ctx=sc1)
    qkv_toks = emit_gemm_fm(p, f"l1fqkv{il}", sh_tok, io["wqkv"][il], D, 3 * INNER,
                            lambda kb: h_bf[:, kb, :],
                            lambda mb, c0, w: qkv[:, mb, c0:c0 + w], panel=256)
    qkv_tok = merge_deps(qkv_toks)

    sc2 = p.scope()
    for nm in ("l1f_cs", "l1f_ss", "l1f_ci", "l1f_si"):
        cst[nm] = p.sbuf(nm + "S", [128, T], BF16, ctx=sc2)
    tab_tok = load_const_tiles(p, io["caps"], [(nm, cst[nm][:, :]) for nm in
                                               ("l1f_cs", "l1f_ss", "l1f_ci", "l1f_si")])
    ps_ss = p.psum("l1f_ss_ps", [2, 512], F32, ctx=sc2)
    ps_bc = p.psum("l1f_bc_ps", [128, 512], F32, ctx=sc2)
    ps_rot = p.psum("l1f_rot_ps", [128, 512], F32, ctx=sc2)
    sq_sb = p.sbuf("l1f_sq", [128, T], BF16, ctx=sc2)
    ss_sb = p.sbuf("l1f_ssn", [2, 2, T], F32, ctx=sc2)
    rope_scr = p.sbuf("l1f_rsc", [128, 512], BF16, ctx=sc2)
    rope_toks = []
    prev = None
    for which, toff, sccol, tabs in (("q", 0, "qsT_sb", ("l1f_cs", "l1f_ss")),
                                     ("k", 4, "ksT_sb", ("l1f_ci", "l1f_si"))):
        for ti in range(4):
            tile = qkv[:, toff + ti, :]

            def sqb(eng, sig, tile=tile):
                sig(eng.activation(out=sq_sb[:, :], in_=tile, func=AF.Square))

            sq_tok = p.sec("scalar", [qkv_tok, prev, tab_tok], sqb)

            iv_tok = None
            for cs in range(0, T, 512):
                def smb(eng, sig, cs=cs):
                    sig(eng.matmul(ps_ss[:, :], cst["blk64T_b"][:, 0:2],
                                   sq_sb[:, cs:cs + 512], start=True, stop=True))

                sm_tok = p.sec("tensor", [sq_tok, iv_tok], smb)

                def invb(eng, sig, cs=cs):
                    sig(eng.tensor_scalar(ss_sb[:, 0, cs:cs + 512], ps_ss[:, :],
                                          1e-24, None, OP.max))

                iv_tok = p.sec("vector", [sm_tok], invb)

            def lnb(eng, sig):
                eng.activation(out=ss_sb[:, 1, :], in_=ss_sb[:, 0, :], func=AF.Ln)
                eng.drain()
                sig(eng.activation(out=ss_sb[:, 0, :], in_=ss_sb[:, 1, :], func=AF.Exp,
                                   scale=-0.5))

            ln_tok = p.sec("scalar", [iv_tok], lnb)
            for c0 in range(0, T, 512):
                def bcb(eng, sig, c0=c0):
                    sig(eng.matmul(ps_bc[:, :], cst["blk64_f"][0:2, :], ss_sb[:, 0, c0:c0 + 512],
                                   start=True, stop=True))

                bc_tok = p.sec("tensor", [ln_tok, prev], bcb)

                def nb(eng, sig, tile=tile, c0=c0, sccol=sccol):
                    sig(eng.scalar_tensor_tensor(tile[:, c0:c0 + 512], tile[:, c0:c0 + 512],
                                                 io[sccol][il][:, 0:1], ps_bc[:, :],
                                                 OP.mult, OP.mult))

                n_tok = p.sec("vector", [bc_tok], nb)

                def rotb(eng, sig, tile=tile, c0=c0):
                    sig(eng.matmul(ps_rot[:, :], cst["l1_rot_sb"][:, :], tile[:, c0:c0 + 512],
                                   start=True, stop=True))

                r_tok = p.sec("tensor", [n_tok], rotb)

                def rapp(eng, sig, tile=tile, c0=c0, tabs=tabs):
                    eng.tensor_tensor(rope_scr[:, :], ps_rot[:, :],
                                      cst[tabs[1]][:, c0:c0 + 512], OP.mult)
                    eng.drain()
                    eng.tensor_tensor(tile[:, c0:c0 + 512], tile[:, c0:c0 + 512],
                                      cst[tabs[0]][:, c0:c0 + 512], OP.mult)
                    eng.drain()
                    sig(eng.tensor_tensor(tile[:, c0:c0 + 512], tile[:, c0:c0 + 512],
                                          rope_scr[:, :], OP.add))

                prev = p.sec("vector", [r_tok], rapp)
                rope_toks.append(prev)
    rope_tok = merge_deps(rope_toks)
    p.close_scope(sc2, rope_tok)

    sc3 = p.scope()
    v_tm = p.sbuf("l1f_vtm", [128, 16, 512], BF16, ctx=sc3)
    pairs = []
    for h in range(8):
        tv, ov = 8 + h // 2, (h % 2) * 64
        for tcch in range(16):
            idn = cst["identb"][0:64, 0:64] if ov == 0 else cst["identb_hi"][64:128, 0:64]
            pairs.append((qkv[:, tv, tcch * 128:(tcch + 1) * 128][ov:ov + 64, :],
                          v_tm[:, tcch, h * 64:(h + 1) * 64], idn))
    vt_tok = emit_transpose_grid(p, [qkv_tok], pairs, cst["identb"], BF16, tag="l1fvt")

    S_sb = p.sbuf("l1f_S", [128, 1024], F32, ctx=sc3)
    P_sb = p.sbuf("l1f_P", [128, 1024], BF16, ctx=sc3)
    PT_sb = p.sbuf("l1f_PT", [128, 8, 128], BF16, ctx=sc3)
    mcol = p.sbuf("l1f_m", [128, 4], F32, ctx=sc3)
    values = p.sbuf("l1f_val", [128, 4, T], BF16, ctx=sc3)
    ps_s = [p.psum("l1f_sps", [128, 512], F32, ctx=sc3) for _ in range(2)]
    ps_o = p.psum("l1f_ops", [128, 128], F32, ctx=sc3)
    prev_cycle = None
    prev_pv = None
    attn_toks = []
    for h in range(8):
        tq, oq = h // 2, (h % 2) * 64
        tk, ok = 4 + h // 2, (h % 2) * 64
        for w_ in range(4):
            for s_ in range(4):
                qb = w_ * 4 + s_
                kv0 = (w_ - 1) * 512

                def sbody(eng, sig, tq=tq, oq=oq, tk=tk, ok=ok, qb=qb, kv0=kv0):
                    for half in range(2):
                        a0 = kv0 + half * 512
                        if a0 < 0:
                            continue
                        ins = eng.matmul(ps_s[half][:, :],
                                         qkv[:, tq, qb * 128:(qb + 1) * 128][oq:oq + 64, :],
                                         qkv[:, tk, a0:a0 + 512][ok:ok + 64, :],
                                         start=True, stop=True)
                    sig(ins)

                s_tok = p.sec("tensor", [rope_tok, vt_tok, prev_cycle], sbody)
                mask = cst["l1f_mask_sb"]

                def evbody(eng, sig, s_=s_, kv0=kv0, mask=mask):
                    for half in range(2):
                        a0 = kv0 + half * 512
                        if a0 < 0:
                            ins = eng.memset(S_sb[:, half * 512:(half + 1) * 512], -1e30)
                        else:
                            ins = eng.tensor_tensor(S_sb[:, half * 512:(half + 1) * 512],
                                                    ps_s[half][:, :],
                                                    mask[:, s_, half * 512:(half + 1) * 512],
                                                    OP.add)
                    sig(ins)

                ev_tok = p.sec("vector", [s_tok], evbody)

                def mxbody(eng, sig):
                    eng.tensor_reduce(mcol[:, 0:1], S_sb[:, :], AX.X, OP.max)
                    eng.drain()
                    sig(eng.tensor_scalar(mcol[:, 1:2], mcol[:, 0:1], -8.0, None, OP.mult))

                mx_tok = p.sec("vector", [ev_tok], mxbody)

                def expbody(eng, sig):
                    sig(eng.activation(out=P_sb[:, :], in_=S_sb[:, :], func=AF.Exp,
                                       scale=8.0, bias=mcol[:, 1:2], accum_out=mcol[:, 2:3]))

                ex_tok = p.sec("scalar", [mx_tok], expbody)

                def nrmbody(eng, sig):
                    eng.reciprocal(mcol[:, 3:4], mcol[:, 2:3])
                    eng.drain()
                    sig(eng.tensor_scalar(P_sb[:, :], P_sb[:, :], mcol[:, 3:4], None, OP.mult))

                nr_tok = p.sec("vector", [ex_tok], nrmbody)
                tp_pairs = [(P_sb[:, ci * 128:(ci + 1) * 128], PT_sb[:, ci, :])
                            for ci in range(8)]
                pt_tok = emit_transpose_grid(p, [nr_tok], tp_pairs, cst["identb"], BF16,
                                             tag="l1fpt")
                prev_cycle = pt_tok

                def pvbody(eng, sig, h=h, qb=qb, kv0=kv0):
                    ob = (h % 2) * 64
                    first = True
                    for ci in range(8):
                        a0 = kv0 + ci * 128
                        if a0 < 0:
                            continue
                        tc_i = a0 // 128
                        ins = eng.matmul(ps_o[ob:ob + 64, :],
                                         v_tm[:, tc_i, h * 64:(h + 1) * 64],
                                         PT_sb[:, ci, :],
                                         start=first, stop=(ci == 7))
                        first = False
                    sig(ins)

                pv_tok = p.sec("tensor", [pt_tok, prev_pv], pvbody)

                def vebody(eng, sig, h=h, qb=qb):
                    ob = (h % 2) * 64
                    sig(eng.tensor_copy(
                        values[:, h // 2, qb * 128:(qb + 1) * 128][ob:ob + 64, :],
                        ps_o[ob:ob + 64, :]))

                ve_tok = p.sec("vector", [pv_tok], vebody)
                prev_pv = ve_tok
                attn_toks.append(ve_tok)

    op_toks = emit_gemm_fm(p, f"l1fo{il}", attn_toks, io["wo"][il], INNER, D,
                           lambda kb: values[:, kb, :], None,
                           resid_fm=y_fm, resid_dep=y_tok)
    y_tok2 = merge_deps(op_toks)
    p.close_scope(sc3, y_tok2)
    p.close_scope(sc1, y_tok2)

    HIDP = 2816
    sc4 = p.scope()
    h2ln = p.sbuf("l1f_h2ln", [128, NFC, T], BF16, ctx=sc4)
    sc5 = p.scope()
    h2 = p.sbuf("l1f_h2", [128, NFC, T], BF16, ctx=sc5)
    sh2_tok = emit_shift_cast(p, [y_tok2], y_fm, h2, f"l1fs2{il}")
    ln_tok = emit_inner_ln(p, [sh2_tok], h2, h2ln, cst,
                           io["lngT_sb"][il], io["lnbT_sb"][il])
    p.close_scope(sc5, ln_tok)

    for tc in range(2):
        c_base = tc * 1024
        sc6 = p.scope()
        gg = p.sbuf("l1f_gg", [128, 22, 1024], BF16, ctx=sc6)
        hh = gg
        g_toks = emit_gemm_fm(p, f"l1fg{il}", ln_tok, io["w1p"][il], D, HIDP,
                              lambda kb: h2ln[:, kb, c_base:c_base + 1024],
                              lambda mb, c0, w: gg[:, mb, c0:c0 + w],
                              bias_sb=io["b1gT_sb"][il], act=AF.Gelu, Tlen=1024)
        a_toks = emit_gemm_fm_amul(p, f"l1fa{il}", [g_toks, ln_tok], io["w1p"][il], D, HIDP,
                                   lambda kb: h2ln[:, kb, c_base:c_base + 1024],
                                   gg, hh, io["b1aT_sb"][il], col_off=HIDP, Tlen=1024)
        dn_toks = emit_gemm_fm(p, f"l1fw2{il}", a_toks, io["w2p"][il], HIDP, D,
                               lambda kb: hh[:, kb, :], None, Tlen=1024,
                               bias_sb=io["b2T_sb"][il],
                               resid_fm=y_fm, resid_dep=y_tok2, resid_off=c_base)
        y_tok2 = merge_deps([y_tok2, dn_toks])
        p.close_scope(sc6, y_tok2)
    p.close_scope(sc4, y_tok2)
    return y_tok2


def emit_gemm_fm_amul(p, name, deps, w_src, K, Mtot, rhs_tiles, gg, hh, bias_sb,
                      col_off, Tlen):
    kb_n = (K + 127) // 128
    sc = p.scope()
    wbufs = [p.sbuf(f"{name}_w", [128, kb_n, 512], BF16, ctx=sc) for _ in range(2)]
    pss = [p.psum(f"{name}_ps", [128, 512], F32, ctx=sc) for _ in range(2)]
    npanels = (Mtot + 511) // 512
    load_toks = [None] * npanels
    comp_done = [[] for _ in range(npanels)]
    load_toks[0] = load_weight_panel(p, w_src[:, col_off:], K, 0, 512, wbufs[0], [deps])
    ps_free = [None, None]
    psi = 0
    toks = []
    last_pa = -1
    for mb in range(Mtot // 128):
        pa = (mb * 128) // 512
        if pa != last_pa:
            last_pa = pa
            if pa + 1 < npanels:
                m0, m1 = (pa + 1) * 512, min(Mtot, (pa + 2) * 512)
                dep = [deps] if pa == 0 else [deps, comp_done[pa - 1]]
                load_toks[pa + 1] = load_weight_panel(p, w_src[:, col_off:], K, m0, m1,
                                                      wbufs[(pa + 1) % 2], dep)
        for c0 in range(0, Tlen, 512):
            w = min(512, Tlen - c0)
            b = psi % 2
            psi += 1
            ps = pss[b]

            def tbody(eng, sig, mb=mb, c0=c0, w=w, ps=ps, pa=pa):
                for k in range(kb_n):
                    ins = eng.matmul(ps[:, 0:w],
                                     wbufs[pa % 2][:, k, mb * 128 - pa * 512:mb * 128 - pa * 512 + 128],
                                     rhs_tiles(k)[:, c0:c0 + w],
                                     start=(k == 0), stop=(k == kb_n - 1))
                sig(ins)

            t_tok = p.sec("tensor", [load_toks[pa], ps_free[b]], tbody)
            comp_done[pa].append(t_tok)

            def ebody(eng, sig, mb=mb, c0=c0, w=w, ps=ps):
                sig(eng.scalar_tensor_tensor(hh[:, mb, c0:c0 + w], ps[:, 0:w],
                                             bias_sb[:, mb:mb + 1], gg[:, mb, c0:c0 + w],
                                             OP.add, OP.mult))

            e_tok = p.sec("vector", [t_tok], ebody)
            ps_free[b] = e_tok
            toks.append(e_tok)
    p.close_scope(sc, toks)
    return toks


def declare_io_l1(nc, stage, pref="", xap=None, yap=None):
    io = {"x": xap if xap is not None else
          nc.dram_tensor("x", [T, D], F32, kind="ExternalInput").ap(),
          "y": yap if yap is not None else
          nc.dram_tensor("y", [T, D], F32, kind="ExternalOutput").ap()}
    names = []
    if stage in ("attn", "full"):
        names += [("Wq", [D, D]), ("bqT", [128, 8]), ("Wk", [D, 256]), ("bkT", [128, 2]),
                  ("Wv", [D, 256]), ("bvT", [128, 2]), ("Wo", [D, D]), ("boT", [128, 8]),
                  ("hwT", [128, 8])]
    if stage in ("ffn", "full"):
        names += [("wqkv0", [D, 1536]), ("wqkv1", [D, 1536]),
                  ("wo0", [512, D]), ("wo1", [512, D]),
                  ("w1p0", [D, 5632]), ("w1p1", [D, 5632]),
                  ("w2p0", [2816, D]), ("w2p1", [2816, D]),
                  ("qsT", [128, 2]), ("ksT", [128, 2]),
                  ("lngT", [128, 16]), ("lnbT", [128, 16]),
                  ("b1gT", [128, 44]), ("b1aT", [128, 44]), ("b2T", [128, 16])]
    for nm, shp in names:
        io[nm] = nc.dram_tensor(pref + nm, shp, F32, kind="ExternalInput").ap()
    return io


def host_prep_l1(inp):
    d = {}
    slots = l1_q_slotmap()
    Wq = np.asarray(inp["L1_attn_Wq"])
    bq = np.asarray(inp["L1_attn_bq"])
    Wo = np.asarray(inp["L1_attn_Wo"])
    hw = np.asarray(inp["L1_attn_head_weights"])
    qperm = np.zeros(D, np.int64)
    for h in range(16):
        s_ = slots[h]
        qperm[s_ * 64:(s_ + 1) * 64] = np.arange(h * 64, (h + 1) * 64)
    d["Wq"] = np.ascontiguousarray(Wq[:, qperm])
    d["bqT"] = _col128(bq[qperm])
    d["Wk"] = np.ascontiguousarray(inp["L1_attn_Wk"])
    d["bkT"] = np.ascontiguousarray(np.asarray(inp["L1_attn_bk"]).reshape(2, 128).T)
    d["Wv"] = np.ascontiguousarray(inp["L1_attn_Wv"])
    d["bvT"] = np.ascontiguousarray(np.asarray(inp["L1_attn_bv"]).reshape(2, 128).T)
    d["Wo"] = np.ascontiguousarray(Wo[qperm, :])
    d["boT"] = _col128(np.asarray(inp["L1_attn_bo"]))
    hwT = np.zeros((128, 8), np.float32)
    for h in range(16):
        s_ = slots[h]
        hwT[(s_ % 2) * 64:(s_ % 2) * 64 + 64, s_ // 2] = hw[h]
    d["hwT"] = hwT
    wqkv = np.asarray(inp["L1_ffn_wqkv"])
    wo = np.asarray(inp["L1_ffn_wo"])
    w1 = np.asarray(inp["L1_ffn_w1"])
    b1 = np.asarray(inp["L1_ffn_b1"])
    w2 = np.asarray(inp["L1_ffn_w2"])
    b2 = np.asarray(inp["L1_ffn_b2"])
    HID = 2730
    HIDP = 2816
    for il in range(2):
        d[f"wqkv{il}"] = np.ascontiguousarray(wqkv[il])
        d[f"wo{il}"] = np.ascontiguousarray(wo[il])
        w1a, w1g = w1[il][:, :HID], w1[il][:, HID:]
        w1p = np.zeros((D, 2 * HIDP), np.float32)
        w1p[:, 0:HID] = w1g
        w1p[:, HIDP:HIDP + HID] = w1a
        d[f"w1p{il}"] = w1p
        w2p = np.zeros((HIDP, D), np.float32)
        w2p[0:HID, :] = w2[il]
        d[f"w2p{il}"] = w2p
    d["qsT"] = np.stack([np.tile(np.asarray(inp["L1_ffn_q_scale"][il]), 2) for il in range(2)], 1)
    d["ksT"] = np.stack([np.tile(np.asarray(inp["L1_ffn_k_scale"][il]), 2) for il in range(2)], 1)
    d["lngT"] = np.concatenate([_col128(np.asarray(inp["L1_ffn_ln_g"][il])) for il in range(2)], 1)
    d["lnbT"] = np.concatenate([_col128(np.asarray(inp["L1_ffn_ln_b"][il])) for il in range(2)], 1)
    b1gT = []
    b1aT = []
    b2T = []
    for il in range(2):
        b1a, b1g = b1[il][:HID], b1[il][HID:]
        pg = np.zeros(HIDP, np.float32)
        pg[:HID] = b1g
        pa = np.zeros(HIDP, np.float32)
        pa[:HID] = b1a
        b1gT.append(_col128(pg))
        b1aT.append(_col128(pa))
        b2T.append(_col128(b2[il]))
    d["b1gT"] = np.concatenate(b1gT, 1)
    d["b1aT"] = np.concatenate(b1aT, 1)
    d["b2T"] = np.concatenate(b2T, 1)
    return d


def build_l1(stage="full"):
    import ml_dtypes
    nc = bass.Bass("TRN2", target_bir_lowering=False)
    consts = make_consts()
    make_l1_consts(consts)
    make_l1f_consts(consts)
    consts["l1_masks"] = consts["l1_masks"].astype(ml_dtypes.bfloat16)
    consts["l1f_mask"] = consts["l1f_mask"].astype(ml_dtypes.bfloat16)
    blk = np.zeros((128, 2), np.float32)
    blk[0:64, 0] = 1.0
    blk[64:128, 1] = 1.0
    consts["blk64T_b"] = blk.astype(ml_dtypes.bfloat16)
    consts["blk64_f"] = np.ascontiguousarray(blk.T)
    caps = declare_consts(nc, consts)
    io = declare_io_l1(nc, stage)
    do_attn = stage in ("attn", "full")
    do_ffn = stage in ("ffn", "full")
    y0_dram = nc.dram_tensor("y0scr", [NFC, 128, T], F32).ap()
    p = P(nc)
    small = []
    if do_attn:
        for nm in ("bqT", "bkT", "bvT", "boT", "hwT"):
            io[nm + "_sb"] = p.sbuf(nm + "S", list(io[nm].shape), F32)
            small.append((nm, io[nm + "_sb"][:, :]))
    if do_ffn:
        for nm in ("qsT", "ksT", "lngT", "lnbT", "b1gT", "b1aT", "b2T"):
            t_ = p.sbuf(nm + "S", list(io[nm].shape), F32)
            cols = io[nm].shape[1] // 2
            io[nm + "_sb"] = [t_[:, il * cols:(il + 1) * cols] for il in range(2)]
            small.append((nm, t_[:, :]))
        io["wqkv"] = [io["wqkv0"], io["wqkv1"]]
        io["wo"] = [io["wo0"], io["wo1"]]
        io["w1p"] = [io["w1p0"], io["w1p1"]]
        io["w2p"] = [io["w2p0"], io["w2p1"]]
        io["caps"] = caps
    cst_pre = {}
    cst_pre["blk64T_b"] = p.sbuf("blk64T", [128, 2], BF16)
    cst_pre["blk64_f"] = p.sbuf("blk64f", [2, 128], F32)
    cst_pre["l1_rot_sb"] = p.sbuf("l1rot", [128, 128], BF16)
    small += [("blk64T_b", cst_pre["blk64T_b"][:, :]), ("blk64_f", cst_pre["blk64_f"][:, :]),
              ("l1_rot", cst_pre["l1_rot_sb"][:, :])]

    scSX = p.scope()
    cst, x_fm, t_tok = emit_common_prelude(p, caps, io, xctx=scSX)
    cst.update(cst_pre)
    sm_tok = load_const_tiles(p, dict(list(caps.items()) + list(io.items())), small)
    scX = p.scope()
    xn_fm = p.sbuf("xn_fm", [128, NFC, T], BF16, ctx=scX)
    scMR = p.scope()
    mean_sb = p.sbuf("ln_mean", [128, T], F32, ctx=scMR)
    rstd_sb = p.sbuf("ln_rstd", [128, T], F32, ctx=scMR)
    n_tok = emit_outer_ln(p, [t_tok, sm_tok], x_fm, xn_fm, cst, mean_sb, rstd_sb)
    y0_tok = emit_y0_to_dram(p, [n_tok], x_fm, mean_sb, rstd_sb, y0_dram)
    p.close_scope(scMR, y0_tok)
    x_tok = n_tok
    if do_attn:
        scATT = p.scope()
        cst["l1_cos"] = p.sbuf("l1cos", [128, T], BF16, ctx=scATT)
        cst["l1_sin"] = p.sbuf("l1sin", [128, T], BF16, ctx=scATT)
        cst["l1_masks_sb"] = p.sbuf("l1mask", [128, 8, 128], BF16, ctx=scATT)
        at_tok = load_const_tiles(p, caps, [("l1_cos", cst["l1_cos"][:, :]),
                                            ("l1_sin", cst["l1_sin"][:, :]),
                                            ("l1_masks", cst["l1_masks_sb"][:, :, :])])
        x_tok = emit_l1_attn(p, [n_tok, sm_tok, at_tok], xn_fm, x_fm, io, cst, x_tok)
        p.close_scope(scATT, x_tok)
    p.close_scope(scX, merge_deps([x_tok, y0_tok]))
    st_tok = emit_store_y(p, [x_tok], x_fm, io["y"], cst)
    p.close_scope(scSX, st_tok)
    if do_ffn:
        scF = p.scope()
        cst2 = dict(cst_pre)
        for nm in ("identb", "identb_hi", "ones1x128_f", "ones128x1_b", "ident", "regs"):
            cst2[nm] = cst[nm]
        cst2["l1f_mask_sb"] = p.sbuf("l1fmask", [128, 4, 1024], BF16, ctx=scF)
        f_tok = load_const_tiles(p, caps, [("l1f_mask", cst2["l1f_mask_sb"][:, :, :])])
        emit_l1_ffn(p, [y0_tok, sm_tok, f_tok, st_tok], io["y"], y0_dram, io, cst2, cst)
        p.close_scope(scF, None)
    p.barrier_close()
    return nc, consts



def make_l2_consts(c):
    import ml_dtypes
    oct_ = np.zeros((128, 16, 128), np.float32)
    for g in range(16):
        for pp in range(128):
            oct_[8 * g + pp // 16, g, pp] = 1.0
    c["l2_octmap"] = oct_.astype(ml_dtypes.bfloat16)
    ym = np.zeros((128, 16, 128), np.float32)
    for g in range(16):
        for pp in range(128):
            ym[pp, g, 8 * g + pp // 16] = 1.0
    c["l2_ymap"] = ym.astype(ml_dtypes.bfloat16)
    nm = np.zeros((128, 128), np.float32)
    for pp in range(128):
        nm[pp % 16, pp] = 1.0
        nm[32 + pp % 16, pp] = 1.0
    c["l2_nmap"] = nm.astype(ml_dtypes.bfloat16)
    return c


def emit_l2_attn(p, deps, xn_fm, io, cst, y_out_dram):
    DI = 2048
    DIB = 16
    caps = io["caps"]
    res_dram = io["l2_res_dram"]
    yd_dram = io["l2_y_dram"]
    sc1 = p.scope()
    res_toks = emit_gemm_to_dram(p, "l2res", deps, io["W_in"][:, DI:], D, DI,
                                 lambda kb: xn_fm[:, kb, :], res_dram,
                                 act=AF.Silu)
    xc = p.sbuf("l2_xc", [128, DIB, T], BF16, ctx=sc1)
    scx = p.scope()
    x_in = p.sbuf("l2_xin", [128, DIB, T], BF16, ctx=scx)
    xin_toks = emit_gemm_fm(p, "l2xin", deps, io["W_in"][:, 0:DI], D, DI,
                            lambda kb: xn_fm[:, kb, :],
                            lambda mb, c0, w: x_in[:, mb, c0:c0 + w])
    acc = p.sbuf("l2_cacc", [128, T], F32, ctx=scx)
    conv_toks = []
    prev_c = None
    for ch in range(DIB):
        def cbody(eng, sig, ch=ch):
            eng.tensor_scalar(acc[:, :], x_in[:, ch, :], io["cwT_sb"][:, 3 * 16 + ch:3 * 16 + ch + 1],
                              io["cbT_sb"][:, ch:ch + 1], OP.mult, OP.add)
            for k in range(3):
                sh = 3 - k
                eng.drain()
                ins = eng.scalar_tensor_tensor(acc[:, sh:T], x_in[:, ch, 0:T - sh],
                                               io["cwT_sb"][:, k * 16 + ch:k * 16 + ch + 1],
                                               acc[:, sh:T], OP.mult, OP.add)
            sig(ins)

        c_tok = p.sec("vector", [xin_toks, prev_c], cbody)

        def sbody(eng, sig, ch=ch):
            sig(eng.activation(out=xc[:, ch, :], in_=acc[:, :], func=AF.Silu))

        prev_c = p.sec("scalar", [c_tok], sbody)
        conv_toks.append(prev_c)
    conv_tok = merge_deps(conv_toks)
    p.close_scope(scx, conv_tok)

    sc2 = p.scope()
    dbc = p.sbuf("l2_dbc", [128, T], F32, ctx=sc2)
    ps_dbc = [p.psum("l2_dbc_ps", [128, 512], F32, ctx=sc2) for _ in range(2)]
    wx_sb = p.sbuf("l2_wx", [128, DIB, 128], BF16, ctx=sc2)

    def wxload(eng, sig):
        src = io["Wx_pad"].rearrange("(kb p) m -> p kb m", p=128)
        sig(eng.dma_start(out=wx_sb[:, :, :], in_=src), dma=True)

    wx_tok = p.sec("gpsimd", [deps], wxload, dma="sw")
    dbc_toks = []
    ev_prev = None
    for cs in range(4):
        def dmm(eng, sig, cs=cs):
            for kb in range(DIB):
                ins = eng.matmul(ps_dbc[cs % 2][:, :], wx_sb[:, kb, :],
                                 xc[:, kb, cs * 512:(cs + 1) * 512],
                                 start=(kb == 0), stop=(kb == DIB - 1))
            sig(ins)

        m_tok = p.sec("tensor", [conv_tok, wx_tok, ev_prev], dmm)

        def dev(eng, sig, cs=cs):
            sig(eng.tensor_copy(dbc[:, cs * 512:(cs + 1) * 512], ps_dbc[cs % 2][:, :]))

        ev_prev = p.sec("vector", [m_tok], dev)
        dbc_toks.append(ev_prev)
    dbc_tok = merge_deps(dbc_toks)
    dbc_b = p.sbuf("l2_dbcb", [128, T], BF16, ctx=sc2)

    def dbccast(eng, sig):
        sig(eng.activation(out=dbc_b[:, :], in_=dbc[:, :], func=AF.Identity))

    dbc_tok = p.sec("scalar", [dbc_tok], dbccast)

    A_sb = p.sbuf("l2_A", [128, 256], F32, ctx=sc2)
    Bb = p.sbuf("l2_Bb", [128, T], BF16, ctx=sc2)
    Cb = p.sbuf("l2_Cb", [128, T], BF16, ctx=sc2)

    def abody(eng, sig):
        sig(eng.activation(out=A_sb[:, :], in_=io["AlogT_sb"][:, :], func=AF.Exp, scale=1.0))

    a1 = p.sec("scalar", [deps], abody)

    def a2body(eng, sig):
        sig(eng.tensor_scalar(A_sb[:, :], A_sb[:, :], -1.0, None, OP.mult))

    a_tok = p.sec("vector", [a1], a2body)

    bc_toks = []
    ev2 = None
    for which, dst, base in (("B", Bb, 0), ("C", Cb, 32)):
        for cs in range(4):
            def bmm(eng, sig, base=base, cs=cs):
                sig(eng.matmul(ps_dbc[cs % 2][:, :], cst["l2_nmap_sb"][base:base + 16, :],
                               dbc_b[base:base + 16, cs * 512:(cs + 1) * 512],
                               start=True, stop=True))

            m_tok = p.sec("tensor", [dbc_tok, ev2], bmm)

            def bev(eng, sig, dst=dst, cs=cs):
                sig(eng.tensor_copy(dst[:, cs * 512:(cs + 1) * 512], ps_dbc[cs % 2][:, :]))

            ev2 = p.sec("vector", [m_tok], bev)
            bc_toks.append(ev2)
    bc_tok = merge_deps(bc_toks)

    delta = p.sbuf("l2_delta", [128, T], F32, ctx=sc2)
    delta_b = p.sbuf("l2_deltab", [128, T], BF16, ctx=sc2)
    du = p.sbuf("l2_du", [128, T], BF16, ctx=sc2)
    sp1 = p.sbuf("l2_sp1", [128, T], F32, ctx=sc2)
    dA = p.sbuf("l2_dA", [128, T], BF16, ctx=sc2)
    dBu = p.sbuf("l2_dBu", [128, T], BF16, ctx=sc2)
    h_sb = p.sbuf("l2_h", [128, T], BF16, ctx=sc2)
    ysb = p.sbuf("l2_ysb", [128, T], BF16, ctx=sc2)
    ps_g = [p.psum("l2_gps", [128, 512], F32, ctx=sc2) for _ in range(2)]
    ps_y = [p.psum("l2_yps", [128, 512], F32, ctx=sc2) for _ in range(4)]
    octmap = cst["l2_octmap_sb"]
    ymap = cst["l2_ymap_sb"]
    prev_round = None
    scan_toks = []
    for dc in range(DIB):
        dl_toks = []
        ev3 = None
        for cs in range(4):
            def dtmm(eng, sig, dc=dc, cs=cs):
                sig(eng.matmul(ps_dbc[cs % 2][:, :],
                               io["wdt_sb"][64:128, dc * 128:(dc + 1) * 128],
                               dbc_b[64:128, cs * 512:(cs + 1) * 512], start=True, stop=True))

            m_tok = p.sec("tensor", [dbc_tok, bc_tok, a_tok, io["wdt_tok"], ev3, prev_round], dtmm)

            def spl(eng, sig, dc=dc, cs=cs):
                sl = slice(cs * 512, (cs + 1) * 512)
                eng.activation(out=sp1[:, sl], in_=ps_dbc[cs % 2][:, :], func=AF.Abs,
                               bias=io["bdtT_sb"][:, dc:dc + 1])
                eng.drain()
                eng.activation(out=delta[:, sl], in_=ps_dbc[cs % 2][:, :], func=AF.Relu,
                               bias=io["bdtT_sb"][:, dc:dc + 1])
                sig(eng.activation(out=sp1[:, sl], in_=sp1[:, sl], func=AF.Exp, scale=-1.0))

            ev3 = p.sec("scalar", [m_tok], spl)
            dl_toks.append(ev3)

        def spl2(eng, sig, dc=dc):
            sig(eng.activation(out=sp1[:, :], in_=sp1[:, :], func=AF.Ln, bias=1.0))

        l_tok = p.sec("scalar", [dl_toks], spl2)

        def dsum(eng, sig, dc=dc):
            eng.tensor_tensor(delta[:, :], delta[:, :], sp1[:, :], OP.add)
            eng.drain()
            eng.tensor_copy(delta_b[:, :], delta[:, :])
            eng.drain()
            sig(eng.tensor_tensor(du[:, :], delta[:, :], xc[:, dc, :], OP.mult))

        d_tok = p.sec("vector", [l_tok], dsum)

        y_done = []
        for g in range(16):
            acol = A_sb[:, dc * 16 + g:dc * 16 + g + 1]
            da_toks = []
            for cs in range(4):
                def gmm(eng, sig, g=g, cs=cs):
                    sig(eng.matmul(ps_g[0][:, :], octmap[:, g, :],
                                   delta_b[:, cs * 512:(cs + 1) * 512], start=True, stop=True))

                m1 = p.sec("tensor", [d_tok, da_toks[-1] if da_toks else None,
                                      scan_toks[-1] if scan_toks else None], gmm)

                def dexp(eng, sig, cs=cs, acol=acol):
                    sig(eng.activation(out=dA[:, cs * 512:(cs + 1) * 512], in_=ps_g[0][:, :],
                                       func=AF.Exp, scale=acol))

                e1 = p.sec("scalar", [m1], dexp)

                def gmm2(eng, sig, g=g, cs=cs):
                    sig(eng.matmul(ps_g[1][:, :], octmap[:, g, :],
                                   du[:, cs * 512:(cs + 1) * 512], start=True, stop=True))

                m2 = p.sec("tensor", [e1], gmm2)

                def dbu(eng, sig, cs=cs):
                    sig(eng.tensor_tensor(dBu[:, cs * 512:(cs + 1) * 512], ps_g[1][:, :],
                                          Bb[:, cs * 512:(cs + 1) * 512], OP.mult))

                da_toks.append(p.sec("vector", [m2, bc_tok, a_tok], dbu))

            def scanb(eng, sig):
                eng.tensor_tensor_scan(h_sb[:, :], dA[:, :], dBu[:, :], 0.0, OP.mult, OP.add)
                eng.drain()
                sig(eng.tensor_tensor(h_sb[:, :], h_sb[:, :], Cb[:, :], OP.mult))

            s_tok = p.sec("vector", [da_toks], scanb)
            scan_toks.append(s_tok)

            def ymm(eng, sig, g=g):
                for cs in range(4):
                    ins = eng.matmul(ps_y[cs][:, :], ymap[:, g, :],
                                     h_sb[:, cs * 512:(cs + 1) * 512],
                                     start=(g == 0), stop=(g == 15))
                sig(ins)

            y_done.append(p.sec("tensor", [s_tok, prev_round if g == 0 else None], ymm))

        def yev(eng, sig, dc=dc):
            for cs in range(4):
                sl = slice(cs * 512, (cs + 1) * 512)
                ins = eng.scalar_tensor_tensor(ysb[:, sl], xc[:, dc, sl],
                                               io["DT_sb"][:, dc:dc + 1], ps_y[cs][:, :],
                                               OP.mult, OP.add)
            sig(ins)

        ye_tok = p.sec("vector", [y_done], yev)

        def ydma(eng, sig, dc=dc):
            sig(eng.dma_start(out=yd_dram[dc, :, :], in_=ysb[:, :]), dma=True)

        prev_round = p.sec("sync", [ye_tok], ydma, dma=True)
        scan_toks.append(prev_round)
    scan_done = merge_deps([prev_round])
    p.close_scope(sc2, scan_done)
    p.close_scope(sc1, merge_deps([scan_done, res_toks]))

    sc3 = p.scope()
    yr = p.sbuf("l2_yr", [128, DIB, T], BF16, ctx=sc3)
    ld = p.sbuf("l2_ld", [128, 2, T], BF16, ctx=sc3)
    yr_toks = []
    prev_l = None
    for kb in range(DIB):
        def lbody(eng, sig, kb=kb):
            sig(eng.dma_start(out=ld[:, 0, :], in_=yd_dram[kb, :, :]), dma=True)
            sig(eng.dma_start(out=ld[:, 1, :], in_=res_dram[kb, :, :]), dma=True)

        l_tok = p.sec("sync", [scan_done, res_toks, prev_l], lbody, dma=True)

        def mbody(eng, sig, kb=kb):
            sig(eng.tensor_tensor(yr[:, kb, :], ld[:, 0, :], ld[:, 1, :], OP.mult))

        prev_l = p.sec("vector", [l_tok], mbody)
        yr_toks.append(prev_l)

    out_toks = emit_gemm_to_dram(p, "l2out", yr_toks, io["W_out"], DI, D,
                                 lambda kb: yr[:, kb, :], io["l2_attnout_dram"])
    p.close_scope(sc3, out_toks)
    return out_toks


def emit_gemm_to_dram(p, name, deps, w_src, K, Mtot, rhs_tiles, out_dram, *, act=None,
                      bias_sb=None, nch=512, panel=512):
    sc = p.scope()
    stage = p.sbuf(f"{name}_st", [128, 2, nch], BF16, ctx=sc)
    dma_tok = [None, None]

    def out_fn(mb, c0, w, b=[0]):
        return None

    toks = []
    kb_n = (K + 127) // 128
    wbufs = [p.sbuf(f"{name}_w", [128, kb_n, panel], BF16, ctx=sc) for _ in range(2)]
    pss = [p.psum(f"{name}_ps", [128, nch], F32, ctx=sc) for _ in range(2)]
    npanels = (Mtot + panel - 1) // panel
    load_toks = [None] * npanels
    comp_done = [[] for _ in range(npanels)]
    load_toks[0] = load_weight_panel(p, w_src, K, 0, min(Mtot, panel), wbufs[0], [deps])
    ps_free = [None, None]
    psi = 0
    last_pa = -1
    for mb in range((Mtot + 127) // 128):
        pa = (mb * 128) // panel
        if pa != last_pa:
            last_pa = pa
            if pa + 1 < npanels:
                m0, m1 = (pa + 1) * panel, min(Mtot, (pa + 2) * panel)
                dep = [deps] if pa == 0 else [deps, comp_done[pa - 1]]
                load_toks[pa + 1] = load_weight_panel(p, w_src, K, m0, m1,
                                                      wbufs[(pa + 1) % 2], dep)
        for c0 in range(0, T, nch):
            w = min(nch, T - c0)
            b = psi % 2
            sb = psi % 2
            psi += 1
            ps = pss[b]

            def tbody(eng, sig, mb=mb, c0=c0, w=w, ps=ps, pa=pa):
                for k in range(kb_n):
                    ins = eng.matmul(ps[:, 0:w],
                                     wbufs[pa % 2][:, k, mb * 128 - pa * panel:mb * 128 - pa * panel + 128],
                                     rhs_tiles(k)[:, c0:c0 + w],
                                     start=(k == 0), stop=(k == kb_n - 1))
                sig(ins)

            t_tok = p.sec("tensor", [load_toks[pa], ps_free[b]], tbody)
            comp_done[pa].append(t_tok)

            def ebody(eng, sig, mb=mb, c0=c0, w=w, ps=ps, sb=sb):
                if act is not None:
                    kw = {}
                    if bias_sb is not None:
                        kw["bias"] = bias_sb[:, mb:mb + 1]
                    ins = eng.activation(out=stage[:, sb, 0:w], in_=ps[:, 0:w], func=act, **kw)
                elif bias_sb is not None:
                    ins = eng.tensor_scalar(stage[:, sb, 0:w], ps[:, 0:w],
                                            bias_sb[:, mb:mb + 1], None, OP.add)
                else:
                    ins = eng.tensor_copy(stage[:, sb, 0:w], ps[:, 0:w])
                sig(ins)

            e_tok = p.sec("scalar" if act is not None else "vector",
                          [t_tok, dma_tok[sb]], ebody)
            ps_free[b] = e_tok

            def dbody(eng, sig, mb=mb, c0=c0, w=w, sb=sb):
                sig(eng.dma_start(out=out_dram[mb, :, c0:c0 + w], in_=stage[:, sb, 0:w]),
                    dma=True)

            dma_tok[sb] = p.sec("sync", [e_tok], dbody, dma=True)
            toks.append(dma_tok[sb])
    last = merge_deps(toks)
    p.close_scope(sc, last)
    return last



def emit_gemm_fm_rowmajor(p, name, deps, w_dram, Mrows, K, rhs_tiles, out_fn, cst,
                          act=None, Tlen=T, nch=512, bias_sb=None):
    sc = p.scope()
    kb_n = K // 128
    wrow = [p.sbuf(f"{name}_wr", [128, K], BF16, ctx=sc) for _ in range(2)]
    wT = [p.sbuf(f"{name}_wt", [128, kb_n, 128], BF16, ctx=sc) for _ in range(2)]
    pss = [p.psum(f"{name}_ps", [128, nch], F32, ctx=sc) for _ in range(2)]
    prev = [None, None]
    toks = []
    psi = 0
    ps_free = [None, None]
    for mb in range(Mrows // 128):
        bi = mb % 2

        def lbody(eng, sig, mb=mb, bi=bi):
            sig(eng.dma_start(out=wrow[bi][:, :], in_=w_dram[mb * 128:(mb + 1) * 128, 0:K]),
                dma=True)

        l_tok = p.sec("gpsimd", [deps, prev[bi]], lbody, dma="sw")
        pairs = [(wrow[bi][:, kb * 128:(kb + 1) * 128], wT[bi][:, kb, :])
                 for kb in range(kb_n)]
        t_tok = emit_transpose_grid(p, [l_tok], pairs, cst["identb"], BF16, tag=f"{name}_t")
        mm_toks = []
        for c0 in range(0, Tlen, nch):
            w = min(nch, Tlen - c0)
            b = psi % 2
            psi += 1
            ps = pss[b]

            def tbody(eng, sig, bi=bi, c0=c0, w=w, ps=ps):
                for k in range(kb_n):
                    ins = eng.matmul(ps[:, 0:w], wT[bi][:, k, :],
                                     rhs_tiles(k)[:, c0:c0 + w],
                                     start=(k == 0), stop=(k == kb_n - 1))
                sig(ins)

            m_tok = p.sec("tensor", [t_tok, ps_free[b]], tbody)

            def ebody(eng, sig, mb=mb, c0=c0, w=w, ps=ps):
                if act is not None:
                    kw = {}
                    if bias_sb is not None:
                        kw["bias"] = bias_sb[:, mb:mb + 1]
                    ins = eng.activation(out=out_fn(mb, c0, w), in_=ps[:, 0:w], func=act, **kw)
                elif bias_sb is not None:
                    ins = eng.tensor_scalar(out_fn(mb, c0, w), ps[:, 0:w],
                                            bias_sb[:, mb:mb + 1], None, OP.add)
                else:
                    ins = eng.tensor_copy(out_fn(mb, c0, w), ps[:, 0:w])
                sig(ins)

            e_tok = p.sec("scalar" if act is not None else "vector", [m_tok], ebody)
            ps_free[b] = e_tok
            mm_toks.append(e_tok)
        prev[bi] = merge_deps(mm_toks)
        toks += mm_toks
    last = merge_deps(toks)
    p.close_scope(sc, last)
    return last


def emit_l2_ffn(p, deps, xn_fm, io, cst, single=False):
    caps = io["caps"]
    sc_g = p.scope()
    ghd = p.sbuf("l2f_ghd", [128, 8, T], BF16, ctx=sc_g)
    gd_tok = emit_gemm_fm_rowmajor(p, "l2fxd", deps, io["expert_down"], 1024, D,
                                   lambda kb: xn_fm[:, kb, :],
                                   lambda mb, c0, w: ghd[:, mb, c0:c0 + w],
                                   cst, act=AF.Gelu)

    sc_q = p.scope()
    q_fm = p.sbuf("l2f_q", [128, 16, T], BF16, ctx=sc_q)
    q_toks = emit_gemm_fm(p, "l2fq", deps, io["Wq"], D, 2048,
                          lambda kb: xn_fm[:, kb, :],
                          lambda mb, c0, w: q_fm[:, mb, c0:c0 + w],
                          bias_sb=io["bqT_sb"])
    stat = p.sbuf("l2f_stat", [128, 16], F32, ctx=sc_q)
    scr1 = p.sbuf("l2f_scr", [128, 2048], BF16, ctx=sc_q)
    st_toks = []
    for ti in range(16):
        def stb(eng, sig, ti=ti):
            eng.activation(out=scr1[:, :], in_=q_fm[:, ti, :], func=AF.Identity,
                           accum_out=stat[:, ti % 2:ti % 2 + 1] if False else stat[:, 4 + ti % 4: 5 + ti % 4])
            sig(eng.activation(out=scr1[:, :], in_=q_fm[:, ti, :], func=AF.Square,
                               accum_out=stat[:, 8 + ti % 4:9 + ti % 4]))
        pass
    def zb(eng, sig):
        sig(eng.memset(stat[:, 0:4], 0.0))

    z_tok = p.sec("vector", [q_toks], zb)
    prev_s = z_tok
    for ti in range(16):
        def stb(eng, sig, ti=ti):
            eng.activation(out=scr1[:, :], in_=q_fm[:, ti, :], func=AF.Identity,
                           accum_out=stat[:, 4:5])
            eng.drain()
            sig(eng.activation(out=scr1[:, :], in_=q_fm[:, ti, :], func=AF.Square,
                               accum_out=stat[:, 5:6]))

        a_tok = p.sec("scalar", [prev_s], stb)

        def acc(eng, sig, ti=ti):
            par = ti % 2
            eng.tensor_tensor(stat[:, par:par + 1], stat[:, par:par + 1], stat[:, 4:5], OP.add)
            eng.drain()
            sig(eng.tensor_tensor(stat[:, 2 + par:3 + par], stat[:, 2 + par:3 + par],
                                  stat[:, 5:6], OP.add))

        prev_s = p.sec("vector", [a_tok], acc)

    def tocc(eng, sig):
        sig(eng.dma_start(out=io["ccin"].ap(), in_=stat[:, 0:4]), dma=True)

    cc1 = p.sec("sync", [prev_s], tocc, dma=True)
    if single:
        def ccbody(eng, sig):
            sig(eng.dma_start(out=io["ccout"].ap(), in_=io["ccin"].ap()), dma=True)

        cc2 = p.sec("gpsimd", [cc1], ccbody, dma="sw")
    else:
        def ccbody(eng, sig):
            sig(eng.collective_compute("AllReduce", OP.add,
                                       replica_groups=[list(range(NCORES))],
                                       ins=[io["ccin"].ap().opt()],
                                       outs=[io["ccout"].ap().opt()]))

        cc2 = p.sec("gpsimd", [cc1], ccbody)

    def fromcc(eng, sig):
        sig(eng.dma_start(out=stat[:, 0:4], in_=io["ccout"].ap()), dma=True)

    cc3 = p.sec("sync", [cc2], fromcc, dma=True)

    NTOK = float(NCORES * T * 8)

    def affb(eng, sig):
        eng.tensor_scalar(stat[:, 6:8], stat[:, 0:2], 1.0 / NTOK, None, OP.mult)
        eng.tensor_scalar(stat[:, 8:10], stat[:, 2:4], 1.0 / NTOK, None, OP.mult)
        eng.drain()
        eng.tensor_tensor(stat[:, 10:12], stat[:, 6:8], stat[:, 6:8], OP.mult)
        eng.drain()
        sig(eng.scalar_tensor_tensor(stat[:, 10:12], stat[:, 8:10], 1e-5, stat[:, 10:12],
                                     OP.add, OP.subtract))

    af1 = p.sec("vector", [cc3], affb)

    def af2(eng, sig):
        eng.activation(out=stat[:, 12:14], in_=stat[:, 10:12], func=AF.Ln)
        eng.drain()
        sig(eng.activation(out=stat[:, 12:14], in_=stat[:, 12:14], func=AF.Exp, scale=-0.5))

    af2t = p.sec("scalar", [af1], af2)

    def af3(eng, sig):
        eng.tensor_tensor(stat[:, 12:14], stat[:, 12:14], io["bngT_sb"][:, :], OP.mult)
        eng.drain()
        eng.tensor_tensor(stat[:, 14:16], stat[:, 6:8], stat[:, 12:14], OP.mult)
        eng.drain()
        sig(eng.scalar_tensor_tensor(stat[:, 14:16], stat[:, 14:16], -1.0,
                                     io["bnbT_sb"][:, :], OP.mult, OP.add))

    af3t = p.sec("vector", [af2t], af3)

    bn_toks = []
    for ti in range(16):
        def bnb(eng, sig, ti=ti):
            par = ti % 2
            sig(eng.tensor_scalar(q_fm[:, ti, :], q_fm[:, ti, :], stat[:, 12 + par:13 + par],
                                  stat[:, 14 + par:15 + par], OP.mult, OP.add))

        bn_toks.append(p.sec("vector", [af3t], bnb))

    sc_n = p.scope()
    ps_ss = p.psum("l2f_ssps", [1, 512], F32, ctx=sc_n)
    ps_bc = p.psum("l2f_bcps", [128, 512], F32, ctx=sc_n)
    ssrow = p.sbuf("l2f_ssrow", [1, 2, T], F32, ctx=sc_n)
    nrm_toks = []
    prev_n = None
    for ti in range(16):
        def sqb(eng, sig, ti=ti):
            sig(eng.activation(out=scr1[:, :], in_=q_fm[:, ti, :], func=AF.Square))

        sq_tok = p.sec("scalar", [bn_toks, prev_n], sqb)
        cs_toks = []
        ev = None
        for cs in range(4):
            def smb(eng, sig, cs=cs):
                sig(eng.matmul(ps_ss[:, :], cst["ones128x1_b"][:, 0:1],
                               scr1[:, cs * 512:(cs + 1) * 512], start=True, stop=True))

            m_tok = p.sec("tensor", [sq_tok, ev], smb)

            def sev(eng, sig, cs=cs):
                sig(eng.tensor_scalar(ssrow[:, 0, cs * 512:(cs + 1) * 512], ps_ss[:, :],
                                      1e-24, None, OP.max))

            ev = p.sec("vector", [m_tok], sev)
            cs_toks.append(ev)

        def lnx(eng, sig):
            eng.activation(out=ssrow[:, 1, :], in_=ssrow[:, 0, :], func=AF.Ln)
            eng.drain()
            sig(eng.activation(out=ssrow[:, 0, :], in_=ssrow[:, 1, :], func=AF.Exp,
                               scale=-0.5))

        ln_tok = p.sec("scalar", [cs_toks], lnx)
        sub_toks = []
        for cs in range(4):
            def bcb(eng, sig, cs=cs):
                sig(eng.matmul(ps_bc[:, :], cst["ones1x128_f"][0:1, :],
                               ssrow[0:1, 0, cs * 512:(cs + 1) * 512], start=True, stop=True))

            b_tok = p.sec("tensor", [ln_tok, sub_toks[-1] if sub_toks else None], bcb)

            def nmul(eng, sig, ti=ti, cs=cs):
                sig(eng.tensor_tensor(q_fm[:, ti, cs * 512:(cs + 1) * 512],
                                      q_fm[:, ti, cs * 512:(cs + 1) * 512],
                                      ps_bc[:, :], OP.mult))

            sub_toks.append(p.sec("vector", [b_tok], nmul))
        prev_n = merge_deps(sub_toks)
        nrm_toks.append(prev_n)
    nrm_tok = merge_deps(nrm_toks)
    p.close_scope(sc_n, nrm_tok)

    sc_k = p.scope()
    krow = p.sbuf("l2f_krow", [32, 2, 128], F32, ctx=sc_k)
    kcol = p.sbuf("l2f_kcol", [32, 4], F32, ctx=sc_k)
    knT = p.sbuf("l2f_knT", [128, 2, 32], BF16, ctx=sc_k)
    ps_kT = p.psum("l2f_kTps", [128, 64], BF16, ctx=sc_k)

    def kload(eng, sig):
        sig(eng.dma_start(out=krow[:, 0, :], in_=io["sub_key0"]), dma=True)
        sig(eng.dma_start(out=krow[:, 1, :], in_=io["sub_key1"]), dma=True)

    kl_tok = p.sec("sync", [deps], kload, dma=True)

    def knorm(eng, sig):
        for j in range(2):
            eng.tensor_reduce(kcol[:, j:j + 1], krow[:, j, :], AX.X, OP.max,
                              apply_absolute_value=True) if False else None
        eng.tensor_tensor(krow[:, 0, :], krow[:, 0, :], krow[:, 0, :], OP.mult) if False else None
        sig(eng.memset(kcol[:, 0:4], 0.0))

    kn1 = p.sec("vector", [kl_tok], knorm)
    ksq = p.sbuf("l2f_ksq", [32, 128], F32, ctx=sc_k)
    kn_toks = []
    for j in range(2):
        def ksqb(eng, sig, j=j):
            sig(eng.activation(out=ksq[:, :], in_=krow[:, j, :], func=AF.Square))

        k1 = p.sec("scalar", [kn1, kn_toks[-1] if kn_toks else None], ksqb)

        def kred(eng, sig, j=j):
            eng.tensor_reduce(kcol[:, j:j + 1], ksq[:, :], AX.X, OP.add)
            eng.drain()
            sig(eng.tensor_scalar(kcol[:, j:j + 1], kcol[:, j:j + 1], 1e-24, None, OP.max))

        k2 = p.sec("vector", [k1], kred)

        def kln(eng, sig, j=j):
            eng.activation(out=kcol[:, 2 + j:3 + j], in_=kcol[:, j:j + 1], func=AF.Ln)
            eng.drain()
            sig(eng.activation(out=kcol[:, j:j + 1], in_=kcol[:, 2 + j:3 + j], func=AF.Exp,
                               scale=-0.5))

        k3 = p.sec("scalar", [kln], kln) if False else p.sec("scalar", [k2], kln)

        def kmul(eng, sig, j=j):
            sig(eng.tensor_scalar(krow[:, j, :], krow[:, j, :], kcol[:, j:j + 1], None,
                                  OP.mult))

        kn_toks.append(p.sec("vector", [k3], kmul))

    kt_pairs = [(krow[0:32, j, :], knT[:, j, :]) for j in range(2)]
    krowb = p.sbuf("l2f_krowb", [32, 2, 128], BF16, ctx=sc_k)

    def kcast(eng, sig):
        eng.tensor_copy(krowb[:, 0, :], krow[:, 0, :])
        sig(eng.tensor_copy(krowb[:, 1, :], krow[:, 1, :]))

    kc_tok = p.sec("vector", [kn_toks], kcast)
    kt_tok = emit_transpose_grid(p, [kc_tok],
                                 [(krowb[0:32, j, :], knT[:, j, :]) for j in range(2)],
                                 cst["identb"], BF16, tag="l2fkT")

    wm_dram = io["l2f_wm_dram"]
    wm_t = p.sbuf("l2f_wmt", [128, 8, 1024], BF16, ctx=sc_k)
    ue = p.sbuf("l2f_ue", [128, 2, 32], BF16, ctx=sc_k)
    uv = p.sbuf("l2f_uv", [128, 2, 32], BF16, ctx=sc_k)
    mc = p.sbuf("l2f_mc", [128, 8], F32, ctx=sc_k)
    mc8 = p.sbuf("l2f_mc8", [128, 16], F32, ctx=sc_k)
    ps_sc = p.psum("l2f_scps", [128, 64], F32, ctx=sc_k)
    prev_w = None
    prev_dma = None
    for tt in range(16):
        h_toks = []
        for h in range(8):
            def smm(eng, sig, tt=tt, h=h):
                eng.matmul(ps_sc[:, 0:32], q_fm[:, 2 * h, tt * 128:(tt + 1) * 128],
                           knT[:, 0, :], start=True, stop=True)
                sig(eng.matmul(ps_sc[:, 32:64], q_fm[:, 2 * h + 1, tt * 128:(tt + 1) * 128],
                               knT[:, 1, :], start=True, stop=True))

            s_tok = p.sec("tensor", [nrm_tok, kt_tok, prev_w, prev_dma], smm)

            def vbm(eng, sig):
                eng.max(out=mc8[:, 0:8], in_=ps_sc[:, 0:32])
                eng.drain()
                eng.max(out=mc8[:, 8:16], in_=ps_sc[:, 32:64])
                eng.drain()
                eng.tensor_scalar(mc[:, 0:1], mc8[:, 0:1], -1.0, None, OP.mult)
                sig(eng.tensor_scalar(mc[:, 2:3], mc8[:, 8:9], -1.0, None, OP.mult))

            m_tok = p.sec("vector", [s_tok], vbm)

            def eb(eng, sig):
                eng.activation(out=ue[:, 0, :], in_=ps_sc[:, 0:32], func=AF.Exp,
                               bias=mc[:, 0:1])
                sig(eng.activation(out=ue[:, 1, :], in_=ps_sc[:, 32:64], func=AF.Exp,
                                   bias=mc[:, 2:3]))

            e_tok = p.sec("scalar", [m_tok], eb)

            def vb2(eng, sig, tt=tt, h=h):
                eng.scalar_tensor_tensor(uv[:, 0, :], ps_sc[:, 0:32], mc8[:, 3:4],
                                         ue[:, 0, :], OP.is_ge, OP.mult,
                                         accum_out=mc[:, 4:5])
                eng.drain()
                eng.scalar_tensor_tensor(uv[:, 1, :], ps_sc[:, 32:64], mc8[:, 11:12],
                                         ue[:, 1, :], OP.is_ge, OP.mult,
                                         accum_out=mc[:, 5:6])
                eng.drain()
                eng.tensor_tensor(mc[:, 6:7], mc[:, 4:5], mc[:, 5:6], OP.mult)
                eng.drain()
                eng.reciprocal(mc[:, 7:8], mc[:, 6:7])
                eng.drain()
                eng.tensor_scalar(uv[:, 0, :], uv[:, 0, :], mc[:, 7:8], None, OP.mult)
                eng.drain()
                sig(eng.tensor_tensor(
                    wm_t[:, h, :].rearrange("p (a b) -> p a b", b=32),
                    uv[:, 0, :, None].to_broadcast([128, 32, 32]),
                    uv[:, 1, None, :].to_broadcast([128, 32, 32]), OP.mult))

            prev_w = p.sec("vector", [e_tok], vb2)
            h_toks.append(prev_w)

        def wdma(eng, sig, tt=tt):
            sig(eng.dma_start(out=wm_dram[tt, :, :, :], in_=wm_t[:, :, :]), dma=True)

        prev_dma = p.sec("sync", [h_toks, prev_dma], wdma, dma=True)
    wm_tok = prev_dma
    p.close_scope(sc_k, wm_tok)
    p.close_scope(sc_q, wm_tok)

    sc_o = p.scope()
    out_sb = p.sbuf("l2f_out", [128, 8, T], F32, ctx=sc_o)
    rhs_h = p.sbuf("l2f_rhs", [128, 8, T], BF16, ctx=sc_o)
    wmT_ld = p.sbuf("l2f_wmT", [128, T], BF16, ctx=sc_o)

    def zo(eng, sig):
        for mb in range(8):
            ins = eng.memset(out_sb[:, mb, :], 0.0)
        sig(ins)

    out_tok = p.sec("vector", [deps], zo)
    for h in range(8):
        prep_toks = []
        prev_t = None
        for kb in range(8):
            def tld(eng, sig, h=h, kb=kb):
                src = wm_dram.rearrange("tt p h e -> (tt p) h e")[:, h, kb * 128:(kb + 1) * 128]
                sig(eng.dma_start(out=wmT_ld[:, :], in_=src, transpose=True), dma=True)

            t_tok = p.sec("sync", [wm_tok, gd_tok, prev_t], tld, dma=True)

            def tmul(eng, sig, kb=kb):
                sig(eng.tensor_tensor(rhs_h[:, kb, :], wmT_ld[:, :], ghd[:, kb, :], OP.mult))

            prev_t = p.sec("vector", [t_tok], tmul)
            prep_toks.append(prev_t)

        up_toks = emit_gemm_fm(p, f"l2fup{h}", prep_toks, io["expert_up"], 1024, 1024,
                               lambda kb: rhs_h[:, kb, :], None,
                               resid_fm=out_sb, resid_dep=out_tok)
        out_tok = merge_deps(up_toks)

    ob = p.sbuf("l2f_ob", [128, T], BF16, ctx=sc_o)
    st_toks = []
    prev_st = None
    for mb in range(8):
        def cb(eng, sig, mb=mb):
            sig(eng.tensor_copy(ob[:, :], out_sb[:, mb, :]))

        c_tok = p.sec("vector", [out_tok, prev_st], cb)

        def sb_(eng, sig, mb=mb):
            sig(eng.dma_start(out=io["l2f_out_dram"][mb, :, :], in_=ob[:, :]), dma=True)

        prev_st = p.sec("sync", [c_tok], sb_, dma=True)
        st_toks.append(prev_st)
    last = merge_deps(st_toks)
    p.close_scope(sc_o, last)
    p.close_scope(sc_g, last)
    return last


def emit_fm_dram_add(p, deps, src_dram, y_out, cst, nMB=NFC, tag="fma"):
    sc = p.scope()
    ld = p.sbuf(f"{tag}_ld", [128, nMB, T], BF16, ctx=sc)
    s_tm = p.sbuf(f"{tag}_tm", [128, 16, nMB * 128], BF16, ctx=sc)

    def lb(eng, sig):
        sig(eng.dma_start(out=ld[:, :, :], in_=src_dram.rearrange("mb p t -> p mb t")),
            dma=True)

    l_tok = p.sec("sync", [deps], lb, dma=True)
    pairs = []
    for mb in range(nMB):
        for tc in range(16):
            pairs.append((ld[:, mb, tc * 128:(tc + 1) * 128],
                          s_tm[:, tc, mb * 128:(mb + 1) * 128]))
    t_tok = emit_transpose_grid(p, [l_tok], pairs, cst["identb"], BF16, tag=tag + "t")

    def ab(eng, sig):
        dst = y_out.rearrange("(tc p) d -> p tc d", p=128)
        sig(eng.dma_start(out=dst, in_=s_tm[:, :, :], accum_op=OP.add), dma=True)

    tok = p.sec("gpsimd", [t_tok], ab, dma="sw")
    p.close_scope(sc, tok)
    return tok


def emit_copy_x_to_y(p, deps, x_dram, y_dram):
    def body(eng, sig):
        sig(eng.dma_start(out=y_dram, in_=x_dram), dma=True)
    return p.sec("sync", [deps], body, dma=True)


def declare_io_l2(nc, stage, pref="", xap=None, yap=None):
    io = {"x": xap if xap is not None else
          nc.dram_tensor("x", [T, D], F32, kind="ExternalInput").ap(),
          "y": yap if yap is not None else
          nc.dram_tensor("y", [T, D], F32, kind="ExternalOutput").ap()}
    names = []
    if stage in ("attn", "full"):
        names += [("W_in", [D, 4096]), ("Wx_pad", [2048, 128]), ("Wdt_pad", [128, 2048]),
                  ("cwT", [128, 64]), ("cbT", [128, 16]), ("bdtT", [128, 16]),
                  ("AlogT", [128, 256]), ("DT", [128, 16]), ("W_out", [2048, D])]
    if stage in ("ffn", "full"):
        names += [("Wq", [D, 2048]), ("bqT", [128, 16]), ("bngT", [128, 2]),
                  ("bnbT", [128, 2]), ("sub_key0", [32, 128]), ("sub_key1", [32, 128]),
                  ("expert_down", [1024, 1024]), ("expert_up", [1024, 1024])]
    for nm, shp in names:
        io[nm] = nc.dram_tensor(pref + nm, shp, F32, kind="ExternalInput").ap()
    return io


def host_prep_l2(inp):
    d = {}
    W_x = np.asarray(inp["L2_attn_W_x"])
    wxp = np.zeros((2048, 128), np.float32)
    wxp[:, 0:16] = W_x[:, 64:80]
    wxp[:, 32:48] = W_x[:, 80:96]
    wxp[:, 64:128] = W_x[:, 0:64]
    d["Wx_pad"] = wxp
    wdtp = np.zeros((128, 2048), np.float32)
    wdtp[64:128, :] = np.asarray(inp["L2_attn_W_dt"])
    d["Wdt_pad"] = wdtp
    cw = np.asarray(inp["L2_attn_conv_w"])
    cwT = np.zeros((128, 64), np.float32)
    for k in range(4):
        cwT[:, k * 16:(k + 1) * 16] = cw[:, k].reshape(16, 128).T
    d["cwT"] = cwT
    d["cbT"] = _col128(np.asarray(inp["L2_attn_conv_b"]))
    d["bdtT"] = _col128(np.asarray(inp["L2_attn_b_dt"]))
    Alog = np.asarray(inp["L2_attn_A_log"])
    AT = np.zeros((128, 256), np.float32)
    for dc in range(16):
        for g in range(16):
            for pp in range(128):
                AT[pp, dc * 16 + g] = Alog[dc * 128 + 8 * g + pp // 16, pp % 16]
    d["AlogT"] = AT
    d["DT"] = _col128(np.asarray(inp["L2_attn_D"]))
    d["W_in"] = np.asarray(inp["L2_attn_W_in"])
    d["W_out"] = np.asarray(inp["L2_attn_W_out"])
    d["Wq"] = np.asarray(inp["L2_ffn_Wq"])
    d["bqT"] = _col128(np.asarray(inp["L2_ffn_bq"]))
    d["bngT"] = np.ascontiguousarray(np.asarray(inp["L2_ffn_bn_gamma"]).reshape(2, 128).T)
    d["bnbT"] = np.ascontiguousarray(np.asarray(inp["L2_ffn_bn_beta"]).reshape(2, 128).T)
    d["sub_key0"] = np.asarray(inp["L2_ffn_sub_key0"])
    d["sub_key1"] = np.asarray(inp["L2_ffn_sub_key1"])
    d["expert_down"] = np.asarray(inp["L2_ffn_expert_down"])
    d["expert_up"] = np.asarray(inp["L2_ffn_expert_up"])
    return d


def build_l2(stage="full", single=False):
    import ml_dtypes
    nc = bass.Bass("TRN2", target_bir_lowering=False)
    consts = make_consts()
    make_l2_consts(consts)
    caps = declare_consts(nc, consts)
    io = declare_io_l2(nc, stage)
    io["caps"] = caps
    do_attn = stage in ("attn", "full")
    do_ffn = stage in ("ffn", "full")
    if do_attn:
        io["l2_res_dram"] = nc.dram_tensor("l2res", [16, 128, T], BF16).ap()
        io["l2_y_dram"] = nc.dram_tensor("l2y", [16, 128, T], BF16).ap()
        io["l2_attnout_dram"] = nc.dram_tensor("l2ao", [8, 128, T], BF16).ap()
    if do_ffn:
        io["ccin"] = nc.dram_tensor("ccin", [128, 4], F32)
        io["ccout"] = nc.dram_tensor("ccout", [128, 4], F32)
        io["l2f_wm_dram"] = nc.dram_tensor("l2wm", [16, 128, 8, 1024], BF16).ap()
        io["l2f_out_dram"] = nc.dram_tensor("l2fo", [8, 128, T], BF16).ap()
    p = P(nc)
    small = []
    if do_attn:
        for nm in ("cwT", "cbT", "bdtT", "AlogT", "DT"):
            io[nm + "_sb"] = p.sbuf(nm + "S", list(io[nm].shape), F32)
            small.append((nm, io[nm + "_sb"][:, :]))
        cstn = {}
        for nm in ("l2_octmap", "l2_ymap", "l2_nmap"):
            t_ = p.sbuf(nm + "S", list(consts[nm].shape), BF16)
            cstn[nm + "_sb"] = t_
            small.append((nm, t_[:, :, :] if len(consts[nm].shape) == 3 else t_[:, :]))
        io["wdt_sb"] = p.sbuf("wdtS", [128, 2048], BF16)
    if do_ffn:
        for nm in ("bqT", "bngT", "bnbT"):
            io[nm + "_sb"] = p.sbuf(nm + "S", list(io[nm].shape), F32)
            small.append((nm, io[nm + "_sb"][:, :]))
    sm_tok = load_const_tiles(p, dict(list(caps.items()) + list(io.items())), small)
    if do_attn:
        def wdtl(eng, sig):
            sig(eng.dma_start(out=io["wdt_sb"][:, :], in_=io["Wdt_pad"]), dma=True)

        io["wdt_tok"] = p.sec("gpsimd", [], wdtl, dma="sw")

    xn_fm = p.sbuf("xn_fm", [128, NFC, T], BF16)
    scSX = p.scope()
    cst, x_fm, t_tok = emit_common_prelude(p, caps, io, xctx=scSX)
    if do_attn:
        cst.update(cstn)
    n_tok = emit_outer_ln(p, [t_tok, sm_tok], x_fm, xn_fm, cst)
    cp_tok = emit_copy_x_to_y(p, [t_tok], io["x"], io["y"])
    p.close_scope(scSX, merge_deps([n_tok, cp_tok]))
    a_tok = None
    f_tok = None
    if do_attn:
        at = emit_l2_attn(p, [n_tok, sm_tok, io["wdt_tok"]], xn_fm, io, cst, io["y"])
        a_tok = emit_fm_dram_add(p, [at, cp_tok], io["l2_attnout_dram"], io["y"], cst,
                                 nMB=8, tag="l2aadd")
    if do_ffn:
        ft = emit_l2_ffn(p, [n_tok, sm_tok], xn_fm, io, cst, single=single)
        f_tok = emit_fm_dram_add(p, [ft, cp_tok, a_tok], io["l2f_out_dram"], io["y"], cst,
                                 nMB=8, tag="l2fadd")
    p.barrier_close()
    return nc, consts



def make_l3_consts(c):
    import ml_dtypes
    i_ = np.arange(128)[:, None]
    j_ = np.arange(128)[None, :]
    c["l3_triu_incl"] = np.where(i_ <= j_, 1.0, 0.0).astype(ml_dtypes.bfloat16)
    c["l3_triu_strict"] = np.where(i_ < j_, 1.0, 0.0).astype(ml_dtypes.bfloat16)
    em = np.zeros((128, 16), np.float32)
    for pp in range(128):
        em[pp, pp % 16] = 1.0
    c["l3_emap"] = em
    c["l3_emapT"] = np.ascontiguousarray(em.T)
    tri8 = np.zeros((64, 8), np.float32)
    for base in (0, 32):
        for pp in range(8):
            for i in range(8):
                if pp < i:
                    tri8[base + pp, i] = 1.0
    c["l3_tri8"] = tri8
    c["l3_ones8"] = np.ones((64, 1), np.float32)
    sl = np.zeros((128, 8), np.float32)
    for pp in range(128):
        for st in range(8):
            sl[pp, st] = st * 128 + pp + 1
    c["l3f_slotiota"] = sl
    gid = np.zeros((64, T), np.float32)
    for base in (0, 32):
        for r in range(8):
            gid[base + r, :] = r * T + np.arange(T)
    c["l3f_gid"] = gid
    return c


def emit_lin_attn(p, deps, Qp, Kp, vchunk, o_dram, out_sb, cst, strict, tag):
    sc = p.scope()
    S_sb = p.sbuf(f"{tag}_S", [128, 16, 64], F32, ctx=sc)
    S_bf = p.sbuf(f"{tag}_Sb", [128, 16, 64], BF16, ctx=sc)
    sT = p.sbuf(f"{tag}_sT", [128, 128], BF16, ctx=sc)
    Vt = [p.sbuf(f"{tag}_Vt", [128, 1024], BF16, ctx=sc) for _ in range(2)]
    KT = [p.sbuf(f"{tag}_KT", [128, 1024], BF16, ctx=sc) for _ in range(2)]
    Ob = [p.sbuf(f"{tag}_Ob", [128, 1024], BF16, ctx=sc) for _ in range(2)] \
        if out_sb is None else None
    ps_sc = p.psum(f"{tag}_scps", [128, 128], F32, ctx=sc)
    ps_o = [p.psum(f"{tag}_ops", [128, 64], F32, ctx=sc) for _ in range(2)]
    ps_su = p.psum(f"{tag}_sups", [128, 64], F32, ctx=sc)
    mask = cst["l3_triu_strict_sb"] if strict else cst["l3_triu_incl_sb"]

    def zb(eng, sig):
        eng.memset(S_sb[:, :, :], 0.0)
        eng.drain()
        sig(eng.memset(S_bf[:, :, :], 0.0))

    z_tok = p.sec("vector", [deps], zb)
    toks = []
    prev_ev = None
    prev_o = [None, None]
    prev_sacc = [None] * 16
    su_prev = None
    v_prev = [None, None]
    k_prev = [None, None]
    fl_prev = [None, None]
    for cc in range(16):
        t0 = cc * 128
        b = cc % 2
        v_tok = vchunk(p, [deps, v_prev[b]], cc, Vt[b])
        kt_pairs = [(Kp[:, fc, t0:t0 + 128], KT[b][:, fc * 128:(fc + 1) * 128])
                    for fc in range(8)]
        kt_tok = emit_transpose_grid(p, [deps, k_prev[b]], kt_pairs, cst["identb"],
                                     BF16, tag=f"{tag}kt")
        ch_toks = []
        for h in range(16):
            fch, oh = h // 2, (h % 2) * 64

            def smm(eng, sig, fch=fch, oh=oh, t0=t0):
                sig(eng.matmul(ps_sc[:, :], Kp[:, fch, t0:t0 + 128][oh:oh + 64, :],
                               Qp[:, fch, t0:t0 + 128][oh:oh + 64, :],
                               start=True, stop=True))

            s_tok = p.sec("tensor", [z_tok, prev_ev], smm)

            def sev(eng, sig):
                sig(eng.tensor_tensor(sT[:, :], ps_sc[:, :], mask[:, :], OP.mult))

            prev_ev = p.sec("vector", [s_tok], sev)

            def pv(eng, sig, h=h, fch=fch, oh=oh, t0=t0, b=b):
                eng.matmul(ps_o[b][:, :], sT[:, :], Vt[b][:, h * 64:(h + 1) * 64],
                           start=True, stop=False)
                sig(eng.matmul(ps_o[b][:, :], Qp[:, fch, t0:t0 + 128][oh:oh + 64, :],
                               S_bf[oh:oh + 64, h, :], start=False, stop=True))

            pv_tok = p.sec("tensor", [prev_ev, v_tok, prev_o[b], prev_sacc[h]], pv)

            def oev(eng, sig, h=h, cc=cc, b=b):
                if out_sb is None:
                    ins = eng.tensor_copy(Ob[b][:, h * 64:(h + 1) * 64], ps_o[b][:, :])
                else:
                    ins = eng.tensor_copy(out_sb[:, cc, h * 64:(h + 1) * 64], ps_o[b][:, :])
                sig(ins)

            o_tok = p.sec("vector",
                          [pv_tok, fl_prev[b] if out_sb is None else None], oev)
            prev_o[b] = o_tok
            ch_toks.append(o_tok)

            if cc < 15:
                def supd(eng, sig, h=h, oh=oh, b=b):
                    sig(eng.matmul(ps_su[oh:oh + 64, :], KT[b][:, h * 64:(h + 1) * 64],
                                   Vt[b][:, h * 64:(h + 1) * 64], start=True, stop=True))

                su_tok = p.sec("tensor", [pv_tok, kt_tok, v_tok, su_prev], supd)

                def sacc(eng, sig, h=h, oh=oh):
                    eng.tensor_tensor(S_sb[oh:oh + 64, h, :], S_sb[oh:oh + 64, h, :],
                                      ps_su[oh:oh + 64, :], OP.add)
                    eng.drain()
                    sig(eng.tensor_copy(S_bf[oh:oh + 64, h, :], S_sb[oh:oh + 64, h, :]))

                prev_sacc[h] = p.sec("vector", [su_tok], sacc)
                su_prev = prev_sacc[h]
                ch_toks.append(prev_sacc[h])
        users = merge_deps(ch_toks)
        v_prev[b] = users
        k_prev[b] = users
        if out_sb is None:
            def fl(eng, sig, t0=t0, b=b):
                sig(eng.dma_start(out=o_dram[t0:t0 + 128, :], in_=Ob[b][:, :]), dma=True)

            fl_prev[b] = p.sec("sync", [users], fl, dma=True)
            toks.append(fl_prev[b])
        else:
            toks.append(users)
    last = merge_deps(toks)
    p.close_scope(sc, last)
    return last


def emit_l3_attn(p, deps, xn_fm, io, cst, single=False):
    sc1 = p.scope()
    Qp = p.sbuf("l3_Qp", [128, 8, T], BF16, ctx=sc1)
    Kb = p.sbuf("l3_K", [128, 8, T], BF16, ctx=sc1)
    sc1b = p.scope()
    Qb = p.sbuf("l3_Q", [128, 8, T], BF16, ctx=sc1b)
    zrow = p.sbuf("l3_zrow", [128, 8], BF16, ctx=sc1b)
    q_toks = emit_gemm_fm(p, "l3q", deps, io["q1_w"], D, D,
                          lambda kb: xn_fm[:, kb, :],
                          lambda mb, c0, w: Qb[:, mb, c0:c0 + w], bias_sb=io["q1bT_sb"])
    k_toks = emit_gemm_fm(p, "l3k", deps, io["k1_w"], D, D,
                          lambda kb: xn_fm[:, kb, :],
                          lambda mb, c0, w: Kb[:, mb, c0:c0 + w], bias_sb=io["k1bT_sb"])

    def vz(eng, sig):
        sig(eng.memset(zrow[:, :], 0.0))

    z1 = p.sec("vector", [deps], vz)

    def vzd(eng, sig):
        dst = io["l3_v_dram"][T:T + 1, :].rearrange("one (a b) -> a (one b)", a=128)
        sig(eng.dma_start(out=dst, in_=zrow[:, :]), dma=True)

    vzd_tok = p.sec("sync", [z1], vzd, dma=True)

    sc_g = p.scope()
    Gch = p.sbuf("l3_Gch", [128, T], F32, ctx=sc_g)
    Gc2 = p.sbuf("l3_Gc2", [128, T], F32, ctx=sc_g)
    Rch = p.sbuf("l3_Rch", [128, T], F32, ctx=sc_g)
    onesT = p.sbuf("l3_onesT", [128, T], F32, ctx=sc_g)
    ps_g = [p.psum("l3_gps", [128, 512], F32, ctx=sc_g) for _ in range(2)]

    def oz(eng, sig):
        sig(eng.memset(onesT[:, :], 1.0))

    oz_tok = p.sec("vector", [deps], oz)
    ap_toks = []
    prev_mm = [None, None]
    prev_iter = None
    for fc in range(NFC):
        g_toks = []
        for cs in range(4):
            b = cs % 2

            def gmm(eng, sig, fc=fc, cs=cs, b=b):
                sig(eng.matmul(ps_g[b][:, :], io["Wg2_sb"][:, :],
                               xn_fm[:, fc, cs * 512:(cs + 1) * 512],
                               start=True, stop=True))

            m1 = p.sec("tensor", [deps, io["wgs_tok"], prev_mm[b]], gmm)

            def gact(eng, sig, cs=cs, b=b):
                sig(eng.activation(out=Gch[:, cs * 512:(cs + 1) * 512], in_=ps_g[b][:, :],
                                   func=AF.Sigmoid, bias=io["gwb2_sb"][:, 0:1]))

            prev_mm[b] = p.sec("scalar", [m1, prev_iter], gact)
            g_toks.append(prev_mm[b])

        def lmx(eng, sig):
            sig(eng.tensor_scalar(Gch[:, :], Gch[:, :], 1e-6, None, OP.max))

        c1 = p.sec("vector", [g_toks, oz_tok], lmx)

        def lnn(eng, sig):
            sig(eng.activation(out=Gch[:, :], in_=Gch[:, :], func=AF.Ln))

        c2 = p.sec("scalar", [c1], lnn)

        def csc(eng, sig):
            eng.tensor_tensor_scan(Gch[:, :], onesT[:, :], Gch[:, :], 0.0,
                                   OP.mult, OP.add)
            eng.drain()
            sig(eng.tensor_scalar(Gch[:, :], Gch[:, :], -30.0, 30.0, OP.max, OP.min))

        c3 = p.sec("vector", [c2], csc)

        def gex(eng, sig):
            sig(eng.activation(out=Gc2[:, :], in_=Gch[:, :], func=AF.Exp))

        c4 = p.sec("scalar", [c3], gex)

        def gpl(eng, sig):
            eng.tensor_scalar(Gc2[:, :], Gc2[:, :], 1e-6, None, OP.add)
            eng.drain()
            sig(eng.reciprocal(Gch[:, :], Gc2[:, :]))

        c5 = p.sec("vector", [c4], gpl)

        def qap(eng, sig, fc=fc):
            sig(eng.tensor_tensor(Qp[:, fc, :], Qb[:, fc, :], Gc2[:, :], OP.mult))

        a1 = p.sec("vector", [c5, q_toks], qap)
        r_toks = []
        for cs in range(4):
            b = cs % 2

            def rmm(eng, sig, fc=fc, cs=cs, b=b):
                sig(eng.matmul(ps_g[b][:, :], io["Ws2_sb"][:, :],
                               Kb[:, fc, cs * 512:(cs + 1) * 512],
                               start=True, stop=True))

            m2 = p.sec("tensor", [k_toks, io["wgs_tok"], prev_mm[b], a1], rmm)

            def ract(eng, sig, cs=cs, b=b):
                sig(eng.activation(out=Rch[:, cs * 512:(cs + 1) * 512], in_=ps_g[b][:, :],
                                   func=AF.Silu, bias=io["swb2_sb"][:, 0:1]))

            prev_mm[b] = p.sec("scalar", [m2], ract)
            r_toks.append(prev_mm[b])

        def kap(eng, sig, fc=fc):
            eng.tensor_tensor(Rch[:, :], Rch[:, :], Gch[:, :], OP.mult)
            eng.drain()
            sig(eng.tensor_tensor(Kb[:, fc, :], Kb[:, fc, :], Rch[:, :], OP.mult))

        kap_tok = p.sec("vector", [r_toks, c5], kap)
        prev_iter = merge_deps([a1, kap_tok])
        ap_toks.append(prev_iter)
    f_tok = merge_deps(ap_toks)
    p.close_scope(sc_g, f_tok)

    scrA = p.sbuf("l3_scrA", [128, T], BF16, ctx=sc1b)
    q2_toks = []
    prev_q2 = None
    for fc in range(NFC):
        def lk(eng, sig, fc=fc):
            eng.tensor_scalar(scrA[:, :], Qb[:, fc, :], 0.125, None, OP.mult)
            eng.drain()
            eng.tensor_scalar(Qb[:, fc, :], scrA[:, :], 0.0, None, OP.min)
            eng.drain()
            eng.tensor_scalar(scrA[:, :], scrA[:, :], 0.0, None, OP.max)
            eng.drain()
            eng.scalar_tensor_tensor(Qb[:, fc, :], scrA[:, :], 0.02, Qb[:, fc, :],
                                     OP.mult, OP.add)
            eng.drain()
            eng.memset(scrA[:, 0:1], 0.0)
            eng.drain()
            sig(eng.tensor_copy(scrA[:, 1:T], Qb[:, fc, 0:T - 1]))

        l_tok = p.sec("vector", [f_tok, prev_q2], lk)

        def qsp(eng, sig, fc=fc):
            sig(eng.dma_start(out=io["l3_q2p_dram"][fc], in_=scrA[:, :]), dma=True)

        prev_q2 = p.sec("sync", [l_tok], qsp, dma=True)
        q2_toks.append(prev_q2)

    k2_toks = emit_gemm_fm(p, "l3k2", [deps, q2_toks], io["k2_w"], D, D,
                           lambda kb: xn_fm[:, kb, :],
                           lambda mb, c0, w: Qb[:, mb, c0:c0 + w],
                           bias_sb=io["k2bsT_sb"], act=AF.Sigmoid,
                           act_scale=0.02 / 32.0)

    def k2dma(eng, sig):
        sig(eng.dma_start(out=io["l3_k2_dram"].rearrange("mb p t -> p mb t"),
                          in_=Qb[:, :, :]), dma=True)

    k2d_tok = p.sec("sync", [k2_toks], k2dma, dma=True)
    p.close_scope(sc1b, merge_deps([k2d_tok, vzd_tok]))

    vd_prev = {0: None, 1: None}

    def vchunk1(pp, deps2, cc, dst):
        t0 = cc * 128
        pairs = [(xn_fm[:, fc, t0:t0 + 128], dst[:, fc * 128:(fc + 1) * 128])
                 for fc in range(NFC)]
        tt = emit_transpose_grid(pp, deps2 + [vd_prev[cc % 2]], pairs, cst["identb"],
                                 BF16, tag="l3vt")

        def vd(eng, sig, t0=t0):
            sig(eng.dma_start(out=io["l3_v_dram"][t0:t0 + 128, :], in_=dst[:, :]),
                dma=True)

        vd_prev[cc % 2] = pp.sec("sync", [tt], vd, dma=True)
        return tt

    o1_tok = emit_lin_attn(p, merge_deps([f_tok, deps]), Qp, Kb, vchunk1,
                           io["l3_o1_dram"], None, cst, strict=False, tag="l3o1")
    o1_done = fence(p, merge_deps([o1_tok, vd_prev[0], vd_prev[1]]))
    p.close_scope(sc1, o1_done)

    sc3 = p.scope()
    O2_tm = p.sbuf("l3_O2", [128, 16, 1024], BF16, ctx=sc3)
    sc3b = p.scope()
    q2p_fm = p.sbuf("l3_q2pfm", [128, 8, T], BF16, ctx=sc3b)
    k2_fm = p.sbuf("l3_k2fm", [128, 8, T], BF16, ctx=sc3b)
    e_scr = [p.sbuf("l3_escr", [128, 1024], BF16, ctx=sc3b) for _ in range(2)]

    def rl1(eng, sig):
        sig(eng.dma_start(out=q2p_fm[:, :, :],
                          in_=io["l3_q2p_dram"].rearrange("mb p t -> p mb t")), dma=True)
        sig(eng.dma_start(out=k2_fm[:, :, :],
                          in_=io["l3_k2_dram"].rearrange("mb p t -> p mb t")), dma=True)

    rl0_tok = p.sec("sync", [prev_q2, k2d_tok], rl1, dma=True)

    def q0z(eng, sig):
        sig(eng.memset(q2p_fm[:, :, 0:1], 0.0))

    rl_tok = p.sec("vector", [rl0_tok], q0z)
    e_prev = {0: None, 1: None}

    def vchunk2(pp, deps2, cc, dst):
        t0 = cc * 128
        b = cc % 2

        def ld(eng, sig, t0=t0, b=b):
            sig(eng.dma_start(out=e_scr[b][:, :], in_=io["l3_o1_dram"][t0:t0 + 128, :]),
                dma=True)
            sig(eng.dma_start(out=dst[:, :], in_=io["l3_v_dram"][t0 + 1:t0 + 129, :]),
                dma=True)

        l_tok = pp.sec("sync", deps2 + [o1_done, e_prev[b]], ld, dma=True)

        def sub(eng, sig, b=b):
            sig(eng.tensor_tensor(dst[:, :], dst[:, :], e_scr[b][:, :], OP.subtract))

        e_prev[b] = pp.sec("vector", [l_tok], sub)
        return e_prev[b]

    o2_tok = emit_lin_attn(p, merge_deps([rl_tok, o1_done]), q2p_fm, k2_fm, vchunk2,
                           None, O2_tm, cst, strict=True, tag="l3o2")
    p.close_scope(sc3b, o2_tok)

    sc3c = p.scope()
    O1ld = p.sbuf("l3_O1ld", [128, 2, 1024], BF16, ctx=sc3c)
    O12_fm = p.sbuf("l3_O12", [128, 8, T], BF16, ctx=sc3c)
    add_toks = []
    prev_ld = [None, None]
    for cc in range(16):
        b = cc % 2

        def ol(eng, sig, cc=cc, b=b):
            sig(eng.dma_start(out=O1ld[:, b, :],
                              in_=io["l3_o1_dram"][cc * 128:(cc + 1) * 128, :]), dma=True)

        ol_tok = p.sec("sync", [o2_tok, prev_ld[b]], ol, dma=True)

        def oa(eng, sig, cc=cc, b=b):
            sig(eng.tensor_tensor(O2_tm[:, cc, :], O2_tm[:, cc, :], O1ld[:, b, :],
                                  OP.add))

        prev_ld[b] = p.sec("vector", [ol_tok], oa)
        add_toks.append(prev_ld[b])
    oa_tok = merge_deps(add_toks)
    pairs_o = []
    for fc in range(NFC):
        for tc in range(16):
            pairs_o.append((O2_tm[:, tc, fc * 128:(fc + 1) * 128],
                            O12_fm[:, fc, tc * 128:(tc + 1) * 128]))
    of_tok = emit_transpose_grid(p, [oa_tok], pairs_o, cst["identb"], BF16, tag="l3of")
    out_toks = emit_gemm_to_dram(p, "l3cp", [of_tok], io["cp_w"], D, D,
                                 lambda kb: O12_fm[:, kb, :], io["l3a_out_dram"],
                                 bias_sb=io["cpbT_sb"])
    p.close_scope(sc3c, out_toks)
    p.close_scope(sc3, out_toks)
    return out_toks


def emit_ag(p, deps, gin, gout, single, tag):
    if single:
        rows = gin.shape[0]

        def body(eng, sig):
            for r in range(NCORES):
                sig(eng.dma_start(out=gout[r * rows:(r + 1) * rows], in_=gin[:]), dma=True)

        return fence(p, p.sec("gpsimd", [deps], body, dma="sw"))

    def body(eng, sig):
        sig(eng.collective_compute("AllGather", OP.bypass,
                                   replica_groups=[list(range(NCORES))],
                                   ins=[gin.opt()], outs=[gout.opt()]))

    return p.sec("gpsimd", [deps], body)


def emit_l3_ffn(p, deps, xn_fm, io, cst, single=False):
    caps = io["caps"]
    sc_a = p.scope()
    xr_fm = p.sbuf("l3f_xr", [128, NFC, T], BF16, ctx=sc_a)
    sc_a1 = p.scope()
    sq = p.sbuf("l3f_sq", [128, 512], BF16, ctx=sc_a1)
    srow = p.sbuf("l3f_srow", [1, 2, T], F32, ctx=sc_a1)
    ps_s = p.psum("l3f_sps", [1, 512], F32, ctx=sc_a1)
    ps_b = p.psum("l3f_bps", [128, 512], F32, ctx=sc_a1)
    st_toks = []
    mm_tok = None
    prev_sev = None
    for cs in range(4):
        for fc in range(NFC):
            def sqb(eng, sig, fc=fc, cs=cs):
                sig(eng.activation(out=sq[:, :], in_=xn_fm[:, fc, cs * 512:(cs + 1) * 512],
                                   func=AF.Square))

            c_tok = p.sec("scalar", [deps, mm_tok], sqb)

            def smb(eng, sig, fc=fc):
                sig(eng.matmul(ps_s[:, :], cst["ones128x1_b"][:, 0:1], sq[:, :],
                               start=(fc == 0), stop=(fc == NFC - 1)))

            mm_tok = p.sec("tensor", [c_tok, prev_sev], smb)

        def sev(eng, sig, cs=cs):
            sig(eng.tensor_scalar(srow[:, 0, cs * 512:(cs + 1) * 512], ps_s[:, :],
                                  1.0 / D, 1e-5, OP.mult, OP.add))

        prev_sev = p.sec("vector", [mm_tok], sev)
        st_toks.append(prev_sev)

    def rstd(eng, sig):
        eng.activation(out=srow[:, 1, :], in_=srow[:, 0, :], func=AF.Ln)
        eng.drain()
        sig(eng.activation(out=srow[:, 0, :], in_=srow[:, 1, :], func=AF.Exp, scale=-0.5))

    r_tok = p.sec("scalar", [st_toks], rstd)
    xr_toks = []
    bc_prev = None
    for cs in range(4):
        def bcb(eng, sig, cs=cs):
            sig(eng.matmul(ps_b[:, :], cst["ones1x128_f"][0:1, :],
                           srow[0:1, 0, cs * 512:(cs + 1) * 512], start=True, stop=True))

        b_tok = p.sec("tensor", [r_tok, bc_prev], bcb)

        def xrb(eng, sig, cs=cs):
            for fc in range(NFC):
                sl = slice(cs * 512, (cs + 1) * 512)
                eng.tensor_tensor(xr_fm[:, fc, sl], xn_fm[:, fc, sl], ps_b[:, :], OP.mult)
                eng.drain()
                ins = eng.tensor_scalar(xr_fm[:, fc, sl], xr_fm[:, fc, sl],
                                        io["nwT_sb"][:, fc:fc + 1], None, OP.mult)
                if fc < NFC - 1:
                    eng.drain()
            sig(ins)

        bc_prev = p.sec("vector", [b_tok], xrb)
        xr_toks.append(bc_prev)
    xr_tok = merge_deps(xr_toks)
    p.close_scope(sc_a1, xr_tok)
    sc_a2 = p.scope()
    xr_tm = p.sbuf("l3f_xrtm", [128, 16, D], BF16, ctx=sc_a2)
    t_tok = emit_fm_to_tm(p, [xr_tok], xr_fm, xr_tm, cst["identb"], BF16, tag="l3fxt")

    def xd(eng, sig):
        sig(eng.dma_start(out=io["l3f_xr_dram"].rearrange("(tc p) d -> p tc d", p=128),
                          in_=xr_tm[:, :, :]), dma=True)

    xrd_tok = p.sec("sync", [t_tok], xd, dma=True)
    p.close_scope(sc_a2, xrd_tok)

    sc_b = p.scope()
    lg_row = p.sbuf("l3f_lg", [16, T], F32, ctx=sc_b)
    rw_sb = p.sbuf("l3f_rw", [128, 8, 16], BF16, ctx=sc_b)
    ps_l = p.psum("l3f_lps", [16, 512], F32, ctx=sc_b)

    def rwl(eng, sig):
        sig(eng.dma_start(out=rw_sb[:, :, :],
                          in_=io["router_wT"].rearrange("(kb p) e -> p kb e", p=128)),
            dma=True)

    rw_tok = p.sec("gpsimd", [deps], rwl, dma="sw")
    lg_toks = []
    ev = None
    for cs in range(4):
        def lmm(eng, sig, cs=cs):
            for kb in range(8):
                ins = eng.matmul(ps_l[:, :], rw_sb[:, kb, :],
                                 xr_fm[:, kb, cs * 512:(cs + 1) * 512],
                                 start=(kb == 0), stop=(kb == 7))
            sig(ins)

        m_tok = p.sec("tensor", [xr_tok, rw_tok, ev], lmm)

        def lev(eng, sig, cs=cs):
            sig(eng.tensor_scalar(lg_row[:, cs * 512:(cs + 1) * 512], ps_l[:, :],
                                  io["rbT_sb"][0:16, 0:1], None, OP.add))

        ev = p.sec("vector", [m_tok], lev)
        lg_toks.append(ev)
    lg_tok = merge_deps(lg_toks)

    lg_tm = p.sbuf("l3f_lgtm", [128, 16, 16], F32, ctx=sc_b)
    pr_tm = p.sbuf("l3f_prtm", [128, 16, 16], F32, ctx=sc_b)
    mcl = p.sbuf("l3f_mcl", [128, 4], F32, ctx=sc_b)
    lgb = p.sbuf("l3f_lgb", [16, T], BF16, ctx=sc_b)

    def lgc(eng, sig):
        sig(eng.tensor_copy(lgb[:, :], lg_row[:, :]))

    lgc_tok = p.sec("vector", [lg_tok], lgc)
    tp_pairs = [(lgb[0:16, tc * 128:(tc + 1) * 128], lg_tm[:, tc, :].rearrange("p e -> p e"))
                for tc in range(16)]
    lg_tmb = p.sbuf("l3f_lgtmb", [128, 16, 16], BF16, ctx=sc_b)
    tpt = emit_transpose_grid(p, [lgc_tok],
                              [(lgb[0:16, tc * 128:(tc + 1) * 128], lg_tmb[:, tc, :])
                               for tc in range(16)], cst["identb"], BF16, tag="l3flgt")
    pr_toks = []
    prev_pr = None
    for tc in range(16):
        def smx(eng, sig, tc=tc):
            eng.tensor_reduce(mcl[:, 0:1], lg_tmb[:, tc, :], AX.X, OP.max)
            eng.drain()
            sig(eng.tensor_scalar(mcl[:, 1:2], mcl[:, 0:1], -1.0, None, OP.mult))

        m1 = p.sec("vector", [tpt, prev_pr], smx)

        def sex(eng, sig, tc=tc):
            sig(eng.activation(out=pr_tm[:, tc, :], in_=lg_tmb[:, tc, :], func=AF.Exp,
                               bias=mcl[:, 1:2], accum_out=mcl[:, 2:3]))

        m2 = p.sec("scalar", [m1], sex)

        def snr(eng, sig, tc=tc):
            eng.reciprocal(mcl[:, 3:4], mcl[:, 2:3])
            eng.drain()
            sig(eng.tensor_scalar(pr_tm[:, tc, :], pr_tm[:, tc, :], mcl[:, 3:4], None,
                                  OP.mult))

        prev_pr = p.sec("vector", [m2], snr)
        pr_toks.append(prev_pr)

    def prd(eng, sig):
        sig(eng.dma_start(out=io["l3f_pr_dram"].rearrange("(tc p) e -> p tc e", p=128),
                          in_=pr_tm[:, :, :]), dma=True)
        sig(eng.dma_start(out=io["l3f_lgT_dram"], in_=lg_row[:, :]), dma=True)

    prd_tok = p.sec("sync", [pr_toks, lg_tok], prd, dma=True)
    p.close_scope(sc_b, prd_tok)

    ag1 = emit_ag(p, [prd_tok], io["l3f_lgT_dram"], io["l3f_lgall_dram"], single, "aglg")
    ag2 = emit_ag(p, [prd_tok], io["l3f_pr_dram"], io["l3f_prall_dram"], single, "agpr")
    ag3 = emit_ag(p, [xrd_tok], io["l3f_xr_dram"], io["l3f_xrall_dram"], single, "agxr")

    sc_c = p.scope()
    GL = p.sbuf("l3f_GL", [128, T], F32, ctx=sc_c)
    cmp_scr = p.sbuf("l3f_cmp", [128, T], BF16, ctx=sc_c)
    bis = p.sbuf("l3f_bis", [128, 8], F32, ctx=sc_c)
    b16 = p.sbuf("l3f_b16", [16, 8], F32, ctx=sc_c)
    ps_c = p.psum("l3f_cps", [16, 8], F32, ctx=sc_c)
    ps_cb = p.psum("l3f_cbps", [128, 8], F32, ctx=sc_c)

    def gll(eng, sig):
        sig(eng.dma_start(out=GL[:, :], in_=io["l3f_lgall_dram"]), dma=True)

    gl_tok = p.sec("sync", [ag1], gll, dma=True)

    def mnmx(eng, sig):
        eng.tensor_reduce(bis[:, 2:3], GL[:, :], AX.X, OP.min)
        eng.drain()
        eng.tensor_reduce(bis[:, 3:4], GL[:, :], AX.X, OP.max)
        eng.drain()
        eng.tensor_scalar(bis[:, 4:5], bis[:, 2:3], -1.0, None, OP.mult)
        eng.drain()
        sig(eng.tensor_copy(bisb[:, 0:3], bis[:, 2:5]))

    bisb = p.sbuf("l3f_bisb", [128, 4], BF16, ctx=sc_c)
    rowr = p.sbuf("l3f_rowr", [4, 128], BF16, ctx=sc_c)
    rowcol = p.sbuf("l3f_rowc", [4, 4], BF16, ctx=sc_c)
    rowflat = p.sbuf("l3f_rowf", [1, 2, 4], BF16, ctx=sc_c)
    rowf32 = p.sbuf("l3f_rf32", [1, 4], F32, ctx=sc_c)
    mm0 = p.sec("vector", [gl_tok], mnmx)
    tr_tok = emit_transpose_grid(p, [mm0], [(bisb[:, 0:3], rowr[0:3, :])],
                                 cst["identb"], BF16, tag="l3fbt")

    def rowred(eng, sig):
        sig(eng.tensor_reduce(rowcol[0:3, 0:1], rowr[0:3, :], AX.X, OP.max))

    rr_tok = p.sec("vector", [tr_tok], rowred)
    tr2_tok = emit_transpose_grid(p, [rr_tok], [(rowcol[0:3, 0:1], rowflat[0:1, 0, 0:3])],
                                  cst["identb"], BF16, tag="l3fbt2")

    def rcast(eng, sig):
        eng.tensor_copy(rowf32[:, 0:3], rowflat[:, 0, 0:3])
        eng.drain()
        eng.tensor_scalar(rowf32[:, 2:3], rowf32[:, 2:3], -1.0, None, OP.mult)
        eng.drain()
        eng.tensor_tensor(rowf32[:, 3:4], rowf32[:, 2:3], rowf32[:, 2:3], OP.abs_max) if False else None
        eng.tensor_scalar(rowf32[:, 0:1], rowf32[:, 2:3], 1.0, -1.0, OP.mult, OP.add)
        eng.drain()
        sig(eng.tensor_scalar(rowf32[:, 1:2], rowf32[:, 1:2], 1.0, 1.0, OP.mult, OP.add))

    rc_tok = p.sec("vector", [tr2_tok], rcast)
    ps_bb = p.psum("l3f_bbps", [128, 8], F32, ctx=sc_c)

    def bbc(eng, sig):
        sig(eng.matmul(ps_bb[:, 0:2], cst["ones1x128_f"][0:1, :], rowf32[0:1, 0:2],
                       start=True, stop=True))

    bb_tok = p.sec("tensor", [rc_tok], bbc)

    def binit2(eng, sig):
        eng.tensor_copy(bis[:, 2:3], ps_bb[:, 0:1])
        eng.drain()
        sig(eng.tensor_copy(bis[:, 3:4], ps_bb[:, 1:2]))

    bi_tok = p.sec("vector", [bb_tok], binit2)

    it_tok = bi_tok
    for it in range(42):
        def bit(eng, sig):
            eng.tensor_tensor(bis[:, 0:1], bis[:, 2:3], bis[:, 3:4], OP.add)
            eng.drain()
            eng.tensor_scalar(bis[:, 0:1], bis[:, 0:1], 0.5, None, OP.mult)
            eng.drain()
            sig(eng.tensor_scalar(cmp_scr[:, :], GL[:, :], bis[:, 0:1], 0.0, OP.is_gt,
                                  OP.add, accum_out=bis[:, 1:2]))

        t1 = p.sec("vector", [it_tok], bit)

        def csum(eng, sig):
            sig(eng.matmul(ps_c[:, 0:1], cst["l3_emap_sb"][:, 0:16], bis[:, 1:2],
                           start=True, stop=True))

        t2 = p.sec("tensor", [t1], csum)

        def cbc(eng, sig):
            sig(eng.matmul(ps_cb[:, 0:1], cst["l3_emapT_sb"][0:16, :], ps_c[0:16, 0:1],
                           start=True, stop=True))

        def cev(eng, sig):
            sig(eng.tensor_copy(b16[:, 0:1], ps_c[0:16, 0:1]))

        t3 = p.sec("vector", [t2], cev)

        def cbc2(eng, sig):
            sig(eng.matmul(ps_cb[:, 0:1], cst["l3_emapT_sb"][0:16, :], b16[0:16, 0:1],
                           start=True, stop=True))

        t4 = p.sec("tensor", [t3], cbc2)

        def upd(eng, sig):
            eng.tensor_scalar(bis[:, 4:5], ps_cb[:, 0:1], 1023.5, None, OP.is_ge)
            eng.drain()
            eng.tensor_tensor(bis[:, 5:6], bis[:, 0:1], bis[:, 2:3], OP.subtract)
            eng.drain()
            eng.scalar_tensor_tensor(bis[:, 5:6], bis[:, 5:6], 1.0, bis[:, 4:5],
                                     OP.mult, OP.mult)
            eng.drain()
            eng.tensor_tensor(bis[:, 2:3], bis[:, 2:3], bis[:, 5:6], OP.add)
            eng.drain()
            eng.tensor_tensor(bis[:, 5:6], bis[:, 0:1], bis[:, 3:4], OP.subtract)
            eng.drain()
            eng.tensor_scalar(bis[:, 6:7], bis[:, 4:5], 1.0, -1.0, OP.subtract, OP.mult)
            eng.drain()
            eng.tensor_tensor(bis[:, 5:6], bis[:, 5:6], bis[:, 6:7], OP.mult)
            eng.drain()
            sig(eng.tensor_tensor(bis[:, 3:4], bis[:, 3:4], bis[:, 5:6], OP.add))

        it_tok = p.sec("vector", [t4], upd)

    M_dram = io["l3f_m_dram"]

    def mfin(eng, sig):
        sig(eng.tensor_scalar(GL[:, :], GL[:, :], bis[:, 2:3], None, OP.is_gt))

    mf_tok = p.sec("vector", [it_tok], mfin)

    def mstore(eng, sig):
        sig(eng.dma_start(out=M_dram, in_=GL[:, :]), dma=True)

    ms_tok = p.sec("sync", [mf_tok], mstore, dma=True)
    p.close_scope(sc_c, ms_tok)
    p.close_scope(sc_a, merge_deps([ms_tok, lg_tok]))
    return ms_tok, ag2, ag3


def emit_l3_ffn_part2(p, io, cst, ms_tok, ag2, ag3, single, y_out, extra_dep=None):
    sc_f = p.scope()
    own = p.sbuf("l3f_own", [64, 3, T], F32, ctx=sc_f)
    idxf = p.sbuf("l3f_idxcf", [128, 16], F32, ctx=sc_f)
    idxu = p.sbuf("l3f_idxcu", [128, 16], U32, ctx=sc_f)
    idxfT = p.sbuf("l3f_idxft", [16, 128], F32, ctx=sc_f)
    tcol = p.sbuf("l3f_tcol", [64, 1], F32, ctx=sc_f)
    cntb = p.sbuf("l3f_cntb", [128, 2], F32, ctx=sc_f)
    trow = p.sbuf("l3f_trow", [1, 2], F32, ctx=sc_f)
    ownidx = p.sbuf("l3f_oidx", [128, 1], U32, ctx=sc_f)
    rowq = p.sbuf("l3f_rowq", [1, 2, 512], F32, ctx=sc_f)
    posq = p.sbuf("l3f_posq", [128, 4096], F32, ctx=sc_f)
    tokq = p.sbuf("l3f_tokq", [128, 4096], F32, ctx=sc_f)
    sink = p.sbuf("l3f_sink", [128, 4096], F32, ctx=sc_f)
    part = p.sbuf("l3f_part", [128, 2], F32, ctx=sc_f)
    ps_p = p.psum("l3f_pps", [64, 1], F32, ctx=sc_f)
    ps_t = p.psum("l3f_tps", [1, 2], F32, ctx=sc_f)
    ps_tb = p.psum("l3f_tbps", [128, 2], F32, ctx=sc_f)
    ps_qa = p.psum("l3f_qpsa", [128, 512], F32, ctx=sc_f)
    ps_qb = p.psum("l3f_qpsb", [128, 512], F32, ctx=sc_f)

    def oil(eng, sig):
        sig(eng.dma_start(out=ownidx[:, :], in_=io["own_rows"]), dma=True)

    oi_tok = p.sec("sync", [], oil, dma=True)

    def omz(eng, sig):
        sig(eng.memset(own[:, 0, :], 0.0))

    oz_tok = p.sec("vector", [], omz)

    def ogather(eng, sig):
        sig(ind_dma(eng, lambda: eng.indirect_dma_start(
            out=own[0:64, 0, :], out_offset=None, in_=io["l3f_m_dram"],
            in_offset=IOA(ap=ownidx[0:64, :], axis=0),
            bounds_check=127, oob_is_err=False)), dma=True)

    og_tok = p.sec("gpsimd", [oz_tok, oi_tok, ms_tok], ogather, dma="sw")

    comp_toks = []
    pt_dma = []
    for j in range(2):
        base = 32 * j

        def posb(eng, sig, base=base):
            eng.memset(own[base:base + 8, 1, :], 1.0)
            eng.drain()
            eng.tensor_tensor_scan(own[base:base + 8, 1, :], own[base:base + 8, 1, :],
                                   own[base:base + 8, 0, :], 0.0, OP.mult, OP.add)
            eng.drain()
            sig(eng.tensor_tensor(own[base:base + 8, 2, :], own[base:base + 8, 0, :],
                                  cst["l3f_gid_sb"][base:base + 8, :], OP.mult))

        p1 = p.sec("vector", [og_tok, comp_toks[-1] if comp_toks else None], posb)

        def pref(eng, sig, base=base):
            sig(eng.matmul(ps_p[base:base + 8, 0:1], cst["l3_tri8_sb"][base:base + 8, 0:8],
                           own[base:base + 8, 1, T - 1:T], start=True, stop=True))

        p2 = p.sec("tensor", [p1], pref)

        def totm(eng, sig, base=base, j=j):
            sig(eng.matmul(ps_t[0:1, j:j + 1], cst["l3_ones8_sb"][base:base + 8, 0:1],
                           own[base:base + 8, 1, T - 1:T], start=True, stop=True))

        p2b = p.sec("tensor", [p2], totm)

        def pos2(eng, sig, base=base):
            eng.tensor_copy(tcol[base:base + 8, 0:1], ps_p[base:base + 8, 0:1])
            eng.drain()
            sig(eng.tensor_scalar(own[base:base + 8, 1, :], own[base:base + 8, 1, :],
                                  tcol[base:base + 8, 0:1], None, OP.add))

        p3 = p.sec("vector", [p2, p2b], pos2)
        comp_toks.append(p3)

        def pd(eng, sig, base=base, j=j):
            sig(eng.dma_start(out=io["l3f_pos_dram"][j], in_=own[base:base + 8, 1, :]),
                dma=True)
            sig(eng.dma_start(out=io["l3f_tok_dram"][j], in_=own[base:base + 8, 2, :]),
                dma=True)

        pt_dma.append(fence(p, p.sec("sync", [p3], pd, dma=True)))

    def totev(eng, sig):
        sig(eng.tensor_copy(trow[0:1, 0:2], ps_t[0:1, 0:2]))

    te_tok = p.sec("vector", [comp_toks], totev)

    def totbc(eng, sig):
        sig(eng.matmul(ps_tb[:, 0:2], cst["ones1x128_f"][0:1, :], trow[0:1, 0:2],
                       start=True, stop=True))

    tb_tok = p.sec("tensor", [te_tok], totbc)

    def totcp(eng, sig):
        sig(eng.tensor_copy(cntb[:, 0:2], ps_tb[:, 0:2]))

    tc_tok = p.sec("vector", [tb_tok], totcp)

    ex_toks = []
    stt_last = None
    ev_prev = None
    for j in range(2):
        def zidx(eng, sig, j=j):
            sig(eng.memset(idxf[:, j * 8:(j + 1) * 8], 0.0))

        z_tok = p.sec("vector", [stt_last], zidx)
        for q in range(4):
            ev_toks = []
            for nh in range(8):
                def rl(eng, sig, j=j, q=q, nh=nh):
                    b0 = q * 4096 + nh * 512
                    srcp = io["l3f_pos_dram"][j].rearrange("r t -> (r t)")
                    srct = io["l3f_tok_dram"][j].rearrange("r t -> (r t)")
                    sig(eng.dma_start(out=rowq[0:1, 0, :], in_=srcp[None, b0:b0 + 512]),
                        dma=True)
                    sig(eng.dma_start(out=rowq[0:1, 1, :], in_=srct[None, b0:b0 + 512]),
                        dma=True)

                ld = p.sec("sync", [pt_dma[j], ev_prev, stt_last], rl, dma=True)

                def bmm(eng, sig):
                    eng.matmul(ps_qa[:, :], cst["ones1x128_f"][0:1, :], rowq[0:1, 0, :],
                               start=True, stop=True)
                    sig(eng.matmul(ps_qb[:, :], cst["ones1x128_f"][0:1, :],
                                   rowq[0:1, 1, :], start=True, stop=True))

                b1 = p.sec("tensor", [ld, ev_prev], bmm)

                def ev(eng, sig, nh=nh):
                    eng.tensor_copy(posq[:, nh * 512:(nh + 1) * 512], ps_qa[:, :])
                    eng.drain()
                    sig(eng.tensor_copy(tokq[:, nh * 512:(nh + 1) * 512], ps_qb[:, :]))

                ev_prev = p.sec("vector", [b1], ev)
                ev_toks.append(ev_prev)
            stp = None
            for st in range(8):
                def ext(eng, sig, st=st):
                    sig(eng.scalar_tensor_tensor(sink[:, :], posq[:, :],
                                                 cst["l3f_slotiota_sb"][:, st:st + 1],
                                                 tokq[:, :], OP.is_equal, OP.mult,
                                                 accum_out=part[:, 0:1]))

                e1 = p.sec("vector", [ev_toks, stp], ext)

                def eacc(eng, sig, j=j, st=st):
                    c = j * 8 + st
                    sig(eng.tensor_tensor(idxf[:, c:c + 1], idxf[:, c:c + 1],
                                          part[:, 0:1], OP.add))

                stp = p.sec("vector", [e1, z_tok], eacc)
            stt_last = stp

        def oobf(eng, sig, j=j):
            for st in range(8):
                eng.scalar_tensor_tensor(part[:, 1:2], cst["l3f_slotiota_sb"][:, st:st + 1],
                                         cntb[:, j:j + 1],
                                         cst["l3f_slotiota_sb"][:, st:st + 1], OP.is_gt,
                                         OP.bypass)
                eng.drain()
                c = j * 8 + st
                ins = eng.scalar_tensor_tensor(idxf[:, c:c + 1], part[:, 1:2], 2e9,
                                               idxf[:, c:c + 1], OP.mult, OP.add)
                if st < 7:
                    eng.drain()
            sig(ins)

        stt_last = p.sec("vector", [stt_last, tc_tok], oobf)
        ex_toks.append(stt_last)
    ex_tok = merge_deps(ex_toks)

    def ucast(eng, sig):
        sig(eng.tensor_copy(idxu[:, :], idxf[:, :]))

    uc_tok = p.sec("vector", [ex_tok], ucast)
    tf_tok = emit_transpose_grid(p, [ex_tok], [(idxf[:, 0:16], idxfT[0:16, :])],
                                 cst["ident"][:, :], F32, tag="l3fit")

    def idma(eng, sig):
        sig(eng.dma_start(out=io["l3f_idx_dram"], in_=idxu[:, :]), dma=True)
        dst = io["l3f_idxf_dram"].rearrange("(js pp) c -> js (pp c)", pp=128)
        sig(eng.dma_start(out=dst, in_=idxfT[0:16, :]), dma=True)

    id_tok = fence(p, p.sec("sync", [uc_tok, tf_tok], idma, dma=True))
    p.close_scope(sc_f, id_tok)

    sc_g2 = p.scope()
    idxu2 = p.sbuf("l3f_idxu2", [128, 16], U32, ctx=sc_g2)

    def irld(eng, sig):
        sig(eng.dma_start(out=idxu2[:, :], in_=io["l3f_idx_dram"]), dma=True)

    ir_tok = fence(p, p.sec("sync", [id_tok], irld, dma=True))
    con_toks = []
    for j in range(2):
        sc_e = p.scope()
        gtile = p.sbuf("l3f_gt", [128, 16], F32, ctx=sc_e)
        gcol = p.sbuf("l3f_gcol", [128, 8], F32, ctx=sc_e)
        xe_tm = p.sbuf("l3f_xetm", [128, 1024], BF16, ctx=sc_e)
        xeT = p.sbuf("l3f_xeT", [128, 8, 1024], BF16, ctx=sc_e)
        prev_g = None
        for st in range(8):
            def gg(eng, sig, j=j, st=st):
                sig(ind_dma(eng, lambda: eng.indirect_dma_start(
                    out=gtile[:, :], out_offset=None, in_=io["l3f_prall_dram"],
                    in_offset=IOA(ap=idxu2[:, j * 8 + st:j * 8 + st + 1], axis=0),
                    bounds_check=16383, oob_is_err=False)), dma=True)

            g1 = p.sec("gpsimd", [ir_tok, ag2, prev_g], gg, dma="sw")

            def gdot(eng, sig, j=j, st=st):
                eng.tensor_tensor(gtile[:, :], gtile[:, :],
                                  io["ehot_sb"][:, j * 16:(j + 1) * 16], OP.mult)
                eng.drain()
                sig(eng.tensor_reduce(gcol[:, st:st + 1], gtile[:, :], AX.X, OP.add))

            prev_g = p.sec("vector", [g1], gdot)

            def xg(eng, sig, j=j, st=st):
                sig(ind_dma(eng, lambda: eng.indirect_dma_start(
                    out=xe_tm[:, :], out_offset=None, in_=io["l3f_xrall_dram"],
                    in_offset=IOA(ap=idxu2[:, j * 8 + st:j * 8 + st + 1], axis=0),
                    bounds_check=16383, oob_is_err=False)), dma=True)

            x1 = p.sec("gpsimd", [ir_tok, ag3, prev_g], xg, dma="sw")
            tp = [(xe_tm[:, kb * 128:(kb + 1) * 128], xeT[:, kb, st * 128:(st + 1) * 128])
                  for kb in range(8)]
            prev_g = emit_transpose_grid(p, [x1], tp, cst["identb"], BF16,
                                         tag=f"l3fxe{j}")
        ge_tok = prev_g

        h_fm = p.sbuf("l3f_h", [128, 32, 1024], BF16, ctx=sc_e)
        f1_toks = emit_gemm_fm_rowmajor(p, f"l3fc1_{j}", ge_tok, io["fc1w"][j], 4096, D,
                                        lambda kb: xeT[:, kb, :],
                                        lambda mb, c0, w: h_fm[:, mb, c0:c0 + w],
                                        cst, act=AF.Gelu, Tlen=1024,
                                        bias_sb=io["fc1bT_sb"][:, j * 32:(j + 1) * 32])
        y_fm = p.sbuf("l3f_yfm", [128, 8, 1024], BF16, ctx=sc_e)
        f2_toks = emit_gemm_fm_rowmajor(p, f"l3fc2_{j}", f1_toks, io["fc2w"][j], 1024,
                                        4096, lambda kb: h_fm[:, kb, :],
                                        lambda mb, c0, w: y_fm[:, mb, c0:c0 + w],
                                        cst, Tlen=1024,
                                        bias_sb=io["fc2bT_sb"][:, j * 8:(j + 1) * 8])
        grow2 = p.sbuf("l3f_grow2", [1, 1024], BF16, ctx=sc_e)
        gcolT = p.sbuf("l3f_gcolT", [8, 128], F32, ctx=sc_e)
        ps_gb = p.psum("l3f_gbps", [128, 512], F32, ctx=sc_e)
        gt_tok = emit_transpose_grid(p, [ge_tok], [(gcol[:, 0:8], gcolT[0:8, :])],
                                     cst["ident"][:, :], F32, tag=f"l3fgt{j}")

        def gwr(eng, sig, j=j):
            sig(eng.dma_start(out=io["l3f_g_dram"][j], in_=gcolT[0:8, :]), dma=True)

        gw_tok = p.sec("sync", [gt_tok], gwr, dma=True)

        def grd(eng, sig, j=j):
            src = io["l3f_g_dram"][j].rearrange("st pp -> (st pp)")
            sig(eng.dma_start(out=grow2[:, :], in_=src[None, :]), dma=True)

        gr_tok = p.sec("gpsimd", [gw_tok], grd, dma="sw")
        gm_toks = []
        ev3 = None
        for cs in range(2):
            def gbc(eng, sig, cs=cs):
                sig(eng.matmul(ps_gb[:, :], cst["ones1x128_b"][0:1, :],
                               grow2[0:1, cs * 512:(cs + 1) * 512], start=True, stop=True))

            b1 = p.sec("tensor", [gr_tok, ev3], gbc)

            def gmul(eng, sig, cs=cs):
                for mb in range(8):
                    ins = eng.tensor_tensor(y_fm[:, mb, cs * 512:(cs + 1) * 512],
                                            y_fm[:, mb, cs * 512:(cs + 1) * 512],
                                            ps_gb[:, :], OP.mult)
                    if mb < 7:
                        eng.drain()
                sig(ins)

            ev3 = p.sec("vector", [b1, f2_toks], gmul)
            gm_toks.append(ev3)
        gm_tok = merge_deps(gm_toks)

        crow = p.sbuf("l3f_crow", [128, 8, 1032], BF16, ctx=sc_e)
        tpb = [(y_fm[:, mb, st * 128:(st + 1) * 128], crow[:, st, mb * 128:(mb + 1) * 128])
               for mb in range(8) for st in range(8)]
        ct_tok = emit_transpose_grid(p, [gm_tok], tpb, cst["identb"], BF16,
                                     tag=f"l3fct{j}")

        def gapp(eng, sig, j=j):
            eng.memset(crow[:, :, 1025:1032], 0.0)
            eng.drain()
            for st in range(8):
                ins = eng.tensor_copy(crow[:, st, 1024:1025], gcol[:, st:st + 1])
                if st < 7:
                    eng.drain()
            sig(ins)

        ga_tok = p.sec("vector", [ct_tok, ge_tok], gapp)

        def cdma(eng, sig, j=j):
            dst = io["l3f_con_dram"][j * 1024:(j + 1) * 1024, :]
            sig(eng.dma_start(out=dst.rearrange("(st pp) c -> pp st c", pp=128),
                              in_=crow[:, :, :]), dma=True)

        con_toks.append(p.sec("sync", [ga_tok], cdma, dma=True))
        p.close_scope(sc_e, con_toks[-1])
    con_tok = merge_deps(con_toks)
    p.close_scope(sc_g2, con_tok)

    ag4 = emit_ag(p, [con_tok], io["l3f_con_dram"], io["l3f_conall_dram"], single,
                  "agcon")
    ag5 = emit_ag(p, [id_tok], io["l3f_idxf_dram"], io["l3f_idxall_dram"], single,
                  "agidx")

    sc_s = p.scope()
    zt = p.sbuf("l3f_zt", [128, 16, 1032], F32, ctx=sc_s)
    crld = p.sbuf("l3f_crld", [128, 1032], BF16, ctx=sc_s)
    crf = p.sbuf("l3f_crf", [128, 1032], F32, ctx=sc_s)
    lidxf = p.sbuf("l3f_lidxf", [128, 1], F32, ctx=sc_s)
    lidxu = p.sbuf("l3f_lidxu", [128, 1], U32, ctx=sc_s)
    negf = p.sbuf("l3f_negf", [128, 1], F32, ctx=sc_s)
    icol = p.sbuf("l3f_icol", [128, 1], F32, ctx=sc_s)

    def zz(eng, sig):
        sig(eng.memset(zt[:, :, :], 0.0))

    z1 = p.sec("vector", [], zz)

    def zd(eng, sig):
        sig(eng.dma_start(out=io["l3f_acc_dram"].rearrange("(tc pp) c -> pp tc c", pp=128),
                          in_=zt[:, :, :]), dma=True)

    z2 = fence(p, p.sec("sync", [z1], zd, dma=True))

    prev_sc = z2
    for e in range(16):
        for st in range(8):
            def crl(eng, sig, e=e, st=st):
                r0 = e * 1024 + st * 128
                sig(eng.dma_start(out=crld[:, :],
                                  in_=io["l3f_conall_dram"][r0:r0 + 128, :]), dma=True)
                sig(eng.dma_start(out=icol[:, :],
                                  in_=io["l3f_idxall_dram"][r0:r0 + 128, :]), dma=True)

            c1 = p.sec("sync", [ag4, ag5, prev_sc], crl, dma=True)

            def lix(eng, sig, e=e, st=st):
                eng.tensor_copy(crf[:, :], crld[:, :])
                eng.drain()
                eng.tensor_copy(lidxf[:, :], icol[:, :])
                eng.drain()
                eng.tensor_scalar(lidxf[:, :], lidxf[:, :], io["mylo_sb"][:, 0:1], None,
                                  OP.subtract)
                eng.drain()
                eng.tensor_scalar(negf[:, :], lidxf[:, :], 0.0, 4e6, OP.is_lt, OP.mult)
                eng.drain()
                eng.tensor_tensor(lidxf[:, :], lidxf[:, :], negf[:, :], OP.add)
                eng.drain()
                sig(eng.tensor_copy(lidxu[:, :], lidxf[:, :]))

            l1 = p.sec("vector", [c1], lix)

            def sca(eng, sig):
                sig(ind_dma(eng, lambda: eng.indirect_dma_start(
                    out=io["l3f_acc_dram"], out_offset=IOA(ap=lidxu[:, :], axis=0),
                    in_=crf[:, :], in_offset=None, bounds_check=T - 1,
                    oob_is_err=False, compute_op=OP.add)), dma=True)

            prev_sc = p.sec("gpsimd", [l1], sca, dma="sw")
    sc_tok = prev_sc

    acct = p.sbuf("l3f_acct", [128, 2, 1034], F32, ctx=sc_s)
    outb = p.sbuf("l3f_outb", [128, 2, 1024], BF16, ctx=sc_s)
    fin_toks = []
    prev_f = [None, None]
    for tc in range(16):
        b = tc % 2

        def al(eng, sig, tc=tc, b=b):
            sig(eng.dma_start(out=acct[:, b, 0:1032],
                              in_=io["l3f_acc_dram"][tc * 128:(tc + 1) * 128, :]),
                dma=True)

        a1 = p.sec("sync", [sc_tok, prev_f[b], extra_dep], al, dma=True)

        def fz(eng, sig, tc=tc, b=b):
            eng.tensor_scalar(acct[:, b, 1033:1034], acct[:, b, 1024:1025], 0.0, None,
                              OP.is_le)
            eng.drain()
            eng.tensor_tensor(acct[:, b, 1024:1025], acct[:, b, 1024:1025],
                              acct[:, b, 1033:1034], OP.add)
            eng.drain()
            eng.reciprocal(acct[:, b, 1033:1034], acct[:, b, 1024:1025])
            eng.drain()
            sig(eng.tensor_scalar(outb[:, b, :], acct[:, b, 0:1024],
                                  acct[:, b, 1033:1034], None, OP.mult))

        f1 = p.sec("vector", [a1], fz)

        def fd(eng, sig, tc=tc, b=b):
            dst = y_out.rearrange("(tc pp) d -> pp tc d", pp=128)[:, tc, :]
            sig(eng.dma_start(out=dst, in_=outb[:, b, :], accum_op=OP.add), dma=True)

        prev_f[b] = p.sec("gpsimd", [f1], fd, dma="sw")
        fin_toks.append(prev_f[b])
    last = merge_deps(fin_toks)
    p.close_scope(sc_s, last)
    return last


def declare_io_l3(nc, stage, pref="", xap=None, yap=None):
    io = {"x": xap if xap is not None else
          nc.dram_tensor("x", [T, D], F32, kind="ExternalInput").ap(),
          "y": yap if yap is not None else
          nc.dram_tensor("y", [T, D], F32, kind="ExternalOutput").ap()}
    names = []
    if stage in ("attn", "full"):
        names += [("q1_w", [D, D]), ("q1bT", [128, 8]), ("k1_w", [D, D]),
                  ("k1bT", [128, 8]), ("k2_w", [D, D]), ("k2bsT", [128, 8]),
                  ("Wg2", [128, 128]), ("gwb2", [128, 1]), ("Ws2", [128, 128]),
                  ("swb2", [128, 1]), ("cp_w", [D, D]), ("cpbT", [128, 8])]
    if stage in ("ffn", "full"):
        names += [("nwT", [128, 8]), ("router_wT", [D, 16]), ("rbT", [16, 1]),
                  ("fc1bT", [128, 64]), ("fc2bT", [128, 16]), ("ehot", [128, 32]),
                  ("mylo", [128, 1]), ("fc1w", [2, 4096, D]), ("fc2w", [2, D, 4096])]
    for nm, shp in names:
        io[nm] = nc.dram_tensor(pref + nm, shp, F32, kind="ExternalInput").ap()
    if stage in ("ffn", "full"):
        io["own_rows"] = nc.dram_tensor(pref + "own_rows", [128, 1], U32,
                                        kind="ExternalInput").ap()
    return io


def host_prep_l3(inp, cid):
    d = {}
    d["q1_w"] = np.asarray(inp["L3_attn_q1_w"], np.float32)
    d["q1bT"] = _col128(inp["L3_attn_q1_b"])
    d["k1_w"] = np.asarray(inp["L3_attn_k1_w"], np.float32)
    d["k1bT"] = _col128(inp["L3_attn_k1_b"])
    d["k2_w"] = np.asarray(inp["L3_attn_k2_w"], np.float32)
    d["k2bsT"] = _col128(np.asarray(inp["L3_attn_k2_b"], np.float32) * (0.02 / 32.0))
    for nm, bnm, wk, bk in (("Wg2", "gwb2", "L3_attn_gw_w", "L3_attn_gw_b"),
                            ("Ws2", "swb2", "L3_attn_sw_w", "L3_attn_sw_b")):
        w = np.broadcast_to(np.asarray(inp[wk], np.float32), (64, 64))
        b = np.broadcast_to(np.asarray(inp[bk], np.float32), (64,))
        W2 = np.zeros((128, 128), np.float32)
        W2[0:64, 0:64] = w
        W2[64:128, 64:128] = w
        d[nm] = W2
        d[bnm] = np.tile(b, 2)[:, None].copy()
    d["cp_w"] = np.asarray(inp["L3_attn_cp_w"], np.float32)
    d["cpbT"] = _col128(inp["L3_attn_cp_b"])
    d["nwT"] = _col128(inp["L3_ffn_norm_weight"])
    d["router_wT"] = np.ascontiguousarray(
        np.asarray(inp["L3_ffn_router_weight"], np.float32).T)
    d["rbT"] = np.asarray(inp["L3_ffn_router_bias"], np.float32)[:, None]
    f1b = np.asarray(inp["L3_ffn_fc1_bias"], np.float32)
    f2b = np.asarray(inp["L3_ffn_fc2_bias"], np.float32)
    fc1bT = np.zeros((128, 64), np.float32)
    fc2bT = np.zeros((128, 16), np.float32)
    for j in range(2):
        e = 2 * cid + j
        fc1bT[:, j * 32:(j + 1) * 32] = f1b[e].reshape(32, 128).T
        fc2bT[:, j * 8:(j + 1) * 8] = f2b[e].reshape(8, 128).T
    d["fc1bT"] = fc1bT
    d["fc2bT"] = fc2bT
    d["fc1w"] = np.ascontiguousarray(
        np.asarray(inp["L3_ffn_fc1_weight"], np.float32)[2 * cid:2 * cid + 2])
    d["fc2w"] = np.ascontiguousarray(
        np.asarray(inp["L3_ffn_fc2_weight"], np.float32)[2 * cid:2 * cid + 2])
    orows = np.full((128, 1), 4000000000, np.uint32)
    for j in range(2):
        e = 2 * cid + j
        orows[32 * j:32 * j + 8, 0] = np.arange(8) * 16 + e
    d["own_rows"] = orows
    eh = np.zeros((128, 32), np.float32)
    eh[:, 2 * cid] = 1.0
    eh[:, 16 + 2 * cid + 1] = 1.0
    d["ehot"] = eh
    d["mylo"] = np.full((128, 1), cid * T, np.float32)
    return d


def build_l3(stage="full", single=False):
    nc = bass.Bass("TRN2", target_bir_lowering=False)
    consts = make_consts()
    make_l3_consts(consts)
    caps = declare_consts(nc, consts)
    io = declare_io_l3(nc, stage)
    io["caps"] = caps
    do_attn = stage in ("attn", "full")
    do_ffn = stage in ("ffn", "full")
    if do_attn:
        io["l3_q2p_dram"] = nc.dram_tensor("l3q2p", [8, 128, T], BF16).ap()
        io["l3_k2_dram"] = nc.dram_tensor("l3k2d", [8, 128, T], BF16).ap()
        io["l3_v_dram"] = nc.dram_tensor("l3vd", [T + 1, D], BF16).ap()
        io["l3_o1_dram"] = nc.dram_tensor("l3o1", [T, D], BF16).ap()
        io["l3a_out_dram"] = nc.dram_tensor("l3ao", [8, 128, T], BF16).ap()
    if do_ffn:
        io["l3f_xr_dram"] = nc.dram_tensor("l3fxr", [T, D], BF16).ap()
        io["l3f_xrall_dram"] = nc.dram_tensor("l3fxra", [8 * T, D], BF16).ap()
        io["l3f_pr_dram"] = nc.dram_tensor("l3fpr", [T, 16], F32).ap()
        io["l3f_prall_dram"] = nc.dram_tensor("l3fpra", [8 * T, 16], F32).ap()
        io["l3f_lgT_dram"] = nc.dram_tensor("l3flg", [16, T], F32).ap()
        io["l3f_lgall_dram"] = nc.dram_tensor("l3flga", [128, T], F32).ap()
        io["l3f_m_dram"] = nc.dram_tensor("l3fm", [128, T], F32).ap()
        io["l3f_pos_dram"] = nc.dram_tensor("l3fpos", [2, 8, T], F32).ap()
        io["l3f_tok_dram"] = nc.dram_tensor("l3ftok", [2, 8, T], F32).ap()
        io["l3f_idx_dram"] = nc.dram_tensor("l3fidx", [128, 16], U32).ap()
        io["l3f_idxf_dram"] = nc.dram_tensor("l3fidxf", [2048, 1], F32).ap()
        io["l3f_idxall_dram"] = nc.dram_tensor("l3fidxa", [16384, 1], F32).ap()
        io["l3f_g_dram"] = nc.dram_tensor("l3fgd", [2, 128, 8], F32).ap()
        io["l3f_con_dram"] = nc.dram_tensor("l3fcon", [2048, 1032], BF16).ap()
        io["l3f_conall_dram"] = nc.dram_tensor("l3fcona", [16384, 1032], BF16).ap()
        io["l3f_acc_dram"] = nc.dram_tensor("l3facc", [T, 1032], F32).ap()
    p = P(nc)
    small = []
    cstn = {}
    cnames = []
    if do_attn:
        cnames += ["l3_triu_incl", "l3_triu_strict"]
    if do_ffn:
        cnames += ["l3_emap", "l3_emapT", "l3_tri8", "l3_ones8", "l3f_slotiota",
                   "l3f_gid", "ones1x128_b"]
    for nm in cnames:
        import ml_dtypes
        dt = BF16 if consts[nm].dtype == ml_dtypes.bfloat16 else F32
        t_ = p.sbuf(nm + "S", list(consts[nm].shape), dt)
        cstn[nm + "_sb" if nm.startswith("l3") else nm] = t_
        small.append((nm, t_[:, :]))
    if do_attn:
        for nm in ("q1bT", "k1bT", "k2bsT", "gwb2", "swb2", "cpbT"):
            io[nm + "_sb"] = p.sbuf(nm + "S", list(io[nm].shape), F32)
            small.append((nm, io[nm + "_sb"][:, :]))
        io["Wg2_sb"] = p.sbuf("Wg2S", [128, 128], BF16)
        io["Ws2_sb"] = p.sbuf("Ws2S", [128, 128], BF16)
    if do_ffn:
        for nm in ("nwT", "rbT", "fc1bT", "fc2bT", "ehot", "mylo"):
            io[nm + "_sb"] = p.sbuf(nm + "S", list(io[nm].shape), F32)
            small.append((nm, io[nm + "_sb"][:, :]))
    sm_tok = load_const_tiles(p, dict(list(caps.items()) + list(io.items())), small)
    if do_attn:
        def wgl(eng, sig):
            sig(eng.dma_start(out=io["Wg2_sb"][:, :], in_=io["Wg2"]), dma=True)
            sig(eng.dma_start(out=io["Ws2_sb"][:, :], in_=io["Ws2"]), dma=True)

        io["wgs_tok"] = p.sec("gpsimd", [], wgl, dma="sw")
    xn_fm = p.sbuf("xn_fm", [128, NFC, T], BF16)
    scSX = p.scope()
    cst, x_fm, t_tok = emit_common_prelude(p, caps, io, xctx=scSX)
    cst.update(cstn)
    n_tok = emit_outer_ln(p, [t_tok, sm_tok], x_fm, xn_fm, cst)
    cp_tok = emit_copy_x_to_y(p, [t_tok], io["x"], io["y"])
    p.close_scope(scSX, merge_deps([n_tok, cp_tok]))
    a_tok = None
    if do_attn:
        at = emit_l3_attn(p, [n_tok, sm_tok], xn_fm, io, cst, single=single)
        a_tok = emit_fm_dram_add(p, [at, cp_tok], io["l3a_out_dram"], io["y"], cst,
                                 nMB=8, tag="l3aadd")
    if do_ffn:
        ms_tok, ag2, ag3 = emit_l3_ffn(p, [n_tok, sm_tok], xn_fm, io, cst,
                                       single=single)
        emit_l3_ffn_part2(p, io, cst, ms_tok, ag2, ag3, single, io["y"],
                          extra_dep=merge_deps([a_tok, cp_tok]))
    p.barrier_close()
    return nc, consts



def host_prep_l0(inp):
    perm = l0_qkv_perm()
    Wqkv = np.asarray(inp["L0_attn_W_qkv"], np.float32)
    bqkv = np.asarray(inp["L0_attn_b_qkv"], np.float32)
    return {"Wqkv": np.ascontiguousarray(Wqkv[:, perm]),
            "bqkvT": _col128(bqkv[perm]),
            "Wout": np.asarray(inp["L0_attn_W_out"], np.float32),
            "boutT": _col128(np.asarray(inp["L0_attn_b_out"], np.float32)),
            "wg": np.asarray(inp["L0_ffn_w_gate"], np.float32),
            "wu": np.asarray(inp["L0_ffn_w_up"], np.float32),
            "wd": np.asarray(inp["L0_ffn_w_down"], np.float32)}


def kernel_unfused(**inputs):
    x = np.asarray(inputs["L0_attn_x"], np.float32)
    acts = [np.ascontiguousarray(x[c]) for c in range(NCORES)]
    for build, prep in ((build_l0, host_prep_l0), (build_l1, host_prep_l1),
                        (build_l2, host_prep_l2)):
        nc, consts = build("full")
        shared = prep(inputs)
        inmaps = []
        for c in range(NCORES):
            im = dict(consts)
            im.update(shared)
            im["x"] = acts[c]
            inmaps.append(im)
        res = run_bass_kernel_spmd(nc, inmaps, core_ids=list(range(NCORES)))
        acts = [np.asarray(res.results[c]["y"], np.float32) for c in range(NCORES)]
    nc, consts = build_l3("full")
    inmaps = []
    for c in range(NCORES):
        im = dict(consts)
        im.update(host_prep_l3(inputs, c))
        im["x"] = acts[c]
        inmaps.append(im)
    res = run_bass_kernel_spmd(nc, inmaps, core_ids=list(range(NCORES)))
    return np.stack([np.asarray(res.results[c]["y"], np.float32)
                     for c in range(NCORES)])



def build_full():
    import ml_dtypes
    nc = bass.Bass("TRN2", target_bir_lowering=False)
    consts = make_consts()
    make_l1_consts(consts)
    make_l1f_consts(consts)
    consts["l1_masks"] = consts["l1_masks"].astype(ml_dtypes.bfloat16)
    consts["l1f_mask"] = consts["l1f_mask"].astype(ml_dtypes.bfloat16)
    blk = np.zeros((128, 2), np.float32)
    blk[0:64, 0] = 1.0
    blk[64:128, 1] = 1.0
    consts["blk64T_b"] = blk.astype(ml_dtypes.bfloat16)
    consts["blk64_f"] = np.ascontiguousarray(blk.T)
    make_l2_consts(consts)
    make_l3_consts(consts)
    caps = declare_consts(nc, consts)
    x_in = nc.dram_tensor("x", [T, D], F32, kind="ExternalInput").ap()
    y_out = nc.dram_tensor("y", [T, D], F32, kind="ExternalOutput").ap()
    act0 = nc.dram_tensor("act0", [T, D], F32).ap()
    act1 = nc.dram_tensor("act1", [T, D], F32).ap()
    act2 = nc.dram_tensor("act2", [T, D], F32).ap()
    io0 = declare_io_l0(nc, "full", pref="a_", xap=x_in, yap=act0)
    io1 = declare_io_l1(nc, "full", pref="b_", xap=act0, yap=act1)
    io2d = declare_io_l2(nc, "full", pref="c_", xap=act1, yap=act2)
    io3 = declare_io_l3(nc, "full", pref="d_", xap=act2, yap=y_out)
    p = P(nc)

    scL = p.scope()
    cst, x_fm, t_tok = emit_common_prelude(p, caps, io0, xctx=scL, cctx=scL)
    xq_fm = p.sbuf("l0xq_fm", [128, NFC, T], BF16, ctx=scL)
    ioA = {"Wqkv": io0["Wqkv"], "Wout": io0["Wout"]}
    ioA["bqkvT_sb"] = p.sbuf("bqkvT_sb", [128, 24], F32, ctx=scL)
    ioA["boutT_sb"] = p.sbuf("boutT_sb", [128, 8], F32, ctx=scL)
    b_tok = load_const_tiles(p, {"bqkvT": io0["bqkvT"], "boutT": io0["boutT"]},
                             [("bqkvT", ioA["bqkvT_sb"][:, :]),
                              ("boutT", ioA["boutT_sb"][:, :])])
    scA = p.scope()
    xn_fm = p.sbuf("xn_fm", [128, NFC, T], BF16, ctx=scA)
    scB = p.scope()
    mean_sb = p.sbuf("ln_mean", [128, T], F32, ctx=scB)
    rstd_sb = p.sbuf("ln_rstd", [128, T], F32, ctx=scB)
    n_tok = emit_outer_ln(p, [t_tok], x_fm, xn_fm, cst, mean_sb, rstd_sb)
    q_tok = emit_quant_x_fm(p, [n_tok], x_fm, mean_sb, rstd_sb, xq_fm)
    p.close_scope(scB, q_tok)
    x_tok = emit_l0_attn(p, [n_tok, b_tok], xn_fm, x_fm, ioA, cst, n_tok)
    p.close_scope(scA, merge_deps([x_tok, q_tok]))
    io0["wgq"] = nc.dram_tensor("wgq", [D, 2816], BF16).ap()
    io0["wuq"] = nc.dram_tensor("wuq", [D, 2816], BF16).ap()
    io0["wdq"] = nc.dram_tensor("wdq", [2816, D], BF16).ap()
    x_tok = emit_l0_ffn(p, [q_tok], xq_fm, x_fm, io0, cst, x_tok)
    st_tok = emit_store_y(p, [x_tok], x_fm, io0["y"], cst)
    p.close_scope(scL, st_tok)
    p.mem_tok = merge_deps([fence(p, [p.mem_tok, st_tok])])

    scL = p.scope()
    y0_dram = nc.dram_tensor("y0scr", [NFC, 128, T], F32).ap()
    io1["wqkv"] = [io1["wqkv0"], io1["wqkv1"]]
    io1["wo"] = [io1["wo0"], io1["wo1"]]
    io1["w1p"] = [io1["w1p0"], io1["w1p1"]]
    io1["w2p"] = [io1["w2p0"], io1["w2p1"]]
    io1["caps"] = caps
    small = []
    for nm in ("bqT", "bkT", "bvT", "boT", "hwT"):
        io1[nm + "_sb"] = p.sbuf(nm + "S", list(io1[nm].shape), F32, ctx=scL)
        small.append((nm, io1[nm + "_sb"][:, :]))
    for nm in ("qsT", "ksT", "lngT", "lnbT", "b1gT", "b1aT", "b2T"):
        t_ = p.sbuf(nm + "S", list(io1[nm].shape), F32, ctx=scL)
        cols = io1[nm].shape[1] // 2
        io1[nm + "_sb"] = [t_[:, il * cols:(il + 1) * cols] for il in range(2)]
        small.append((nm, t_[:, :]))
    cst_pre = {}
    cst_pre["blk64T_b"] = p.sbuf("blk64T", [128, 2], BF16, ctx=scL)
    cst_pre["blk64_f"] = p.sbuf("blk64f", [2, 128], F32, ctx=scL)
    cst_pre["l1_rot_sb"] = p.sbuf("l1rot", [128, 128], BF16, ctx=scL)
    small += [("blk64T_b", cst_pre["blk64T_b"][:, :]),
              ("blk64_f", cst_pre["blk64_f"][:, :]),
              ("l1_rot", cst_pre["l1_rot_sb"][:, :])]
    scSX = p.scope()
    cst, x_fm, t_tok = emit_common_prelude(p, caps, io1, xctx=scSX, cctx=scL)
    cst.update(cst_pre)
    sm_tok = load_const_tiles(p, dict(list(caps.items()) + list(io1.items())), small)
    scX = p.scope()
    xn_fm = p.sbuf("xn_fm", [128, NFC, T], BF16, ctx=scX)
    scMR = p.scope()
    mean_sb = p.sbuf("ln_mean", [128, T], F32, ctx=scMR)
    rstd_sb = p.sbuf("ln_rstd", [128, T], F32, ctx=scMR)
    n_tok = emit_outer_ln(p, [t_tok, sm_tok], x_fm, xn_fm, cst, mean_sb, rstd_sb)
    y0_tok = emit_y0_to_dram(p, [n_tok], x_fm, mean_sb, rstd_sb, y0_dram)
    p.close_scope(scMR, y0_tok)
    scATT = p.scope()
    cst["l1_cos"] = p.sbuf("l1cos", [128, T], BF16, ctx=scATT)
    cst["l1_sin"] = p.sbuf("l1sin", [128, T], BF16, ctx=scATT)
    cst["l1_masks_sb"] = p.sbuf("l1mask", [128, 8, 128], BF16, ctx=scATT)
    at_tok = load_const_tiles(p, caps, [("l1_cos", cst["l1_cos"][:, :]),
                                        ("l1_sin", cst["l1_sin"][:, :]),
                                        ("l1_masks", cst["l1_masks_sb"][:, :, :])])
    x_tok = emit_l1_attn(p, [n_tok, sm_tok, at_tok], xn_fm, x_fm, io1, cst, n_tok)
    p.close_scope(scATT, x_tok)
    p.close_scope(scX, merge_deps([x_tok, y0_tok]))
    st_tok = emit_store_y(p, [x_tok], x_fm, io1["y"], cst)
    p.close_scope(scSX, st_tok)
    scF = p.scope()
    cst2 = dict(cst_pre)
    for nm in ("identb", "identb_hi", "ones1x128_f", "ones128x1_b", "ident", "regs"):
        cst2[nm] = cst[nm]
    cst2["l1f_mask_sb"] = p.sbuf("l1fmask", [128, 4, 1024], BF16, ctx=scF)
    f_tok = load_const_tiles(p, caps, [("l1f_mask", cst2["l1f_mask_sb"][:, :, :])])
    f1_tok = emit_l1_ffn(p, [y0_tok, sm_tok, f_tok, st_tok], io1["y"], y0_dram,
                         io1, cst2, cst)
    p.close_scope(scF, f1_tok)
    p.close_scope(scL, f1_tok)
    p.mem_tok = merge_deps([fence(p, [p.mem_tok, f1_tok, st_tok])])

    scL = p.scope()
    io2d["caps"] = caps
    io2d["l2_res_dram"] = nc.dram_tensor("l2res", [16, 128, T], BF16).ap()
    io2d["l2_y_dram"] = nc.dram_tensor("l2y", [16, 128, T], BF16).ap()
    io2d["l2_attnout_dram"] = nc.dram_tensor("l2ao", [8, 128, T], BF16).ap()
    io2d["ccin"] = nc.dram_tensor("ccin", [128, 4], F32)
    io2d["ccout"] = nc.dram_tensor("ccout", [128, 4], F32)
    io2d["l2f_wm_dram"] = nc.dram_tensor("l2wm", [16, 128, 8, 1024], BF16).ap()
    io2d["l2f_out_dram"] = nc.dram_tensor("l2fo", [8, 128, T], BF16).ap()
    small = []
    for nm in ("cwT", "cbT", "bdtT", "AlogT", "DT"):
        io2d[nm + "_sb"] = p.sbuf(nm + "S", list(io2d[nm].shape), F32, ctx=scL)
        small.append((nm, io2d[nm + "_sb"][:, :]))
    cstn = {}
    for nm in ("l2_octmap", "l2_ymap", "l2_nmap"):
        t_ = p.sbuf(nm + "S", list(consts[nm].shape), BF16, ctx=scL)
        cstn[nm + "_sb"] = t_
        small.append((nm, t_[:, :, :] if len(consts[nm].shape) == 3 else t_[:, :]))
    io2d["wdt_sb"] = p.sbuf("wdtS", [128, 2048], BF16, ctx=scL)
    for nm in ("bqT", "bngT", "bnbT"):
        io2d[nm + "_sb"] = p.sbuf(nm + "S2", list(io2d[nm].shape), F32, ctx=scL)
        small.append((nm, io2d[nm + "_sb"][:, :]))
    sm_tok = load_const_tiles(p, dict(list(caps.items()) + list(io2d.items())), small)

    def wdtl(eng, sig):
        sig(eng.dma_start(out=io2d["wdt_sb"][:, :], in_=io2d["Wdt_pad"]), dma=True)

    io2d["wdt_tok"] = p.sec("gpsimd", [], wdtl, dma="sw")
    xn_fm = p.sbuf("xn_fm", [128, NFC, T], BF16, ctx=scL)
    scSX = p.scope()
    cst, x_fm, t_tok = emit_common_prelude(p, caps, io2d, xctx=scSX, cctx=scL)
    cst.update(cstn)
    n_tok = emit_outer_ln(p, [t_tok, sm_tok], x_fm, xn_fm, cst)
    cp_tok = emit_copy_x_to_y(p, [t_tok], io2d["x"], io2d["y"])
    p.close_scope(scSX, merge_deps([n_tok, cp_tok]))
    at = emit_l2_attn(p, [n_tok, sm_tok, io2d["wdt_tok"]], xn_fm, io2d, cst,
                      io2d["y"])
    a_tok = emit_fm_dram_add(p, [at, cp_tok], io2d["l2_attnout_dram"], io2d["y"],
                             cst, nMB=8, tag="l2aadd")
    ft = emit_l2_ffn(p, [n_tok, sm_tok], xn_fm, io2d, cst, single=False)
    f_tok = emit_fm_dram_add(p, [ft, cp_tok, a_tok], io2d["l2f_out_dram"],
                             io2d["y"], cst, nMB=8, tag="l2fadd")
    p.close_scope(scL, merge_deps([f_tok, a_tok]))
    p.mem_tok = merge_deps([fence(p, [p.mem_tok, f_tok, a_tok])])

    scL = p.scope()
    io3["caps"] = caps
    io3["l3_q2p_dram"] = nc.dram_tensor("l3q2p", [8, 128, T], BF16).ap()
    io3["l3_k2_dram"] = nc.dram_tensor("l3k2d", [8, 128, T], BF16).ap()
    io3["l3_v_dram"] = nc.dram_tensor("l3vd", [T + 1, D], BF16).ap()
    io3["l3_o1_dram"] = nc.dram_tensor("l3o1", [T, D], BF16).ap()
    io3["l3a_out_dram"] = nc.dram_tensor("l3ao", [8, 128, T], BF16).ap()
    io3["l3f_xr_dram"] = nc.dram_tensor("l3fxr", [T, D], BF16).ap()
    io3["l3f_xrall_dram"] = nc.dram_tensor("l3fxra", [8 * T, D], BF16).ap()
    io3["l3f_pr_dram"] = nc.dram_tensor("l3fpr", [T, 16], F32).ap()
    io3["l3f_prall_dram"] = nc.dram_tensor("l3fpra", [8 * T, 16], F32).ap()
    io3["l3f_lgT_dram"] = nc.dram_tensor("l3flg", [16, T], F32).ap()
    io3["l3f_lgall_dram"] = nc.dram_tensor("l3flga", [128, T], F32).ap()
    io3["l3f_m_dram"] = nc.dram_tensor("l3fm", [128, T], F32).ap()
    io3["l3f_pos_dram"] = nc.dram_tensor("l3fpos", [2, 8, T], F32).ap()
    io3["l3f_tok_dram"] = nc.dram_tensor("l3ftok", [2, 8, T], F32).ap()
    io3["l3f_idx_dram"] = nc.dram_tensor("l3fidx", [128, 16], U32).ap()
    io3["l3f_idxf_dram"] = nc.dram_tensor("l3fidxf", [2048, 1], F32).ap()
    io3["l3f_idxall_dram"] = nc.dram_tensor("l3fidxa", [16384, 1], F32).ap()
    io3["l3f_g_dram"] = nc.dram_tensor("l3fgd", [2, 8, 128], F32).ap()
    io3["l3f_con_dram"] = nc.dram_tensor("l3fcon", [2048, 1032], BF16).ap()
    io3["l3f_conall_dram"] = nc.dram_tensor("l3fcona", [16384, 1032], BF16).ap()
    io3["l3f_acc_dram"] = nc.dram_tensor("l3facc", [T, 1032], F32).ap()
    small = []
    cstn = {}
    for nm in ("l3_triu_incl", "l3_triu_strict", "l3_emap", "l3_emapT", "l3_tri8",
               "l3_ones8", "l3f_slotiota", "l3f_gid", "ones1x128_b"):
        dt = BF16 if consts[nm].dtype == ml_dtypes.bfloat16 else F32
        t_ = p.sbuf(nm + "S", list(consts[nm].shape), dt, ctx=scL)
        cstn[nm + "_sb" if nm.startswith("l3") else nm] = t_
        small.append((nm, t_[:, :]))
    for nm in ("q1bT", "k1bT", "k2bsT", "gwb2", "swb2", "cpbT"):
        io3[nm + "_sb"] = p.sbuf(nm + "S", list(io3[nm].shape), F32, ctx=scL)
        small.append((nm, io3[nm + "_sb"][:, :]))
    io3["Wg2_sb"] = p.sbuf("Wg2S", [128, 128], BF16, ctx=scL)
    io3["Ws2_sb"] = p.sbuf("Ws2S", [128, 128], BF16, ctx=scL)
    for nm in ("nwT", "rbT", "fc1bT", "fc2bT", "ehot", "mylo"):
        io3[nm + "_sb"] = p.sbuf(nm + "S", list(io3[nm].shape), F32, ctx=scL)
        small.append((nm, io3[nm + "_sb"][:, :]))
    sm_tok = load_const_tiles(p, dict(list(caps.items()) + list(io3.items())), small)

    def wgl(eng, sig):
        sig(eng.dma_start(out=io3["Wg2_sb"][:, :], in_=io3["Wg2"]), dma=True)
        sig(eng.dma_start(out=io3["Ws2_sb"][:, :], in_=io3["Ws2"]), dma=True)

    io3["wgs_tok"] = p.sec("gpsimd", [], wgl, dma="sw")
    xn_fm = p.sbuf("xn_fm", [128, NFC, T], BF16, ctx=scL)
    scSX = p.scope()
    cst, x_fm, t_tok = emit_common_prelude(p, caps, io3, xctx=scSX, cctx=scL)
    cst.update(cstn)
    n_tok = emit_outer_ln(p, [t_tok, sm_tok], x_fm, xn_fm, cst)
    cp_tok = emit_copy_x_to_y(p, [t_tok], io3["x"], io3["y"])
    p.close_scope(scSX, merge_deps([n_tok, cp_tok]))
    at = emit_l3_attn(p, [n_tok, sm_tok], xn_fm, io3, cst, single=False)
    a_tok = emit_fm_dram_add(p, [at, cp_tok], io3["l3a_out_dram"], io3["y"], cst,
                             nMB=8, tag="l3aadd")
    ms_tok, ag2, ag3 = emit_l3_ffn(p, [n_tok, sm_tok], xn_fm, io3, cst,
                                   single=False)
    emit_l3_ffn_part2(p, io3, cst, ms_tok, ag2, ag3, False, io3["y"],
                      extra_dep=merge_deps([a_tok, cp_tok]))
    p.barrier_close()
    return nc, consts


def kernel_fused(**inputs):
    x = np.asarray(inputs["L0_attn_x"], np.float32)
    nc, consts = build_full()
    h0 = host_prep_l0(inputs)
    h1 = host_prep_l1(inputs)
    h2 = host_prep_l2(inputs)
    inmaps = []
    for c in range(NCORES):
        im = dict(consts)
        for pref, hp in (("a_", h0), ("b_", h1), ("c_", h2),
                         ("d_", host_prep_l3(inputs, c))):
            for k, v in hp.items():
                im[pref + k] = v
        im["x"] = np.ascontiguousarray(x[c])
        inmaps.append(im)
    res = run_bass_kernel_spmd(nc, inmaps, core_ids=list(range(NCORES)))
    return np.stack([np.asarray(res.results[c]["y"], np.float32)
                     for c in range(NCORES)])


kernel = kernel_fused
